# version 13
# baseline (speedup 1.0000x reference)
import contextlib
import numpy as np
import ml_dtypes
import concourse.bass as bass
import concourse.mybir as mybir
from concourse.bass_utils import run_bass_kernel_spmd
from concourse.alu_op_type import AluOpType as OP

F32 = mybir.dt.float32
BF16 = mybir.dt.bfloat16
I32 = mybir.dt.int32
AF = mybir.ActivationFunctionType
AX = mybir.AxisListType.X

S, E, NH = 4096, 1024, 16
NT = 2048
K0 = 644
WQH = WKH = 1288
NE, EI, SHI = 16, 512, 1024
SLACK = 16 * WQH


class Sem:
    def __init__(self, nc, es, name):
        self.h = es.enter_context(nc.semaphore(name))
        self.n = 0


class Prog:
    def __init__(self, nc, es):
        self.nc = nc
        self.ops = {k: [] for k in "ygtvs"}
        self.sems = {}
        for nm in ("DY", "DG", "DS", "T", "V", "SS", "GP", "BR"):
            self.sems[nm] = Sem(nc, es, nm)
        self.nbar = 0

    def dma(self, q, out_ap, in_ap, waits=()):
        sem = self.sems[{"y": "DY", "g": "DG", "s": "DS"}[q]]
        sem.n += 16
        tgt = sem.n
        w = tuple(waits)

        def f(eng, o=out_ap, i=in_ap, w=w, sh=sem.h):
            for (s, v) in w:
                eng.wait_ge(s.h, v)
            eng.dma_start(out=o, in_=i).then_inc(sh, 16)
        self.ops[q].append(f)
        return tgt

    def op(self, q, fn, waits=(), inc=None):
        w = tuple(waits)
        tgt = None
        if inc is not None:
            inc.n += 1
            tgt = inc.n

        def f(eng, fn=fn, w=w, inc=inc):
            for (s, v) in w:
                eng.wait_ge(s.h, v)
            r = fn(eng)
            if inc is not None:
                if r is None:
                    r = eng.nop()
                r.then_inc(inc.h, 1)
        self.ops[q].append(f)
        return tgt

    def barrier(self):
        br = self.sems["BR"]
        self.nbar += 1
        tgt = self.nbar * 5
        br.n += 5
        for q in "ygtvs":
            def f(eng, t=tgt, h=br.h):
                eng.nop().then_inc(h, 1)
                eng.wait_ge(h, t)
            self.ops[q].append(f)


def classify(T0, m):
    d2 = 2 * T0 - 256 * m
    if abs(d2) <= 1024:
        return "g"
    return "p" if d2 >= 1280 else "f"




def _finish(nc, pr, es):
    DS = pr.sems["DS"]
    fin = DS.n

    def f(eng, fin=fin):
        eng.wait_ge(DS.h, fin)
        return None
    pr.op("s", f)
    pr.barrier()
    with nc.Block() as block:
        @block.sync
        def _(eng):
            for f2 in pr.ops["y"]:
                f2(eng)

        @block.gpsimd
        def _(eng):
            for f2 in pr.ops["g"]:
                f2(eng)

        @block.tensor
        def _(eng):
            for f2 in pr.ops["t"]:
                f2(eng)

        @block.vector
        def _(eng):
            for f2 in pr.ops["v"]:
                f2(eng)

        @block.scalar
        def _(eng):
            for f2 in pr.ops["s"]:
                f2(eng)
    es.close()
    return nc


def build_program(phases=9, dbg=False, nheads=NH, p2mode=5):
    nc = bass.Bass("TRN2", target_bir_lowering=False)
    es = contextlib.ExitStack()
    pr = Prog(nc, es)
    DY, DG, DS, T, V, SS, GP = (pr.sems[k] for k in ("DY", "DG", "DS", "T", "V", "SS", "GP"))

    def dt(name, shape, dtp, kind="ExternalInput"):
        return nc.dram_tensor(name, shape, dtp, kind=kind).ap()

    x_b = dt("x_b", [S, E], F32)
    wq_d = dt("wq_d", [E, E], F32); wk_d = dt("wk_d", [E, E], F32)
    wv_d = dt("wv_d", [E, E], F32); wo_d = dt("wo_d", [E, E], F32)
    bqr_d = dt("bqr_d", [1, E], F32); bkr_d = dt("bkr_d", [1, E], F32)
    ones512_d = dt("ones512_d", [1, 512], BF16)
    bv_row = dt("bv_row", [1, E], F32); bo_row = dt("bo_row", [1, E], F32)
    gam_row = dt("gam_row", [1, E], F32); bet_row = dt("bet_row", [1, E], F32)
    tq0_d = dt("tq0_d", [128, WQH], F32); tq1_d = dt("tq1_d", [128, WQH], F32)
    tk0_d = dt("tk0_d", [128, WKH], F32); tk1_d = dt("tk1_d", [128, WKH], F32)
    e0_d = dt("e0_d", [128, 512], F32); em_d = dt("em_d", [128, 512], F32)
    id_d = dt("id_d", [128, 128], BF16)
    onesf_d = dt("onesf_d", [1, 128], F32)
    onesb_d = dt("onesb_d", [1, 128], BF16)
    ones64_d = dt("ones64_d", [64, 1], BF16)
    gw_d = dt("gw_d", [E, NE], F32)
    gbr_d = dt("gbr_d", [1, 256], F32)
    wgT_d = dt("wgT_d", [NE, E, EI], F32)
    wuT_d = dt("wuT_d", [NE, E, EI], F32)
    wdT_d = dt("wdT_d", [NE, EI, E], F32)
    wsgT_d = dt("wsgT_d", [E, SHI], F32)
    wsuT_d = dt("wsuT_d", [E, SHI], F32)
    wsdT_d = dt("wsdT_d", [SHI, E], F32)
    outp = dt("outp", [NT, E], F32, kind="ExternalOutput")
    IN = "Internal"
    DBG = "ExternalOutput" if dbg else IN
    kt_sp = dt("kt_sp", [E, S], BF16, kind=DBG)
    qt_sp = dt("qt_sp", [E, NT], BF16, kind=DBG)
    v_sp = dt("v_sp", [S, E], BF16, kind=DBG)
    qdh_sp = dt("qdh_sp", [8 * NT * WQH + 4 * SLACK], BF16, kind=DBG)
    kdh_sp = dt("kdh_sp", [8 * NT * WKH + 4 * SLACK], BF16, kind=DBG)
    out1_sp = dt("out1_sp", [NT, E], F32, kind=DBG)
    actw_sp = dt("actw_sp", [NE * EI * NT], BF16, kind=DBG)
    gs_sp = dt("gs_sp", [SHI * NT], BF16, kind=DBG)
    cw_sp = dt("cw_sp", [NE * NT], BF16, kind=DBG)
    rt_dump = dt("rt_dump", [128, 1536], F32, kind=DBG) if dbg else None

    def sb(name, shape, dtp):
        return es.enter_context(nc.sbuf_tensor(name, shape, dtp))

    A = sb("A", [128, 36864], BF16)
    Bf = sb("Bf", [128, 8192], F32)
    C = sb("C", [128, 16384], BF16)
    fscr = sb("fscr", [128, 1024], F32)
    o1t = C[:, 9216:11264].bitcast(F32)
    ep1 = C[:, 11264:13312].bitcast(F32)
    ep2 = C[:, 13312:15360].bitcast(F32)
    bvb = sb("bvb", [128, E], BF16)
    bob = C[:, 0:2048].bitcast(F32)
    gamb = C[:, 2048:4096].bitcast(F32)
    betb = C[:, 4096:6144].bitcast(F32)
    sbqr = sb("sbqr", [1, E], BF16)
    sbkr = sb("sbkr", [1, E], BF16)
    sbvr = sb("sbvr", [1, E], BF16)
    ones512 = sb("ones512", [1, 512], BF16)
    dq = [sb(f"dq{r}", [128, WQH], BF16) for r in (0, 1)]
    dk = [sb(f"dk{r}", [128, WKH], BF16) for r in (0, 1)]
    se0 = sb("se0", [128, 512], BF16)
    sem_ = sb("sem_", [128, 512], BF16)
    ident = sb("ident", [128, 128], BF16)
    sonesf = sb("sonesf", [1, 128], F32)
    sonesb = sb("sonesb", [1, 128], BF16)
    sones64 = sb("sones64", [64, 1], BF16)
    sgw = sb("sgw", [128, 8 * NE], BF16)
    sgbr = sb("sgbr", [128, 256], F32)
    mxt = sb("mxt", [1, 16], F32)
    chs = sb("chs", [1, 8], F32)
    negC = sb("negC", [1, 1], F32)
    negCb = sb("negCb", [128, 1], F32)
    mxe = sb("mxe", [1, 1], F32)
    rcp = A[:, 34560:35584].bitcast(F32)[0:1, :]
    rcpb = A[:, 35584:36096][0:1, :]
    rcpB = A[64:128, 34560:35072]
    ctxT = sb("ctxT", [128, 8 * NT], BF16)
    colA = sb("colA", [128, 2], F32)
    colB = sb("colB", [128, 2], F32)
    colC = sb("colC", [128, 2], F32)
    rt_sc = A[:, 2048:2560].bitcast(F32)
    rt_sfc = A[:, 2560:3072].bitcast(F32)
    rt_msk = A[:, 3072:3584].bitcast(F32)
    rt_rank = A[:, 3584:4096].bitcast(F32)
    rt_eqr = A[:, 4096:4608].bitcast(F32)
    rt_sel = A[:, 4608:5120].bitcast(F32)
    rt_gs = sb("rt_gs", [128, 64], F32)
    rt_t = [sb(f"rt_t{i}", [128, 64], F32) for i in range(6)]
    rt_gm = sb("rt_gm", [128, 64], F32)
    rt_em = A[:, 5120:5632].bitcast(I32)
    rt_neg = A[:, 5632:6144].bitcast(F32)
    rt_w = A[:, 6144:6656].bitcast(F32)
    rt_ws = sb("rt_ws", [128, 16], F32)
    rt_wsi = sb("rt_wsi", [128, 16], F32)
    rt_cw = sb("rt_cw", [128, 256], BF16)
    rt_c1 = sb("rt_c1", [128, 16], BF16)
    logits = A[:, 6656:7168].bitcast(F32)
    cwB = [A[:, 32768 + i * 512:32768 + (i + 1) * 512] for i in (0, 1)]
    silb = A[:, 33792:34304]
    actb = A[:, 34304:34816]
    actwb = [A[:, 34816 + i * 512:34816 + (i + 1) * 512] for i in (0, 1)]

    def psal(name, shape):
        return es.enter_context(nc.psum_tensor(name, shape, F32))

    PS0 = psal("PS0", [128, 512])
    PS1 = psal("PS1", [128, 512])
    PSW = psal("PSW", [128, 1024])
    PSC = psal("PSC", [65, 2048])
    PSS = [PS0, PS1]

    def ln_chain(xt, outbuf, eps, out_bf=None, waits=(), center_into=None, par=0):
        tgt = center_into if center_into is not None else xt
        cA = colA[:, par:par + 1]
        cB = colB[:, par:par + 1]
        cC = colC[:, par:par + 1]

        def f(eng, xt=xt, tgt=tgt, cA=cA):
            eng.tensor_reduce(cA, xt, axis=AX, op=OP.add)
            eng.drain()
            eng.tensor_scalar(cA, cA, 1.0 / E, None, OP.mult)
            eng.drain()
            eng.tensor_scalar(tgt, xt, cA, None, OP.subtract)
            return eng.drain()
        v1 = pr.op("v", f, waits=list(waits), inc=V)

        def f(eng, tgt=tgt, cB=cB):
            eng.activation(fscr[:], tgt, AF.Square, accum_out=cB)
            return eng.drain()
        s1 = pr.op("s", f, waits=[(V, v1)], inc=SS)

        def f(eng, eps=eps, cB=cB):
            eng.tensor_scalar(cB, cB, 1.0 / E, eps, OP.mult, OP.add)
            return eng.drain()
        v2 = pr.op("v", f, waits=[(SS, s1)], inc=V)

        def f(eng, cB=cB, cC=cC):
            eng.activation(cC, cB, AF.Ln)
            eng.drain()
            eng.activation(cC, cC, AF.Exp, scale=-0.5)
            return eng.drain()
        s2 = pr.op("s", f, waits=[(V, v2)], inc=SS)
        return s2

    pr.dma("g", bvb[:], bv_row.partition_broadcast(128))
    pr.dma("g", sbqr[:], bqr_d)
    pr.dma("g", sbkr[:], bkr_d)
    pr.dma("g", sbvr[:], bv_row)
    pr.dma("y", ones512[:], ones512_d)
    for r in (0, 1):
        pr.dma("g", dq[r][:], (tq0_d, tq1_d)[r])
        pr.dma("g", dk[r][:], (tk0_d, tk1_d)[r])
    pr.dma("g", se0[:], e0_d)
    pr.dma("g", sem_[:], em_d)
    pr.dma("y", ident[:], id_d)
    pr.dma("y", sonesf[:], onesf_d)
    pr.dma("y", sonesb[:], onesb_d)
    pr.dma("y", sones64[:], ones64_d)
    pr.dma("g", sgw[:].rearrange("p (f e) -> p f e", e=NE),
           gw_d.rearrange("(f p) e -> p f e", p=128))
    pr.dma("y", sgbr[:], gbr_d.partition_broadcast(128))
    wq_v = A[:, 0:8192]; wk_v = A[:, 8192:16384]; wv_v = A[:, 16384:24576]
    for (w_in, w_sb) in ((wq_d, wq_v), (wk_d, wk_v), (wv_d, wv_v)):
        pr.dma("g", w_sb.rearrange("p (f c) -> p f c", f=8),
               w_in.rearrange("(f p) c -> p f c", p=128))
    dg_p0 = DG.n
    dy_p0 = DY.n
    pr.barrier()

    sqv = [A[0:64, 24576 + i * WKH:24576 + (i + 1) * WKH] for i in (0, 1)]

    def f(eng):
        eng.activation(sqv[0], dk[0][0:64, :], AF.Square)
        eng.drain()
        eng.activation(sqv[1], dk[1][0:64, :], AF.Square)
        return eng.drain()
    sqd = pr.op("s", f, waits=[(DG, dg_p0)], inc=SS)
    nmm = 0
    for r in (0, 1):
        for cc in range(3):
            w0 = cc * 512
            wid = min(512, WKH - w0)

            def f(eng, r=r, w0=w0, wid=wid):
                return eng.matmul(PS0[0:1, 0:wid], sones64[:], sqv[r][:, w0:w0 + wid],
                                  start=True, stop=True)
            tv = pr.op("t", f, waits=[(SS, sqd)] + ([(V, V.n)] if nmm else []), inc=T)

            def f(eng, i=nmm, wid=wid):
                eng.reduce_max(mxt[:, i:i + 1], PS0[0:1, 0:wid], axis=AX)
                return eng.drain()
            pr.op("v", f, waits=[(T, tv)], inc=V)
            nmm += 1

    def f(eng):
        eng.reduce_max(mxe[:], mxt[:, 0:6], axis=AX)
        return eng.drain()
    pr.op("v", f, inc=V)
    pr.barrier()

    if phases < 1:
        return _finish(nc, pr, es)
    ln1T = [A[:, 24576 + k * 4096:24576 + (k + 1) * 4096] for k in (0, 1)]
    lnrow = [A[:, 32768 + k * 1024:32768 + (k + 1) * 1024] for k in range(2)]
    stg = [A[:, 34816 + k * 512:34816 + (k + 1) * 512] for k in range(4)]
    xfree = {}
    psw_free = 0
    lnrow_free = {}
    pss_free = [0, 0]
    stg_free = {}
    kq_iter = 0
    for c in range(8):
        cb = c % 2
        ln_done = []
        for i in range(4):
            xt = Bf[:, (cb * 4 + i) * 1024:(cb * 4 + i + 1) * 1024]
            lw = [(V, xfree[(c - 2, i)])] if (c - 2, i) in xfree else []
            xl = pr.dma("y", xt, x_b[c * 512 + i * 128:c * 512 + (i + 1) * 128, :], waits=lw)
            lrs = lnrow[i % 2]
            lr_w = [(T, lnrow_free[i % 2])] if i % 2 in lnrow_free else []
            s2 = ln_chain(xt, None, 1e-5, waits=[(DY, xl)] + lr_w, par=i % 2)

            def f(eng, xt=xt, lrs=lrs, i=i):
                eng.tensor_scalar(lrs, xt, colC[:, i % 2:i % 2 + 1], None, OP.mult)
                return eng.drain()
            v3 = pr.op("v", f, waits=[(SS, s2)], inc=V)
            xfree[(c, i)] = v3

            def f(eng, lrs=lrs):
                r = None
                for j in range(8):
                    r = eng.matmul(PSW[:, j * 128:(j + 1) * 128], lrs[:, j * 128:(j + 1) * 128],
                                   ident[:], start=True, stop=True)
                return r
            t1 = pr.op("t", f, waits=[(V, v3), (DY, dy_p0)] + ([(V, psw_free)] if psw_free else []), inc=T)
            lnrow_free[i % 2] = t1

            def f(eng, cb=cb, i=i):
                src = PSW[:].rearrange("p (j t) -> p j t", t=128)
                dst = ln1T[cb].rearrange("p (j t) -> p j t", t=512)[:, :, i * 128:(i + 1) * 128]
                eng.tensor_copy(dst, src)
                return eng.drain()
            v4 = pr.op("v", f, waits=[(T, t1)], inc=V)
            psw_free = v4
            ln_done.append(v4)

        for proj in ("k", "q"):
            wmat = wk_v if proj == "k" else wq_v
            for fo in range(8):
                slot = kq_iter % 2
                kq_iter += 1
                pw = [(SS, pss_free[slot][1])] if pss_free[slot] else []

                bvec = sbkr if proj == "k" else sbqr

                def f(eng, wmat=wmat, fo=fo, cb=cb, slot=slot, bvec=bvec):
                    eng.matmul(PSS[slot][:], bvec[0:1, fo * 128:(fo + 1) * 128], ones512[:], start=True, stop=False,
                               skip_group_check=True)
                    r = None
                    for fi in range(8):
                        r = eng.matmul(PSS[slot][:],
                                       wmat[:, fi * 1024 + fo * 128:fi * 1024 + (fo + 1) * 128],
                                       ln1T[cb][:, fi * 512:(fi + 1) * 512],
                                       start=False, stop=(fi == 7), skip_group_check=True)
                    return r
                t2 = pr.op("t", f, waits=[(V, ln_done[3]), (DG, dg_p0)] + pw, inc=T)
                if proj == "k":
                    kw = [(DS, stg_free["k"])] if "k" in stg_free else []

                    def f(eng, fo=fo, slot=slot):
                        eng.copy(stg[0][:, 0:256], PSS[slot][:, 0::2])
                        return eng.copy(stg[1][:, 0:256], PSS[slot][:, 1::2])
                    v5 = pr.op("s", f, waits=[(T, t2)] + kw, inc=SS)
                    pr.dma("s", kt_sp[fo * 128:(fo + 1) * 128, 256 * c:256 * c + 256], stg[0][:, 0:256], waits=[(SS, v5)])
                    d2 = pr.dma("s", kt_sp[fo * 128:(fo + 1) * 128, 2048 + 256 * c:2048 + 256 * c + 256],
                                stg[1][:, 0:256], waits=[(SS, v5)])
                    stg_free["k"] = d2
                    pss_free[slot] = ("s", v5)
                else:
                    qw = [(DS, stg_free["q"])] if "q" in stg_free else []

                    def f(eng, fo=fo, slot=slot):
                        return eng.copy(stg[2][:, 0:256], PSS[slot][:, 0::2])
                    v5 = pr.op("s", f, waits=[(T, t2)] + qw, inc=SS)
                    d1 = pr.dma("s", qt_sp[fo * 128:(fo + 1) * 128, 256 * c:256 * c + 256], stg[2][:, 0:256], waits=[(SS, v5)])
                    stg_free["q"] = d1
                    pss_free[slot] = ("s", v5)
        for i in range(4):
            for half in range(2):
                slot = kq_iter % 2
                kq_iter += 1
                pw = [(SS, pss_free[slot][1])] if pss_free[slot] else []

                def f(eng, cb=cb, i=i, half=half, slot=slot):
                    eng.matmul(PSS[slot][:], sonesb[:], sbvr[0:1, half * 512:half * 512 + 512],
                               start=True, stop=False, skip_group_check=True)
                    r = None
                    for fi in range(8):
                        r = eng.matmul(PSS[slot][:],
                                       ln1T[cb][:, fi * 512 + i * 128:fi * 512 + (i + 1) * 128],
                                       wv_v[:, fi * 1024 + half * 512:fi * 1024 + half * 512 + 512],
                                       start=False, stop=(fi == 7), skip_group_check=True)
                    return r
                t3 = pr.op("t", f, waits=[(V, ln_done[i])] + pw, inc=T)
                vw = [(DS, stg_free["v"])] if "v" in stg_free else []

                def f(eng, half=half, slot=slot):
                    return eng.copy(stg[3], PSS[slot][:])
                v6 = pr.op("s", f, waits=[(T, t3)] + vw, inc=SS)
                d1 = pr.dma("s", v_sp[c * 512 + i * 128:c * 512 + (i + 1) * 128, half * 512:half * 512 + 512],
                            stg[3], waits=[(SS, v6)])
                stg_free["v"] = d1
                pss_free[slot] = ("s", v6)
    pr.barrier()

    if phases < 2:
        return _finish(nc, pr, es)
    kt_v = [A[:, 0:4096], A[:, 4096:8192]]
    qt_v = [A[:, 8192:10240], A[:, 10240:12288]]
    vh_v = [A[:, 12288:14368], A[:, 14368:16448]]
    gq = [A[:, 16448 + k * 512:16448 + (k + 1) * 512] for k in range(4)]
    gk = [A[:, 18496 + k * 512:18496 + (k + 1) * 512] for k in range(4)]
    probs = [A[:, 20544 + k * 2048:20544 + (k + 1) * 2048] for k in range(2)]
    sqq = A[:, 24640:26688]
    stgq = [A[:, 26688 + k * 2624:26688 + (k + 1) * 2624] for k in range(3)]
    stage = [Bf[:, k * 2048:(k + 1) * 2048] for k in range(2)]
    tiles = [(m, rho) for m in range(16) for rho in (0, 1)]

    st2 = {"it": 0, "cons": {}, "gq_free": {}, "gk_free": {}, "probs_free": {},
           "stage_free": {}, "stg_free": {}, "psw_ev": {}, "pswc": 0,
           "slot_T_done": {}, "slot_V_done": {}, "qdkd_done": {}, "psfree0": 0,
           "ctx_V_done": 0}

    def emit_loads(hd):
        slot = (hd // 2) % 2
        if hd % 2 == 0:
            w = []
            if hd >= 4:
                w = [(pr.sems["T"], st2["slot_T_done"][(hd // 2 - 2) % 2]),
                     (pr.sems["V"], st2["slot_V_done"][(hd // 2 - 2) % 2])]
            l_kt = pr.dma("y", kt_v[slot][:, :], kt_sp[64 * hd:64 * hd + 128, :], waits=w)
            l_qt = pr.dma("y", qt_v[slot][:, :], qt_sp[64 * hd:64 * hd + 128, :])
            st2.setdefault("plk", {})[slot] = (l_kt, l_qt)
        else:
            l_kt, l_qt = st2["plk"][slot]
        l_vh = DY.n
        vslot = hd % 2
        vw = []
        if hd >= 2:
            vw = [(pr.sems["T"], st2.get("vh_T_done", {}).get(vslot, 0))]
            vw = [x for x in vw if x[1]]
        for (m, rho) in tiles:
            blk = 2 * m + rho
            l_vh = pr.dma("y", vh_v[vslot][:, blk * 65:blk * 65 + 64],
                          v_sp[256 * m + rho:256 * m + rho + 255:2, 64 * hd:64 * hd + 64],
                          waits=vw if (m, rho) == (0, 0) else [])
        if hd < 2:
            def f(eng, vslot=vslot):
                eng.memset(vh_v[vslot].rearrange("p (b c) -> p b c", c=65)[:, :, 64:65], 1.0)
                return eng.drain()
            pr.op("v", f, waits=[(DY, l_vh)], inc=V)
        return l_kt, l_qt, l_vh

    def gen_qdkd(hd, l_kt, l_qt):
        pslt = (hd // 2) % 2
        sidx = 0
        for kind in ("q", "k"):
            WW = WQH if kind == "q" else WKH
            dtab = dq if kind == "q" else dk
            spd = qdh_sp if kind == "q" else kdh_sp
            statv = qt_v[pslt] if kind == "q" else kt_v[pslt]
            lwait = l_qt if kind == "q" else l_kt
            for rho in (0, 1):
                for tt in range(16):
                    coff = (rho * 2048 if kind == "k" else 0) + tt * 128
                    stgX = stgq[sidx % 3]
                    for ci, (c0, c1) in enumerate(((0, 512), (512, 1024), (1024, WW))):
                        wl = [(DY, lwait), (DG, dg_p0)]
                        if "pk" in st2["psw_ev"]:
                            es_, ev_ = st2["psw_ev"]["pk"]
                            wl.append((SS if es_ == "s" else V, ev_))
                        if hd >= 4 and ci == 0 and tt == 0 and rho == 0:
                            wl.append((pr.sems["T"], st2["slot_T_done"].get((hd // 2 - 2) % 2, 0)))
                            wl.append((V, st2["slot_V_done"].get((hd // 2 - 2) % 2, 0)))

                        def f(eng, statv=statv, coff=coff, dtab=dtab, rho=rho, c0=c0, c1=c1):
                            eng.matmul(PSW[:, 0:c1 - c0], statv[0:64, coff:coff + 128],
                                       dtab[rho][0:64, c0:c1], start=True, stop=True,
                                       tile_position=(0, 0))
                            return eng.matmul(PSW[:, 512:512 + c1 - c0], statv[64:128, coff:coff + 128],
                                              dtab[rho][64:128, c0:c1], start=True, stop=True,
                                              tile_position=(64, 0))
                        t4 = pr.op("t", f, waits=wl, inc=T)
                        eq = "v"

                        def f(eng, stgX=stgX, c0=c0, c1=c1, WW=WW):
                            dst = stgX[:, 0:2 * WW].rearrange("p (h w) -> p h w", h=2)[:, :, c0:c1]
                            srcv = PSW[:].rearrange("p (h w) -> p h w", h=2)[:, :, 0:c1 - c0]
                            return eng.tensor_copy(dst, srcv)
                        prev = st2["stg_free"].get(sidx % 3)
                        ev = pr.op(eq, f, waits=[(T, t4)] + ([(DS, prev)] if (prev and ci == 0) else []),
                                   inc=V)
                        st2["psw_ev"]["pk"] = (eq, ev)
                        if ci == 2:
                            evL = (eq, ev)
                    base = pslt * 4 * NT * WW + (rho * NT + tt * 128) * (2 * WW)
                    eqL, evLv = evL
                    d = pr.dma("s", spd[base:base + 128 * 2 * WW].rearrange("(p w) -> p w", w=2 * WW),
                               stgX[:, 0:2 * WW], waits=[(SS if eqL == "s" else V, evLv)])
                    st2["stg_free"][sidx % 3] = d
                    sidx += 1
                    yield
        st2["qdkd_done"][pslt] = DS.n

    def emit_chead(hd, l_kt, l_qt, l_vh, feed=None):
        slot = (hd // 2) % 2
        hh = 64 * (hd % 2)

        def fq(eng, slot=slot, hh=hh):
            eng.activation(sqq[0:64, 0:2048], qt_v[slot][hh:hh + 64, :], AF.Square)
            return eng.drain()
        sqd1 = pr.op("s", fq, waits=[(DY, l_qt), (SS, SS.n)], inc=SS)
        nmx = 0
        w0 = []
        if st2["psfree0"]:
            w0 = [(V, st2["psfree0"]), (SS, st2["cons_ss"])] if st2.get("cons_ss") else [(V, st2["psfree0"])]
        for cc in range(4):
            def f(eng, cc=cc, slot=slot):
                return eng.matmul(PS0[0:1, 0:512], sones64[:], sqq[0:64, cc * 512:(cc + 1) * 512],
                                  start=True, stop=True)
            tv = pr.op("t", f, waits=[(SS, sqd1), (V, V.n)] + (w0 if cc == 0 else []), inc=T)

            def f(eng, i=nmx):
                eng.reduce_max(mxt[:, i:i + 1], PS0[0:1, 0:512], axis=AX)
                return eng.drain()
            pr.op("v", f, waits=[(T, tv)], inc=V)
            nmx += 1
            if feed:
                next(feed, None)

        for half2 in range(2):
            def fk(eng, slot=slot, half2=half2, hh=hh):
                eng.activation(sqq[0:64, 0:2048], kt_v[slot][hh:hh + 64, half2 * 2048:half2 * 2048 + 2048], AF.Square)
                return eng.drain()
            sqd2 = pr.op("s", fk, waits=[(V, V.n), (DY, l_kt)], inc=SS)
            for cc in range(4):
                def f(eng, cc=cc):
                    return eng.matmul(PS0[0:1, 0:512], sones64[:], sqq[0:64, cc * 512:(cc + 1) * 512],
                                      start=True, stop=True)
                tv = pr.op("t", f, waits=[(SS, sqd2), (V, V.n)], inc=T)

                def f(eng, i=nmx):
                    eng.reduce_max(mxt[:, i:i + 1], PS0[0:1, 0:512], axis=AX)
                    return eng.drain()
                pr.op("v", f, waits=[(T, tv)], inc=V)
                nmx += 1
                if feed:
                    next(feed, None)

        def f(eng):
            eng.reduce_max(chs[:, 0:1], mxt[:, 0:4], axis=AX)
            eng.drain()
            eng.reduce_max(chs[:, 1:2], mxt[:, 4:12], axis=AX)
            eng.drain()
            eng.tensor_tensor(chs[:, 2:3], chs[:, 0:1], chs[:, 1:2], OP.mult)
            eng.drain()
            eng.tensor_tensor(chs[:, 3:4], chs[:, 0:1], mxe[:], OP.mult)
            eng.drain()
            eng.tensor_tensor(chs[:, 4:5], chs[:, 1:2], mxe[:], OP.mult)
            return eng.drain()
        vch = pr.op("v", f, inc=V)

        def f(eng):
            eng.activation(chs[:, 2:5], chs[:, 2:5], AF.Ln)
            eng.drain()
            eng.activation(chs[:, 2:5], chs[:, 2:5], AF.Exp, scale=0.5)
            return eng.drain()
        sch = pr.op("s", f, waits=[(V, vch)], inc=SS)

        def f(eng):
            eng.tensor_reduce(chs[:, 5:6], chs[:, 2:5], axis=AX, op=OP.add)
            eng.drain()
            eng.tensor_scalar(negC[:], chs[:, 5:6], -0.125, None, OP.mult)
            return eng.drain()
        vch2 = pr.op("v", f, waits=[(SS, sch)], inc=V)

        def f(eng):
            return eng.matmul(PS0[:, 0:1], sonesf[:], negC[:], start=True, stop=True)
        tch = pr.op("t", f, waits=[(V, vch2)], inc=T)

        def f(eng):
            eng.tensor_copy(negCb[:], PS0[:, 0:1])
            return eng.drain()
        return pr.op("v", f, waits=[(T, tch)], inc=V)

    def emit_scores(hd, ps0t_free_v, l_vh, feed=None):
        slot = (hd // 2) % 2
        vslot = hd % 2
        hh = 64 * (hd % 2)
        qdkd_done = st2["qdkd_done"][slot]
        cons = st2["cons"]
        gq_free = st2["gq_free"]
        gk_free = st2["gk_free"]
        probs_free = st2["probs_free"]
        stage_free2 = st2["stage_free"]
        first_tile = True
        v_ones = V.n
        pend_ctx = [None]

        def flush_ctx():
            if pend_ctx[0] is None:
                return
            fc, cw, tic = pend_ctx[0]
            pend_ctx[0] = None
            probs_free[tic] = pr.op("t", fc, waits=cw, inc=T)

        for (m, rho) in tiles:
            ti = 2 * m + rho
            if feed:
                next(feed, None)
                next(feed, None)
            pslot = ti % 2
            band_cols = []
            for cc in range(4):
                it = st2["it"]
                sl2 = it % 2
                gsl = it % 4
                cls = [classify(cc * 512 + s * 128, m) for s in range(4)]
                runs = []
                s = 0
                while s < 4:
                    e = s
                    while e + 1 < 4 and cls[e + 1] == cls[s]:
                        e += 1
                    runs.append((cls[s], s, e + 1))
                    s = e + 1
                wl = []
                if (it - 2) in cons:
                    cv, csv = cons[it - 2]
                    if cv:
                        wl.append((V, cv))
                    if csv:
                        wl.append((SS, csv))
                if first_tile:
                    wl.append((V, ps0t_free_v))
                g_run = [r for r in runs if r[0] == "g"]
                gq_t = gk_t = None
                if g_run:
                    _, gs_, ge_ = g_run[0]
                    ns = ge_ - gs_
                    T0a = cc * 512 + gs_ * 128
                    wg = [(DS, qdkd_done)]
                    if gsl in gq_free:
                        wg.append((pr.sems["T"], gq_free[gsl]))
                    baseq = slot * 4 * NT * WQH + rho * NT * 2 * WQH + hh * WQH // 64 \
                        + T0a * (2 * WQH - 1) + K0 + 128 * m
                    baseq = slot * 4 * NT * WQH + rho * NT * 2 * WQH + (hd % 2) * WQH \
                        + T0a * (2 * WQH - 1) + K0 + 128 * m
                    srcq = qdh_sp[baseq:baseq + ns * 128 * (2 * WQH - 1)] \
                        .rearrange("(p w) -> p w", w=2 * WQH - 1) \
                        .rearrange("(s p) w -> p s w", p=128)[:, :, 0:128]
                    gq_t = pr.dma("y", gq[gsl][:, 0:ns * 128].rearrange("p (s j) -> p s j", j=128),
                                  srcq, waits=wg)
                    wk2 = [(DS, qdkd_done)]
                    if gsl in gk_free:
                        wk2.append((V, gk_free[gsl]))
                    basek = slot * 4 * NT * WKH + rho * NT * 2 * WKH + (hd % 2) * WKH \
                        + 128 * m * (2 * WKH - 1) + K0 + T0a
                    srck = kdh_sp[basek:basek + 128 * (2 * WKH - 1)] \
                        .rearrange("(p w) -> p w", w=2 * WKH - 1)[0:128, 0:ns * 128]
                    gk_t = pr.dma("y", gk[gsl][:, 0:ns * 128], srck, waits=wk2)

                def f(eng, m=m, rho=rho, cc=cc, sl2=sl2, gsl=gsl, runs=runs, g_run=g_run, slot=slot, hh=hh):
                    ps = PSS[sl2]
                    ktt = kt_v[slot][hh:hh + 64, rho * 2048 + m * 128:rho * 2048 + (m + 1) * 128]
                    r = eng.matmul(ps[:], ktt, qt_v[slot][hh:hh + 64, cc * 512:(cc + 1) * 512],
                                   start=True, stop=False)
                    for (k_, s_, e_) in runs:
                        if k_ == "f":
                            r = eng.matmul(ps[:, s_ * 128:e_ * 128], ktt,
                                           se0[hh:hh + 64, 0:(e_ - s_) * 128], start=False, stop=False,
                                           skip_group_check=True)
                        elif k_ == "p":
                            r = eng.matmul(ps[:, s_ * 128:e_ * 128], ktt,
                                           sem_[hh:hh + 64, 0:(e_ - s_) * 128], start=False, stop=False,
                                           skip_group_check=True)
                    for (k_, s_, e_) in runs:
                        if k_ == "f":
                            r = eng.matmul(ps[:, s_ * 128:e_ * 128], se0[hh:hh + 64, 0:128],
                                           qt_v[slot][hh:hh + 64, cc * 512 + s_ * 128:cc * 512 + e_ * 128],
                                           start=False, stop=False, skip_group_check=True)
                        elif k_ == "p":
                            r = eng.matmul(ps[:, s_ * 128:e_ * 128], sem_[hh:hh + 64, 0:128],
                                           qt_v[slot][hh:hh + 64, cc * 512 + s_ * 128:cc * 512 + e_ * 128],
                                           start=False, stop=False, skip_group_check=True)
                    if g_run:
                        _, gs_, ge_ = g_run[0]
                        for s_ in range(gs_, ge_):
                            r = eng.matmul(ps[:, s_ * 128:(s_ + 1) * 128],
                                           gq[gsl][:, (s_ - gs_) * 128:(s_ - gs_ + 1) * 128],
                                           ident[:], start=False, stop=(s_ == ge_ - 1),
                                           skip_group_check=True)
                    return r
                wl2 = list(wl)
                if g_run and gq_t:
                    wl2.append((DY, gq_t))
                t5 = pr.op("t", f, waits=wl2, inc=T)
                if cc == 1:
                    flush_ctx()
                if g_run:
                    gq_free[gsl] = t5

                v_mark = 0
                s_mark = 0
                if g_run:
                    _, gs_, ge_ = g_run[0]
                    band_cols.append((cc * 512 + gs_ * 128, cc * 512 + ge_ * 128))
                    ws = [(T, t5)] + ([(DY, gk_t)] if gk_t else [])
                    if stage_free2.get(pslot):
                        ws.append((SS, stage_free2[pslot]))

                    def f(eng, cc=cc, gs_=gs_, ge_=ge_, sl2=sl2, gsl=gsl, pslot=pslot):
                        return eng.tensor_tensor(stage[pslot][:, cc * 512 + gs_ * 128:cc * 512 + ge_ * 128],
                                          PSS[sl2][:, gs_ * 128:ge_ * 128],
                                          gk[gsl][:, 0:(ge_ - gs_) * 128], OP.add)
                    v8 = pr.op("v", f, waits=ws, inc=V)
                    gk_free[gsl] = v8
                    v_mark = v8
                for (k_, s_, e_) in runs:
                    if k_ == "g":
                        continue
                    pw2 = [(T, t5)]
                    if v_mark:
                        pw2.append((V, v_mark))
                    if (ti - 2) in probs_free or hd > 0:
                        pw2.append((pr.sems["T"], probs_free.get(ti - 2, probs_free.get(30 + (ti % 2), 0)) or T.n - 1000 if False else probs_free.get(ti - 2, 0)))
                    pw2 = [x for x in pw2 if not (x[0] is pr.sems["T"] and x[1] == 0)]

                    def f(eng, cc=cc, s_=s_, e_=e_, sl2=sl2, pslot=pslot):
                        return eng.activation(probs[pslot][:, cc * 512 + s_ * 128:cc * 512 + e_ * 128],
                                       PSS[sl2][:, s_ * 128:e_ * 128], AF.Exp,
                                       bias=negCb[:], scale=0.125)
                    s_mark = pr.op("s", f, waits=pw2, inc=SS)
                cons[it] = (v_mark, s_mark)
                st2["it"] += 1

            it = st2["it"]
            if band_cols:
                lo = min(b[0] for b in band_cols)
                hi = max(b[1] for b in band_cols)
                vmax = max((cons[it - 1 - k][0] or 0) for k in range(4))
                pw3 = ([(V, vmax)] if vmax else [])
                if (ti - 2) in probs_free:
                    pw3.append((pr.sems["T"], probs_free[ti - 2]))

                def f(eng, lo=lo, hi=hi, pslot=pslot):
                    return eng.activation(probs[pslot][:, lo:hi], stage[pslot][:, lo:hi],
                                   AF.Exp, bias=negCb[:], scale=0.125)
                exp_val = pr.op("s", f, waits=pw3, inc=SS)
                stage_free2[pslot] = exp_val

            ctx_w = [(SS, SS.n), (DY, l_vh)]
            if first_tile:
                ctx_w.append((V, st2["ctx_V_done"]))

            def f(eng, blk=ti, first=first_tile, last=(ti == 31), vslot=vslot, pslot=pslot):
                r = None
                for ccq in range(4):
                    r = eng.matmul(PSC[0:65, ccq * 512:(ccq + 1) * 512],
                                   vh_v[vslot][:, blk * 65:blk * 65 + 65],
                                   probs[pslot][:, ccq * 512:(ccq + 1) * 512],
                                   start=first, stop=last, skip_group_check=True)
                return r
            pend_ctx[0] = (f, ctx_w, ti)
            first_tile = False
        flush_ctx()
        pf30, pf31 = probs_free.get(30, 0), probs_free.get(31, 0)
        probs_free.clear()
        probs_free[-2] = pf30
        probs_free[-1] = pf31
        st2["cons_ss"] = max((cons[st2["it"] - 1 - k][1] or 0) for k in range(2))

        tlast = T.n
        for ccq in range(4):
            if feed:
                next(feed, None)

            def f(eng, ccq=ccq):
                eng.reciprocal(rcp[:, 0:512], PSC[64:65, ccq * 512:(ccq + 1) * 512])
                eng.drain()
                eng.tensor_copy(rcpb[:, 0:512], rcp[:, 0:512])
                return eng.drain()
            v9 = pr.op("v", f, waits=[(T, tlast if ccq == 0 else T.n)], inc=V)

            def f(eng, ccq=ccq):
                return eng.matmul(PS0[0:64, :], sonesb[:, 0:64], rcpb[:, 0:512],
                                  start=True, stop=True)
            t7 = pr.op("t", f, waits=[(V, v9)], inc=T)

            def f(eng, ccq=ccq, hd=hd):
                eng.tensor_copy(rcpB, PS0[0:64, :])
                eng.drain()
                half = hd % 2
                pair = hd // 2
                eng.tensor_tensor(ctxT[half * 64:half * 64 + 64,
                                       pair * 2048 + ccq * 512:pair * 2048 + (ccq + 1) * 512],
                                  PSC[0:64, ccq * 512:(ccq + 1) * 512], rcpB, OP.mult)
                return eng.drain()
            fv = pr.op("v", f, waits=[(T, t7)], inc=V)
        st2["psfree0"] = fv
        st2["ctx_V_done"] = fv
        if hd % 2 == 1:
            st2["slot_T_done"][slot] = T.n
            st2["slot_V_done"][slot] = V.n
        st2.setdefault("vh_T_done", {})[vslot] = T.n

    st2["gens"] = []

    def feeder():
        while True:
            done1 = False
            while st2["gens"] and not done1:
                try:
                    next(st2["gens"][0])
                    done1 = True
                except StopIteration:
                    st2["gens"].pop(0)
            yield

    feed = feeder()
    pend = None
    for hd in range(nheads):
        lks = emit_loads(hd)
        if hd % 2 == 0:
            g = gen_qdkd(hd, lks[0], lks[1])
            if hd == 0:
                for _ in g:
                    pass
            else:
                st2["gens"].append(g)
        if pend is not None:
            ncb = emit_chead(pend[0], pend[1], pend[2], pend[3], feed=feed)
            emit_scores(pend[0], ncb, pend[3], feed=feed)
        pend = (hd, lks[0], lks[1], lks[2])
    ncb = emit_chead(pend[0], pend[1], pend[2], pend[3])
    emit_scores(pend[0], ncb, pend[3])
    for g in st2["gens"]:
        for _ in g:
            pass
    pr.barrier()


    wo_v = A[:, 0:8192]
    pr.dma("g", wo_v.rearrange("p (f c) -> p f c", f=8),
           wo_d.rearrange("(f p) c -> p f c", p=128))
    bc_ld = pr.dma("y", bob, bo_row.partition_broadcast(128))
    pr.dma("y", gamb, gam_row.partition_broadcast(128))
    bc_ld = pr.dma("y", betb, bet_row.partition_broadcast(128))
    psw_free_v = 0
    p3_last = {}
    for tt in range(16):
        par = tt % 2
        xt = Bf[:, par * 4096:par * 4096 + 1024]
        h2 = Bf[:, par * 4096 + 1024:par * 4096 + 2048]
        sc1_ = Bf[:, par * 4096 + 2048:par * 4096 + 3072]
        sc2_ = Bf[:, par * 4096 + 3072:par * 4096 + 4096]
        lw = [(V, p3_last[par])] if par in p3_last else []
        xl = pr.dma("y", xt, x_b[2 * tt * 128:2 * tt * 128 + 256:2, :], waits=lw)

        def f(eng, tt=tt):
            r = None
            for pp in range(8):
                for half in range(2):
                    r = eng.matmul(PSW[:, half * 512:half * 512 + 512],
                                   ctxT[:, pp * 2048 + tt * 128:pp * 2048 + (tt + 1) * 128],
                                   wo_v[:, pp * 1024 + half * 512:pp * 1024 + half * 512 + 512],
                                   start=(pp == 0), stop=(pp == 7), skip_group_check=True)
            return r
        wl = [(DG, DG.n)]
        if psw_free_v:
            wl.append((V, psw_free_v))
        t8 = pr.op("t", f, waits=wl, inc=T)

        s4 = ln_chain(xt, None, 1e-5, waits=[(DY, xl)], center_into=sc1_, par=par)

        def f(eng, xt=xt, h2=h2, sc1_=sc1_, par=par):
            eng.tensor_scalar(sc1_, sc1_, colC[:, par:par + 1], None, OP.mult)
            eng.drain()
            eng.tensor_tensor(h2, PSW[:], bob, OP.add)
            eng.drain()
            eng.tensor_tensor(h2, h2, sc1_, OP.add)
            return eng.drain()
        v12 = pr.op("v", f, waits=[(SS, s4), (T, t8), (DY, bc_ld)], inc=V)
        psw_free_v = v12

        s6 = ln_chain(h2, None, 1e-12, waits=[(V, v12)], par=par)

        def f(eng, h2=h2, sc2_=sc2_, par=par):
            eng.tensor_scalar(h2, h2, colC[:, par:par + 1], None, OP.mult)
            eng.drain()
            eng.tensor_tensor(sc2_, h2, gamb, OP.mult)
            eng.drain()
            eng.tensor_tensor(sc2_, sc2_, betb, OP.add)
            return eng.drain()
        v14 = pr.op("v", f, waits=[(SS, s6)], inc=V)

        s8 = ln_chain(sc2_, None, 1e-5, waits=[(V, v14)], par=par)

        def f(eng, xt=xt, sc2_=sc2_, h2=h2, par=par):
            eng.tensor_scalar(sc2_, sc2_, colC[:, par:par + 1], None, OP.mult)
            eng.drain()
            eng.tensor_tensor(h2, xt, sc2_, OP.add)
            return eng.drain()
        v16 = pr.op("v", f, waits=[(SS, s8)], inc=V)
        p3_last[par] = v16
        pr.dma("s", out1_sp[tt * 128:(tt + 1) * 128, :], h2, waits=[(V, v16)])
    pr.barrier()

    ln2T = C[:, 0:16384]
    lnrow2 = [A[:, 0:1024], A[:, 1024:2048]]
    psw_free_v = 0
    p4_last = {}
    for bb in range(16):
        par = bb % 2
        xt = Bf[:, par * 4096:par * 4096 + 1024]
        lw = [(V, p4_last[par])] if par in p4_last else [(V, V.n)] if bb == 0 else []
        xl = pr.dma("y", xt, out1_sp[bb:bb + 16 * 127 + 1:16, :], waits=lw)
        s10 = ln_chain(xt, None, 1e-5, waits=[(DY, xl)], par=par)

        def f(eng, xt=xt, par=par, bb=bb):
            eng.tensor_scalar(lnrow2[par], xt, colC[:, par:par + 1], None, OP.mult)
            return eng.drain()
        v19 = pr.op("v", f, waits=[(SS, s10)], inc=V)
        p4_last[par] = v19

        def f(eng, par=par):
            r = None
            for j in range(8):
                r = eng.matmul(PSW[:, j * 128:(j + 1) * 128], lnrow2[par][:, j * 128:(j + 1) * 128],
                               ident[:], start=True, stop=True)
            return r
        wl = [(V, v19)]
        if psw_free_v:
            wl.append((V, psw_free_v))
        t9 = pr.op("t", f, waits=wl, inc=T)

        def f(eng, bb=bb):
            src = PSW[:].rearrange("p (j t) -> p j t", t=128)
            dst = ln2T.rearrange("p (j t) -> p j t", t=2048)[:, :, bb::16]
            eng.tensor_copy(dst, src)
            return eng.drain()
        v20 = pr.op("v", f, waits=[(T, t9)], inc=V)
        psw_free_v = v20

        def f(eng, bb=bb):
            r = None
            l2v = ln2T.rearrange("p (j t) -> p j t", t=2048)
            for fi in range(8):
                r = eng.matmul(PS0[:, 0:16], l2v[:, fi, bb::16],
                               sgw[:, fi * 16:(fi + 1) * 16], start=(fi == 0), stop=(fi == 7))
            return r
        t10 = pr.op("t", f, waits=[(V, v20)], inc=T)

        def f(eng, bb=bb):
            eng.tensor_copy(logits[:, bb * 16:(bb + 1) * 16], PS0[:, 0:16])
            return eng.drain()
        pr.op("v", f, waits=[(T, t10)], inc=V)
    pr.barrier()

    if phases < 5:
        return _finish(nc, pr, es)
    def f(eng):
        eng.activation(rt_sc[:], logits[:], AF.Sigmoid)
        return eng.drain()
    s11 = pr.op("s", f, inc=SS)

    def f(eng):
        eng.memset(rt_neg[:], -1e30)
        eng.drain()
        eng.tensor_tensor(rt_sfc[:], rt_sc[:], sgbr[:], OP.add)
        eng.drain()
        g4 = rt_sfc.rearrange("p (x k) -> p x k", k=4)
        eng.tensor_tensor(rt_t[0][:], g4[:, :, 0], g4[:, :, 1], OP.max)
        eng.tensor_tensor(rt_t[1][:], g4[:, :, 0], g4[:, :, 1], OP.min)
        eng.tensor_tensor(rt_t[2][:], g4[:, :, 2], g4[:, :, 3], OP.max)
        eng.tensor_tensor(rt_t[3][:], g4[:, :, 2], g4[:, :, 3], OP.min)
        eng.drain()
        eng.tensor_tensor(rt_t[4][:], rt_t[0][:], rt_t[2][:], OP.max)
        eng.tensor_tensor(rt_t[5][:], rt_t[0][:], rt_t[2][:], OP.min)
        eng.drain()
        eng.tensor_tensor(rt_t[0][:], rt_t[1][:], rt_t[3][:], OP.max)
        eng.drain()
        eng.tensor_tensor(rt_t[5][:], rt_t[5][:], rt_t[0][:], OP.max)
        eng.drain()
        eng.tensor_tensor(rt_gs[:], rt_t[4][:], rt_t[5][:], OP.add)
        return eng.drain()
    v22 = pr.op("v", f, waits=[(SS, s11)], inc=V)

    def f(eng):
        gs4 = rt_gs.rearrange("p (x g) -> p x g", g=4)
        for g in range(4):
            dsts = []
            k = 0
            for g2 in range(4):
                if g2 == g:
                    continue
                eng.tensor_tensor(rt_t[k][:, 0:16], gs4[:, :, g2], gs4[:, :, g], OP.is_gt)
                dsts.append(k)
                k += 1
            for g2 in range(g):
                eng.tensor_tensor(rt_t[k][:, 0:16], gs4[:, :, g2], gs4[:, :, g], OP.is_equal)
                dsts.append(k)
                k += 1
            eng.drain()
            acc = dsts[0]
            for d2 in dsts[1:]:
                eng.tensor_tensor(rt_t[acc][:, 0:16], rt_t[acc][:, 0:16], rt_t[d2][:, 0:16], OP.add)
                eng.drain()
            eng.tensor_scalar(rt_gm[:, g * 16:(g + 1) * 16], rt_t[acc][:, 0:16], 1.5, None, OP.is_lt)
            eng.drain()
        for g in range(4):
            for kk in range(4):
                eng.tensor_copy(rt_em[:, 4 * g + kk::16], rt_gm[:, g * 16:(g + 1) * 16])
        eng.drain()
        eng.select(rt_msk[:], rt_em[:], rt_sfc[:], rt_neg[:], add_drain=True)
        return eng.drain()
    v23 = pr.op("v", f, waits=[(V, v22)], inc=V)

    def f(eng):
        for b in range(16):
            for e in range(NE):
                eng.tensor_scalar(Bf[:, (b * 16 + e) * 16:(b * 16 + e) * 16 + 16],
                                  rt_msk[:, b * 16:(b + 1) * 16],
                                  rt_msk[:, b * 16 + e:b * 16 + e + 1], None, OP.is_gt)
        eng.drain()
        for b in range(16):
            for e in range(NE):
                eng.tensor_reduce(rt_rank[:, b * 16 + e:b * 16 + e + 1],
                                  Bf[:, (b * 16 + e) * 16:(b * 16 + e) * 16 + 16], axis=AX, op=OP.add)
        eng.drain()
        eng.memset(rt_eqr[:], 0.0)
        eng.drain()
        for b in range(16):
            for e in range(1, NE):
                eng.tensor_scalar(Bf[:, (b * 16 + e) * 16:(b * 16 + e) * 16 + e],
                                  rt_msk[:, b * 16:b * 16 + e],
                                  rt_msk[:, b * 16 + e:b * 16 + e + 1], None, OP.is_equal)
        eng.drain()
        for b in range(16):
            for e in range(1, NE):
                eng.tensor_reduce(rt_eqr[:, b * 16 + e:b * 16 + e + 1],
                                  Bf[:, (b * 16 + e) * 16:(b * 16 + e) * 16 + e], axis=AX, op=OP.add)
        eng.drain()
        eng.tensor_tensor(rt_rank[:], rt_rank[:], rt_eqr[:], OP.add)
        eng.drain()
        eng.tensor_scalar(rt_sel[:], rt_rank[:], 3.5, None, OP.is_lt)
        eng.drain()
        eng.tensor_tensor(rt_w[:], rt_sc[:], rt_sel[:], OP.mult)
        eng.drain()
        eng.tensor_reduce(rt_ws[:], rt_w.rearrange("p (x e) -> p x e", e=16), axis=AX, op=OP.add)
        eng.drain()
        eng.tensor_scalar(rt_wsi[:], rt_ws[:], 1e-20, None, OP.add)
        eng.drain()
        eng.reciprocal(rt_ws[:], rt_wsi[:])
        eng.drain()
        for b in range(16):
            eng.tensor_scalar(rt_cw[:, b * 16:(b + 1) * 16], rt_w[:, b * 16:(b + 1) * 16],
                              rt_ws[:, b:b + 1], 2.5, OP.mult, OP.mult)
        return eng.drain()
    v24 = pr.op("v", f, waits=[(V, v23)], inc=V)
    if dbg:
        def f(eng):
            eng.tensor_copy(C[:, 0:512].bitcast(F32), logits[:])
            eng.tensor_copy(C[:, 512:1024].bitcast(F32), rt_sfc[:])
            eng.tensor_copy(C[:, 1024:1152].bitcast(F32), rt_gs[:])
            eng.tensor_copy(C[:, 1152:1280].bitcast(F32), rt_gm[:])
            eng.tensor_copy(C[:, 1536:2048].bitcast(F32), rt_rank[:])
            eng.tensor_copy(C[:, 2048:2560].bitcast(F32), rt_sel[:])
            return eng.drain()
        vdmp = pr.op("v", f, waits=[(V, v24)], inc=V)
        pr.dma("s", rt_dump[:, 0:256], C[:, 0:512].bitcast(F32), waits=[(V, vdmp)])
        pr.dma("s", rt_dump[:, 256:512], C[:, 512:1024].bitcast(F32), waits=[(V, vdmp)])
        pr.dma("s", rt_dump[:, 512:576], C[:, 1024:1152].bitcast(F32), waits=[(V, vdmp)])
        pr.dma("s", rt_dump[:, 640:704], C[:, 1152:1280].bitcast(F32), waits=[(V, vdmp)])
        pr.dma("s", rt_dump[:, 768:1024], C[:, 1536:2048].bitcast(F32), waits=[(V, vdmp)])
        pr.dma("s", rt_dump[:, 1024:1280], C[:, 2048:2560].bitcast(F32), waits=[(V, vdmp)])
    for e in range(NE):
        def f(eng, e=e):
            eng.tensor_copy(rt_c1[:], rt_cw[:, e::16])
            return eng.drain()
        vcp = pr.op("v", f, waits=[(V, v24)] + ([(DS, DS.n)] if e else []), inc=V)
        pr.dma("s", cw_sp[e * NT:(e + 1) * NT].rearrange("(p b) -> p b", b=16),
               rt_c1[:], waits=[(V, vcp)])
    pr.barrier()

    if phases < 6:
        return _finish(nc, pr, es)
    wsg_v = A[:, 16384:24576]
    wsu_v = A[:, 24576:32768]
    wg_v = [A[:, 0:4096], A[:, 4096:8192]]
    wu_v = [A[:, 8192:12288], A[:, 12288:16384]]
    pr.dma("g", wsg_v.rearrange("p (f c) -> p f c", f=8),
           wsgT_d.rearrange("(f p) c -> p f c", p=128))
    pr.dma("g", wsu_v.rearrange("p (f c) -> p f c", f=8),
           wsuT_d.rearrange("(f p) c -> p f c", p=128))
    wslot_free = {}
    ps0_free = 0
    ps1_free = 0
    actb_free = 0
    awb_free = {}
    last_t_of = {}
    cw_free = {}
    aidx = 0
    for e in range(NE + 1):
        shared = (e == NE)
        sl = e % 2
        nit = 8 if shared else 4
        if not shared:
            ww = [(T, last_t_of[e - 2])] if (e - 2) in last_t_of else []
            pr.dma("g", wg_v[sl].rearrange("p (f i) -> p f i", f=8),
                   wgT_d[e].rearrange("(f p) i -> p f i", p=128), waits=ww)
            wload = pr.dma("g", wu_v[sl].rearrange("p (f i) -> p f i", f=8),
                           wuT_d[e].rearrange("(f p) i -> p f i", p=128))
        last_t = 0
        for cc in range(4):
            if not shared:
                cwl = [(GP, cw_free[cc % 2])] if (cc % 2) in cw_free else []
                cwt = pr.dma("y", cwB[cc % 2],
                             cw_sp[e * NT + cc * 512:e * NT + (cc + 1) * 512]
                             .rearrange("(o t) -> o t", o=1).partition_broadcast(128),
                             waits=cwl)
            for ii in range(nit):
                wv1 = wsg_v if shared else wg_v[sl]
                wv2 = wsu_v if shared else wu_v[sl]
                wid = 1024 if shared else 512

                def f(eng, wv1=wv1, cc=cc, ii=ii, wid=wid):
                    r = None
                    for fi in range(8):
                        r = eng.matmul(PS0[:], wv1[:, fi * wid + ii * 128:fi * wid + ii * 128 + 128],
                                       ln2T[:, fi * 2048 + cc * 512:fi * 2048 + (cc + 1) * 512],
                                       start=(fi == 0), stop=(fi == 7))
                    return r
                wl = [] if shared else [(DG, wload)]
                if ps0_free:
                    wl.append((SS, ps0_free))
                tg = pr.op("t", f, waits=wl, inc=T)

                def f(eng, wv2=wv2, cc=cc, ii=ii, wid=wid):
                    r = None
                    for fi in range(8):
                        r = eng.matmul(PS1[:], wv2[:, fi * wid + ii * 128:fi * wid + ii * 128 + 128],
                                       ln2T[:, fi * 2048 + cc * 512:fi * 2048 + (cc + 1) * 512],
                                       start=(fi == 0), stop=(fi == 7))
                    return r
                wl = []
                if ps1_free:
                    wl.append((V, ps1_free))
                tu = pr.op("t", f, waits=wl, inc=T)
                last_t = tu

                def f(eng):
                    eng.activation(silb, PS0[:], AF.Silu)
                    return eng.drain()
                sg2 = pr.op("s", f, waits=[(T, tg)], inc=SS)
                ps0_free = sg2

                asl = aidx % 2
                aw = [(T, tu), (SS, sg2)]
                if shared and asl in awb_free:
                    aw.append((DS, awb_free[asl]))
                if not shared and actb_free:
                    aw.append((GP, actb_free))

                def f(eng, shared=shared, asl=asl):
                    dst = actwb[asl] if shared else actb
                    eng.scalar_tensor_tensor(dst, PS1[:], 1.0, silb, OP.mult, OP.mult)
                    return eng.drain()
                vv = pr.op("v", f, waits=aw, inc=V)
                ps1_free = vv

                if not shared:
                    gw2 = [(V, vv), (DY, cwt)]
                    if asl in awb_free:
                        gw2.append((DS, awb_free[asl]))

                    def f(eng, asl=asl, cc=cc):
                        eng.tensor_tensor(actwb[asl], actb, cwB[cc % 2], OP.mult)
                        return eng.drain()
                    gg = pr.op("g", f, waits=gw2, inc=GP)
                    cw_free[cc % 2] = gg
                    actb_free = gg
                    base = e * EI * NT + ii * 128 * NT
                    dd = pr.dma("s", actw_sp[base:base + 128 * NT]
                                .rearrange("(p t) -> p t", t=NT)[:, cc * 512:(cc + 1) * 512],
                                actwb[asl], waits=[(GP, gg)])
                else:
                    base = ii * 128 * NT
                    dd = pr.dma("s", gs_sp[base:base + 128 * NT]
                                .rearrange("(p t) -> p t", t=NT)[:, cc * 512:(cc + 1) * 512],
                                actwb[asl], waits=[(V, vv)])
                awb_free[asl] = dd
                aidx += 1
        if not shared:
            last_t_of[e] = last_t
    pr.barrier()

    if phases < 7:
        return _finish(nc, pr, es)
    y_ps_free = {}
    last_ty = 0
    prev_out = 0
    for half in range(2):
        wl0 = [(T, T.n)]
        for e in range(NE):
            pr.dma("g", A[:, e * 2048:(e + 1) * 2048].rearrange("p (i c) -> p i c", i=4),
                   wdT_d[e].rearrange("(i p) c -> p i c", p=128)[:, :, half * 512:half * 512 + 512],
                   waits=wl0 if e == 0 else [])
        dwl = pr.dma("g", A[:, 32768:36864].rearrange("p (i c) -> p i c", i=8),
                     wsdT_d.rearrange("(i p) c -> p i c", p=128)[:, :, half * 512:half * 512 + 512])
        for ts_ in range(16):
            for e in range(NE):
                pr.dma("y", C[:, e * 512:(e + 1) * 512].rearrange("p (i t) -> p i t", t=128),
                       actw_sp[e * EI * NT:(e + 1) * EI * NT]
                       .rearrange("(i p t) -> p i t", p=128, t=NT)[:, :, ts_ * 128:(ts_ + 1) * 128],
                       waits=[(T, last_ty)] if e == 0 else [])
            aload = pr.dma("y", C[:, 8192:9216].rearrange("p (i t) -> p i t", t=128),
                           gs_sp.rearrange("(i p t) -> p i t", p=128, t=NT)[:, :, ts_ * 128:(ts_ + 1) * 128])
            slot = ts_ % 2

            def f(eng, slot=slot):
                r = None
                first = True
                for e in range(NE):
                    for ii in range(4):
                        r = eng.matmul(PSS[slot][:], C[:, e * 512 + ii * 128:e * 512 + (ii + 1) * 128],
                                       A[:, e * 2048 + ii * 512:e * 2048 + (ii + 1) * 512],
                                       start=first, stop=False, skip_group_check=True)
                        first = False
                for jj in range(8):
                    r = eng.matmul(PSS[slot][:], C[:, 8192 + jj * 128:8192 + (jj + 1) * 128],
                                   A[:, 32768 + jj * 512:32768 + (jj + 1) * 512],
                                   start=False, stop=(jj == 7), skip_group_check=True)
                return r
            wl = [(DY, aload), (DG, dwl)]
            if (slot, "ps") in y_ps_free:
                wl.append((V, y_ps_free[(slot, "ps")]))
            ty = pr.op("t", f, waits=wl, inc=T)
            last_ty = ty

            if half == 0:
                def f(eng, ts_=ts_, slot=slot):
                    eng.tensor_copy(Bf[:, ts_ * 512:(ts_ + 1) * 512], PSS[slot][:])
                    return eng.drain()
                vv = pr.op("v", f, waits=[(T, ty)], inc=V)
                y_ps_free[(slot, "ps")] = vv
            else:
                xl = pr.dma("y", o1t[:], out1_sp[ts_ * 128:(ts_ + 1) * 128, :], waits=[(V, V.n)])
                ep_w = [(DY, xl)]
                if prev_out:
                    ep_w.append((DS, prev_out))
                s12 = ln_chain(o1t, None, 1e-5, waits=ep_w, center_into=ep1)

                def f(eng, ts_=ts_, slot=slot):
                    eng.tensor_scalar(ep1[:], ep1[:], colC[:, 0:1], None, OP.mult)
                    eng.drain()
                    eng.tensor_tensor(ep2[:, 0:512], ep1[:, 0:512], Bf[:, ts_ * 512:(ts_ + 1) * 512], OP.add)
                    eng.drain()
                    eng.tensor_tensor(ep2[:, 512:1024], ep1[:, 512:1024], PSS[slot][:], OP.add)
                    return eng.drain()
                v25 = pr.op("v", f, waits=[(SS, s12), (T, ty)], inc=V)
                y_ps_free[(slot, "ps")] = v25

                s13 = ln_chain(ep2, None, 1e-5, waits=[(V, v25)])

                def f(eng):
                    eng.tensor_scalar(ep2[:], ep2[:], colC[:, 0:1], None, OP.mult)
                    eng.drain()
                    eng.tensor_tensor(ep1[:], o1t[:], ep2[:], OP.add)
                    return eng.drain()
                v26 = pr.op("v", f, waits=[(SS, s13)], inc=V)
                prev_out = pr.dma("s", outp[ts_ * 128:(ts_ + 1) * 128, :], ep1[:], waits=[(V, v26)])

    return _finish(nc, pr, es)


def host_maps(inp):
    bf = ml_dtypes.bfloat16
    f32 = np.float32
    ac = np.ascontiguousarray
    x = np.asarray(inp["L0_attn_hidden_states"], dtype=f32)
    dist = np.asarray(inp["L0_attn_dist_emb"], dtype=f32)
    Wq = ac(np.asarray(inp["L0_attn_Wq"], f32)); Wk = ac(np.asarray(inp["L0_attn_Wk"], f32))
    Wv = ac(np.asarray(inp["L0_attn_Wv"], f32)); Wo = ac(np.asarray(inp["L0_attn_Wo"], f32))
    bq = np.asarray(inp["L0_attn_bq"], f32); bk = np.asarray(inp["L0_attn_bk"], f32)
    bv = np.asarray(inp["L0_attn_bv"], f32); bo = np.asarray(inp["L0_attn_bo"], f32)
    lng = np.asarray(inp["L0_attn_ln_gamma"], f32); lnb = np.asarray(inp["L0_attn_ln_beta"], f32)
    gw = np.asarray(inp["L0_ffn_gate_w"], f32)
    gbias = np.asarray(inp["L0_ffn_gate_bias"], f32)
    Wg = np.asarray(inp["L0_ffn_Wg"], f32)
    Wu = np.asarray(inp["L0_ffn_Wu"], f32)
    Wd = np.asarray(inp["L0_ffn_Wd"], f32)
    Wsg = np.asarray(inp["L0_ffn_Ws_g"], f32)
    Wsu = np.asarray(inp["L0_ffn_Ws_u"], f32)
    Wsd = np.asarray(inp["L0_ffn_Ws_d"], f32)

    w = np.arange(WQH)
    com = {
        "wq_d": Wq, "wk_d": Wk, "wv_d": Wv, "wo_d": Wo,
        "bqr_d": bq[None, :], "bkr_d": bk[None, :],
        "ones512_d": np.ones((1, 512), bf),
        "bv_row": bv[None, :], "bo_row": bo[None, :],
        "gam_row": lng[None, :], "bet_row": lnb[None, :],
        "tq0_d": ac(np.tile(dist[np.clip(2 * (K0 - w) + 1023, 0, 2046)].T, (2, 1))),
        "tk0_d": ac(np.tile(dist[np.clip(2 * (w - K0) + 1023, 0, 2046)].T, (2, 1))),
        "e0_d": ac(np.tile(np.tile(dist[0][:, None], (1, 512)), (2, 1))),
        "em_d": ac(np.tile(np.tile(dist[2046][:, None], (1, 512)), (2, 1))),
        "id_d": np.eye(128, dtype=bf),
        "onesf_d": np.ones((1, 128), f32),
        "onesb_d": np.ones((1, 128), bf),
        "ones64_d": np.ones((64, 1), bf),
        "gw_d": ac(gw.T),
        "gbr_d": ac(np.tile(gbias, 16)[None, :]),
        "wgT_d": ac(Wg.transpose(0, 2, 1)),
        "wuT_d": ac(Wu.transpose(0, 2, 1)),
        "wdT_d": ac(Wd.transpose(0, 2, 1)),
        "wsgT_d": ac(Wsg.T), "wsuT_d": ac(Wsu.T), "wsdT_d": ac(Wsd.T),
    }
    in_maps = []
    for core in range(8):
        b, h = core // 2, core % 2
        xp = x[b]
        if h:
            xp = xp.reshape(NT, 2, E)[:, ::-1, :].reshape(S, E)
        m = dict(com)
        m["x_b"] = ac(xp)
        m["tq1_d"] = ac(np.tile(dist[np.clip(2 * (K0 - w + h) + 1022, 0, 2046)].T, (2, 1)))
        m["tk1_d"] = ac(np.tile(dist[np.clip(2 * (w - K0 + h) + 1022, 0, 2046)].T, (2, 1)))
        in_maps.append(m)

    return in_maps


def kernel(**inp):
    in_maps = host_maps(inp)
    nc = build_program()
    res = run_bass_kernel_spmd(nc, in_maps, core_ids=list(range(8)))
    out = np.empty((4, S, E), np.float32)
    for core in range(8):
        b, h = core // 2, core % 2
        out[b, h::2] = np.asarray(res.results[core]["outp"], np.float32)
    return out


# version 14
# speedup vs baseline: 1.0758x; 1.0758x over previous
import contextlib
import numpy as np
import ml_dtypes
import concourse.bass as bass
import concourse.mybir as mybir
from concourse.bass_utils import run_bass_kernel_spmd
from concourse.alu_op_type import AluOpType as OP

F32 = mybir.dt.float32
BF16 = mybir.dt.bfloat16
I32 = mybir.dt.int32
AF = mybir.ActivationFunctionType
AX = mybir.AxisListType.X

S, E, NH = 4096, 1024, 16
NT = 2048
K0 = 644
WQH = WKH = 1288
NE, EI, SHI = 16, 512, 1024
SLACK = 16 * WQH


class Sem:
    def __init__(self, nc, es, name):
        self.h = es.enter_context(nc.semaphore(name))
        self.n = 0


class Prog:
    def __init__(self, nc, es):
        self.nc = nc
        self.ops = {k: [] for k in "ygtvs"}
        self.sems = {}
        for nm in ("DY", "DG", "DS", "T", "V", "SS", "GP", "BR"):
            self.sems[nm] = Sem(nc, es, nm)
        self.nbar = 0

    def dma(self, q, out_ap, in_ap, waits=()):
        sem = self.sems[{"y": "DY", "g": "DG", "s": "DS"}[q]]
        sem.n += 16
        tgt = sem.n
        w = tuple(waits)

        def f(eng, o=out_ap, i=in_ap, w=w, sh=sem.h):
            for (s, v) in w:
                eng.wait_ge(s.h, v)
            eng.dma_start(out=o, in_=i).then_inc(sh, 16)
        self.ops[q].append(f)
        return tgt

    def op(self, q, fn, waits=(), inc=None):
        w = tuple(waits)
        tgt = None
        if inc is not None:
            inc.n += 1
            tgt = inc.n

        def f(eng, fn=fn, w=w, inc=inc):
            for (s, v) in w:
                eng.wait_ge(s.h, v)
            r = fn(eng)
            if inc is not None:
                if r is None:
                    r = eng.nop()
                r.then_inc(inc.h, 1)
        self.ops[q].append(f)
        return tgt

    def barrier(self):
        br = self.sems["BR"]
        self.nbar += 1
        tgt = self.nbar * 5
        br.n += 5
        for q in "ygtvs":
            def f(eng, t=tgt, h=br.h):
                eng.nop().then_inc(h, 1)
                eng.wait_ge(h, t)
            self.ops[q].append(f)


def classify(T0, m):
    d2 = 2 * T0 - 256 * m
    if abs(d2) <= 1024:
        return "g"
    return "p" if d2 >= 1280 else "f"




def _finish(nc, pr, es):
    DS = pr.sems["DS"]
    fin = DS.n

    def f(eng, fin=fin):
        eng.wait_ge(DS.h, fin)
        return None
    pr.op("s", f)
    pr.barrier()
    with nc.Block() as block:
        @block.sync
        def _(eng):
            for f2 in pr.ops["y"]:
                f2(eng)

        @block.gpsimd
        def _(eng):
            for f2 in pr.ops["g"]:
                f2(eng)

        @block.tensor
        def _(eng):
            for f2 in pr.ops["t"]:
                f2(eng)

        @block.vector
        def _(eng):
            for f2 in pr.ops["v"]:
                f2(eng)

        @block.scalar
        def _(eng):
            for f2 in pr.ops["s"]:
                f2(eng)
    es.close()
    return nc


def build_program(phases=9, dbg=False, nheads=NH, p2mode=5):
    nc = bass.Bass("TRN2", target_bir_lowering=False)
    es = contextlib.ExitStack()
    pr = Prog(nc, es)
    DY, DG, DS, T, V, SS, GP = (pr.sems[k] for k in ("DY", "DG", "DS", "T", "V", "SS", "GP"))

    def dt(name, shape, dtp, kind="ExternalInput"):
        return nc.dram_tensor(name, shape, dtp, kind=kind).ap()

    x_b = dt("x_b", [S, E], F32)
    wq_d = dt("wq_d", [E, E], F32); wk_d = dt("wk_d", [E, E], F32)
    wv_d = dt("wv_d", [E, E], F32); wo_d = dt("wo_d", [E, E], F32)
    bqr_d = dt("bqr_d", [1, E], F32); bkr_d = dt("bkr_d", [1, E], F32)
    ones512_d = dt("ones512_d", [1, 512], BF16)
    bv_row = dt("bv_row", [1, E], F32); bo_row = dt("bo_row", [1, E], F32)
    gam_row = dt("gam_row", [1, E], F32); bet_row = dt("bet_row", [1, E], F32)
    tq0_d = dt("tq0_d", [128, WQH], F32); tq1_d = dt("tq1_d", [128, WQH], F32)
    tk0_d = dt("tk0_d", [128, WKH], F32); tk1_d = dt("tk1_d", [128, WKH], F32)
    e0_d = dt("e0_d", [128, 512], F32); em_d = dt("em_d", [128, 512], F32)
    id_d = dt("id_d", [128, 128], BF16)
    onesf_d = dt("onesf_d", [1, 128], F32)
    onesb_d = dt("onesb_d", [1, 128], BF16)
    ones64_d = dt("ones64_d", [64, 1], BF16)
    gw_d = dt("gw_d", [E, NE], F32)
    gbr_d = dt("gbr_d", [1, 256], F32)
    wgT_d = dt("wgT_d", [NE, E, EI], F32)
    wuT_d = dt("wuT_d", [NE, E, EI], F32)
    wdT_d = dt("wdT_d", [NE, EI, E], F32)
    wsgT_d = dt("wsgT_d", [E, SHI], F32)
    wsuT_d = dt("wsuT_d", [E, SHI], F32)
    wsdT_d = dt("wsdT_d", [SHI, E], F32)
    outp = dt("outp", [NT, E], F32, kind="ExternalOutput")
    IN = "Internal"
    DBG = "ExternalOutput" if dbg else IN
    kt_sp = dt("kt_sp", [E, S], BF16, kind=DBG)
    qt_sp = dt("qt_sp", [E, NT], BF16, kind=DBG)
    v_sp = dt("v_sp", [S, E], BF16, kind=DBG)
    qdh_sp = dt("qdh_sp", [32 * NT * WQH + 4 * SLACK], BF16, kind=DBG)
    kdh_sp = dt("kdh_sp", [32 * NT * WKH + 4 * SLACK], BF16, kind=DBG)
    out1_sp = dt("out1_sp", [NT, E], F32, kind=DBG)
    actw_sp = dt("actw_sp", [NE * EI * NT], BF16, kind=DBG)
    gs_sp = dt("gs_sp", [SHI * NT], BF16, kind=DBG)
    cw_sp = dt("cw_sp", [NE * NT], BF16, kind=DBG)
    rt_dump = dt("rt_dump", [128, 1536], F32, kind=DBG) if dbg else None

    def sb(name, shape, dtp):
        return es.enter_context(nc.sbuf_tensor(name, shape, dtp))

    A = sb("A", [128, 36864], BF16)
    Bf = sb("Bf", [128, 8192], F32)
    C = sb("C", [128, 16384], BF16)
    fscr = sb("fscr", [128, 1024], F32)
    o1t = C[:, 9216:11264].bitcast(F32)
    ep1 = C[:, 11264:13312].bitcast(F32)
    ep2 = C[:, 13312:15360].bitcast(F32)
    bvb = sb("bvb", [128, E], BF16)
    bob = C[:, 0:2048].bitcast(F32)
    gamb = C[:, 2048:4096].bitcast(F32)
    betb = C[:, 4096:6144].bitcast(F32)
    sbqr = sb("sbqr", [1, E], BF16)
    sbkr = sb("sbkr", [1, E], BF16)
    sbvr = sb("sbvr", [1, E], BF16)
    ones512 = sb("ones512", [1, 512], BF16)
    dq = [sb(f"dq{r}", [128, WQH], BF16) for r in (0, 1)]
    dk = [sb(f"dk{r}", [128, WKH], BF16) for r in (0, 1)]
    se0 = sb("se0", [128, 512], BF16)
    sem_ = sb("sem_", [128, 512], BF16)
    ident = sb("ident", [128, 128], BF16)
    sonesf = sb("sonesf", [1, 128], F32)
    sonesb = sb("sonesb", [1, 128], BF16)
    sones64 = sb("sones64", [64, 1], BF16)
    sgw = sb("sgw", [128, 8 * NE], BF16)
    sgbr = sb("sgbr", [128, 256], F32)
    mxt = sb("mxt", [1, 16], F32)
    chs = sb("chs", [1, 8], F32)
    negC = sb("negC", [1, 1], F32)
    negCb = sb("negCb", [128, 1], F32)
    mxe = sb("mxe", [1, 1], F32)
    rcp = A[:, 34560:35584].bitcast(F32)[0:1, :]
    rcpb = A[:, 35584:36096][0:1, :]
    rcpB = A[64:128, 34560:35072]
    ctxT = sb("ctxT", [128, 8 * NT], BF16)
    colA = sb("colA", [128, 2], F32)
    colB = sb("colB", [128, 2], F32)
    colC = sb("colC", [128, 2], F32)
    rt_sc = A[:, 2048:2560].bitcast(F32)
    rt_sfc = A[:, 2560:3072].bitcast(F32)
    rt_msk = A[:, 3072:3584].bitcast(F32)
    rt_rank = A[:, 3584:4096].bitcast(F32)
    rt_eqr = A[:, 4096:4608].bitcast(F32)
    rt_sel = A[:, 4608:5120].bitcast(F32)
    rt_gs = sb("rt_gs", [128, 64], F32)
    rt_t = [sb(f"rt_t{i}", [128, 64], F32) for i in range(6)]
    rt_gm = sb("rt_gm", [128, 64], F32)
    rt_em = A[:, 5120:5632].bitcast(I32)
    rt_neg = A[:, 5632:6144].bitcast(F32)
    rt_w = A[:, 6144:6656].bitcast(F32)
    rt_ws = sb("rt_ws", [128, 16], F32)
    rt_wsi = sb("rt_wsi", [128, 16], F32)
    rt_cw = sb("rt_cw", [128, 256], BF16)
    rt_c1 = sb("rt_c1", [128, 16], BF16)
    logits = A[:, 6656:7168].bitcast(F32)
    cwB = [A[:, 32768 + i * 512:32768 + (i + 1) * 512] for i in (0, 1)]
    silb = A[:, 33792:34304]
    actb = A[:, 34304:34816]
    actwb = [A[:, 34816 + i * 512:34816 + (i + 1) * 512] for i in (0, 1)]

    def psal(name, shape):
        return es.enter_context(nc.psum_tensor(name, shape, F32))

    PS0 = psal("PS0", [128, 512])
    PS1 = psal("PS1", [128, 512])
    PSW = psal("PSW", [128, 1024])
    PSC = psal("PSC", [65, 2048])
    PSS = [PS0, PS1]

    def ln_chain(xt, outbuf, eps, out_bf=None, waits=(), center_into=None, par=0):
        tgt = center_into if center_into is not None else xt
        cA = colA[:, par:par + 1]
        cB = colB[:, par:par + 1]
        cC = colC[:, par:par + 1]

        def f(eng, xt=xt, tgt=tgt, cA=cA):
            eng.tensor_reduce(cA, xt, axis=AX, op=OP.add)
            eng.drain()
            eng.tensor_scalar(cA, cA, 1.0 / E, None, OP.mult)
            eng.drain()
            eng.tensor_scalar(tgt, xt, cA, None, OP.subtract)
            return eng.drain()
        v1 = pr.op("v", f, waits=list(waits), inc=V)

        def f(eng, tgt=tgt, cB=cB):
            eng.activation(fscr[:], tgt, AF.Square, accum_out=cB)
            return eng.drain()
        s1 = pr.op("s", f, waits=[(V, v1)], inc=SS)

        def f(eng, eps=eps, cB=cB):
            eng.tensor_scalar(cB, cB, 1.0 / E, eps, OP.mult, OP.add)
            return eng.drain()
        v2 = pr.op("v", f, waits=[(SS, s1)], inc=V)

        def f(eng, cB=cB, cC=cC):
            eng.activation(cC, cB, AF.Ln)
            eng.drain()
            eng.activation(cC, cC, AF.Exp, scale=-0.5)
            return eng.drain()
        s2 = pr.op("s", f, waits=[(V, v2)], inc=SS)
        return s2

    pr.dma("g", bvb[:], bv_row.partition_broadcast(128))
    pr.dma("g", sbqr[:], bqr_d)
    pr.dma("g", sbkr[:], bkr_d)
    pr.dma("g", sbvr[:], bv_row)
    pr.dma("y", ones512[:], ones512_d)
    for r in (0, 1):
        pr.dma("g", dq[r][:], (tq0_d, tq1_d)[r])
        pr.dma("g", dk[r][:], (tk0_d, tk1_d)[r])
    pr.dma("g", se0[:], e0_d)
    pr.dma("g", sem_[:], em_d)
    pr.dma("y", ident[:], id_d)
    pr.dma("y", sonesf[:], onesf_d)
    pr.dma("y", sonesb[:], onesb_d)
    pr.dma("y", sones64[:], ones64_d)
    pr.dma("g", sgw[:].rearrange("p (f e) -> p f e", e=NE),
           gw_d.rearrange("(f p) e -> p f e", p=128))
    pr.dma("y", sgbr[:], gbr_d.partition_broadcast(128))
    wq_v = A[:, 0:8192]; wk_v = A[:, 8192:16384]; wv_v = A[:, 16384:24576]
    for (w_in, w_sb) in ((wq_d, wq_v), (wk_d, wk_v), (wv_d, wv_v)):
        pr.dma("g", w_sb.rearrange("p (f c) -> p f c", f=8),
               w_in.rearrange("(f p) c -> p f c", p=128))
    dg_p0 = DG.n
    dy_p0 = DY.n
    pr.barrier()

    sqv = [A[0:64, 24576 + i * WKH:24576 + (i + 1) * WKH] for i in (0, 1)]

    def f(eng):
        eng.activation(sqv[0], dk[0][0:64, :], AF.Square)
        eng.drain()
        eng.activation(sqv[1], dk[1][0:64, :], AF.Square)
        return eng.drain()
    sqd = pr.op("s", f, waits=[(DG, dg_p0)], inc=SS)
    nmm = 0
    for r in (0, 1):
        for cc in range(3):
            w0 = cc * 512
            wid = min(512, WKH - w0)

            def f(eng, r=r, w0=w0, wid=wid):
                return eng.matmul(PS0[0:1, 0:wid], sones64[:], sqv[r][:, w0:w0 + wid],
                                  start=True, stop=True)
            tv = pr.op("t", f, waits=[(SS, sqd)] + ([(V, V.n)] if nmm else []), inc=T)

            def f(eng, i=nmm, wid=wid):
                eng.reduce_max(mxt[:, i:i + 1], PS0[0:1, 0:wid], axis=AX)
                return eng.drain()
            pr.op("v", f, waits=[(T, tv)], inc=V)
            nmm += 1

    def f(eng):
        eng.reduce_max(mxe[:], mxt[:, 0:6], axis=AX)
        return eng.drain()
    pr.op("v", f, inc=V)
    pr.barrier()

    if phases < 1:
        return _finish(nc, pr, es)
    ln1T = [A[:, 24576 + k * 4096:24576 + (k + 1) * 4096] for k in (0, 1)]
    lnrow = [A[:, 32768 + k * 1024:32768 + (k + 1) * 1024] for k in range(2)]
    stg = [A[:, 34816 + k * 512:34816 + (k + 1) * 512] for k in range(4)]
    xfree = {}
    psw_free = 0
    lnrow_free = {}
    pss_free = [0, 0]
    stg_free = {}
    kq_iter = 0
    for c in range(8):
        cb = c % 2
        ln_done = []
        for i in range(4):
            xt = Bf[:, (cb * 4 + i) * 1024:(cb * 4 + i + 1) * 1024]
            lw = [(V, xfree[(c - 2, i)])] if (c - 2, i) in xfree else []
            xl = pr.dma("y", xt, x_b[c * 512 + i * 128:c * 512 + (i + 1) * 128, :], waits=lw)
            lrs = lnrow[i % 2]
            lr_w = [(T, lnrow_free[i % 2])] if i % 2 in lnrow_free else []
            s2 = ln_chain(xt, None, 1e-5, waits=[(DY, xl)] + lr_w, par=i % 2)

            def f(eng, xt=xt, lrs=lrs, i=i):
                eng.tensor_scalar(lrs, xt, colC[:, i % 2:i % 2 + 1], None, OP.mult)
                return eng.drain()
            v3 = pr.op("v", f, waits=[(SS, s2)], inc=V)
            xfree[(c, i)] = v3

            def f(eng, lrs=lrs):
                r = None
                for j in range(8):
                    r = eng.matmul(PSW[:, j * 128:(j + 1) * 128], lrs[:, j * 128:(j + 1) * 128],
                                   ident[:], start=True, stop=True)
                return r
            t1 = pr.op("t", f, waits=[(V, v3), (DY, dy_p0)] + ([(V, psw_free)] if psw_free else []), inc=T)
            lnrow_free[i % 2] = t1

            def f(eng, cb=cb, i=i):
                src = PSW[:].rearrange("p (j t) -> p j t", t=128)
                dst = ln1T[cb].rearrange("p (j t) -> p j t", t=512)[:, :, i * 128:(i + 1) * 128]
                eng.tensor_copy(dst, src)
                return eng.drain()
            v4 = pr.op("v", f, waits=[(T, t1)], inc=V)
            psw_free = v4
            ln_done.append(v4)

        for proj in ("k", "q"):
            wmat = wk_v if proj == "k" else wq_v
            for fo in range(8):
                slot = kq_iter % 2
                kq_iter += 1
                pw = [(SS, pss_free[slot][1])] if pss_free[slot] else []

                bvec = sbkr if proj == "k" else sbqr

                def f(eng, wmat=wmat, fo=fo, cb=cb, slot=slot, bvec=bvec):
                    eng.matmul(PSS[slot][:], bvec[0:1, fo * 128:(fo + 1) * 128], ones512[:], start=True, stop=False,
                               skip_group_check=True)
                    r = None
                    for fi in range(8):
                        r = eng.matmul(PSS[slot][:],
                                       wmat[:, fi * 1024 + fo * 128:fi * 1024 + (fo + 1) * 128],
                                       ln1T[cb][:, fi * 512:(fi + 1) * 512],
                                       start=False, stop=(fi == 7), skip_group_check=True)
                    return r
                t2 = pr.op("t", f, waits=[(V, ln_done[3]), (DG, dg_p0)] + pw, inc=T)
                if proj == "k":
                    kw = [(DS, stg_free["k"])] if "k" in stg_free else []

                    def f(eng, fo=fo, slot=slot):
                        eng.copy(stg[0][:, 0:256], PSS[slot][:, 0::2])
                        return eng.copy(stg[1][:, 0:256], PSS[slot][:, 1::2])
                    v5 = pr.op("s", f, waits=[(T, t2)] + kw, inc=SS)
                    pr.dma("s", kt_sp[fo * 128:(fo + 1) * 128, 256 * c:256 * c + 256], stg[0][:, 0:256], waits=[(SS, v5)])
                    d2 = pr.dma("s", kt_sp[fo * 128:(fo + 1) * 128, 2048 + 256 * c:2048 + 256 * c + 256],
                                stg[1][:, 0:256], waits=[(SS, v5)])
                    stg_free["k"] = d2
                    pss_free[slot] = ("s", v5)
                else:
                    qw = [(DS, stg_free["q"])] if "q" in stg_free else []

                    def f(eng, fo=fo, slot=slot):
                        return eng.copy(stg[2][:, 0:256], PSS[slot][:, 0::2])
                    v5 = pr.op("s", f, waits=[(T, t2)] + qw, inc=SS)
                    d1 = pr.dma("s", qt_sp[fo * 128:(fo + 1) * 128, 256 * c:256 * c + 256], stg[2][:, 0:256], waits=[(SS, v5)])
                    stg_free["q"] = d1
                    pss_free[slot] = ("s", v5)
        for i in range(4):
            for half in range(2):
                slot = kq_iter % 2
                kq_iter += 1
                pw = [(SS, pss_free[slot][1])] if pss_free[slot] else []

                def f(eng, cb=cb, i=i, half=half, slot=slot):
                    eng.matmul(PSS[slot][:], sonesb[:], sbvr[0:1, half * 512:half * 512 + 512],
                               start=True, stop=False, skip_group_check=True)
                    r = None
                    for fi in range(8):
                        r = eng.matmul(PSS[slot][:],
                                       ln1T[cb][:, fi * 512 + i * 128:fi * 512 + (i + 1) * 128],
                                       wv_v[:, fi * 1024 + half * 512:fi * 1024 + half * 512 + 512],
                                       start=False, stop=(fi == 7), skip_group_check=True)
                    return r
                t3 = pr.op("t", f, waits=[(V, ln_done[i])] + pw, inc=T)
                vw = [(DS, stg_free["v"])] if "v" in stg_free else []

                def f(eng, half=half, slot=slot):
                    return eng.copy(stg[3], PSS[slot][:])
                v6 = pr.op("s", f, waits=[(T, t3)] + vw, inc=SS)
                d1 = pr.dma("s", v_sp[c * 512 + i * 128:c * 512 + (i + 1) * 128, half * 512:half * 512 + 512],
                            stg[3], waits=[(SS, v6)])
                stg_free["v"] = d1
                pss_free[slot] = ("s", v6)
    pr.barrier()

    if phases < 2:
        return _finish(nc, pr, es)
    kt_v = [A[:, 0:4096], A[:, 4096:8192]]
    qt_v = [A[:, 8192:10240], A[:, 10240:12288]]
    vh_v = [A[:, 12288:14368], A[:, 14368:16448]]
    gq = [A[:, 16448 + k * 512:16448 + (k + 1) * 512] for k in range(4)]
    gk = [A[:, 18496 + k * 512:18496 + (k + 1) * 512] for k in range(4)]
    probs = [A[:, 20544 + k * 2048:20544 + (k + 1) * 2048] for k in range(2)]
    sqq = A[:, 24640:26688]
    stgq = [A[:, 26688 + k * 2624:26688 + (k + 1) * 2624] for k in range(3)]
    stage = [Bf[:, k * 2048:(k + 1) * 2048] for k in range(2)]
    tiles = [(m, rho) for m in range(16) for rho in (0, 1)]

    st2 = {"it": 0, "cons": {}, "gq_free": {}, "gk_free": {}, "probs_free": {},
           "stage_free": {}, "stg_free": {}, "psw_ev": {}, "pswc": 0,
           "slot_T_done": {}, "slot_V_done": {}, "qdkd_done": {}, "psfree0": 0,
           "ctx_V_done": 0}

    def emit_loads(hd, vh=True):
        slot = (hd // 2) % 2
        if hd % 2 == 0:
            w = []
            if hd >= 4:
                w = [(pr.sems["T"], st2["slot_T_done"].get((hd // 2 - 2) % 2, 0)),
                     (pr.sems["V"], st2["slot_V_done"].get((hd // 2 - 2) % 2, 0))]
                w = [x for x in w if x[1]]
            l_kt = pr.dma("y", kt_v[slot][:, :], kt_sp[64 * hd:64 * hd + 128, :], waits=w)
            l_qt = pr.dma("y", qt_v[slot][:, :], qt_sp[64 * hd:64 * hd + 128, :])
            st2.setdefault("plk", {})[slot] = (l_kt, l_qt)
        else:
            l_kt, l_qt = st2["plk"][slot]
        l_vh = DY.n
        vslot = hd % 2
        if not vh:
            return l_kt, l_qt, DY.n
        vw = []
        if hd >= 2:
            vw = [(pr.sems["T"], st2.get("vh_T_done", {}).get(vslot, 0))]
            vw = [x for x in vw if x[1]]
        for (m, rho) in tiles:
            blk = 2 * m + rho
            l_vh = pr.dma("y", vh_v[vslot][:, blk * 65:blk * 65 + 64],
                          v_sp[256 * m + rho:256 * m + rho + 255:2, 64 * hd:64 * hd + 64],
                          waits=vw if (m, rho) == (0, 0) else [])
        if hd < 2:
            def f(eng, vslot=vslot):
                eng.memset(vh_v[vslot].rearrange("p (b c) -> p b c", c=65)[:, :, 64:65], 1.0)
                return eng.drain()
            pr.op("v", f, waits=[(DY, l_vh)], inc=V)
        return l_kt, l_qt, l_vh

    def gen_qdkd(hd, l_kt, l_qt):
        pslt = hd // 2
        kslot = pslt % 2
        sidx = 0
        for kind in ("q", "k"):
            WW = WQH if kind == "q" else WKH
            dtab = dq if kind == "q" else dk
            spd = qdh_sp if kind == "q" else kdh_sp
            statv = qt_v[kslot] if kind == "q" else kt_v[kslot]
            lwait = l_qt if kind == "q" else l_kt
            for rho in (0, 1):
                for tt in range(16):
                    coff = (rho * 2048 if kind == "k" else 0) + tt * 128
                    stgX = stgq[sidx % 3]
                    for ci, (c0, c1) in enumerate(((0, 512), (512, 1024), (1024, WW))):
                        bp = st2["pswc"] % 2
                        st2["pswc"] += 1
                        wl = [(DY, lwait), (DG, dg_p0)]
                        if bp in st2["psw_ev"]:
                            wl.append((V, st2["psw_ev"][bp]))

                        def f(eng, statv=statv, coff=coff, dtab=dtab, rho=rho, c0=c0, c1=c1, bp=bp):
                            pa = PSW[:, 0:c1 - c0] if bp == 0 else PS0[:, 0:c1 - c0]
                            pb = PSW[:, 512:512 + c1 - c0] if bp == 0 else PS1[:, 0:c1 - c0]
                            eng.matmul(pa, statv[0:64, coff:coff + 128],
                                       dtab[rho][0:64, c0:c1], start=True, stop=True,
                                       tile_position=(0, 0))
                            return eng.matmul(pb, statv[64:128, coff:coff + 128],
                                              dtab[rho][64:128, c0:c1], start=True, stop=True,
                                              tile_position=(64, 0))
                        t4 = pr.op("t", f, waits=wl, inc=T)

                        def f(eng, stgX=stgX, c0=c0, c1=c1, WW=WW, bp=bp):
                            dst = stgX[:, 0:2 * WW].rearrange("p (h w) -> p h w", h=2)[:, :, c0:c1]
                            if bp == 0:
                                srcv = PSW[:].rearrange("p (h w) -> p h w", h=2)[:, :, 0:c1 - c0]
                                return eng.tensor_copy(dst, srcv)
                            eng.tensor_copy(stgX[:, c0:c1], PS0[:, 0:c1 - c0])
                            return eng.tensor_copy(stgX[:, WW + c0:WW + c1], PS1[:, 0:c1 - c0])
                        prev = st2["stg_free"].get(sidx % 3)
                        ev = pr.op("v", f, waits=[(T, t4)] + ([(DS, prev)] if (prev and ci == 0) else []),
                                   inc=V)
                        st2["psw_ev"][bp] = ev
                        if ci == 2:
                            evL = ev
                    base = pslt * 4 * NT * WW + (rho * NT + tt * 128) * (2 * WW)
                    d = pr.dma("s", spd[base:base + 128 * 2 * WW].rearrange("(p w) -> p w", w=2 * WW),
                               stgX[:, 0:2 * WW], waits=[(V, evL)])
                    st2["stg_free"][sidx % 3] = d
                    sidx += 1
                    yield
        st2["qdkd_done"][pslt] = DS.n
        st2["slot_T_done"][kslot] = T.n
        st2["slot_V_done"][kslot] = V.n

    def emit_chead(hd, l_kt, l_qt, l_vh, feed=None):
        slot = (hd // 2) % 2
        hh = 64 * (hd % 2)

        def fq(eng, slot=slot, hh=hh):
            eng.activation(sqq[0:64, 0:2048], qt_v[slot][hh:hh + 64, :], AF.Square)
            return eng.drain()
        sqd1 = pr.op("s", fq, waits=[(DY, l_qt), (SS, SS.n)], inc=SS)
        nmx = 0
        w0 = []
        if st2["psfree0"]:
            w0 = [(V, st2["psfree0"]), (SS, st2["cons_ss"])] if st2.get("cons_ss") else [(V, st2["psfree0"])]
        for cc in range(4):
            def f(eng, cc=cc, slot=slot):
                return eng.matmul(PS0[0:1, 0:512], sones64[:], sqq[0:64, cc * 512:(cc + 1) * 512],
                                  start=True, stop=True)
            tv = pr.op("t", f, waits=[(SS, sqd1), (V, V.n)] + (w0 if cc == 0 else []), inc=T)

            def f(eng, i=nmx):
                eng.reduce_max(mxt[:, i:i + 1], PS0[0:1, 0:512], axis=AX)
                return eng.drain()
            pr.op("v", f, waits=[(T, tv)], inc=V)
            nmx += 1
            if feed:
                next(feed, None)

        for half2 in range(2):
            def fk(eng, slot=slot, half2=half2, hh=hh):
                eng.activation(sqq[0:64, 0:2048], kt_v[slot][hh:hh + 64, half2 * 2048:half2 * 2048 + 2048], AF.Square)
                return eng.drain()
            sqd2 = pr.op("s", fk, waits=[(V, V.n), (DY, l_kt)], inc=SS)
            for cc in range(4):
                def f(eng, cc=cc):
                    return eng.matmul(PS0[0:1, 0:512], sones64[:], sqq[0:64, cc * 512:(cc + 1) * 512],
                                      start=True, stop=True)
                tv = pr.op("t", f, waits=[(SS, sqd2), (V, V.n)], inc=T)

                def f(eng, i=nmx):
                    eng.reduce_max(mxt[:, i:i + 1], PS0[0:1, 0:512], axis=AX)
                    return eng.drain()
                pr.op("v", f, waits=[(T, tv)], inc=V)
                nmx += 1
                if feed:
                    next(feed, None)

        def f(eng):
            eng.reduce_max(chs[:, 0:1], mxt[:, 0:4], axis=AX)
            eng.drain()
            eng.reduce_max(chs[:, 1:2], mxt[:, 4:12], axis=AX)
            eng.drain()
            eng.tensor_tensor(chs[:, 2:3], chs[:, 0:1], chs[:, 1:2], OP.mult)
            eng.drain()
            eng.tensor_tensor(chs[:, 3:4], chs[:, 0:1], mxe[:], OP.mult)
            eng.drain()
            eng.tensor_tensor(chs[:, 4:5], chs[:, 1:2], mxe[:], OP.mult)
            return eng.drain()
        vch = pr.op("v", f, inc=V)

        def f(eng):
            eng.activation(chs[:, 2:5], chs[:, 2:5], AF.Ln)
            eng.drain()
            eng.activation(chs[:, 2:5], chs[:, 2:5], AF.Exp, scale=0.5)
            return eng.drain()
        sch = pr.op("s", f, waits=[(V, vch)], inc=SS)

        def f(eng):
            eng.tensor_reduce(chs[:, 5:6], chs[:, 2:5], axis=AX, op=OP.add)
            eng.drain()
            eng.tensor_scalar(negC[:], chs[:, 5:6], -0.125, None, OP.mult)
            return eng.drain()
        vch2 = pr.op("v", f, waits=[(SS, sch)], inc=V)

        def f(eng):
            return eng.matmul(PS0[:, 0:1], sonesf[:], negC[:], start=True, stop=True)
        tch = pr.op("t", f, waits=[(V, vch2)], inc=T)

        def f(eng):
            eng.tensor_copy(negCb[:], PS0[:, 0:1])
            return eng.drain()
        return pr.op("v", f, waits=[(T, tch)], inc=V)

    def emit_scores(hd, ps0t_free_v, l_vh, feed=None):
        slot = (hd // 2) % 2
        vslot = hd % 2
        hh = 64 * (hd % 2)
        qdkd_done = st2["qdkd_done"][hd // 2]
        cons = st2["cons"]
        gq_free = st2["gq_free"]
        gk_free = st2["gk_free"]
        probs_free = st2["probs_free"]
        stage_free2 = st2["stage_free"]
        first_tile = True
        v_ones = V.n
        pend_ctx = [None]

        def flush_ctx():
            if pend_ctx[0] is None:
                return
            fc, cw, tic = pend_ctx[0]
            pend_ctx[0] = None
            probs_free[tic] = pr.op("t", fc, waits=cw, inc=T)

        for (m, rho) in tiles:
            ti = 2 * m + rho
            if feed:
                next(feed, None)
                next(feed, None)
            pslot = ti % 2
            band_cols = []
            for cc in range(4):
                it = st2["it"]
                sl2 = it % 2
                gsl = it % 4
                cls = [classify(cc * 512 + s * 128, m) for s in range(4)]
                runs = []
                s = 0
                while s < 4:
                    e = s
                    while e + 1 < 4 and cls[e + 1] == cls[s]:
                        e += 1
                    runs.append((cls[s], s, e + 1))
                    s = e + 1
                wl = []
                if (it - 2) in cons:
                    cv, csv = cons[it - 2]
                    if cv:
                        wl.append((V, cv))
                    if csv:
                        wl.append((SS, csv))
                if first_tile:
                    wl.append((V, ps0t_free_v))
                g_run = [r for r in runs if r[0] == "g"]
                gq_t = gk_t = None
                if g_run:
                    _, gs_, ge_ = g_run[0]
                    ns = ge_ - gs_
                    T0a = cc * 512 + gs_ * 128
                    wg = [(DS, qdkd_done)]
                    if gsl in gq_free:
                        wg.append((pr.sems["T"], gq_free[gsl]))
                    baseq = (hd // 2) * 4 * NT * WQH + rho * NT * 2 * WQH + (hd % 2) * WQH \
                        + T0a * (2 * WQH - 1) + K0 + 128 * m
                    srcq = qdh_sp[baseq:baseq + ns * 128 * (2 * WQH - 1)] \
                        .rearrange("(p w) -> p w", w=2 * WQH - 1) \
                        .rearrange("(s p) w -> p s w", p=128)[:, :, 0:128]
                    gq_t = pr.dma("y", gq[gsl][:, 0:ns * 128].rearrange("p (s j) -> p s j", j=128),
                                  srcq, waits=wg)
                    wk2 = [(DS, qdkd_done)]
                    if gsl in gk_free:
                        wk2.append((V, gk_free[gsl]))
                    basek = (hd // 2) * 4 * NT * WKH + rho * NT * 2 * WKH + (hd % 2) * WKH \
                        + 128 * m * (2 * WKH - 1) + K0 + T0a
                    srck = kdh_sp[basek:basek + 128 * (2 * WKH - 1)] \
                        .rearrange("(p w) -> p w", w=2 * WKH - 1)[0:128, 0:ns * 128]
                    gk_t = pr.dma("y", gk[gsl][:, 0:ns * 128], srck, waits=wk2)

                def f(eng, m=m, rho=rho, cc=cc, sl2=sl2, gsl=gsl, runs=runs, g_run=g_run, slot=slot, hh=hh):
                    ps = PSS[sl2]
                    ktt = kt_v[slot][hh:hh + 64, rho * 2048 + m * 128:rho * 2048 + (m + 1) * 128]
                    r = eng.matmul(ps[:], ktt, qt_v[slot][hh:hh + 64, cc * 512:(cc + 1) * 512],
                                   start=True, stop=False)
                    for (k_, s_, e_) in runs:
                        if k_ == "f":
                            r = eng.matmul(ps[:, s_ * 128:e_ * 128], ktt,
                                           se0[hh:hh + 64, 0:(e_ - s_) * 128], start=False, stop=False,
                                           skip_group_check=True)
                        elif k_ == "p":
                            r = eng.matmul(ps[:, s_ * 128:e_ * 128], ktt,
                                           sem_[hh:hh + 64, 0:(e_ - s_) * 128], start=False, stop=False,
                                           skip_group_check=True)
                    for (k_, s_, e_) in runs:
                        if k_ == "f":
                            r = eng.matmul(ps[:, s_ * 128:e_ * 128], se0[hh:hh + 64, 0:128],
                                           qt_v[slot][hh:hh + 64, cc * 512 + s_ * 128:cc * 512 + e_ * 128],
                                           start=False, stop=False, skip_group_check=True)
                        elif k_ == "p":
                            r = eng.matmul(ps[:, s_ * 128:e_ * 128], sem_[hh:hh + 64, 0:128],
                                           qt_v[slot][hh:hh + 64, cc * 512 + s_ * 128:cc * 512 + e_ * 128],
                                           start=False, stop=False, skip_group_check=True)
                    if g_run:
                        _, gs_, ge_ = g_run[0]
                        for s_ in range(gs_, ge_):
                            r = eng.matmul(ps[:, s_ * 128:(s_ + 1) * 128],
                                           gq[gsl][:, (s_ - gs_) * 128:(s_ - gs_ + 1) * 128],
                                           ident[:], start=False, stop=(s_ == ge_ - 1),
                                           skip_group_check=True)
                    return r
                wl2 = list(wl)
                if g_run and gq_t:
                    wl2.append((DY, gq_t))
                t5 = pr.op("t", f, waits=wl2, inc=T)
                if cc == 1:
                    flush_ctx()
                if g_run:
                    gq_free[gsl] = t5

                v_mark = 0
                s_mark = 0
                if g_run:
                    _, gs_, ge_ = g_run[0]
                    band_cols.append((cc * 512 + gs_ * 128, cc * 512 + ge_ * 128))
                    ws = [(T, t5)] + ([(DY, gk_t)] if gk_t else [])
                    if stage_free2.get(pslot):
                        ws.append((SS, stage_free2[pslot]))

                    def f(eng, cc=cc, gs_=gs_, ge_=ge_, sl2=sl2, gsl=gsl, pslot=pslot):
                        return eng.tensor_tensor(stage[pslot][:, cc * 512 + gs_ * 128:cc * 512 + ge_ * 128],
                                          PSS[sl2][:, gs_ * 128:ge_ * 128],
                                          gk[gsl][:, 0:(ge_ - gs_) * 128], OP.add)
                    v8 = pr.op("v", f, waits=ws, inc=V)
                    gk_free[gsl] = v8
                    v_mark = v8
                for (k_, s_, e_) in runs:
                    if k_ == "g":
                        continue
                    pw2 = [(T, t5)]
                    if v_mark:
                        pw2.append((V, v_mark))
                    if (ti - 2) in probs_free or hd > 0:
                        pw2.append((pr.sems["T"], probs_free.get(ti - 2, probs_free.get(30 + (ti % 2), 0)) or T.n - 1000 if False else probs_free.get(ti - 2, 0)))
                    pw2 = [x for x in pw2 if not (x[0] is pr.sems["T"] and x[1] == 0)]

                    def f(eng, cc=cc, s_=s_, e_=e_, sl2=sl2, pslot=pslot):
                        return eng.activation(probs[pslot][:, cc * 512 + s_ * 128:cc * 512 + e_ * 128],
                                       PSS[sl2][:, s_ * 128:e_ * 128], AF.Exp,
                                       bias=negCb[:], scale=0.125)
                    s_mark = pr.op("s", f, waits=pw2, inc=SS)
                cons[it] = (v_mark, s_mark)
                st2["it"] += 1

            it = st2["it"]
            if band_cols:
                lo = min(b[0] for b in band_cols)
                hi = max(b[1] for b in band_cols)
                vmax = max((cons[it - 1 - k][0] or 0) for k in range(4))
                pw3 = ([(V, vmax)] if vmax else [])
                if (ti - 2) in probs_free:
                    pw3.append((pr.sems["T"], probs_free[ti - 2]))

                def f(eng, lo=lo, hi=hi, pslot=pslot):
                    return eng.activation(probs[pslot][:, lo:hi], stage[pslot][:, lo:hi],
                                   AF.Exp, bias=negCb[:], scale=0.125)
                exp_val = pr.op("s", f, waits=pw3, inc=SS)
                stage_free2[pslot] = exp_val

            ctx_w = [(SS, SS.n), (DY, l_vh)]
            if first_tile:
                ctx_w.append((V, st2["ctx_V_done"]))

            def f(eng, blk=ti, first=first_tile, last=(ti == 31), vslot=vslot, pslot=pslot):
                r = None
                for ccq in range(4):
                    r = eng.matmul(PSC[0:65, ccq * 512:(ccq + 1) * 512],
                                   vh_v[vslot][:, blk * 65:blk * 65 + 65],
                                   probs[pslot][:, ccq * 512:(ccq + 1) * 512],
                                   start=first, stop=last, skip_group_check=True)
                return r
            pend_ctx[0] = (f, ctx_w, ti)
            first_tile = False
        flush_ctx()
        pf30, pf31 = probs_free.get(30, 0), probs_free.get(31, 0)
        probs_free.clear()
        probs_free[-2] = pf30
        probs_free[-1] = pf31
        st2["cons_ss"] = max((cons[st2["it"] - 1 - k][1] or 0) for k in range(2))

        tlast = T.n
        for ccq in range(4):
            if feed:
                next(feed, None)

            def f(eng, ccq=ccq):
                eng.reciprocal(rcp[:, 0:512], PSC[64:65, ccq * 512:(ccq + 1) * 512])
                eng.drain()
                eng.tensor_copy(rcpb[:, 0:512], rcp[:, 0:512])
                return eng.drain()
            v9 = pr.op("v", f, waits=[(T, tlast if ccq == 0 else T.n)], inc=V)

            def f(eng, ccq=ccq):
                return eng.matmul(PS0[0:64, :], sonesb[:, 0:64], rcpb[:, 0:512],
                                  start=True, stop=True)
            t7 = pr.op("t", f, waits=[(V, v9)], inc=T)

            def f(eng, ccq=ccq, hd=hd):
                eng.tensor_copy(rcpB, PS0[0:64, :])
                eng.drain()
                half = hd % 2
                pair = hd // 2
                eng.tensor_tensor(ctxT[half * 64:half * 64 + 64,
                                       pair * 2048 + ccq * 512:pair * 2048 + (ccq + 1) * 512],
                                  PSC[0:64, ccq * 512:(ccq + 1) * 512], rcpB, OP.mult)
                return eng.drain()
            fv = pr.op("v", f, waits=[(T, t7)], inc=V)
        st2["psfree0"] = fv
        st2["ctx_V_done"] = fv
        if hd % 2 == 1:
            st2["slot_T_done"][slot] = T.n
            st2["slot_V_done"][slot] = V.n
        st2.setdefault("vh_T_done", {})[vslot] = T.n

    for hd in range(0, nheads, 2):
        lks = emit_loads(hd, vh=False)
        for _ in gen_qdkd(hd, lks[0], lks[1]):
            pass
    pr.barrier()
    st2["slot_T_done"].clear()
    st2["slot_V_done"].clear()
    st2["plk"] = {}
    pend = None
    for hd in range(nheads):
        lks = emit_loads(hd)
        if pend is not None:
            ncb = emit_chead(pend[0], pend[1], pend[2], pend[3])
            emit_scores(pend[0], ncb, pend[3])
        pend = (hd, lks[0], lks[1], lks[2])
    ncb = emit_chead(pend[0], pend[1], pend[2], pend[3])
    emit_scores(pend[0], ncb, pend[3])
    pr.barrier()


    wo_v = A[:, 0:8192]
    pr.dma("g", wo_v.rearrange("p (f c) -> p f c", f=8),
           wo_d.rearrange("(f p) c -> p f c", p=128))
    bc_ld = pr.dma("y", bob, bo_row.partition_broadcast(128))
    pr.dma("y", gamb, gam_row.partition_broadcast(128))
    bc_ld = pr.dma("y", betb, bet_row.partition_broadcast(128))
    psw_free_v = 0
    p3_last = {}
    for tt in range(16):
        par = tt % 2
        xt = Bf[:, par * 4096:par * 4096 + 1024]
        h2 = Bf[:, par * 4096 + 1024:par * 4096 + 2048]
        sc1_ = Bf[:, par * 4096 + 2048:par * 4096 + 3072]
        sc2_ = Bf[:, par * 4096 + 3072:par * 4096 + 4096]
        lw = [(V, p3_last[par])] if par in p3_last else []
        xl = pr.dma("y", xt, x_b[2 * tt * 128:2 * tt * 128 + 256:2, :], waits=lw)

        def f(eng, tt=tt):
            r = None
            for pp in range(8):
                for half in range(2):
                    r = eng.matmul(PSW[:, half * 512:half * 512 + 512],
                                   ctxT[:, pp * 2048 + tt * 128:pp * 2048 + (tt + 1) * 128],
                                   wo_v[:, pp * 1024 + half * 512:pp * 1024 + half * 512 + 512],
                                   start=(pp == 0), stop=(pp == 7), skip_group_check=True)
            return r
        wl = [(DG, DG.n)]
        if psw_free_v:
            wl.append((V, psw_free_v))
        t8 = pr.op("t", f, waits=wl, inc=T)

        s4 = ln_chain(xt, None, 1e-5, waits=[(DY, xl)], center_into=sc1_, par=par)

        def f(eng, xt=xt, h2=h2, sc1_=sc1_, par=par):
            eng.tensor_scalar(sc1_, sc1_, colC[:, par:par + 1], None, OP.mult)
            eng.drain()
            eng.tensor_tensor(h2, PSW[:], bob, OP.add)
            eng.drain()
            eng.tensor_tensor(h2, h2, sc1_, OP.add)
            return eng.drain()
        v12 = pr.op("v", f, waits=[(SS, s4), (T, t8), (DY, bc_ld)], inc=V)
        psw_free_v = v12

        s6 = ln_chain(h2, None, 1e-12, waits=[(V, v12)], par=par)

        def f(eng, h2=h2, sc2_=sc2_, par=par):
            eng.tensor_scalar(h2, h2, colC[:, par:par + 1], None, OP.mult)
            eng.drain()
            eng.tensor_tensor(sc2_, h2, gamb, OP.mult)
            eng.drain()
            eng.tensor_tensor(sc2_, sc2_, betb, OP.add)
            return eng.drain()
        v14 = pr.op("v", f, waits=[(SS, s6)], inc=V)

        s8 = ln_chain(sc2_, None, 1e-5, waits=[(V, v14)], par=par)

        def f(eng, xt=xt, sc2_=sc2_, h2=h2, par=par):
            eng.tensor_scalar(sc2_, sc2_, colC[:, par:par + 1], None, OP.mult)
            eng.drain()
            eng.tensor_tensor(h2, xt, sc2_, OP.add)
            return eng.drain()
        v16 = pr.op("v", f, waits=[(SS, s8)], inc=V)
        p3_last[par] = v16
        pr.dma("s", out1_sp[tt * 128:(tt + 1) * 128, :], h2, waits=[(V, v16)])
    pr.barrier()

    ln2T = C[:, 0:16384]
    lnrow2 = [A[:, 0:1024], A[:, 1024:2048]]
    psw_free_v = 0
    p4_last = {}
    for bb in range(16):
        par = bb % 2
        xt = Bf[:, par * 4096:par * 4096 + 1024]
        lw = [(V, p4_last[par])] if par in p4_last else [(V, V.n)] if bb == 0 else []
        xl = pr.dma("y", xt, out1_sp[bb:bb + 16 * 127 + 1:16, :], waits=lw)
        s10 = ln_chain(xt, None, 1e-5, waits=[(DY, xl)], par=par)

        def f(eng, xt=xt, par=par, bb=bb):
            eng.tensor_scalar(lnrow2[par], xt, colC[:, par:par + 1], None, OP.mult)
            return eng.drain()
        v19 = pr.op("v", f, waits=[(SS, s10)], inc=V)
        p4_last[par] = v19

        def f(eng, par=par):
            r = None
            for j in range(8):
                r = eng.matmul(PSW[:, j * 128:(j + 1) * 128], lnrow2[par][:, j * 128:(j + 1) * 128],
                               ident[:], start=True, stop=True)
            return r
        wl = [(V, v19)]
        if psw_free_v:
            wl.append((V, psw_free_v))
        t9 = pr.op("t", f, waits=wl, inc=T)

        def f(eng, bb=bb):
            src = PSW[:].rearrange("p (j t) -> p j t", t=128)
            dst = ln2T.rearrange("p (j t) -> p j t", t=2048)[:, :, bb::16]
            eng.tensor_copy(dst, src)
            return eng.drain()
        v20 = pr.op("v", f, waits=[(T, t9)], inc=V)
        psw_free_v = v20

        def f(eng, bb=bb):
            r = None
            l2v = ln2T.rearrange("p (j t) -> p j t", t=2048)
            for fi in range(8):
                r = eng.matmul(PS0[:, 0:16], l2v[:, fi, bb::16],
                               sgw[:, fi * 16:(fi + 1) * 16], start=(fi == 0), stop=(fi == 7))
            return r
        t10 = pr.op("t", f, waits=[(V, v20)], inc=T)

        def f(eng, bb=bb):
            eng.tensor_copy(logits[:, bb * 16:(bb + 1) * 16], PS0[:, 0:16])
            return eng.drain()
        pr.op("v", f, waits=[(T, t10)], inc=V)
    pr.barrier()

    if phases < 5:
        return _finish(nc, pr, es)
    def f(eng):
        eng.activation(rt_sc[:], logits[:], AF.Sigmoid)
        return eng.drain()
    s11 = pr.op("s", f, inc=SS)

    def f(eng):
        eng.memset(rt_neg[:], -1e30)
        eng.drain()
        eng.tensor_tensor(rt_sfc[:], rt_sc[:], sgbr[:], OP.add)
        eng.drain()
        g4 = rt_sfc.rearrange("p (x k) -> p x k", k=4)
        eng.tensor_tensor(rt_t[0][:], g4[:, :, 0], g4[:, :, 1], OP.max)
        eng.tensor_tensor(rt_t[1][:], g4[:, :, 0], g4[:, :, 1], OP.min)
        eng.tensor_tensor(rt_t[2][:], g4[:, :, 2], g4[:, :, 3], OP.max)
        eng.tensor_tensor(rt_t[3][:], g4[:, :, 2], g4[:, :, 3], OP.min)
        eng.drain()
        eng.tensor_tensor(rt_t[4][:], rt_t[0][:], rt_t[2][:], OP.max)
        eng.tensor_tensor(rt_t[5][:], rt_t[0][:], rt_t[2][:], OP.min)
        eng.drain()
        eng.tensor_tensor(rt_t[0][:], rt_t[1][:], rt_t[3][:], OP.max)
        eng.drain()
        eng.tensor_tensor(rt_t[5][:], rt_t[5][:], rt_t[0][:], OP.max)
        eng.drain()
        eng.tensor_tensor(rt_gs[:], rt_t[4][:], rt_t[5][:], OP.add)
        return eng.drain()
    v22 = pr.op("v", f, waits=[(SS, s11)], inc=V)

    def f(eng):
        gs4 = rt_gs.rearrange("p (x g) -> p x g", g=4)
        for g in range(4):
            dsts = []
            k = 0
            for g2 in range(4):
                if g2 == g:
                    continue
                eng.tensor_tensor(rt_t[k][:, 0:16], gs4[:, :, g2], gs4[:, :, g], OP.is_gt)
                dsts.append(k)
                k += 1
            for g2 in range(g):
                eng.tensor_tensor(rt_t[k][:, 0:16], gs4[:, :, g2], gs4[:, :, g], OP.is_equal)
                dsts.append(k)
                k += 1
            eng.drain()
            acc = dsts[0]
            for d2 in dsts[1:]:
                eng.tensor_tensor(rt_t[acc][:, 0:16], rt_t[acc][:, 0:16], rt_t[d2][:, 0:16], OP.add)
                eng.drain()
            eng.tensor_scalar(rt_gm[:, g * 16:(g + 1) * 16], rt_t[acc][:, 0:16], 1.5, None, OP.is_lt)
            eng.drain()
        for g in range(4):
            for kk in range(4):
                eng.tensor_copy(rt_em[:, 4 * g + kk::16], rt_gm[:, g * 16:(g + 1) * 16])
        eng.drain()
        eng.select(rt_msk[:], rt_em[:], rt_sfc[:], rt_neg[:], add_drain=True)
        return eng.drain()
    v23 = pr.op("v", f, waits=[(V, v22)], inc=V)

    def f(eng):
        for b in range(16):
            for e in range(NE):
                eng.tensor_scalar(Bf[:, (b * 16 + e) * 16:(b * 16 + e) * 16 + 16],
                                  rt_msk[:, b * 16:(b + 1) * 16],
                                  rt_msk[:, b * 16 + e:b * 16 + e + 1], None, OP.is_gt)
        eng.drain()
        for b in range(16):
            for e in range(NE):
                eng.tensor_reduce(rt_rank[:, b * 16 + e:b * 16 + e + 1],
                                  Bf[:, (b * 16 + e) * 16:(b * 16 + e) * 16 + 16], axis=AX, op=OP.add)
        eng.drain()
        eng.memset(rt_eqr[:], 0.0)
        eng.drain()
        for b in range(16):
            for e in range(1, NE):
                eng.tensor_scalar(Bf[:, (b * 16 + e) * 16:(b * 16 + e) * 16 + e],
                                  rt_msk[:, b * 16:b * 16 + e],
                                  rt_msk[:, b * 16 + e:b * 16 + e + 1], None, OP.is_equal)
        eng.drain()
        for b in range(16):
            for e in range(1, NE):
                eng.tensor_reduce(rt_eqr[:, b * 16 + e:b * 16 + e + 1],
                                  Bf[:, (b * 16 + e) * 16:(b * 16 + e) * 16 + e], axis=AX, op=OP.add)
        eng.drain()
        eng.tensor_tensor(rt_rank[:], rt_rank[:], rt_eqr[:], OP.add)
        eng.drain()
        eng.tensor_scalar(rt_sel[:], rt_rank[:], 3.5, None, OP.is_lt)
        eng.drain()
        eng.tensor_tensor(rt_w[:], rt_sc[:], rt_sel[:], OP.mult)
        eng.drain()
        eng.tensor_reduce(rt_ws[:], rt_w.rearrange("p (x e) -> p x e", e=16), axis=AX, op=OP.add)
        eng.drain()
        eng.tensor_scalar(rt_wsi[:], rt_ws[:], 1e-20, None, OP.add)
        eng.drain()
        eng.reciprocal(rt_ws[:], rt_wsi[:])
        eng.drain()
        for b in range(16):
            eng.tensor_scalar(rt_cw[:, b * 16:(b + 1) * 16], rt_w[:, b * 16:(b + 1) * 16],
                              rt_ws[:, b:b + 1], 2.5, OP.mult, OP.mult)
        return eng.drain()
    v24 = pr.op("v", f, waits=[(V, v23)], inc=V)
    if dbg:
        def f(eng):
            eng.tensor_copy(C[:, 0:512].bitcast(F32), logits[:])
            eng.tensor_copy(C[:, 512:1024].bitcast(F32), rt_sfc[:])
            eng.tensor_copy(C[:, 1024:1152].bitcast(F32), rt_gs[:])
            eng.tensor_copy(C[:, 1152:1280].bitcast(F32), rt_gm[:])
            eng.tensor_copy(C[:, 1536:2048].bitcast(F32), rt_rank[:])
            eng.tensor_copy(C[:, 2048:2560].bitcast(F32), rt_sel[:])
            return eng.drain()
        vdmp = pr.op("v", f, waits=[(V, v24)], inc=V)
        pr.dma("s", rt_dump[:, 0:256], C[:, 0:512].bitcast(F32), waits=[(V, vdmp)])
        pr.dma("s", rt_dump[:, 256:512], C[:, 512:1024].bitcast(F32), waits=[(V, vdmp)])
        pr.dma("s", rt_dump[:, 512:576], C[:, 1024:1152].bitcast(F32), waits=[(V, vdmp)])
        pr.dma("s", rt_dump[:, 640:704], C[:, 1152:1280].bitcast(F32), waits=[(V, vdmp)])
        pr.dma("s", rt_dump[:, 768:1024], C[:, 1536:2048].bitcast(F32), waits=[(V, vdmp)])
        pr.dma("s", rt_dump[:, 1024:1280], C[:, 2048:2560].bitcast(F32), waits=[(V, vdmp)])
    for e in range(NE):
        def f(eng, e=e):
            eng.tensor_copy(rt_c1[:], rt_cw[:, e::16])
            return eng.drain()
        vcp = pr.op("v", f, waits=[(V, v24)] + ([(DS, DS.n)] if e else []), inc=V)
        pr.dma("s", cw_sp[e * NT:(e + 1) * NT].rearrange("(p b) -> p b", b=16),
               rt_c1[:], waits=[(V, vcp)])
    pr.barrier()

    if phases < 6:
        return _finish(nc, pr, es)
    wsg_v = A[:, 16384:24576]
    wsu_v = A[:, 24576:32768]
    wg_v = [A[:, 0:4096], A[:, 4096:8192]]
    wu_v = [A[:, 8192:12288], A[:, 12288:16384]]
    pr.dma("g", wsg_v.rearrange("p (f c) -> p f c", f=8),
           wsgT_d.rearrange("(f p) c -> p f c", p=128))
    pr.dma("g", wsu_v.rearrange("p (f c) -> p f c", f=8),
           wsuT_d.rearrange("(f p) c -> p f c", p=128))
    wslot_free = {}
    ps0_free = 0
    ps1_free = 0
    actb_free = 0
    awb_free = {}
    last_t_of = {}
    cw_free = {}
    aidx = 0
    for e in range(NE + 1):
        shared = (e == NE)
        sl = e % 2
        nit = 8 if shared else 4
        if not shared:
            ww = [(T, last_t_of[e - 2])] if (e - 2) in last_t_of else []
            pr.dma("g", wg_v[sl].rearrange("p (f i) -> p f i", f=8),
                   wgT_d[e].rearrange("(f p) i -> p f i", p=128), waits=ww)
            wload = pr.dma("g", wu_v[sl].rearrange("p (f i) -> p f i", f=8),
                           wuT_d[e].rearrange("(f p) i -> p f i", p=128))
        last_t = 0
        for cc in range(4):
            if not shared:
                cwl = [(GP, cw_free[cc % 2])] if (cc % 2) in cw_free else []
                cwt = pr.dma("y", cwB[cc % 2],
                             cw_sp[e * NT + cc * 512:e * NT + (cc + 1) * 512]
                             .rearrange("(o t) -> o t", o=1).partition_broadcast(128),
                             waits=cwl)
            for ii in range(nit):
                wv1 = wsg_v if shared else wg_v[sl]
                wv2 = wsu_v if shared else wu_v[sl]
                wid = 1024 if shared else 512

                def f(eng, wv1=wv1, cc=cc, ii=ii, wid=wid):
                    r = None
                    for fi in range(8):
                        r = eng.matmul(PS0[:], wv1[:, fi * wid + ii * 128:fi * wid + ii * 128 + 128],
                                       ln2T[:, fi * 2048 + cc * 512:fi * 2048 + (cc + 1) * 512],
                                       start=(fi == 0), stop=(fi == 7))
                    return r
                wl = [] if shared else [(DG, wload)]
                if ps0_free:
                    wl.append((SS, ps0_free))
                tg = pr.op("t", f, waits=wl, inc=T)

                def f(eng, wv2=wv2, cc=cc, ii=ii, wid=wid):
                    r = None
                    for fi in range(8):
                        r = eng.matmul(PS1[:], wv2[:, fi * wid + ii * 128:fi * wid + ii * 128 + 128],
                                       ln2T[:, fi * 2048 + cc * 512:fi * 2048 + (cc + 1) * 512],
                                       start=(fi == 0), stop=(fi == 7))
                    return r
                wl = []
                if ps1_free:
                    wl.append((V, ps1_free))
                tu = pr.op("t", f, waits=wl, inc=T)
                last_t = tu

                def f(eng):
                    eng.activation(silb, PS0[:], AF.Silu)
                    return eng.drain()
                sg2 = pr.op("s", f, waits=[(T, tg)], inc=SS)
                ps0_free = sg2

                asl = aidx % 2
                aw = [(T, tu), (SS, sg2)]
                if shared and asl in awb_free:
                    aw.append((DS, awb_free[asl]))
                if not shared and actb_free:
                    aw.append((GP, actb_free))

                def f(eng, shared=shared, asl=asl):
                    dst = actwb[asl] if shared else actb
                    eng.scalar_tensor_tensor(dst, PS1[:], 1.0, silb, OP.mult, OP.mult)
                    return eng.drain()
                vv = pr.op("v", f, waits=aw, inc=V)
                ps1_free = vv

                if not shared:
                    gw2 = [(V, vv), (DY, cwt)]
                    if asl in awb_free:
                        gw2.append((DS, awb_free[asl]))

                    def f(eng, asl=asl, cc=cc):
                        eng.tensor_tensor(actwb[asl], actb, cwB[cc % 2], OP.mult)
                        return eng.drain()
                    gg = pr.op("g", f, waits=gw2, inc=GP)
                    cw_free[cc % 2] = gg
                    actb_free = gg
                    base = e * EI * NT + ii * 128 * NT
                    dd = pr.dma("s", actw_sp[base:base + 128 * NT]
                                .rearrange("(p t) -> p t", t=NT)[:, cc * 512:(cc + 1) * 512],
                                actwb[asl], waits=[(GP, gg)])
                else:
                    base = ii * 128 * NT
                    dd = pr.dma("s", gs_sp[base:base + 128 * NT]
                                .rearrange("(p t) -> p t", t=NT)[:, cc * 512:(cc + 1) * 512],
                                actwb[asl], waits=[(V, vv)])
                awb_free[asl] = dd
                aidx += 1
        if not shared:
            last_t_of[e] = last_t
    pr.barrier()

    if phases < 7:
        return _finish(nc, pr, es)
    y_ps_free = {}
    last_ty = 0
    prev_out = 0
    for half in range(2):
        wl0 = [(T, T.n)]
        for e in range(NE):
            pr.dma("g", A[:, e * 2048:(e + 1) * 2048].rearrange("p (i c) -> p i c", i=4),
                   wdT_d[e].rearrange("(i p) c -> p i c", p=128)[:, :, half * 512:half * 512 + 512],
                   waits=wl0 if e == 0 else [])
        dwl = pr.dma("g", A[:, 32768:36864].rearrange("p (i c) -> p i c", i=8),
                     wsdT_d.rearrange("(i p) c -> p i c", p=128)[:, :, half * 512:half * 512 + 512])
        for ts_ in range(16):
            for e in range(NE):
                pr.dma("y", C[:, e * 512:(e + 1) * 512].rearrange("p (i t) -> p i t", t=128),
                       actw_sp[e * EI * NT:(e + 1) * EI * NT]
                       .rearrange("(i p t) -> p i t", p=128, t=NT)[:, :, ts_ * 128:(ts_ + 1) * 128],
                       waits=[(T, last_ty)] if e == 0 else [])
            aload = pr.dma("y", C[:, 8192:9216].rearrange("p (i t) -> p i t", t=128),
                           gs_sp.rearrange("(i p t) -> p i t", p=128, t=NT)[:, :, ts_ * 128:(ts_ + 1) * 128])
            slot = ts_ % 2

            def f(eng, slot=slot):
                r = None
                first = True
                for e in range(NE):
                    for ii in range(4):
                        r = eng.matmul(PSS[slot][:], C[:, e * 512 + ii * 128:e * 512 + (ii + 1) * 128],
                                       A[:, e * 2048 + ii * 512:e * 2048 + (ii + 1) * 512],
                                       start=first, stop=False, skip_group_check=True)
                        first = False
                for jj in range(8):
                    r = eng.matmul(PSS[slot][:], C[:, 8192 + jj * 128:8192 + (jj + 1) * 128],
                                   A[:, 32768 + jj * 512:32768 + (jj + 1) * 512],
                                   start=False, stop=(jj == 7), skip_group_check=True)
                return r
            wl = [(DY, aload), (DG, dwl)]
            if (slot, "ps") in y_ps_free:
                wl.append((V, y_ps_free[(slot, "ps")]))
            ty = pr.op("t", f, waits=wl, inc=T)
            last_ty = ty

            if half == 0:
                def f(eng, ts_=ts_, slot=slot):
                    eng.tensor_copy(Bf[:, ts_ * 512:(ts_ + 1) * 512], PSS[slot][:])
                    return eng.drain()
                vv = pr.op("v", f, waits=[(T, ty)], inc=V)
                y_ps_free[(slot, "ps")] = vv
            else:
                xl = pr.dma("y", o1t[:], out1_sp[ts_ * 128:(ts_ + 1) * 128, :], waits=[(V, V.n)])
                ep_w = [(DY, xl)]
                if prev_out:
                    ep_w.append((DS, prev_out))
                s12 = ln_chain(o1t, None, 1e-5, waits=ep_w, center_into=ep1)

                def f(eng, ts_=ts_, slot=slot):
                    eng.tensor_scalar(ep1[:], ep1[:], colC[:, 0:1], None, OP.mult)
                    eng.drain()
                    eng.tensor_tensor(ep2[:, 0:512], ep1[:, 0:512], Bf[:, ts_ * 512:(ts_ + 1) * 512], OP.add)
                    eng.drain()
                    eng.tensor_tensor(ep2[:, 512:1024], ep1[:, 512:1024], PSS[slot][:], OP.add)
                    return eng.drain()
                v25 = pr.op("v", f, waits=[(SS, s12), (T, ty)], inc=V)
                y_ps_free[(slot, "ps")] = v25

                s13 = ln_chain(ep2, None, 1e-5, waits=[(V, v25)])

                def f(eng):
                    eng.tensor_scalar(ep2[:], ep2[:], colC[:, 0:1], None, OP.mult)
                    eng.drain()
                    eng.tensor_tensor(ep1[:], o1t[:], ep2[:], OP.add)
                    return eng.drain()
                v26 = pr.op("v", f, waits=[(SS, s13)], inc=V)
                prev_out = pr.dma("s", outp[ts_ * 128:(ts_ + 1) * 128, :], ep1[:], waits=[(V, v26)])

    return _finish(nc, pr, es)


def host_maps(inp):
    bf = ml_dtypes.bfloat16
    f32 = np.float32
    ac = np.ascontiguousarray
    x = np.asarray(inp["L0_attn_hidden_states"], dtype=f32)
    dist = np.asarray(inp["L0_attn_dist_emb"], dtype=f32)
    Wq = ac(np.asarray(inp["L0_attn_Wq"], f32)); Wk = ac(np.asarray(inp["L0_attn_Wk"], f32))
    Wv = ac(np.asarray(inp["L0_attn_Wv"], f32)); Wo = ac(np.asarray(inp["L0_attn_Wo"], f32))
    bq = np.asarray(inp["L0_attn_bq"], f32); bk = np.asarray(inp["L0_attn_bk"], f32)
    bv = np.asarray(inp["L0_attn_bv"], f32); bo = np.asarray(inp["L0_attn_bo"], f32)
    lng = np.asarray(inp["L0_attn_ln_gamma"], f32); lnb = np.asarray(inp["L0_attn_ln_beta"], f32)
    gw = np.asarray(inp["L0_ffn_gate_w"], f32)
    gbias = np.asarray(inp["L0_ffn_gate_bias"], f32)
    Wg = np.asarray(inp["L0_ffn_Wg"], f32)
    Wu = np.asarray(inp["L0_ffn_Wu"], f32)
    Wd = np.asarray(inp["L0_ffn_Wd"], f32)
    Wsg = np.asarray(inp["L0_ffn_Ws_g"], f32)
    Wsu = np.asarray(inp["L0_ffn_Ws_u"], f32)
    Wsd = np.asarray(inp["L0_ffn_Ws_d"], f32)

    w = np.arange(WQH)
    com = {
        "wq_d": Wq, "wk_d": Wk, "wv_d": Wv, "wo_d": Wo,
        "bqr_d": bq[None, :], "bkr_d": bk[None, :],
        "ones512_d": np.ones((1, 512), bf),
        "bv_row": bv[None, :], "bo_row": bo[None, :],
        "gam_row": lng[None, :], "bet_row": lnb[None, :],
        "tq0_d": ac(np.tile(dist[np.clip(2 * (K0 - w) + 1023, 0, 2046)].T, (2, 1))),
        "tk0_d": ac(np.tile(dist[np.clip(2 * (w - K0) + 1023, 0, 2046)].T, (2, 1))),
        "e0_d": ac(np.tile(np.tile(dist[0][:, None], (1, 512)), (2, 1))),
        "em_d": ac(np.tile(np.tile(dist[2046][:, None], (1, 512)), (2, 1))),
        "id_d": np.eye(128, dtype=bf),
        "onesf_d": np.ones((1, 128), f32),
        "onesb_d": np.ones((1, 128), bf),
        "ones64_d": np.ones((64, 1), bf),
        "gw_d": ac(gw.T),
        "gbr_d": ac(np.tile(gbias, 16)[None, :]),
        "wgT_d": ac(Wg.transpose(0, 2, 1)),
        "wuT_d": ac(Wu.transpose(0, 2, 1)),
        "wdT_d": ac(Wd.transpose(0, 2, 1)),
        "wsgT_d": ac(Wsg.T), "wsuT_d": ac(Wsu.T), "wsdT_d": ac(Wsd.T),
    }
    in_maps = []
    for core in range(8):
        b, h = core // 2, core % 2
        xp = x[b]
        if h:
            xp = xp.reshape(NT, 2, E)[:, ::-1, :].reshape(S, E)
        m = dict(com)
        m["x_b"] = ac(xp)
        m["tq1_d"] = ac(np.tile(dist[np.clip(2 * (K0 - w + h) + 1022, 0, 2046)].T, (2, 1)))
        m["tk1_d"] = ac(np.tile(dist[np.clip(2 * (w - K0 + h) + 1022, 0, 2046)].T, (2, 1)))
        in_maps.append(m)

    return in_maps


def kernel(**inp):
    in_maps = host_maps(inp)
    nc = build_program()
    res = run_bass_kernel_spmd(nc, in_maps, core_ids=list(range(8)))
    out = np.empty((4, S, E), np.float32)
    for core in range(8):
        b, h = core // 2, core % 2
        out[b, h::2] = np.asarray(res.results[core]["outp"], np.float32)
    return out


# version 15
# speedup vs baseline: 1.0763x; 1.0004x over previous
import contextlib
import numpy as np
import ml_dtypes
import concourse.bass as bass
import concourse.mybir as mybir
from concourse.bass_utils import run_bass_kernel_spmd
from concourse.alu_op_type import AluOpType as OP

F32 = mybir.dt.float32
BF16 = mybir.dt.bfloat16
I32 = mybir.dt.int32
AF = mybir.ActivationFunctionType
AX = mybir.AxisListType.X

S, E, NH = 4096, 1024, 16
NT = 2048
K0 = 644
WQH = WKH = 1288
NE, EI, SHI = 16, 512, 1024
SLACK = 16 * WQH


class Sem:
    def __init__(self, nc, es, name):
        self.h = es.enter_context(nc.semaphore(name))
        self.n = 0


class Prog:
    def __init__(self, nc, es):
        self.nc = nc
        self.ops = {k: [] for k in "ygtvs"}
        self.sems = {}
        for nm in ("DY", "DG", "DS", "T", "V", "SS", "GP", "BR"):
            self.sems[nm] = Sem(nc, es, nm)
        self.nbar = 0

    def dma(self, q, out_ap, in_ap, waits=()):
        sem = self.sems[{"y": "DY", "g": "DG", "s": "DS"}[q]]
        sem.n += 16
        tgt = sem.n
        w = tuple(waits)

        def f(eng, o=out_ap, i=in_ap, w=w, sh=sem.h):
            for (s, v) in w:
                eng.wait_ge(s.h, v)
            eng.dma_start(out=o, in_=i).then_inc(sh, 16)
        self.ops[q].append(f)
        return tgt

    def op(self, q, fn, waits=(), inc=None):
        w = tuple(waits)
        tgt = None
        if inc is not None:
            inc.n += 1
            tgt = inc.n

        def f(eng, fn=fn, w=w, inc=inc):
            for (s, v) in w:
                eng.wait_ge(s.h, v)
            r = fn(eng)
            if inc is not None:
                if r is None:
                    r = eng.nop()
                r.then_inc(inc.h, 1)
        self.ops[q].append(f)
        return tgt

    def barrier(self):
        br = self.sems["BR"]
        self.nbar += 1
        tgt = self.nbar * 5
        br.n += 5
        for q in "ygtvs":
            def f(eng, t=tgt, h=br.h):
                eng.nop().then_inc(h, 1)
                eng.wait_ge(h, t)
            self.ops[q].append(f)


def classify(T0, m):
    d2 = 2 * T0 - 256 * m
    if abs(d2) <= 1024:
        return "g"
    return "p" if d2 >= 1280 else "f"




def _finish(nc, pr, es):
    DS = pr.sems["DS"]
    fin = DS.n

    def f(eng, fin=fin):
        eng.wait_ge(DS.h, fin)
        return None
    pr.op("s", f)
    pr.barrier()
    with nc.Block() as block:
        @block.sync
        def _(eng):
            for f2 in pr.ops["y"]:
                f2(eng)

        @block.gpsimd
        def _(eng):
            for f2 in pr.ops["g"]:
                f2(eng)

        @block.tensor
        def _(eng):
            for f2 in pr.ops["t"]:
                f2(eng)

        @block.vector
        def _(eng):
            for f2 in pr.ops["v"]:
                f2(eng)

        @block.scalar
        def _(eng):
            for f2 in pr.ops["s"]:
                f2(eng)
    es.close()
    return nc


def build_program(phases=9, dbg=False, nheads=NH, p2mode=5):
    nc = bass.Bass("TRN2", target_bir_lowering=False)
    es = contextlib.ExitStack()
    pr = Prog(nc, es)
    DY, DG, DS, T, V, SS, GP = (pr.sems[k] for k in ("DY", "DG", "DS", "T", "V", "SS", "GP"))

    def dt(name, shape, dtp, kind="ExternalInput"):
        return nc.dram_tensor(name, shape, dtp, kind=kind).ap()

    x_b = dt("x_b", [S, E], F32)
    wq_d = dt("wq_d", [E, E], F32); wk_d = dt("wk_d", [E, E], F32)
    wv_d = dt("wv_d", [E, E], F32); wo_d = dt("wo_d", [E, E], F32)
    bqr_d = dt("bqr_d", [1, E], F32); bkr_d = dt("bkr_d", [1, E], F32)
    ones512_d = dt("ones512_d", [1, 512], BF16)
    bv_row = dt("bv_row", [1, E], F32); bo_row = dt("bo_row", [1, E], F32)
    gam_row = dt("gam_row", [1, E], F32); bet_row = dt("bet_row", [1, E], F32)
    tq0_d = dt("tq0_d", [128, WQH], F32); tq1_d = dt("tq1_d", [128, WQH], F32)
    tk0_d = dt("tk0_d", [128, WKH], F32); tk1_d = dt("tk1_d", [128, WKH], F32)
    e0_d = dt("e0_d", [128, 512], F32); em_d = dt("em_d", [128, 512], F32)
    id_d = dt("id_d", [128, 128], BF16)
    onesf_d = dt("onesf_d", [1, 128], F32)
    onesb_d = dt("onesb_d", [1, 128], BF16)
    ones64_d = dt("ones64_d", [64, 1], BF16)
    gw_d = dt("gw_d", [E, NE], F32)
    gbr_d = dt("gbr_d", [1, 256], F32)
    wgT_d = dt("wgT_d", [NE, E, EI], F32)
    wuT_d = dt("wuT_d", [NE, E, EI], F32)
    wdT_d = dt("wdT_d", [NE, EI, E], F32)
    wsgT_d = dt("wsgT_d", [E, SHI], F32)
    wsuT_d = dt("wsuT_d", [E, SHI], F32)
    wsdT_d = dt("wsdT_d", [SHI, E], F32)
    outp = dt("outp", [NT, E], F32, kind="ExternalOutput")
    IN = "Internal"
    DBG = "ExternalOutput" if dbg else IN
    kt_sp = dt("kt_sp", [E, S], BF16, kind=DBG)
    qt_sp = dt("qt_sp", [E, NT], BF16, kind=DBG)
    v_sp = dt("v_sp", [S, E], BF16, kind=DBG)
    qdh_sp = dt("qdh_sp", [32 * NT * WQH + 4 * SLACK], BF16, kind=DBG)
    kdh_sp = dt("kdh_sp", [32 * NT * WKH + 4 * SLACK], BF16, kind=DBG)
    out1_sp = dt("out1_sp", [NT, E], F32, kind=DBG)
    actw_sp = dt("actw_sp", [NE * EI * NT], BF16, kind=DBG)
    gs_sp = dt("gs_sp", [SHI * NT], BF16, kind=DBG)
    cw_sp = dt("cw_sp", [NE * NT], BF16, kind=DBG)
    rt_dump = dt("rt_dump", [128, 1536], F32, kind=DBG) if dbg else None

    def sb(name, shape, dtp):
        return es.enter_context(nc.sbuf_tensor(name, shape, dtp))

    A = sb("A", [128, 36864], BF16)
    Bf = sb("Bf", [128, 8192], F32)
    C = sb("C", [128, 16384], BF16)
    fscr = sb("fscr", [128, 1024], F32)
    o1t = C[:, 9216:11264].bitcast(F32)
    ep1 = C[:, 11264:13312].bitcast(F32)
    ep2 = C[:, 13312:15360].bitcast(F32)
    bvb = sb("bvb", [128, E], BF16)
    bob = C[:, 0:2048].bitcast(F32)
    gamb = C[:, 2048:4096].bitcast(F32)
    betb = C[:, 4096:6144].bitcast(F32)
    sbqr = sb("sbqr", [1, E], BF16)
    sbkr = sb("sbkr", [1, E], BF16)
    sbvr = sb("sbvr", [1, E], BF16)
    ones512 = sb("ones512", [1, 512], BF16)
    dq = [sb(f"dq{r}", [128, WQH], BF16) for r in (0, 1)]
    dk = [sb(f"dk{r}", [128, WKH], BF16) for r in (0, 1)]
    se0 = sb("se0", [128, 512], BF16)
    sem_ = sb("sem_", [128, 512], BF16)
    ident = sb("ident", [128, 128], BF16)
    sonesf = sb("sonesf", [1, 128], F32)
    sonesb = sb("sonesb", [1, 128], BF16)
    sones64 = sb("sones64", [64, 1], BF16)
    sgw = sb("sgw", [128, 8 * NE], BF16)
    sgbr = sb("sgbr", [128, 256], F32)
    mxt = sb("mxt", [1, 16], F32)
    chs = sb("chs", [1, 8], F32)
    negC = sb("negC", [1, 1], F32)
    negCb = sb("negCb", [128, 1], F32)
    mxe = sb("mxe", [1, 1], F32)
    rcp = A[:, 34560:35584].bitcast(F32)[0:1, :]
    rcpb = A[:, 35584:36096][0:1, :]
    rcpB = A[64:128, 34560:35072]
    ctxT = sb("ctxT", [128, 8 * NT], BF16)
    colA = sb("colA", [128, 2], F32)
    colB = sb("colB", [128, 2], F32)
    colC = sb("colC", [128, 2], F32)
    rt_sc = A[:, 2048:2560].bitcast(F32)
    rt_sfc = A[:, 2560:3072].bitcast(F32)
    rt_msk = A[:, 3072:3584].bitcast(F32)
    rt_rank = A[:, 3584:4096].bitcast(F32)
    rt_eqr = A[:, 4096:4608].bitcast(F32)
    rt_sel = A[:, 4608:5120].bitcast(F32)
    rt_gs = sb("rt_gs", [128, 64], F32)
    rt_t = [sb(f"rt_t{i}", [128, 64], F32) for i in range(6)]
    rt_gm = sb("rt_gm", [128, 64], F32)
    rt_em = A[:, 5120:5632].bitcast(I32)
    rt_neg = A[:, 5632:6144].bitcast(F32)
    rt_w = A[:, 6144:6656].bitcast(F32)
    rt_ws = sb("rt_ws", [128, 16], F32)
    rt_wsi = sb("rt_wsi", [128, 16], F32)
    rt_cw = sb("rt_cw", [128, 256], BF16)
    rt_c1 = sb("rt_c1", [128, 16], BF16)
    logits = A[:, 6656:7168].bitcast(F32)
    cwB = [A[:, 32768 + i * 512:32768 + (i + 1) * 512] for i in (0, 1)]
    silb = A[:, 33792:34304]
    actb = A[:, 34304:34816]
    actwb = [A[:, 34816 + i * 512:34816 + (i + 1) * 512] for i in (0, 1)]

    def psal(name, shape):
        return es.enter_context(nc.psum_tensor(name, shape, F32))

    PS0 = psal("PS0", [128, 512])
    PS1 = psal("PS1", [128, 512])
    PSW = psal("PSW", [128, 1024])
    PSC = psal("PSC", [65, 2048])
    PSS = [PS0, PS1]

    def ln_chain(xt, outbuf, eps, out_bf=None, waits=(), center_into=None, par=0):
        tgt = center_into if center_into is not None else xt
        cA = colA[:, par:par + 1]
        cB = colB[:, par:par + 1]
        cC = colC[:, par:par + 1]

        def f(eng, xt=xt, tgt=tgt, cA=cA):
            eng.tensor_reduce(cA, xt, axis=AX, op=OP.add)
            eng.drain()
            eng.tensor_scalar(cA, cA, 1.0 / E, None, OP.mult)
            eng.drain()
            eng.tensor_scalar(tgt, xt, cA, None, OP.subtract)
            return eng.drain()
        v1 = pr.op("v", f, waits=list(waits), inc=V)

        def f(eng, tgt=tgt, cB=cB):
            eng.activation(fscr[:], tgt, AF.Square, accum_out=cB)
            return eng.drain()
        s1 = pr.op("s", f, waits=[(V, v1)], inc=SS)

        def f(eng, eps=eps, cB=cB):
            eng.tensor_scalar(cB, cB, 1.0 / E, eps, OP.mult, OP.add)
            return eng.drain()
        v2 = pr.op("v", f, waits=[(SS, s1)], inc=V)

        def f(eng, cB=cB, cC=cC):
            eng.activation(cC, cB, AF.Ln)
            eng.drain()
            eng.activation(cC, cC, AF.Exp, scale=-0.5)
            return eng.drain()
        s2 = pr.op("s", f, waits=[(V, v2)], inc=SS)
        return s2

    pr.dma("g", bvb[:], bv_row.partition_broadcast(128))
    pr.dma("g", sbqr[:], bqr_d)
    pr.dma("g", sbkr[:], bkr_d)
    pr.dma("g", sbvr[:], bv_row)
    pr.dma("y", ones512[:], ones512_d)
    for r in (0, 1):
        pr.dma("g", dq[r][:], (tq0_d, tq1_d)[r])
        pr.dma("g", dk[r][:], (tk0_d, tk1_d)[r])
    pr.dma("g", se0[:], e0_d)
    pr.dma("g", sem_[:], em_d)
    pr.dma("y", ident[:], id_d)
    pr.dma("y", sonesf[:], onesf_d)
    pr.dma("y", sonesb[:], onesb_d)
    pr.dma("y", sones64[:], ones64_d)
    pr.dma("g", sgw[:].rearrange("p (f e) -> p f e", e=NE),
           gw_d.rearrange("(f p) e -> p f e", p=128))
    pr.dma("y", sgbr[:], gbr_d.partition_broadcast(128))
    wq_v = A[:, 0:8192]; wk_v = A[:, 8192:16384]; wv_v = A[:, 16384:24576]
    for (w_in, w_sb) in ((wq_d, wq_v), (wk_d, wk_v), (wv_d, wv_v)):
        pr.dma("g", w_sb.rearrange("p (f c) -> p f c", f=8),
               w_in.rearrange("(f p) c -> p f c", p=128))
    dg_p0 = DG.n
    dy_p0 = DY.n
    pr.barrier()

    sqv = [A[0:64, 24576 + i * WKH:24576 + (i + 1) * WKH] for i in (0, 1)]

    def f(eng):
        eng.activation(sqv[0], dk[0][0:64, :], AF.Square)
        eng.drain()
        eng.activation(sqv[1], dk[1][0:64, :], AF.Square)
        return eng.drain()
    sqd = pr.op("s", f, waits=[(DG, dg_p0)], inc=SS)
    nmm = 0
    for r in (0, 1):
        for cc in range(3):
            w0 = cc * 512
            wid = min(512, WKH - w0)

            def f(eng, r=r, w0=w0, wid=wid):
                return eng.matmul(PS0[0:1, 0:wid], sones64[:], sqv[r][:, w0:w0 + wid],
                                  start=True, stop=True)
            tv = pr.op("t", f, waits=[(SS, sqd)] + ([(V, V.n)] if nmm else []), inc=T)

            def f(eng, i=nmm, wid=wid):
                eng.reduce_max(mxt[:, i:i + 1], PS0[0:1, 0:wid], axis=AX)
                return eng.drain()
            pr.op("v", f, waits=[(T, tv)], inc=V)
            nmm += 1

    def f(eng):
        eng.reduce_max(mxe[:], mxt[:, 0:6], axis=AX)
        return eng.drain()
    pr.op("v", f, inc=V)
    pr.barrier()

    if phases < 1:
        return _finish(nc, pr, es)
    ln1T = [A[:, 24576 + k * 4096:24576 + (k + 1) * 4096] for k in (0, 1)]
    lnrow = [A[:, 32768 + k * 1024:32768 + (k + 1) * 1024] for k in range(2)]
    stg = [A[:, 34816 + k * 512:34816 + (k + 1) * 512] for k in range(4)]
    xfree = {}
    psw_free = 0
    lnrow_free = {}
    pss_free = [0, 0]
    stg_free = {}
    kq_iter = 0
    for c in range(8):
        cb = c % 2
        ln_done = []
        for i in range(4):
            xt = Bf[:, (cb * 4 + i) * 1024:(cb * 4 + i + 1) * 1024]
            lw = [(V, xfree[(c - 2, i)])] if (c - 2, i) in xfree else []
            xl = pr.dma("y", xt, x_b[c * 512 + i * 128:c * 512 + (i + 1) * 128, :], waits=lw)
            lrs = lnrow[i % 2]
            lr_w = [(T, lnrow_free[i % 2])] if i % 2 in lnrow_free else []
            s2 = ln_chain(xt, None, 1e-5, waits=[(DY, xl)] + lr_w, par=i % 2)

            def f(eng, xt=xt, lrs=lrs, i=i):
                eng.tensor_scalar(lrs, xt, colC[:, i % 2:i % 2 + 1], None, OP.mult)
                return eng.drain()
            v3 = pr.op("v", f, waits=[(SS, s2)], inc=V)
            xfree[(c, i)] = v3

            def f(eng, lrs=lrs):
                r = None
                for j in range(8):
                    r = eng.matmul(PSW[:, j * 128:(j + 1) * 128], lrs[:, j * 128:(j + 1) * 128],
                                   ident[:], start=True, stop=True)
                return r
            t1 = pr.op("t", f, waits=[(V, v3), (DY, dy_p0)] + ([(V, psw_free)] if psw_free else []), inc=T)
            lnrow_free[i % 2] = t1

            def f(eng, cb=cb, i=i):
                src = PSW[:].rearrange("p (j t) -> p j t", t=128)
                dst = ln1T[cb].rearrange("p (j t) -> p j t", t=512)[:, :, i * 128:(i + 1) * 128]
                eng.tensor_copy(dst, src)
                return eng.drain()
            v4 = pr.op("v", f, waits=[(T, t1)], inc=V)
            psw_free = v4
            ln_done.append(v4)

        for proj in ("k", "q"):
            wmat = wk_v if proj == "k" else wq_v
            for fo in range(8):
                slot = kq_iter % 2
                kq_iter += 1
                pw = [(SS, pss_free[slot][1])] if pss_free[slot] else []

                bvec = sbkr if proj == "k" else sbqr

                def f(eng, wmat=wmat, fo=fo, cb=cb, slot=slot, bvec=bvec):
                    eng.matmul(PSS[slot][:], bvec[0:1, fo * 128:(fo + 1) * 128], ones512[:], start=True, stop=False,
                               skip_group_check=True)
                    r = None
                    for fi in range(8):
                        r = eng.matmul(PSS[slot][:],
                                       wmat[:, fi * 1024 + fo * 128:fi * 1024 + (fo + 1) * 128],
                                       ln1T[cb][:, fi * 512:(fi + 1) * 512],
                                       start=False, stop=(fi == 7), skip_group_check=True)
                    return r
                t2 = pr.op("t", f, waits=[(V, ln_done[3]), (DG, dg_p0)] + pw, inc=T)
                if proj == "k":
                    kw = [(DS, stg_free["k"])] if "k" in stg_free else []

                    def f(eng, fo=fo, slot=slot):
                        eng.copy(stg[0][:, 0:256], PSS[slot][:, 0::2])
                        return eng.copy(stg[1][:, 0:256], PSS[slot][:, 1::2])
                    v5 = pr.op("s", f, waits=[(T, t2)] + kw, inc=SS)
                    pr.dma("s", kt_sp[fo * 128:(fo + 1) * 128, 256 * c:256 * c + 256], stg[0][:, 0:256], waits=[(SS, v5)])
                    d2 = pr.dma("s", kt_sp[fo * 128:(fo + 1) * 128, 2048 + 256 * c:2048 + 256 * c + 256],
                                stg[1][:, 0:256], waits=[(SS, v5)])
                    stg_free["k"] = d2
                    pss_free[slot] = ("s", v5)
                else:
                    qw = [(DS, stg_free["q"])] if "q" in stg_free else []

                    def f(eng, fo=fo, slot=slot):
                        return eng.copy(stg[2][:, 0:256], PSS[slot][:, 0::2])
                    v5 = pr.op("s", f, waits=[(T, t2)] + qw, inc=SS)
                    d1 = pr.dma("s", qt_sp[fo * 128:(fo + 1) * 128, 256 * c:256 * c + 256], stg[2][:, 0:256], waits=[(SS, v5)])
                    stg_free["q"] = d1
                    pss_free[slot] = ("s", v5)
        for i in range(4):
            for half in range(2):
                slot = kq_iter % 2
                kq_iter += 1
                pw = [(SS, pss_free[slot][1])] if pss_free[slot] else []

                def f(eng, cb=cb, i=i, half=half, slot=slot):
                    eng.matmul(PSS[slot][:], sonesb[:], sbvr[0:1, half * 512:half * 512 + 512],
                               start=True, stop=False, skip_group_check=True)
                    r = None
                    for fi in range(8):
                        r = eng.matmul(PSS[slot][:],
                                       ln1T[cb][:, fi * 512 + i * 128:fi * 512 + (i + 1) * 128],
                                       wv_v[:, fi * 1024 + half * 512:fi * 1024 + half * 512 + 512],
                                       start=False, stop=(fi == 7), skip_group_check=True)
                    return r
                t3 = pr.op("t", f, waits=[(V, ln_done[i])] + pw, inc=T)
                vw = [(DS, stg_free["v"])] if "v" in stg_free else []

                def f(eng, half=half, slot=slot):
                    return eng.copy(stg[3], PSS[slot][:])
                v6 = pr.op("s", f, waits=[(T, t3)] + vw, inc=SS)
                d1 = pr.dma("s", v_sp[c * 512 + i * 128:c * 512 + (i + 1) * 128, half * 512:half * 512 + 512],
                            stg[3], waits=[(SS, v6)])
                stg_free["v"] = d1
                pss_free[slot] = ("s", v6)
    pr.barrier()

    if phases < 2:
        return _finish(nc, pr, es)
    kt_v = [A[:, 0:4096], A[:, 4096:8192]]
    qt_v = [A[:, 8192:10240], A[:, 10240:12288]]
    vh_v = [A[:, 12288:14368], A[:, 14368:16448]]
    gq = [A[:, 16448 + k * 512:16448 + (k + 1) * 512] for k in range(4)]
    gk = [A[:, 18496 + k * 512:18496 + (k + 1) * 512] for k in range(4)]
    probs = [A[:, 20544 + k * 2048:20544 + (k + 1) * 2048] for k in range(2)]
    sqq = A[:, 24640:26688]
    stgq = [A[:, 26688 + k * 2624:26688 + (k + 1) * 2624] for k in range(3)]
    stage = [Bf[:, k * 2048:(k + 1) * 2048] for k in range(2)]
    tiles = [(m, rho) for m in range(16) for rho in (0, 1)]

    st2 = {"it": 0, "cons": {}, "gq_free": {}, "gk_free": {}, "probs_free": {},
           "stage_free": {}, "stg_free": {}, "psw_ev": {}, "pswc": 0,
           "slot_T_done": {}, "slot_V_done": {}, "qdkd_done": {}, "psfree0": 0,
           "ctx_V_done": 0}

    def emit_loads(hd, vh=True):
        slot = (hd // 2) % 2
        if hd % 2 == 0:
            w = []
            if hd >= 4:
                w = [(pr.sems["T"], st2["slot_T_done"].get((hd // 2 - 2) % 2, 0)),
                     (pr.sems["V"], st2["slot_V_done"].get((hd // 2 - 2) % 2, 0))]
                w = [x for x in w if x[1]]
            l_kt = pr.dma("y", kt_v[slot][:, :], kt_sp[64 * hd:64 * hd + 128, :], waits=w)
            l_qt = pr.dma("y", qt_v[slot][:, :], qt_sp[64 * hd:64 * hd + 128, :])
            st2.setdefault("plk", {})[slot] = (l_kt, l_qt)
        else:
            l_kt, l_qt = st2["plk"][slot]
        l_vh = DY.n
        vslot = hd % 2
        if not vh:
            return l_kt, l_qt, DY.n
        vw = []
        if hd >= 2:
            vw = [(pr.sems["T"], st2.get("vh_T_done", {}).get(vslot, 0))]
            vw = [x for x in vw if x[1]]
        for (m, rho) in tiles:
            blk = 2 * m + rho
            l_vh = pr.dma("y", vh_v[vslot][:, blk * 65:blk * 65 + 64],
                          v_sp[256 * m + rho:256 * m + rho + 255:2, 64 * hd:64 * hd + 64],
                          waits=vw if (m, rho) == (0, 0) else [])
        if hd < 2:
            def f(eng, vslot=vslot):
                eng.memset(vh_v[vslot].rearrange("p (b c) -> p b c", c=65)[:, :, 64:65], 1.0)
                return eng.drain()
            pr.op("v", f, waits=[(DY, l_vh)], inc=V)
        return l_kt, l_qt, l_vh

    def gen_qdkd(hd, l_kt, l_qt):
        pslt = hd // 2
        kslot = pslt % 2
        sidx = 0
        for kind in ("q", "k"):
            WW = WQH if kind == "q" else WKH
            dtab = dq if kind == "q" else dk
            spd = qdh_sp if kind == "q" else kdh_sp
            statv = qt_v[kslot] if kind == "q" else kt_v[kslot]
            lwait = l_qt if kind == "q" else l_kt
            for rho in (0, 1):
                for tt in range(16):
                    coff = (rho * 2048 if kind == "k" else 0) + tt * 128
                    stgX = stgq[sidx % 3]
                    for ci, (c0, c1) in enumerate(((0, 512), (512, 1024), (1024, WW))):
                        bp = st2["pswc"] % 2
                        st2["pswc"] += 1
                        wl = [(DY, lwait), (DG, dg_p0)]
                        if bp in st2["psw_ev"]:
                            wl.append((V, st2["psw_ev"][bp]))

                        def f(eng, statv=statv, coff=coff, dtab=dtab, rho=rho, c0=c0, c1=c1, bp=bp):
                            pa = PSW[:, 0:c1 - c0] if bp == 0 else PS0[:, 0:c1 - c0]
                            pb = PSW[:, 512:512 + c1 - c0] if bp == 0 else PS1[:, 0:c1 - c0]
                            eng.matmul(pa, statv[0:64, coff:coff + 128],
                                       dtab[rho][0:64, c0:c1], start=True, stop=True,
                                       tile_position=(0, 0))
                            return eng.matmul(pb, statv[64:128, coff:coff + 128],
                                              dtab[rho][64:128, c0:c1], start=True, stop=True,
                                              tile_position=(64, 0))
                        t4 = pr.op("t", f, waits=wl, inc=T)

                        def f(eng, stgX=stgX, c0=c0, c1=c1, WW=WW, bp=bp):
                            dst = stgX[:, 0:2 * WW].rearrange("p (h w) -> p h w", h=2)[:, :, c0:c1]
                            if bp == 0:
                                srcv = PSW[:].rearrange("p (h w) -> p h w", h=2)[:, :, 0:c1 - c0]
                                return eng.tensor_copy(dst, srcv)
                            eng.tensor_copy(stgX[:, c0:c1], PS0[:, 0:c1 - c0])
                            return eng.tensor_copy(stgX[:, WW + c0:WW + c1], PS1[:, 0:c1 - c0])
                        prev = st2["stg_free"].get(sidx % 3)
                        pw_ = []
                        if prev and ci == 0:
                            pq, pv = prev
                            pw_ = [({"s": DS, "y": DY, "g": DG}[pq], pv)]
                        ev = pr.op("v", f, waits=[(T, t4)] + pw_, inc=V)
                        st2["psw_ev"][bp] = ev
                        if ci == 2:
                            evL = ev
                    base = pslt * 4 * NT * WW + (rho * NT + tt * 128) * (2 * WW)
                    qq = ("s", "y", "g")[sidx % 3]
                    d = pr.dma(qq, spd[base:base + 128 * 2 * WW].rearrange("(p w) -> p w", w=2 * WW),
                               stgX[:, 0:2 * WW], waits=[(V, evL)])
                    st2["stg_free"][sidx % 3] = (qq, d)
                    sidx += 1
                    yield
        st2["qdkd_done"][pslt] = (DS.n, DY.n, DG.n)
        st2["slot_T_done"][kslot] = T.n
        st2["slot_V_done"][kslot] = V.n

    def emit_chead(hd, l_kt, l_qt, l_vh, feed=None):
        slot = (hd // 2) % 2
        hh = 64 * (hd % 2)

        def fq(eng, slot=slot, hh=hh):
            eng.activation(sqq[0:64, 0:2048], qt_v[slot][hh:hh + 64, :], AF.Square)
            return eng.drain()
        sqd1 = pr.op("s", fq, waits=[(DY, l_qt), (SS, SS.n)], inc=SS)
        nmx = 0
        w0 = []
        if st2["psfree0"]:
            w0 = [(V, st2["psfree0"]), (SS, st2["cons_ss"])] if st2.get("cons_ss") else [(V, st2["psfree0"])]
        for cc in range(4):
            def f(eng, cc=cc, slot=slot):
                return eng.matmul(PS0[0:1, 0:512], sones64[:], sqq[0:64, cc * 512:(cc + 1) * 512],
                                  start=True, stop=True)
            tv = pr.op("t", f, waits=[(SS, sqd1), (V, V.n)] + (w0 if cc == 0 else []), inc=T)

            def f(eng, i=nmx):
                eng.reduce_max(mxt[:, i:i + 1], PS0[0:1, 0:512], axis=AX)
                return eng.drain()
            pr.op("v", f, waits=[(T, tv)], inc=V)
            nmx += 1
            if feed:
                next(feed, None)

        for half2 in range(2):
            def fk(eng, slot=slot, half2=half2, hh=hh):
                eng.activation(sqq[0:64, 0:2048], kt_v[slot][hh:hh + 64, half2 * 2048:half2 * 2048 + 2048], AF.Square)
                return eng.drain()
            sqd2 = pr.op("s", fk, waits=[(V, V.n), (DY, l_kt)], inc=SS)
            for cc in range(4):
                def f(eng, cc=cc):
                    return eng.matmul(PS0[0:1, 0:512], sones64[:], sqq[0:64, cc * 512:(cc + 1) * 512],
                                      start=True, stop=True)
                tv = pr.op("t", f, waits=[(SS, sqd2), (V, V.n)], inc=T)

                def f(eng, i=nmx):
                    eng.reduce_max(mxt[:, i:i + 1], PS0[0:1, 0:512], axis=AX)
                    return eng.drain()
                pr.op("v", f, waits=[(T, tv)], inc=V)
                nmx += 1
                if feed:
                    next(feed, None)

        def f(eng):
            eng.reduce_max(chs[:, 0:1], mxt[:, 0:4], axis=AX)
            eng.drain()
            eng.reduce_max(chs[:, 1:2], mxt[:, 4:12], axis=AX)
            eng.drain()
            eng.tensor_tensor(chs[:, 2:3], chs[:, 0:1], chs[:, 1:2], OP.mult)
            eng.drain()
            eng.tensor_tensor(chs[:, 3:4], chs[:, 0:1], mxe[:], OP.mult)
            eng.drain()
            eng.tensor_tensor(chs[:, 4:5], chs[:, 1:2], mxe[:], OP.mult)
            return eng.drain()
        vch = pr.op("v", f, inc=V)

        def f(eng):
            eng.activation(chs[:, 2:5], chs[:, 2:5], AF.Ln)
            eng.drain()
            eng.activation(chs[:, 2:5], chs[:, 2:5], AF.Exp, scale=0.5)
            return eng.drain()
        sch = pr.op("s", f, waits=[(V, vch)], inc=SS)

        def f(eng):
            eng.tensor_reduce(chs[:, 5:6], chs[:, 2:5], axis=AX, op=OP.add)
            eng.drain()
            eng.tensor_scalar(negC[:], chs[:, 5:6], -0.125, None, OP.mult)
            return eng.drain()
        vch2 = pr.op("v", f, waits=[(SS, sch)], inc=V)

        def f(eng):
            return eng.matmul(PS0[:, 0:1], sonesf[:], negC[:], start=True, stop=True)
        tch = pr.op("t", f, waits=[(V, vch2)], inc=T)

        def f(eng):
            eng.tensor_copy(negCb[:], PS0[:, 0:1])
            return eng.drain()
        return pr.op("v", f, waits=[(T, tch)], inc=V)

    def emit_scores(hd, ps0t_free_v, l_vh, feed=None):
        slot = (hd // 2) % 2
        vslot = hd % 2
        hh = 64 * (hd % 2)
        qd_s, qd_y, qd_g = st2["qdkd_done"][hd // 2]
        cons = st2["cons"]
        gq_free = st2["gq_free"]
        gk_free = st2["gk_free"]
        probs_free = st2["probs_free"]
        stage_free2 = st2["stage_free"]
        first_tile = True
        v_ones = V.n
        pend_ctx = [None]

        def flush_ctx():
            if pend_ctx[0] is None:
                return
            fc, cw, tic = pend_ctx[0]
            pend_ctx[0] = None
            probs_free[tic] = pr.op("t", fc, waits=cw, inc=T)

        for (m, rho) in tiles:
            ti = 2 * m + rho
            if feed:
                next(feed, None)
                next(feed, None)
            pslot = ti % 2
            band_cols = []
            for cc in range(4):
                it = st2["it"]
                sl2 = it % 2
                gsl = it % 4
                cls = [classify(cc * 512 + s * 128, m) for s in range(4)]
                runs = []
                s = 0
                while s < 4:
                    e = s
                    while e + 1 < 4 and cls[e + 1] == cls[s]:
                        e += 1
                    runs.append((cls[s], s, e + 1))
                    s = e + 1
                wl = []
                if (it - 2) in cons:
                    cv, csv = cons[it - 2]
                    if cv:
                        wl.append((V, cv))
                    if csv:
                        wl.append((SS, csv))
                if first_tile:
                    wl.append((V, ps0t_free_v))
                g_run = [r for r in runs if r[0] == "g"]
                gq_t = gk_t = None
                if g_run:
                    _, gs_, ge_ = g_run[0]
                    ns = ge_ - gs_
                    T0a = cc * 512 + gs_ * 128
                    wg = [(DS, qd_s), (DG, qd_g)]
                    if gsl in gq_free:
                        wg.append((pr.sems["T"], gq_free[gsl]))
                    baseq = (hd // 2) * 4 * NT * WQH + rho * NT * 2 * WQH + (hd % 2) * WQH \
                        + T0a * (2 * WQH - 1) + K0 + 128 * m
                    srcq = qdh_sp[baseq:baseq + ns * 128 * (2 * WQH - 1)] \
                        .rearrange("(p w) -> p w", w=2 * WQH - 1) \
                        .rearrange("(s p) w -> p s w", p=128)[:, :, 0:128]
                    gq_t = pr.dma("y", gq[gsl][:, 0:ns * 128].rearrange("p (s j) -> p s j", j=128),
                                  srcq, waits=wg)
                    wk2 = [(DS, qd_s), (DG, qd_g)]
                    if gsl in gk_free:
                        wk2.append((V, gk_free[gsl]))
                    basek = (hd // 2) * 4 * NT * WKH + rho * NT * 2 * WKH + (hd % 2) * WKH \
                        + 128 * m * (2 * WKH - 1) + K0 + T0a
                    srck = kdh_sp[basek:basek + 128 * (2 * WKH - 1)] \
                        .rearrange("(p w) -> p w", w=2 * WKH - 1)[0:128, 0:ns * 128]
                    gk_t = pr.dma("y", gk[gsl][:, 0:ns * 128], srck, waits=wk2)

                def f(eng, m=m, rho=rho, cc=cc, sl2=sl2, gsl=gsl, runs=runs, g_run=g_run, slot=slot, hh=hh):
                    ps = PSS[sl2]
                    ktt = kt_v[slot][hh:hh + 64, rho * 2048 + m * 128:rho * 2048 + (m + 1) * 128]
                    r = eng.matmul(ps[:], ktt, qt_v[slot][hh:hh + 64, cc * 512:(cc + 1) * 512],
                                   start=True, stop=False)
                    for (k_, s_, e_) in runs:
                        if k_ == "f":
                            r = eng.matmul(ps[:, s_ * 128:e_ * 128], ktt,
                                           se0[hh:hh + 64, 0:(e_ - s_) * 128], start=False, stop=False,
                                           skip_group_check=True)
                        elif k_ == "p":
                            r = eng.matmul(ps[:, s_ * 128:e_ * 128], ktt,
                                           sem_[hh:hh + 64, 0:(e_ - s_) * 128], start=False, stop=False,
                                           skip_group_check=True)
                    for (k_, s_, e_) in runs:
                        if k_ == "f":
                            r = eng.matmul(ps[:, s_ * 128:e_ * 128], se0[hh:hh + 64, 0:128],
                                           qt_v[slot][hh:hh + 64, cc * 512 + s_ * 128:cc * 512 + e_ * 128],
                                           start=False, stop=False, skip_group_check=True)
                        elif k_ == "p":
                            r = eng.matmul(ps[:, s_ * 128:e_ * 128], sem_[hh:hh + 64, 0:128],
                                           qt_v[slot][hh:hh + 64, cc * 512 + s_ * 128:cc * 512 + e_ * 128],
                                           start=False, stop=False, skip_group_check=True)
                    if g_run:
                        _, gs_, ge_ = g_run[0]
                        for s_ in range(gs_, ge_):
                            r = eng.matmul(ps[:, s_ * 128:(s_ + 1) * 128],
                                           gq[gsl][:, (s_ - gs_) * 128:(s_ - gs_ + 1) * 128],
                                           ident[:], start=False, stop=(s_ == ge_ - 1),
                                           skip_group_check=True)
                    return r
                wl2 = list(wl)
                if g_run and gq_t:
                    wl2.append((DY, gq_t))
                t5 = pr.op("t", f, waits=wl2, inc=T)
                if cc == 1:
                    flush_ctx()
                if g_run:
                    gq_free[gsl] = t5

                v_mark = 0
                s_mark = 0
                if g_run:
                    _, gs_, ge_ = g_run[0]
                    band_cols.append((cc * 512 + gs_ * 128, cc * 512 + ge_ * 128))
                    ws = [(T, t5)] + ([(DY, gk_t)] if gk_t else [])
                    if stage_free2.get(pslot):
                        ws.append((SS, stage_free2[pslot]))

                    def f(eng, cc=cc, gs_=gs_, ge_=ge_, sl2=sl2, gsl=gsl, pslot=pslot):
                        return eng.tensor_tensor(stage[pslot][:, cc * 512 + gs_ * 128:cc * 512 + ge_ * 128],
                                          PSS[sl2][:, gs_ * 128:ge_ * 128],
                                          gk[gsl][:, 0:(ge_ - gs_) * 128], OP.add)
                    v8 = pr.op("v", f, waits=ws, inc=V)
                    gk_free[gsl] = v8
                    v_mark = v8
                for (k_, s_, e_) in runs:
                    if k_ == "g":
                        continue
                    pw2 = [(T, t5)]
                    if v_mark:
                        pw2.append((V, v_mark))
                    if (ti - 2) in probs_free or hd > 0:
                        pw2.append((pr.sems["T"], probs_free.get(ti - 2, probs_free.get(30 + (ti % 2), 0)) or T.n - 1000 if False else probs_free.get(ti - 2, 0)))
                    pw2 = [x for x in pw2 if not (x[0] is pr.sems["T"] and x[1] == 0)]

                    def f(eng, cc=cc, s_=s_, e_=e_, sl2=sl2, pslot=pslot):
                        return eng.activation(probs[pslot][:, cc * 512 + s_ * 128:cc * 512 + e_ * 128],
                                       PSS[sl2][:, s_ * 128:e_ * 128], AF.Exp,
                                       bias=negCb[:], scale=0.125)
                    s_mark = pr.op("s", f, waits=pw2, inc=SS)
                cons[it] = (v_mark, s_mark)
                st2["it"] += 1

            it = st2["it"]
            if band_cols:
                lo = min(b[0] for b in band_cols)
                hi = max(b[1] for b in band_cols)
                vmax = max((cons[it - 1 - k][0] or 0) for k in range(4))
                pw3 = ([(V, vmax)] if vmax else [])
                if (ti - 2) in probs_free:
                    pw3.append((pr.sems["T"], probs_free[ti - 2]))

                def f(eng, lo=lo, hi=hi, pslot=pslot):
                    return eng.activation(probs[pslot][:, lo:hi], stage[pslot][:, lo:hi],
                                   AF.Exp, bias=negCb[:], scale=0.125)
                exp_val = pr.op("s", f, waits=pw3, inc=SS)
                stage_free2[pslot] = exp_val

            ctx_w = [(SS, SS.n), (DY, l_vh)]
            if first_tile:
                ctx_w.append((V, st2["ctx_V_done"]))

            def f(eng, blk=ti, first=first_tile, last=(ti == 31), vslot=vslot, pslot=pslot):
                r = None
                for ccq in range(4):
                    r = eng.matmul(PSC[0:65, ccq * 512:(ccq + 1) * 512],
                                   vh_v[vslot][:, blk * 65:blk * 65 + 65],
                                   probs[pslot][:, ccq * 512:(ccq + 1) * 512],
                                   start=first, stop=last, skip_group_check=True)
                return r
            pend_ctx[0] = (f, ctx_w, ti)
            first_tile = False
        flush_ctx()
        pf30, pf31 = probs_free.get(30, 0), probs_free.get(31, 0)
        probs_free.clear()
        probs_free[-2] = pf30
        probs_free[-1] = pf31
        st2["cons_ss"] = max((cons[st2["it"] - 1 - k][1] or 0) for k in range(2))

        tlast = T.n
        for ccq in range(4):
            if feed:
                next(feed, None)

            def f(eng, ccq=ccq):
                eng.reciprocal(rcp[:, 0:512], PSC[64:65, ccq * 512:(ccq + 1) * 512])
                eng.drain()
                eng.tensor_copy(rcpb[:, 0:512], rcp[:, 0:512])
                return eng.drain()
            v9 = pr.op("v", f, waits=[(T, tlast if ccq == 0 else T.n)], inc=V)

            def f(eng, ccq=ccq):
                return eng.matmul(PS0[0:64, :], sonesb[:, 0:64], rcpb[:, 0:512],
                                  start=True, stop=True)
            t7 = pr.op("t", f, waits=[(V, v9)], inc=T)

            def f(eng, ccq=ccq, hd=hd):
                eng.tensor_copy(rcpB, PS0[0:64, :])
                eng.drain()
                half = hd % 2
                pair = hd // 2
                eng.tensor_tensor(ctxT[half * 64:half * 64 + 64,
                                       pair * 2048 + ccq * 512:pair * 2048 + (ccq + 1) * 512],
                                  PSC[0:64, ccq * 512:(ccq + 1) * 512], rcpB, OP.mult)
                return eng.drain()
            fv = pr.op("v", f, waits=[(T, t7)], inc=V)
        st2["psfree0"] = fv
        st2["ctx_V_done"] = fv
        if hd % 2 == 1:
            st2["slot_T_done"][slot] = T.n
            st2["slot_V_done"][slot] = V.n
        st2.setdefault("vh_T_done", {})[vslot] = T.n

    for hd in range(0, nheads, 2):
        lks = emit_loads(hd, vh=False)
        for _ in gen_qdkd(hd, lks[0], lks[1]):
            pass
    pr.barrier()
    st2["slot_T_done"].clear()
    st2["slot_V_done"].clear()
    st2["plk"] = {}
    pend = None
    for hd in range(nheads):
        lks = emit_loads(hd)
        if pend is not None:
            ncb = emit_chead(pend[0], pend[1], pend[2], pend[3])
            emit_scores(pend[0], ncb, pend[3])
        pend = (hd, lks[0], lks[1], lks[2])
    ncb = emit_chead(pend[0], pend[1], pend[2], pend[3])
    emit_scores(pend[0], ncb, pend[3])
    pr.barrier()


    wo_v = A[:, 0:8192]
    pr.dma("g", wo_v.rearrange("p (f c) -> p f c", f=8),
           wo_d.rearrange("(f p) c -> p f c", p=128))
    bc_ld = pr.dma("y", bob, bo_row.partition_broadcast(128))
    pr.dma("y", gamb, gam_row.partition_broadcast(128))
    bc_ld = pr.dma("y", betb, bet_row.partition_broadcast(128))
    psw_free_v = 0
    p3_last = {}
    for tt in range(16):
        par = tt % 2
        xt = Bf[:, par * 4096:par * 4096 + 1024]
        h2 = Bf[:, par * 4096 + 1024:par * 4096 + 2048]
        sc1_ = Bf[:, par * 4096 + 2048:par * 4096 + 3072]
        sc2_ = Bf[:, par * 4096 + 3072:par * 4096 + 4096]
        lw = [(V, p3_last[par])] if par in p3_last else []
        xl = pr.dma("y", xt, x_b[2 * tt * 128:2 * tt * 128 + 256:2, :], waits=lw)

        def f(eng, tt=tt):
            r = None
            for pp in range(8):
                for half in range(2):
                    r = eng.matmul(PSW[:, half * 512:half * 512 + 512],
                                   ctxT[:, pp * 2048 + tt * 128:pp * 2048 + (tt + 1) * 128],
                                   wo_v[:, pp * 1024 + half * 512:pp * 1024 + half * 512 + 512],
                                   start=(pp == 0), stop=(pp == 7), skip_group_check=True)
            return r
        wl = [(DG, DG.n)]
        if psw_free_v:
            wl.append((V, psw_free_v))
        t8 = pr.op("t", f, waits=wl, inc=T)

        s4 = ln_chain(xt, None, 1e-5, waits=[(DY, xl)], center_into=sc1_, par=par)

        def f(eng, xt=xt, h2=h2, sc1_=sc1_, par=par):
            eng.tensor_scalar(sc1_, sc1_, colC[:, par:par + 1], None, OP.mult)
            eng.drain()
            eng.tensor_tensor(h2, PSW[:], bob, OP.add)
            eng.drain()
            eng.tensor_tensor(h2, h2, sc1_, OP.add)
            return eng.drain()
        v12 = pr.op("v", f, waits=[(SS, s4), (T, t8), (DY, bc_ld)], inc=V)
        psw_free_v = v12

        s6 = ln_chain(h2, None, 1e-12, waits=[(V, v12)], par=par)

        def f(eng, h2=h2, sc2_=sc2_, par=par):
            eng.tensor_scalar(h2, h2, colC[:, par:par + 1], None, OP.mult)
            eng.drain()
            eng.tensor_tensor(sc2_, h2, gamb, OP.mult)
            eng.drain()
            eng.tensor_tensor(sc2_, sc2_, betb, OP.add)
            return eng.drain()
        v14 = pr.op("v", f, waits=[(SS, s6)], inc=V)

        s8 = ln_chain(sc2_, None, 1e-5, waits=[(V, v14)], par=par)

        def f(eng, xt=xt, sc2_=sc2_, h2=h2, par=par):
            eng.tensor_scalar(sc2_, sc2_, colC[:, par:par + 1], None, OP.mult)
            eng.drain()
            eng.tensor_tensor(h2, xt, sc2_, OP.add)
            return eng.drain()
        v16 = pr.op("v", f, waits=[(SS, s8)], inc=V)
        p3_last[par] = v16
        pr.dma("s", out1_sp[tt * 128:(tt + 1) * 128, :], h2, waits=[(V, v16)])
    pr.barrier()

    ln2T = C[:, 0:16384]
    lnrow2 = [A[:, 0:1024], A[:, 1024:2048]]
    psw_free_v = 0
    p4_last = {}
    for bb in range(16):
        par = bb % 2
        xt = Bf[:, par * 4096:par * 4096 + 1024]
        lw = [(V, p4_last[par])] if par in p4_last else [(V, V.n)] if bb == 0 else []
        xl = pr.dma("y", xt, out1_sp[bb:bb + 16 * 127 + 1:16, :], waits=lw)
        s10 = ln_chain(xt, None, 1e-5, waits=[(DY, xl)], par=par)

        def f(eng, xt=xt, par=par, bb=bb):
            eng.tensor_scalar(lnrow2[par], xt, colC[:, par:par + 1], None, OP.mult)
            return eng.drain()
        v19 = pr.op("v", f, waits=[(SS, s10)], inc=V)
        p4_last[par] = v19

        def f(eng, par=par):
            r = None
            for j in range(8):
                r = eng.matmul(PSW[:, j * 128:(j + 1) * 128], lnrow2[par][:, j * 128:(j + 1) * 128],
                               ident[:], start=True, stop=True)
            return r
        wl = [(V, v19)]
        if psw_free_v:
            wl.append((V, psw_free_v))
        t9 = pr.op("t", f, waits=wl, inc=T)

        def f(eng, bb=bb):
            src = PSW[:].rearrange("p (j t) -> p j t", t=128)
            dst = ln2T.rearrange("p (j t) -> p j t", t=2048)[:, :, bb::16]
            eng.tensor_copy(dst, src)
            return eng.drain()
        v20 = pr.op("v", f, waits=[(T, t9)], inc=V)
        psw_free_v = v20

        def f(eng, bb=bb):
            r = None
            l2v = ln2T.rearrange("p (j t) -> p j t", t=2048)
            for fi in range(8):
                r = eng.matmul(PS0[:, 0:16], l2v[:, fi, bb::16],
                               sgw[:, fi * 16:(fi + 1) * 16], start=(fi == 0), stop=(fi == 7))
            return r
        t10 = pr.op("t", f, waits=[(V, v20)], inc=T)

        def f(eng, bb=bb):
            eng.tensor_copy(logits[:, bb * 16:(bb + 1) * 16], PS0[:, 0:16])
            return eng.drain()
        pr.op("v", f, waits=[(T, t10)], inc=V)
    pr.barrier()

    if phases < 5:
        return _finish(nc, pr, es)
    def f(eng):
        eng.activation(rt_sc[:], logits[:], AF.Sigmoid)
        return eng.drain()
    s11 = pr.op("s", f, inc=SS)

    def f(eng):
        eng.memset(rt_neg[:], -1e30)
        eng.drain()
        eng.tensor_tensor(rt_sfc[:], rt_sc[:], sgbr[:], OP.add)
        eng.drain()
        g4 = rt_sfc.rearrange("p (x k) -> p x k", k=4)
        eng.tensor_tensor(rt_t[0][:], g4[:, :, 0], g4[:, :, 1], OP.max)
        eng.tensor_tensor(rt_t[1][:], g4[:, :, 0], g4[:, :, 1], OP.min)
        eng.tensor_tensor(rt_t[2][:], g4[:, :, 2], g4[:, :, 3], OP.max)
        eng.tensor_tensor(rt_t[3][:], g4[:, :, 2], g4[:, :, 3], OP.min)
        eng.drain()
        eng.tensor_tensor(rt_t[4][:], rt_t[0][:], rt_t[2][:], OP.max)
        eng.tensor_tensor(rt_t[5][:], rt_t[0][:], rt_t[2][:], OP.min)
        eng.drain()
        eng.tensor_tensor(rt_t[0][:], rt_t[1][:], rt_t[3][:], OP.max)
        eng.drain()
        eng.tensor_tensor(rt_t[5][:], rt_t[5][:], rt_t[0][:], OP.max)
        eng.drain()
        eng.tensor_tensor(rt_gs[:], rt_t[4][:], rt_t[5][:], OP.add)
        return eng.drain()
    v22 = pr.op("v", f, waits=[(SS, s11)], inc=V)

    def f(eng):
        gs4 = rt_gs.rearrange("p (x g) -> p x g", g=4)
        for g in range(4):
            dsts = []
            k = 0
            for g2 in range(4):
                if g2 == g:
                    continue
                eng.tensor_tensor(rt_t[k][:, 0:16], gs4[:, :, g2], gs4[:, :, g], OP.is_gt)
                dsts.append(k)
                k += 1
            for g2 in range(g):
                eng.tensor_tensor(rt_t[k][:, 0:16], gs4[:, :, g2], gs4[:, :, g], OP.is_equal)
                dsts.append(k)
                k += 1
            eng.drain()
            acc = dsts[0]
            for d2 in dsts[1:]:
                eng.tensor_tensor(rt_t[acc][:, 0:16], rt_t[acc][:, 0:16], rt_t[d2][:, 0:16], OP.add)
                eng.drain()
            eng.tensor_scalar(rt_gm[:, g * 16:(g + 1) * 16], rt_t[acc][:, 0:16], 1.5, None, OP.is_lt)
            eng.drain()
        for g in range(4):
            for kk in range(4):
                eng.tensor_copy(rt_em[:, 4 * g + kk::16], rt_gm[:, g * 16:(g + 1) * 16])
        eng.drain()
        eng.select(rt_msk[:], rt_em[:], rt_sfc[:], rt_neg[:], add_drain=True)
        return eng.drain()
    v23 = pr.op("v", f, waits=[(V, v22)], inc=V)

    def f(eng):
        for b in range(16):
            for e in range(NE):
                eng.tensor_scalar(Bf[:, (b * 16 + e) * 16:(b * 16 + e) * 16 + 16],
                                  rt_msk[:, b * 16:(b + 1) * 16],
                                  rt_msk[:, b * 16 + e:b * 16 + e + 1], None, OP.is_gt)
        eng.drain()
        for b in range(16):
            for e in range(NE):
                eng.tensor_reduce(rt_rank[:, b * 16 + e:b * 16 + e + 1],
                                  Bf[:, (b * 16 + e) * 16:(b * 16 + e) * 16 + 16], axis=AX, op=OP.add)
        eng.drain()
        eng.memset(rt_eqr[:], 0.0)
        eng.drain()
        for b in range(16):
            for e in range(1, NE):
                eng.tensor_scalar(Bf[:, (b * 16 + e) * 16:(b * 16 + e) * 16 + e],
                                  rt_msk[:, b * 16:b * 16 + e],
                                  rt_msk[:, b * 16 + e:b * 16 + e + 1], None, OP.is_equal)
        eng.drain()
        for b in range(16):
            for e in range(1, NE):
                eng.tensor_reduce(rt_eqr[:, b * 16 + e:b * 16 + e + 1],
                                  Bf[:, (b * 16 + e) * 16:(b * 16 + e) * 16 + e], axis=AX, op=OP.add)
        eng.drain()
        eng.tensor_tensor(rt_rank[:], rt_rank[:], rt_eqr[:], OP.add)
        eng.drain()
        eng.tensor_scalar(rt_sel[:], rt_rank[:], 3.5, None, OP.is_lt)
        eng.drain()
        eng.tensor_tensor(rt_w[:], rt_sc[:], rt_sel[:], OP.mult)
        eng.drain()
        eng.tensor_reduce(rt_ws[:], rt_w.rearrange("p (x e) -> p x e", e=16), axis=AX, op=OP.add)
        eng.drain()
        eng.tensor_scalar(rt_wsi[:], rt_ws[:], 1e-20, None, OP.add)
        eng.drain()
        eng.reciprocal(rt_ws[:], rt_wsi[:])
        eng.drain()
        for b in range(16):
            eng.tensor_scalar(rt_cw[:, b * 16:(b + 1) * 16], rt_w[:, b * 16:(b + 1) * 16],
                              rt_ws[:, b:b + 1], 2.5, OP.mult, OP.mult)
        return eng.drain()
    v24 = pr.op("v", f, waits=[(V, v23)], inc=V)
    if dbg:
        def f(eng):
            eng.tensor_copy(C[:, 0:512].bitcast(F32), logits[:])
            eng.tensor_copy(C[:, 512:1024].bitcast(F32), rt_sfc[:])
            eng.tensor_copy(C[:, 1024:1152].bitcast(F32), rt_gs[:])
            eng.tensor_copy(C[:, 1152:1280].bitcast(F32), rt_gm[:])
            eng.tensor_copy(C[:, 1536:2048].bitcast(F32), rt_rank[:])
            eng.tensor_copy(C[:, 2048:2560].bitcast(F32), rt_sel[:])
            return eng.drain()
        vdmp = pr.op("v", f, waits=[(V, v24)], inc=V)
        pr.dma("s", rt_dump[:, 0:256], C[:, 0:512].bitcast(F32), waits=[(V, vdmp)])
        pr.dma("s", rt_dump[:, 256:512], C[:, 512:1024].bitcast(F32), waits=[(V, vdmp)])
        pr.dma("s", rt_dump[:, 512:576], C[:, 1024:1152].bitcast(F32), waits=[(V, vdmp)])
        pr.dma("s", rt_dump[:, 640:704], C[:, 1152:1280].bitcast(F32), waits=[(V, vdmp)])
        pr.dma("s", rt_dump[:, 768:1024], C[:, 1536:2048].bitcast(F32), waits=[(V, vdmp)])
        pr.dma("s", rt_dump[:, 1024:1280], C[:, 2048:2560].bitcast(F32), waits=[(V, vdmp)])
    for e in range(NE):
        def f(eng, e=e):
            eng.tensor_copy(rt_c1[:], rt_cw[:, e::16])
            return eng.drain()
        vcp = pr.op("v", f, waits=[(V, v24)] + ([(DS, DS.n)] if e else []), inc=V)
        pr.dma("s", cw_sp[e * NT:(e + 1) * NT].rearrange("(p b) -> p b", b=16),
               rt_c1[:], waits=[(V, vcp)])
    pr.barrier()

    if phases < 6:
        return _finish(nc, pr, es)
    wsg_v = A[:, 16384:24576]
    wsu_v = A[:, 24576:32768]
    wg_v = [A[:, 0:4096], A[:, 4096:8192]]
    wu_v = [A[:, 8192:12288], A[:, 12288:16384]]
    pr.dma("g", wsg_v.rearrange("p (f c) -> p f c", f=8),
           wsgT_d.rearrange("(f p) c -> p f c", p=128))
    pr.dma("g", wsu_v.rearrange("p (f c) -> p f c", f=8),
           wsuT_d.rearrange("(f p) c -> p f c", p=128))
    wslot_free = {}
    ps0_free = 0
    ps1_free = 0
    actb_free = 0
    awb_free = {}
    last_t_of = {}
    cw_free = {}
    aidx = 0
    for e in range(NE + 1):
        shared = (e == NE)
        sl = e % 2
        nit = 8 if shared else 4
        if not shared:
            ww = [(T, last_t_of[e - 2])] if (e - 2) in last_t_of else []
            pr.dma("g", wg_v[sl].rearrange("p (f i) -> p f i", f=8),
                   wgT_d[e].rearrange("(f p) i -> p f i", p=128), waits=ww)
            wload = pr.dma("g", wu_v[sl].rearrange("p (f i) -> p f i", f=8),
                           wuT_d[e].rearrange("(f p) i -> p f i", p=128))
        last_t = 0
        for cc in range(4):
            if not shared:
                cwl = [(GP, cw_free[cc % 2])] if (cc % 2) in cw_free else []
                cwt = pr.dma("y", cwB[cc % 2],
                             cw_sp[e * NT + cc * 512:e * NT + (cc + 1) * 512]
                             .rearrange("(o t) -> o t", o=1).partition_broadcast(128),
                             waits=cwl)
            for ii in range(nit):
                wv1 = wsg_v if shared else wg_v[sl]
                wv2 = wsu_v if shared else wu_v[sl]
                wid = 1024 if shared else 512

                def f(eng, wv1=wv1, cc=cc, ii=ii, wid=wid):
                    r = None
                    for fi in range(8):
                        r = eng.matmul(PS0[:], wv1[:, fi * wid + ii * 128:fi * wid + ii * 128 + 128],
                                       ln2T[:, fi * 2048 + cc * 512:fi * 2048 + (cc + 1) * 512],
                                       start=(fi == 0), stop=(fi == 7))
                    return r
                wl = [] if shared else [(DG, wload)]
                if ps0_free:
                    wl.append((SS, ps0_free))
                tg = pr.op("t", f, waits=wl, inc=T)

                def f(eng, wv2=wv2, cc=cc, ii=ii, wid=wid):
                    r = None
                    for fi in range(8):
                        r = eng.matmul(PS1[:], wv2[:, fi * wid + ii * 128:fi * wid + ii * 128 + 128],
                                       ln2T[:, fi * 2048 + cc * 512:fi * 2048 + (cc + 1) * 512],
                                       start=(fi == 0), stop=(fi == 7))
                    return r
                wl = []
                if ps1_free:
                    wl.append((V, ps1_free))
                tu = pr.op("t", f, waits=wl, inc=T)
                last_t = tu

                def f(eng):
                    eng.activation(silb, PS0[:], AF.Silu)
                    return eng.drain()
                sg2 = pr.op("s", f, waits=[(T, tg)], inc=SS)
                ps0_free = sg2

                asl = aidx % 2
                aw = [(T, tu), (SS, sg2)]
                if shared and asl in awb_free:
                    aw.append((DS, awb_free[asl]))
                if not shared and actb_free:
                    aw.append((GP, actb_free))

                def f(eng, shared=shared, asl=asl):
                    dst = actwb[asl] if shared else actb
                    eng.scalar_tensor_tensor(dst, PS1[:], 1.0, silb, OP.mult, OP.mult)
                    return eng.drain()
                vv = pr.op("v", f, waits=aw, inc=V)
                ps1_free = vv

                if not shared:
                    gw2 = [(V, vv), (DY, cwt)]
                    if asl in awb_free:
                        gw2.append((DS, awb_free[asl]))

                    def f(eng, asl=asl, cc=cc):
                        eng.tensor_tensor(actwb[asl], actb, cwB[cc % 2], OP.mult)
                        return eng.drain()
                    gg = pr.op("g", f, waits=gw2, inc=GP)
                    cw_free[cc % 2] = gg
                    actb_free = gg
                    base = e * EI * NT + ii * 128 * NT
                    dd = pr.dma("s", actw_sp[base:base + 128 * NT]
                                .rearrange("(p t) -> p t", t=NT)[:, cc * 512:(cc + 1) * 512],
                                actwb[asl], waits=[(GP, gg)])
                else:
                    base = ii * 128 * NT
                    dd = pr.dma("s", gs_sp[base:base + 128 * NT]
                                .rearrange("(p t) -> p t", t=NT)[:, cc * 512:(cc + 1) * 512],
                                actwb[asl], waits=[(V, vv)])
                awb_free[asl] = dd
                aidx += 1
        if not shared:
            last_t_of[e] = last_t
    pr.barrier()

    if phases < 7:
        return _finish(nc, pr, es)
    y_ps_free = {}
    last_ty = 0
    prev_out = 0
    for half in range(2):
        wl0 = [(T, T.n)]
        for e in range(NE):
            pr.dma("g", A[:, e * 2048:(e + 1) * 2048].rearrange("p (i c) -> p i c", i=4),
                   wdT_d[e].rearrange("(i p) c -> p i c", p=128)[:, :, half * 512:half * 512 + 512],
                   waits=wl0 if e == 0 else [])
        dwl = pr.dma("g", A[:, 32768:36864].rearrange("p (i c) -> p i c", i=8),
                     wsdT_d.rearrange("(i p) c -> p i c", p=128)[:, :, half * 512:half * 512 + 512])
        for ts_ in range(16):
            for e in range(NE):
                pr.dma("y", C[:, e * 512:(e + 1) * 512].rearrange("p (i t) -> p i t", t=128),
                       actw_sp[e * EI * NT:(e + 1) * EI * NT]
                       .rearrange("(i p t) -> p i t", p=128, t=NT)[:, :, ts_ * 128:(ts_ + 1) * 128],
                       waits=[(T, last_ty)] if e == 0 else [])
            aload = pr.dma("y", C[:, 8192:9216].rearrange("p (i t) -> p i t", t=128),
                           gs_sp.rearrange("(i p t) -> p i t", p=128, t=NT)[:, :, ts_ * 128:(ts_ + 1) * 128])
            slot = ts_ % 2

            def f(eng, slot=slot):
                r = None
                first = True
                for e in range(NE):
                    for ii in range(4):
                        r = eng.matmul(PSS[slot][:], C[:, e * 512 + ii * 128:e * 512 + (ii + 1) * 128],
                                       A[:, e * 2048 + ii * 512:e * 2048 + (ii + 1) * 512],
                                       start=first, stop=False, skip_group_check=True)
                        first = False
                for jj in range(8):
                    r = eng.matmul(PSS[slot][:], C[:, 8192 + jj * 128:8192 + (jj + 1) * 128],
                                   A[:, 32768 + jj * 512:32768 + (jj + 1) * 512],
                                   start=False, stop=(jj == 7), skip_group_check=True)
                return r
            wl = [(DY, aload), (DG, dwl)]
            if (slot, "ps") in y_ps_free:
                wl.append((V, y_ps_free[(slot, "ps")]))
            ty = pr.op("t", f, waits=wl, inc=T)
            last_ty = ty

            if half == 0:
                def f(eng, ts_=ts_, slot=slot):
                    eng.tensor_copy(Bf[:, ts_ * 512:(ts_ + 1) * 512], PSS[slot][:])
                    return eng.drain()
                vv = pr.op("v", f, waits=[(T, ty)], inc=V)
                y_ps_free[(slot, "ps")] = vv
            else:
                xl = pr.dma("y", o1t[:], out1_sp[ts_ * 128:(ts_ + 1) * 128, :], waits=[(V, V.n)])
                ep_w = [(DY, xl)]
                if prev_out:
                    ep_w.append((DS, prev_out))
                s12 = ln_chain(o1t, None, 1e-5, waits=ep_w, center_into=ep1)

                def f(eng, ts_=ts_, slot=slot):
                    eng.tensor_scalar(ep1[:], ep1[:], colC[:, 0:1], None, OP.mult)
                    eng.drain()
                    eng.tensor_tensor(ep2[:, 0:512], ep1[:, 0:512], Bf[:, ts_ * 512:(ts_ + 1) * 512], OP.add)
                    eng.drain()
                    eng.tensor_tensor(ep2[:, 512:1024], ep1[:, 512:1024], PSS[slot][:], OP.add)
                    return eng.drain()
                v25 = pr.op("v", f, waits=[(SS, s12), (T, ty)], inc=V)
                y_ps_free[(slot, "ps")] = v25

                s13 = ln_chain(ep2, None, 1e-5, waits=[(V, v25)])

                def f(eng):
                    eng.tensor_scalar(ep2[:], ep2[:], colC[:, 0:1], None, OP.mult)
                    eng.drain()
                    eng.tensor_tensor(ep1[:], o1t[:], ep2[:], OP.add)
                    return eng.drain()
                v26 = pr.op("v", f, waits=[(SS, s13)], inc=V)
                prev_out = pr.dma("s", outp[ts_ * 128:(ts_ + 1) * 128, :], ep1[:], waits=[(V, v26)])

    return _finish(nc, pr, es)


def host_maps(inp):
    bf = ml_dtypes.bfloat16
    f32 = np.float32
    ac = np.ascontiguousarray
    x = np.asarray(inp["L0_attn_hidden_states"], dtype=f32)
    dist = np.asarray(inp["L0_attn_dist_emb"], dtype=f32)
    Wq = ac(np.asarray(inp["L0_attn_Wq"], f32)); Wk = ac(np.asarray(inp["L0_attn_Wk"], f32))
    Wv = ac(np.asarray(inp["L0_attn_Wv"], f32)); Wo = ac(np.asarray(inp["L0_attn_Wo"], f32))
    bq = np.asarray(inp["L0_attn_bq"], f32); bk = np.asarray(inp["L0_attn_bk"], f32)
    bv = np.asarray(inp["L0_attn_bv"], f32); bo = np.asarray(inp["L0_attn_bo"], f32)
    lng = np.asarray(inp["L0_attn_ln_gamma"], f32); lnb = np.asarray(inp["L0_attn_ln_beta"], f32)
    gw = np.asarray(inp["L0_ffn_gate_w"], f32)
    gbias = np.asarray(inp["L0_ffn_gate_bias"], f32)
    Wg = np.asarray(inp["L0_ffn_Wg"], f32)
    Wu = np.asarray(inp["L0_ffn_Wu"], f32)
    Wd = np.asarray(inp["L0_ffn_Wd"], f32)
    Wsg = np.asarray(inp["L0_ffn_Ws_g"], f32)
    Wsu = np.asarray(inp["L0_ffn_Ws_u"], f32)
    Wsd = np.asarray(inp["L0_ffn_Ws_d"], f32)

    w = np.arange(WQH)
    com = {
        "wq_d": Wq, "wk_d": Wk, "wv_d": Wv, "wo_d": Wo,
        "bqr_d": bq[None, :], "bkr_d": bk[None, :],
        "ones512_d": np.ones((1, 512), bf),
        "bv_row": bv[None, :], "bo_row": bo[None, :],
        "gam_row": lng[None, :], "bet_row": lnb[None, :],
        "tq0_d": ac(np.tile(dist[np.clip(2 * (K0 - w) + 1023, 0, 2046)].T, (2, 1))),
        "tk0_d": ac(np.tile(dist[np.clip(2 * (w - K0) + 1023, 0, 2046)].T, (2, 1))),
        "e0_d": ac(np.tile(np.tile(dist[0][:, None], (1, 512)), (2, 1))),
        "em_d": ac(np.tile(np.tile(dist[2046][:, None], (1, 512)), (2, 1))),
        "id_d": np.eye(128, dtype=bf),
        "onesf_d": np.ones((1, 128), f32),
        "onesb_d": np.ones((1, 128), bf),
        "ones64_d": np.ones((64, 1), bf),
        "gw_d": ac(gw.T),
        "gbr_d": ac(np.tile(gbias, 16)[None, :]),
        "wgT_d": ac(Wg.transpose(0, 2, 1)),
        "wuT_d": ac(Wu.transpose(0, 2, 1)),
        "wdT_d": ac(Wd.transpose(0, 2, 1)),
        "wsgT_d": ac(Wsg.T), "wsuT_d": ac(Wsu.T), "wsdT_d": ac(Wsd.T),
    }
    in_maps = []
    for core in range(8):
        b, h = core // 2, core % 2
        xp = x[b]
        if h:
            xp = xp.reshape(NT, 2, E)[:, ::-1, :].reshape(S, E)
        m = dict(com)
        m["x_b"] = ac(xp)
        m["tq1_d"] = ac(np.tile(dist[np.clip(2 * (K0 - w + h) + 1022, 0, 2046)].T, (2, 1)))
        m["tk1_d"] = ac(np.tile(dist[np.clip(2 * (w - K0 + h) + 1022, 0, 2046)].T, (2, 1)))
        in_maps.append(m)

    return in_maps


def kernel(**inp):
    in_maps = host_maps(inp)
    nc = build_program()
    res = run_bass_kernel_spmd(nc, in_maps, core_ids=list(range(8)))
    out = np.empty((4, S, E), np.float32)
    for core in range(8):
        b, h = core // 2, core % 2
        out[b, h::2] = np.asarray(res.results[core]["outp"], np.float32)
    return out


# version 16
# speedup vs baseline: 1.1087x; 1.0301x over previous
import contextlib
import numpy as np
import ml_dtypes
import concourse.bass as bass
import concourse.mybir as mybir
from concourse.bass_utils import run_bass_kernel_spmd
from concourse.alu_op_type import AluOpType as OP

F32 = mybir.dt.float32
BF16 = mybir.dt.bfloat16
I32 = mybir.dt.int32
AF = mybir.ActivationFunctionType
AX = mybir.AxisListType.X

S, E, NH = 4096, 1024, 16
NT = 2048
K0 = 644
WQH = WKH = 1288
NE, EI, SHI = 16, 512, 1024
SLACK = 16 * WQH


class Sem:
    def __init__(self, nc, es, name):
        self.h = es.enter_context(nc.semaphore(name))
        self.n = 0


class Prog:
    def __init__(self, nc, es):
        self.nc = nc
        self.ops = {k: [] for k in "ygtvs"}
        self.sems = {}
        for nm in ("DY", "DG", "DS", "T", "V", "SS", "GP", "BR"):
            self.sems[nm] = Sem(nc, es, nm)
        self.nbar = 0

    def dma(self, q, out_ap, in_ap, waits=()):
        sem = self.sems[{"y": "DY", "g": "DG", "s": "DS"}[q]]
        sem.n += 16
        tgt = sem.n
        w = tuple(waits)

        def f(eng, o=out_ap, i=in_ap, w=w, sh=sem.h):
            for (s, v) in w:
                eng.wait_ge(s.h, v)
            eng.dma_start(out=o, in_=i).then_inc(sh, 16)
        self.ops[q].append(f)
        return tgt

    def op(self, q, fn, waits=(), inc=None):
        w = tuple(waits)
        tgt = None
        if inc is not None:
            inc.n += 1
            tgt = inc.n

        def f(eng, fn=fn, w=w, inc=inc):
            for (s, v) in w:
                eng.wait_ge(s.h, v)
            r = fn(eng)
            if inc is not None:
                if r is None:
                    r = eng.nop()
                r.then_inc(inc.h, 1)
        self.ops[q].append(f)
        return tgt

    def barrier(self):
        br = self.sems["BR"]
        self.nbar += 1
        tgt = self.nbar * 5
        br.n += 5
        for q in "ygtvs":
            def f(eng, t=tgt, h=br.h):
                eng.nop().then_inc(h, 1)
                eng.wait_ge(h, t)
            self.ops[q].append(f)


def classify(T0, m):
    d2 = 2 * T0 - 256 * m
    if abs(d2) <= 1024:
        return "g"
    return "p" if d2 >= 1280 else "f"




def _finish(nc, pr, es):
    DS = pr.sems["DS"]
    fin = DS.n

    def f(eng, fin=fin):
        eng.wait_ge(DS.h, fin)
        return None
    pr.op("s", f)
    pr.barrier()
    with nc.Block() as block:
        @block.sync
        def _(eng):
            for f2 in pr.ops["y"]:
                f2(eng)

        @block.gpsimd
        def _(eng):
            for f2 in pr.ops["g"]:
                f2(eng)

        @block.tensor
        def _(eng):
            for f2 in pr.ops["t"]:
                f2(eng)

        @block.vector
        def _(eng):
            for f2 in pr.ops["v"]:
                f2(eng)

        @block.scalar
        def _(eng):
            for f2 in pr.ops["s"]:
                f2(eng)
    es.close()
    return nc


def build_program(phases=9, dbg=False, nheads=NH, p2mode=5):
    nc = bass.Bass("TRN2", target_bir_lowering=False)
    es = contextlib.ExitStack()
    pr = Prog(nc, es)
    DY, DG, DS, T, V, SS, GP = (pr.sems[k] for k in ("DY", "DG", "DS", "T", "V", "SS", "GP"))

    def dt(name, shape, dtp, kind="ExternalInput"):
        return nc.dram_tensor(name, shape, dtp, kind=kind).ap()

    x_b = dt("x_b", [S, E], F32)
    wq_d = dt("wq_d", [E, E], F32); wk_d = dt("wk_d", [E, E], F32)
    wv_d = dt("wv_d", [E, E], F32); wo_d = dt("wo_d", [E, E], F32)
    bqr_d = dt("bqr_d", [1, E], F32); bkr_d = dt("bkr_d", [1, E], F32)
    ones512_d = dt("ones512_d", [1, 512], BF16)
    bv_row = dt("bv_row", [1, E], F32); bo_row = dt("bo_row", [1, E], F32)
    gam_row = dt("gam_row", [1, E], F32); bet_row = dt("bet_row", [1, E], F32)
    tq0_d = dt("tq0_d", [128, WQH], F32); tq1_d = dt("tq1_d", [128, WQH], F32)
    tk0_d = dt("tk0_d", [128, WKH], F32); tk1_d = dt("tk1_d", [128, WKH], F32)
    e0_d = dt("e0_d", [128, 512], F32); em_d = dt("em_d", [128, 512], F32)
    id_d = dt("id_d", [128, 128], BF16)
    onesf_d = dt("onesf_d", [1, 128], F32)
    onesb_d = dt("onesb_d", [1, 128], BF16)
    ones64_d = dt("ones64_d", [64, 1], BF16)
    gw_d = dt("gw_d", [E, NE], F32)
    gbr_d = dt("gbr_d", [1, 256], F32)
    wgT_d = dt("wgT_d", [NE, E, EI], F32)
    wuT_d = dt("wuT_d", [NE, E, EI], F32)
    wdT_d = dt("wdT_d", [NE, EI, E], F32)
    wsgT_d = dt("wsgT_d", [E, SHI], F32)
    wsuT_d = dt("wsuT_d", [E, SHI], F32)
    wsdT_d = dt("wsdT_d", [SHI, E], F32)
    outp = dt("outp", [NT, E], F32, kind="ExternalOutput")
    IN = "Internal"
    DBG = "ExternalOutput" if dbg else IN
    kt_sp = dt("kt_sp", [E, S], BF16, kind=DBG)
    qt_sp = dt("qt_sp", [E, NT], BF16, kind=DBG)
    v_sp = dt("v_sp", [S, E], BF16, kind=DBG)
    qdh_sp = dt("qdh_sp", [32 * NT * WQH + 4 * SLACK], BF16, kind=DBG)
    kdh_sp = dt("kdh_sp", [32 * NT * WKH + 4 * SLACK], BF16, kind=DBG)
    out1_sp = dt("out1_sp", [NT, E], F32, kind=DBG)
    actw_sp = dt("actw_sp", [NE * EI * NT], BF16, kind=DBG)
    gs_sp = dt("gs_sp", [SHI * NT], BF16, kind=DBG)
    cw_sp = dt("cw_sp", [NE * NT], BF16, kind=DBG)
    rt_dump = dt("rt_dump", [128, 1536], F32, kind=DBG) if dbg else None

    def sb(name, shape, dtp):
        return es.enter_context(nc.sbuf_tensor(name, shape, dtp))

    A = sb("A", [128, 36864], BF16)
    Bf = sb("Bf", [128, 8192], F32)
    C = sb("C", [128, 16384], BF16)
    fscr = sb("fscr", [128, 1024], F32)
    o1t = C[:, 9216:11264].bitcast(F32)
    ep1 = C[:, 11264:13312].bitcast(F32)
    ep2 = C[:, 13312:15360].bitcast(F32)
    bvb = sb("bvb", [128, E], BF16)
    bob = C[:, 0:2048].bitcast(F32)
    gamb = C[:, 2048:4096].bitcast(F32)
    betb = C[:, 4096:6144].bitcast(F32)
    sbqr = sb("sbqr", [1, E], BF16)
    sbkr = sb("sbkr", [1, E], BF16)
    sbvr = sb("sbvr", [1, E], BF16)
    ones512 = sb("ones512", [1, 512], BF16)
    dq = [sb(f"dq{r}", [128, WQH], BF16) for r in (0, 1)]
    dk = [sb(f"dk{r}", [128, WKH], BF16) for r in (0, 1)]
    se0 = sb("se0", [128, 512], BF16)
    sem_ = sb("sem_", [128, 512], BF16)
    ident = sb("ident", [128, 128], BF16)
    sonesf = sb("sonesf", [1, 128], F32)
    sonesb = sb("sonesb", [1, 128], BF16)
    sones64 = sb("sones64", [64, 1], BF16)
    sgw = sb("sgw", [128, 8 * NE], BF16)
    sgbr = sb("sgbr", [128, 256], F32)
    mxt = sb("mxt", [1, 16], F32)
    chs = sb("chs", [1, 8], F32)
    negC = sb("negC", [1, 1], F32)
    negCb = sb("negCb", [128, 1], F32)
    mxe = sb("mxe", [1, 1], F32)
    rcp = A[:, 34560:35584].bitcast(F32)[0:1, :]
    rcpb = A[:, 35584:36096][0:1, :]
    rcpB = A[64:128, 34560:35072]
    ctxT = sb("ctxT", [128, 8 * NT], BF16)
    colA = sb("colA", [128, 2], F32)
    colB = sb("colB", [128, 2], F32)
    colC = sb("colC", [128, 2], F32)
    rt_sc = A[:, 2048:2560].bitcast(F32)
    rt_sfc = A[:, 2560:3072].bitcast(F32)
    rt_msk = A[:, 3072:3584].bitcast(F32)
    rt_rank = A[:, 3584:4096].bitcast(F32)
    rt_eqr = A[:, 4096:4608].bitcast(F32)
    rt_sel = A[:, 4608:5120].bitcast(F32)
    rt_gs = sb("rt_gs", [128, 64], F32)
    rt_t = [sb(f"rt_t{i}", [128, 64], F32) for i in range(6)]
    rt_gm = sb("rt_gm", [128, 64], F32)
    rt_em = A[:, 5120:5632].bitcast(I32)
    rt_neg = A[:, 5632:6144].bitcast(F32)
    rt_w = A[:, 6144:6656].bitcast(F32)
    rt_ws = sb("rt_ws", [128, 16], F32)
    rt_wsi = sb("rt_wsi", [128, 16], F32)
    rt_cw = sb("rt_cw", [128, 256], BF16)
    rt_c1 = sb("rt_c1", [128, 16], BF16)
    logits = A[:, 6656:7168].bitcast(F32)
    cwB = [A[:, 32768 + i * 512:32768 + (i + 1) * 512] for i in (0, 1)]
    silb = A[:, 33792:34304]
    actb = A[:, 34304:34816]
    actwb = [A[:, 34816 + i * 512:34816 + (i + 1) * 512] for i in (0, 1)]

    def psal(name, shape):
        return es.enter_context(nc.psum_tensor(name, shape, F32))

    PS0 = psal("PS0", [128, 512])
    PS1 = psal("PS1", [128, 512])
    PSW = psal("PSW", [128, 1024])
    PSC = psal("PSC", [65, 2048])
    PSS = [PS0, PS1]
    PSS4 = [PS0, PS1, PSW[:, 0:512], PSW[:, 512:1024]]

    def ln_chain(xt, outbuf, eps, out_bf=None, waits=(), center_into=None, par=0):
        tgt = center_into if center_into is not None else xt
        cA = colA[:, par:par + 1]
        cB = colB[:, par:par + 1]
        cC = colC[:, par:par + 1]

        def f(eng, xt=xt, tgt=tgt, cA=cA):
            eng.tensor_reduce(cA, xt, axis=AX, op=OP.add)
            eng.drain()
            eng.tensor_scalar(cA, cA, 1.0 / E, None, OP.mult)
            eng.drain()
            eng.tensor_scalar(tgt, xt, cA, None, OP.subtract)
            return eng.drain()
        v1 = pr.op("v", f, waits=list(waits), inc=V)

        def f(eng, tgt=tgt, cB=cB):
            eng.activation(fscr[:], tgt, AF.Square, accum_out=cB)
            return eng.drain()
        s1 = pr.op("s", f, waits=[(V, v1)], inc=SS)

        def f(eng, eps=eps, cB=cB):
            eng.tensor_scalar(cB, cB, 1.0 / E, eps, OP.mult, OP.add)
            return eng.drain()
        v2 = pr.op("v", f, waits=[(SS, s1)], inc=V)

        def f(eng, cB=cB, cC=cC):
            eng.activation(cC, cB, AF.Ln)
            eng.drain()
            eng.activation(cC, cC, AF.Exp, scale=-0.5)
            return eng.drain()
        s2 = pr.op("s", f, waits=[(V, v2)], inc=SS)
        return s2

    pr.dma("g", bvb[:], bv_row.partition_broadcast(128))
    pr.dma("g", sbqr[:], bqr_d)
    pr.dma("g", sbkr[:], bkr_d)
    pr.dma("g", sbvr[:], bv_row)
    pr.dma("y", ones512[:], ones512_d)
    for r in (0, 1):
        pr.dma("g", dq[r][:], (tq0_d, tq1_d)[r])
        pr.dma("g", dk[r][:], (tk0_d, tk1_d)[r])
    pr.dma("g", se0[:], e0_d)
    pr.dma("g", sem_[:], em_d)
    pr.dma("y", ident[:], id_d)
    pr.dma("y", sonesf[:], onesf_d)
    pr.dma("y", sonesb[:], onesb_d)
    pr.dma("y", sones64[:], ones64_d)
    pr.dma("g", sgw[:].rearrange("p (f e) -> p f e", e=NE),
           gw_d.rearrange("(f p) e -> p f e", p=128))
    pr.dma("y", sgbr[:], gbr_d.partition_broadcast(128))
    wq_v = A[:, 0:8192]; wk_v = A[:, 8192:16384]; wv_v = A[:, 16384:24576]
    for (w_in, w_sb) in ((wq_d, wq_v), (wk_d, wk_v), (wv_d, wv_v)):
        pr.dma("g", w_sb.rearrange("p (f c) -> p f c", f=8),
               w_in.rearrange("(f p) c -> p f c", p=128))
    dg_p0 = DG.n
    dy_p0 = DY.n
    pr.barrier()

    sqv = [A[0:64, 24576 + i * WKH:24576 + (i + 1) * WKH] for i in (0, 1)]

    def f(eng):
        eng.activation(sqv[0], dk[0][0:64, :], AF.Square)
        eng.drain()
        eng.activation(sqv[1], dk[1][0:64, :], AF.Square)
        return eng.drain()
    sqd = pr.op("s", f, waits=[(DG, dg_p0)], inc=SS)
    nmm = 0
    for r in (0, 1):
        for cc in range(3):
            w0 = cc * 512
            wid = min(512, WKH - w0)

            def f(eng, r=r, w0=w0, wid=wid):
                return eng.matmul(PS0[0:1, 0:wid], sones64[:], sqv[r][:, w0:w0 + wid],
                                  start=True, stop=True)
            tv = pr.op("t", f, waits=[(SS, sqd)] + ([(V, V.n)] if nmm else []), inc=T)

            def f(eng, i=nmm, wid=wid):
                eng.reduce_max(mxt[:, i:i + 1], PS0[0:1, 0:wid], axis=AX)
                return eng.drain()
            pr.op("v", f, waits=[(T, tv)], inc=V)
            nmm += 1

    def f(eng):
        eng.reduce_max(mxe[:], mxt[:, 0:6], axis=AX)
        return eng.drain()
    pr.op("v", f, inc=V)
    pr.barrier()

    if phases < 1:
        return _finish(nc, pr, es)
    ln1T = [A[:, 24576 + k * 4096:24576 + (k + 1) * 4096] for k in (0, 1)]
    lnrow = [A[:, 32768 + k * 1024:32768 + (k + 1) * 1024] for k in range(2)]
    stg = [A[:, 34816 + k * 512:34816 + (k + 1) * 512] for k in range(4)]
    xfree = {}
    psw_free = 0
    lnrow_free = {}
    pss_free = [0, 0]
    stg_free = {}
    kq_iter = 0
    for c in range(8):
        cb = c % 2
        ln_done = []
        for i in range(4):
            xt = Bf[:, (cb * 4 + i) * 1024:(cb * 4 + i + 1) * 1024]
            lw = [(V, xfree[(c - 2, i)])] if (c - 2, i) in xfree else []
            xl = pr.dma("y", xt, x_b[c * 512 + i * 128:c * 512 + (i + 1) * 128, :], waits=lw)
            lrs = lnrow[i % 2]
            lr_w = [(T, lnrow_free[i % 2])] if i % 2 in lnrow_free else []
            s2 = ln_chain(xt, None, 1e-5, waits=[(DY, xl)] + lr_w, par=i % 2)

            def f(eng, xt=xt, lrs=lrs, i=i):
                eng.tensor_scalar(lrs, xt, colC[:, i % 2:i % 2 + 1], None, OP.mult)
                return eng.drain()
            v3 = pr.op("v", f, waits=[(SS, s2)], inc=V)
            xfree[(c, i)] = v3

            def f(eng, lrs=lrs):
                r = None
                for j in range(8):
                    r = eng.matmul(PSW[:, j * 128:(j + 1) * 128], lrs[:, j * 128:(j + 1) * 128],
                                   ident[:], start=True, stop=True)
                return r
            t1 = pr.op("t", f, waits=[(V, v3), (DY, dy_p0)] + ([(V, psw_free)] if psw_free else []), inc=T)
            lnrow_free[i % 2] = t1

            def f(eng, cb=cb, i=i):
                src = PSW[:].rearrange("p (j t) -> p j t", t=128)
                dst = ln1T[cb].rearrange("p (j t) -> p j t", t=512)[:, :, i * 128:(i + 1) * 128]
                eng.tensor_copy(dst, src)
                return eng.drain()
            v4 = pr.op("v", f, waits=[(T, t1)], inc=V)
            psw_free = v4
            ln_done.append(v4)

        for proj in ("k", "q"):
            wmat = wk_v if proj == "k" else wq_v
            for fo in range(8):
                slot = kq_iter % 2
                kq_iter += 1
                pw = [(SS, pss_free[slot][1])] if pss_free[slot] else []

                bvec = sbkr if proj == "k" else sbqr

                def f(eng, wmat=wmat, fo=fo, cb=cb, slot=slot, bvec=bvec):
                    eng.matmul(PSS[slot][:], bvec[0:1, fo * 128:(fo + 1) * 128], ones512[:], start=True, stop=False,
                               skip_group_check=True)
                    r = None
                    for fi in range(8):
                        r = eng.matmul(PSS[slot][:],
                                       wmat[:, fi * 1024 + fo * 128:fi * 1024 + (fo + 1) * 128],
                                       ln1T[cb][:, fi * 512:(fi + 1) * 512],
                                       start=False, stop=(fi == 7), skip_group_check=True)
                    return r
                t2 = pr.op("t", f, waits=[(V, ln_done[3]), (DG, dg_p0)] + pw, inc=T)
                if proj == "k":
                    kw = [(DS, stg_free["k"])] if "k" in stg_free else []

                    def f(eng, fo=fo, slot=slot):
                        eng.copy(stg[0][:, 0:256], PSS[slot][:, 0::2])
                        return eng.copy(stg[1][:, 0:256], PSS[slot][:, 1::2])
                    v5 = pr.op("s", f, waits=[(T, t2)] + kw, inc=SS)
                    pr.dma("s", kt_sp[fo * 128:(fo + 1) * 128, 256 * c:256 * c + 256], stg[0][:, 0:256], waits=[(SS, v5)])
                    d2 = pr.dma("s", kt_sp[fo * 128:(fo + 1) * 128, 2048 + 256 * c:2048 + 256 * c + 256],
                                stg[1][:, 0:256], waits=[(SS, v5)])
                    stg_free["k"] = d2
                    pss_free[slot] = ("s", v5)
                else:
                    qw = [(DS, stg_free["q"])] if "q" in stg_free else []

                    def f(eng, fo=fo, slot=slot):
                        return eng.copy(stg[2][:, 0:256], PSS[slot][:, 0::2])
                    v5 = pr.op("s", f, waits=[(T, t2)] + qw, inc=SS)
                    d1 = pr.dma("s", qt_sp[fo * 128:(fo + 1) * 128, 256 * c:256 * c + 256], stg[2][:, 0:256], waits=[(SS, v5)])
                    stg_free["q"] = d1
                    pss_free[slot] = ("s", v5)
        for i in range(4):
            for half in range(2):
                slot = kq_iter % 2
                kq_iter += 1
                pw = [(SS, pss_free[slot][1])] if pss_free[slot] else []

                def f(eng, cb=cb, i=i, half=half, slot=slot):
                    eng.matmul(PSS[slot][:], sonesb[:], sbvr[0:1, half * 512:half * 512 + 512],
                               start=True, stop=False, skip_group_check=True)
                    r = None
                    for fi in range(8):
                        r = eng.matmul(PSS[slot][:],
                                       ln1T[cb][:, fi * 512 + i * 128:fi * 512 + (i + 1) * 128],
                                       wv_v[:, fi * 1024 + half * 512:fi * 1024 + half * 512 + 512],
                                       start=False, stop=(fi == 7), skip_group_check=True)
                    return r
                t3 = pr.op("t", f, waits=[(V, ln_done[i])] + pw, inc=T)
                vw = [(DS, stg_free["v"])] if "v" in stg_free else []

                def f(eng, half=half, slot=slot):
                    return eng.copy(stg[3], PSS[slot][:])
                v6 = pr.op("s", f, waits=[(T, t3)] + vw, inc=SS)
                d1 = pr.dma("s", v_sp[c * 512 + i * 128:c * 512 + (i + 1) * 128, half * 512:half * 512 + 512],
                            stg[3], waits=[(SS, v6)])
                stg_free["v"] = d1
                pss_free[slot] = ("s", v6)
    pr.barrier()

    if phases < 2:
        return _finish(nc, pr, es)
    kt_v = [A[:, 0:4096], A[:, 4096:8192]]
    qt_v = [A[:, 8192:10240], A[:, 10240:12288]]
    vh_v = [A[:, 12288:14368], A[:, 14368:16448]]
    gq = [A[:, 16448 + k * 512:16448 + (k + 1) * 512] for k in range(4)]
    gk = [A[:, 18496 + k * 512:18496 + (k + 1) * 512] for k in range(4)]
    probs = [A[:, 20544 + k * 2048:20544 + (k + 1) * 2048] for k in range(2)]
    sqq = A[:, 24640:26688]
    stgq = [A[:, 26688 + k * 2624:26688 + (k + 1) * 2624] for k in range(3)]
    stage = [Bf[:, k * 2048:(k + 1) * 2048] for k in range(2)]
    tiles = [(m, rho) for m in range(16) for rho in (0, 1)]

    st2 = {"it": 0, "cons": {}, "gq_free": {}, "gk_free": {}, "probs_free": {},
           "stage_free": {}, "stg_free": {}, "psw_ev": {}, "pswc": 0,
           "slot_T_done": {}, "slot_V_done": {}, "qdkd_done": {}, "psfree0": 0,
           "ctx_V_done": 0}

    def emit_loads(hd, vh=True):
        slot = (hd // 2) % 2
        if hd % 2 == 0:
            w = []
            if hd >= 4:
                w = [(pr.sems["T"], st2["slot_T_done"].get((hd // 2 - 2) % 2, 0)),
                     (pr.sems["V"], st2["slot_V_done"].get((hd // 2 - 2) % 2, 0))]
                w = [x for x in w if x[1]]
            l_kt = pr.dma("y", kt_v[slot][:, :], kt_sp[64 * hd:64 * hd + 128, :], waits=w)
            l_qt = pr.dma("y", qt_v[slot][:, :], qt_sp[64 * hd:64 * hd + 128, :])
            st2.setdefault("plk", {})[slot] = (l_kt, l_qt)
        else:
            l_kt, l_qt = st2["plk"][slot]
        l_vh = DY.n
        vslot = hd % 2
        if not vh:
            return l_kt, l_qt, DY.n
        vw = []
        if hd >= 2:
            vw = [(pr.sems["T"], st2.get("vh_T_done", {}).get(vslot, 0))]
            vw = [x for x in vw if x[1]]
        for (m, rho) in tiles:
            blk = 2 * m + rho
            l_vh = pr.dma("y", vh_v[vslot][:, blk * 65:blk * 65 + 64],
                          v_sp[256 * m + rho:256 * m + rho + 255:2, 64 * hd:64 * hd + 64],
                          waits=vw if (m, rho) == (0, 0) else [])
        if hd < 2:
            def f(eng, vslot=vslot):
                eng.memset(vh_v[vslot].rearrange("p (b c) -> p b c", c=65)[:, :, 64:65], 1.0)
                return eng.drain()
            pr.op("v", f, waits=[(DY, l_vh)], inc=V)
        return l_kt, l_qt, l_vh

    def gen_qdkd(hd, l_kt, l_qt):
        pslt = hd // 2
        kslot = pslt % 2
        sidx = 0
        for kind in ("q", "k"):
            WW = WQH if kind == "q" else WKH
            dtab = dq if kind == "q" else dk
            spd = qdh_sp if kind == "q" else kdh_sp
            statv = qt_v[kslot] if kind == "q" else kt_v[kslot]
            lwait = l_qt if kind == "q" else l_kt
            for rho in (0, 1):
                for tt in range(16):
                    coff = (rho * 2048 if kind == "k" else 0) + tt * 128
                    stgX = stgq[sidx % 3]
                    for ci, (c0, c1) in enumerate(((0, 512), (512, 1024), (1024, WW))):
                        bp = st2["pswc"] % 2
                        st2["pswc"] += 1
                        wl = [(DY, lwait), (DG, dg_p0)]
                        if bp in st2["psw_ev"]:
                            wl.append((V, st2["psw_ev"][bp]))

                        def f(eng, statv=statv, coff=coff, dtab=dtab, rho=rho, c0=c0, c1=c1, bp=bp):
                            pa = PSW[:, 0:c1 - c0] if bp == 0 else PS0[:, 0:c1 - c0]
                            pb = PSW[:, 512:512 + c1 - c0] if bp == 0 else PS1[:, 0:c1 - c0]
                            eng.matmul(pa, statv[0:64, coff:coff + 128],
                                       dtab[rho][0:64, c0:c1], start=True, stop=True,
                                       tile_position=(0, 0))
                            return eng.matmul(pb, statv[64:128, coff:coff + 128],
                                              dtab[rho][64:128, c0:c1], start=True, stop=True,
                                              tile_position=(64, 0))
                        t4 = pr.op("t", f, waits=wl, inc=T)

                        def f(eng, stgX=stgX, c0=c0, c1=c1, WW=WW, bp=bp):
                            dst = stgX[:, 0:2 * WW].rearrange("p (h w) -> p h w", h=2)[:, :, c0:c1]
                            if bp == 0:
                                srcv = PSW[:].rearrange("p (h w) -> p h w", h=2)[:, :, 0:c1 - c0]
                                return eng.tensor_copy(dst, srcv)
                            eng.tensor_copy(stgX[:, c0:c1], PS0[:, 0:c1 - c0])
                            return eng.tensor_copy(stgX[:, WW + c0:WW + c1], PS1[:, 0:c1 - c0])
                        prev = st2["stg_free"].get(sidx % 3)
                        pw_ = []
                        if prev and ci == 0:
                            pq, pv = prev
                            pw_ = [({"s": DS, "y": DY, "g": DG}[pq], pv)]
                        ev = pr.op("v", f, waits=[(T, t4)] + pw_, inc=V)
                        st2["psw_ev"][bp] = ev
                        if ci == 2:
                            evL = ev
                    base = pslt * 4 * NT * WW + (rho * NT + tt * 128) * (2 * WW)
                    qq = ("s", "y", "g")[sidx % 3]
                    d = pr.dma(qq, spd[base:base + 128 * 2 * WW].rearrange("(p w) -> p w", w=2 * WW),
                               stgX[:, 0:2 * WW], waits=[(V, evL)])
                    st2["stg_free"][sidx % 3] = (qq, d)
                    sidx += 1
                    yield
        st2["qdkd_done"][pslt] = (DS.n, DY.n, DG.n)
        st2["slot_T_done"][kslot] = T.n
        st2["slot_V_done"][kslot] = V.n

    def emit_chead(hd, l_kt, l_qt, l_vh, feed=None):
        slot = (hd // 2) % 2
        hh = 64 * (hd % 2)

        def fq(eng, slot=slot, hh=hh):
            eng.activation(sqq[0:64, 0:2048], qt_v[slot][hh:hh + 64, :], AF.Square)
            return eng.drain()
        sqd1 = pr.op("s", fq, waits=[(DY, l_qt), (SS, SS.n)], inc=SS)
        nmx = 0
        w0 = []
        if st2["psfree0"]:
            w0 = [(V, st2["psfree0"]), (SS, st2["cons_ss"])] if st2.get("cons_ss") else [(V, st2["psfree0"])]
        vhist = {}
        for cc in range(4):
            pb = PSS[cc % 2]

            def f(eng, cc=cc, pb=pb):
                return eng.matmul(pb[0:1, 0:512], sones64[:], sqq[0:64, cc * 512:(cc + 1) * 512],
                                  start=True, stop=True)
            tv = pr.op("t", f, waits=[(SS, sqd1)] + ([(V, vhist[cc % 2])] if cc % 2 in vhist else [(V, V.n)]) + (w0 if cc < 2 else []), inc=T)

            def f(eng, i=nmx, pb=pb):
                eng.reduce_max(mxt[:, i:i + 1], pb[0:1, 0:512], axis=AX)
                return eng.drain()
            vhist[cc % 2] = pr.op("v", f, waits=[(T, tv)], inc=V)
            nmx += 1

        for half2 in range(2):
            def fk(eng, slot=slot, half2=half2, hh=hh):
                eng.activation(sqq[0:64, 0:2048], kt_v[slot][hh:hh + 64, half2 * 2048:half2 * 2048 + 2048], AF.Square)
                return eng.drain()
            sqd2 = pr.op("s", fk, waits=[(V, V.n), (DY, l_kt)], inc=SS)
            for cc in range(4):
                pb = PSS[cc % 2]

                def f(eng, cc=cc, pb=pb):
                    return eng.matmul(pb[0:1, 0:512], sones64[:], sqq[0:64, cc * 512:(cc + 1) * 512],
                                      start=True, stop=True)
                tv = pr.op("t", f, waits=[(SS, sqd2)] + ([(V, vhist[cc % 2])] if cc % 2 in vhist else [(V, V.n)]), inc=T)

                def f(eng, i=nmx, pb=pb):
                    eng.reduce_max(mxt[:, i:i + 1], pb[0:1, 0:512], axis=AX)
                    return eng.drain()
                vhist[cc % 2] = pr.op("v", f, waits=[(T, tv)], inc=V)
                nmx += 1

        def f(eng):
            eng.reduce_max(chs[:, 0:1], mxt[:, 0:4], axis=AX)
            eng.drain()
            eng.reduce_max(chs[:, 1:2], mxt[:, 4:12], axis=AX)
            eng.drain()
            eng.tensor_tensor(chs[:, 2:3], chs[:, 0:1], chs[:, 1:2], OP.mult)
            eng.drain()
            eng.tensor_tensor(chs[:, 3:4], chs[:, 0:1], mxe[:], OP.mult)
            eng.drain()
            eng.tensor_tensor(chs[:, 4:5], chs[:, 1:2], mxe[:], OP.mult)
            return eng.drain()
        vch = pr.op("v", f, inc=V)

        def f(eng):
            eng.activation(chs[:, 2:5], chs[:, 2:5], AF.Ln)
            eng.drain()
            eng.activation(chs[:, 2:5], chs[:, 2:5], AF.Exp, scale=0.5)
            return eng.drain()
        sch = pr.op("s", f, waits=[(V, vch)], inc=SS)

        def f(eng):
            eng.tensor_reduce(chs[:, 5:6], chs[:, 2:5], axis=AX, op=OP.add)
            eng.drain()
            eng.tensor_scalar(negC[:], chs[:, 5:6], -0.125, None, OP.mult)
            return eng.drain()
        vch2 = pr.op("v", f, waits=[(SS, sch)], inc=V)

        def f(eng):
            return eng.matmul(PS0[:, 0:1], sonesf[:], negC[:], start=True, stop=True)
        tch = pr.op("t", f, waits=[(V, vch2)], inc=T)

        def f(eng):
            eng.tensor_copy(negCb[:], PS0[:, 0:1])
            return eng.drain()
        return pr.op("v", f, waits=[(T, tch)], inc=V)

    def emit_scores(hd, ps0t_free_v, l_vh, feed=None):
        slot = (hd // 2) % 2
        vslot = hd % 2
        hh = 64 * (hd % 2)
        qd_s, qd_y, qd_g = st2["qdkd_done"][hd // 2]
        cons = st2["cons"]
        gq_free = st2["gq_free"]
        gk_free = st2["gk_free"]
        probs_free = st2["probs_free"]
        stage_free2 = st2["stage_free"]
        first_tile = True
        v_ones = V.n
        pend_ctx = [None]

        def flush_ctx():
            if pend_ctx[0] is None:
                return
            fc, cw, tic = pend_ctx[0]
            pend_ctx[0] = None
            probs_free[tic] = pr.op("t", fc, waits=cw, inc=T)

        for (m, rho) in tiles:
            ti = 2 * m + rho
            if feed:
                next(feed, None)
                next(feed, None)
            pslot = ti % 2
            band_cols = []
            for cc in range(4):
                it = st2["it"]
                sl2 = it % 4
                gsl = it % 4
                cls = [classify(cc * 512 + s * 128, m) for s in range(4)]
                runs = []
                s = 0
                while s < 4:
                    e = s
                    while e + 1 < 4 and cls[e + 1] == cls[s]:
                        e += 1
                    runs.append((cls[s], s, e + 1))
                    s = e + 1
                wl = []
                if (it - 4) in cons:
                    cv, csv = cons[it - 4]
                    if cv:
                        wl.append((V, cv))
                    if csv:
                        wl.append((SS, csv))
                if first_tile:
                    wl.append((V, ps0t_free_v))
                g_run = [r for r in runs if r[0] == "g"]
                gq_t = gk_t = None
                if g_run:
                    _, gs_, ge_ = g_run[0]
                    ns = ge_ - gs_
                    T0a = cc * 512 + gs_ * 128
                    wg = [(DS, qd_s), (DG, qd_g)]
                    if gsl in gq_free:
                        wg.append((pr.sems["T"], gq_free[gsl]))
                    baseq = (hd // 2) * 4 * NT * WQH + rho * NT * 2 * WQH + (hd % 2) * WQH \
                        + T0a * (2 * WQH - 1) + K0 + 128 * m
                    srcq = qdh_sp[baseq:baseq + ns * 128 * (2 * WQH - 1)] \
                        .rearrange("(p w) -> p w", w=2 * WQH - 1) \
                        .rearrange("(s p) w -> p s w", p=128)[:, :, 0:128]
                    gq_t = pr.dma("y", gq[gsl][:, 0:ns * 128].rearrange("p (s j) -> p s j", j=128),
                                  srcq, waits=wg)
                    wk2 = [(DS, qd_s), (DG, qd_g)]
                    if gsl in gk_free:
                        wk2.append((V, gk_free[gsl]))
                    basek = (hd // 2) * 4 * NT * WKH + rho * NT * 2 * WKH + (hd % 2) * WKH \
                        + 128 * m * (2 * WKH - 1) + K0 + T0a
                    srck = kdh_sp[basek:basek + 128 * (2 * WKH - 1)] \
                        .rearrange("(p w) -> p w", w=2 * WKH - 1)[0:128, 0:ns * 128]
                    gk_t = pr.dma("y", gk[gsl][:, 0:ns * 128], srck, waits=wk2)

                def f(eng, m=m, rho=rho, cc=cc, sl2=sl2, gsl=gsl, runs=runs, g_run=g_run, slot=slot, hh=hh):
                    ps = PSS4[sl2]
                    ktt = kt_v[slot][hh:hh + 64, rho * 2048 + m * 128:rho * 2048 + (m + 1) * 128]
                    r = eng.matmul(ps[:], ktt, qt_v[slot][hh:hh + 64, cc * 512:(cc + 1) * 512],
                                   start=True, stop=False)
                    for (k_, s_, e_) in runs:
                        if k_ == "f":
                            r = eng.matmul(ps[:, s_ * 128:e_ * 128], ktt,
                                           se0[hh:hh + 64, 0:(e_ - s_) * 128], start=False, stop=False,
                                           skip_group_check=True)
                        elif k_ == "p":
                            r = eng.matmul(ps[:, s_ * 128:e_ * 128], ktt,
                                           sem_[hh:hh + 64, 0:(e_ - s_) * 128], start=False, stop=False,
                                           skip_group_check=True)
                    for (k_, s_, e_) in runs:
                        if k_ == "f":
                            r = eng.matmul(ps[:, s_ * 128:e_ * 128], se0[hh:hh + 64, 0:128],
                                           qt_v[slot][hh:hh + 64, cc * 512 + s_ * 128:cc * 512 + e_ * 128],
                                           start=False, stop=False, skip_group_check=True)
                        elif k_ == "p":
                            r = eng.matmul(ps[:, s_ * 128:e_ * 128], sem_[hh:hh + 64, 0:128],
                                           qt_v[slot][hh:hh + 64, cc * 512 + s_ * 128:cc * 512 + e_ * 128],
                                           start=False, stop=False, skip_group_check=True)
                    if g_run:
                        _, gs_, ge_ = g_run[0]
                        for s_ in range(gs_, ge_):
                            r = eng.matmul(ps[:, s_ * 128:(s_ + 1) * 128],
                                           gq[gsl][:, (s_ - gs_) * 128:(s_ - gs_ + 1) * 128],
                                           ident[:], start=False, stop=(s_ == ge_ - 1),
                                           skip_group_check=True)
                    return r
                wl2 = list(wl)
                if g_run and gq_t:
                    wl2.append((DY, gq_t))
                t5 = pr.op("t", f, waits=wl2, inc=T)
                if cc == 1:
                    flush_ctx()
                if g_run:
                    gq_free[gsl] = t5

                v_mark = 0
                s_mark = 0
                if g_run:
                    _, gs_, ge_ = g_run[0]
                    band_cols.append((cc * 512 + gs_ * 128, cc * 512 + ge_ * 128))
                    ws = [(T, t5)] + ([(DY, gk_t)] if gk_t else [])
                    if stage_free2.get(pslot):
                        ws.append((SS, stage_free2[pslot]))

                    def f(eng, cc=cc, gs_=gs_, ge_=ge_, sl2=sl2, gsl=gsl, pslot=pslot):
                        return eng.tensor_tensor(stage[pslot][:, cc * 512 + gs_ * 128:cc * 512 + ge_ * 128],
                                          PSS4[sl2][:, gs_ * 128:ge_ * 128],
                                          gk[gsl][:, 0:(ge_ - gs_) * 128], OP.add)
                    v8 = pr.op("v", f, waits=ws, inc=V)
                    gk_free[gsl] = v8
                    v_mark = v8
                for (k_, s_, e_) in runs:
                    if k_ == "g":
                        continue
                    pw2 = [(T, t5)]
                    if v_mark:
                        pw2.append((V, v_mark))
                    if (ti - 2) in probs_free or hd > 0:
                        pw2.append((pr.sems["T"], probs_free.get(ti - 2, probs_free.get(30 + (ti % 2), 0)) or T.n - 1000 if False else probs_free.get(ti - 2, 0)))
                    pw2 = [x for x in pw2 if not (x[0] is pr.sems["T"] and x[1] == 0)]

                    def f(eng, cc=cc, s_=s_, e_=e_, sl2=sl2, pslot=pslot):
                        return eng.activation(probs[pslot][:, cc * 512 + s_ * 128:cc * 512 + e_ * 128],
                                       PSS4[sl2][:, s_ * 128:e_ * 128], AF.Exp,
                                       bias=negCb[:], scale=0.125)
                    s_mark = pr.op("s", f, waits=pw2, inc=SS)
                cons[it] = (v_mark, s_mark)
                st2["it"] += 1

            it = st2["it"]
            if band_cols:
                lo = min(b[0] for b in band_cols)
                hi = max(b[1] for b in band_cols)
                vmax = max((cons[it - 1 - k][0] or 0) for k in range(4))
                pw3 = ([(V, vmax)] if vmax else [])
                if (ti - 2) in probs_free:
                    pw3.append((pr.sems["T"], probs_free[ti - 2]))

                def f(eng, lo=lo, hi=hi, pslot=pslot):
                    return eng.activation(probs[pslot][:, lo:hi], stage[pslot][:, lo:hi],
                                   AF.Exp, bias=negCb[:], scale=0.125)
                exp_val = pr.op("s", f, waits=pw3, inc=SS)
                stage_free2[pslot] = exp_val

            ctx_w = [(SS, SS.n), (DY, l_vh)]
            if first_tile:
                ctx_w.append((V, st2["ctx_V_done"]))

            def f(eng, blk=ti, first=first_tile, last=(ti == 31), vslot=vslot, pslot=pslot):
                r = None
                for ccq in range(4):
                    r = eng.matmul(PSC[0:65, ccq * 512:(ccq + 1) * 512],
                                   vh_v[vslot][:, blk * 65:blk * 65 + 65],
                                   probs[pslot][:, ccq * 512:(ccq + 1) * 512],
                                   start=first, stop=last, skip_group_check=True)
                return r
            pend_ctx[0] = (f, ctx_w, ti)
            first_tile = False
        flush_ctx()
        pf30, pf31 = probs_free.get(30, 0), probs_free.get(31, 0)
        probs_free.clear()
        probs_free[-2] = pf30
        probs_free[-1] = pf31
        st2["cons_ss"] = max((cons[st2["it"] - 1 - k][1] or 0) for k in range(2))

        tlast = T.n
        for ccq in range(4):
            if feed:
                next(feed, None)

            def f(eng, ccq=ccq):
                eng.reciprocal(rcp[:, 0:512], PSC[64:65, ccq * 512:(ccq + 1) * 512])
                eng.drain()
                eng.tensor_copy(rcpb[:, 0:512], rcp[:, 0:512])
                return eng.drain()
            v9 = pr.op("v", f, waits=[(T, tlast if ccq == 0 else T.n)], inc=V)

            def f(eng, ccq=ccq):
                return eng.matmul(PS0[0:64, :], sonesb[:, 0:64], rcpb[:, 0:512],
                                  start=True, stop=True)
            t7 = pr.op("t", f, waits=[(V, v9)], inc=T)

            def f(eng, ccq=ccq, hd=hd):
                eng.tensor_copy(rcpB, PS0[0:64, :])
                eng.drain()
                half = hd % 2
                pair = hd // 2
                eng.tensor_tensor(ctxT[half * 64:half * 64 + 64,
                                       pair * 2048 + ccq * 512:pair * 2048 + (ccq + 1) * 512],
                                  PSC[0:64, ccq * 512:(ccq + 1) * 512], rcpB, OP.mult)
                return eng.drain()
            fv = pr.op("v", f, waits=[(T, t7)], inc=V)
        st2["psfree0"] = fv
        st2["ctx_V_done"] = fv
        if hd % 2 == 1:
            st2["slot_T_done"][slot] = T.n
            st2["slot_V_done"][slot] = V.n
        st2.setdefault("vh_T_done", {})[vslot] = T.n

    for hd in range(0, nheads, 2):
        lks = emit_loads(hd, vh=False)
        for _ in gen_qdkd(hd, lks[0], lks[1]):
            pass
    pr.barrier()
    st2["slot_T_done"].clear()
    st2["slot_V_done"].clear()
    st2["plk"] = {}
    pend = None
    for hd in range(nheads):
        lks = emit_loads(hd)
        if pend is not None:
            ncb = emit_chead(pend[0], pend[1], pend[2], pend[3])
            emit_scores(pend[0], ncb, pend[3])
        pend = (hd, lks[0], lks[1], lks[2])
    ncb = emit_chead(pend[0], pend[1], pend[2], pend[3])
    emit_scores(pend[0], ncb, pend[3])
    pr.barrier()


    wo_v = A[:, 0:8192]
    pr.dma("g", wo_v.rearrange("p (f c) -> p f c", f=8),
           wo_d.rearrange("(f p) c -> p f c", p=128))
    bc_ld = pr.dma("y", bob, bo_row.partition_broadcast(128))
    pr.dma("y", gamb, gam_row.partition_broadcast(128))
    bc_ld = pr.dma("y", betb, bet_row.partition_broadcast(128))
    psw_free_v = 0
    p3_last = {}
    for tt in range(16):
        par = tt % 2
        xt = Bf[:, par * 4096:par * 4096 + 1024]
        h2 = Bf[:, par * 4096 + 1024:par * 4096 + 2048]
        sc1_ = Bf[:, par * 4096 + 2048:par * 4096 + 3072]
        sc2_ = Bf[:, par * 4096 + 3072:par * 4096 + 4096]
        lw = [(V, p3_last[par])] if par in p3_last else []
        xl = pr.dma("y", xt, x_b[2 * tt * 128:2 * tt * 128 + 256:2, :], waits=lw)

        def f(eng, tt=tt):
            r = None
            for pp in range(8):
                for half in range(2):
                    r = eng.matmul(PSW[:, half * 512:half * 512 + 512],
                                   ctxT[:, pp * 2048 + tt * 128:pp * 2048 + (tt + 1) * 128],
                                   wo_v[:, pp * 1024 + half * 512:pp * 1024 + half * 512 + 512],
                                   start=(pp == 0), stop=(pp == 7), skip_group_check=True)
            return r
        wl = [(DG, DG.n)]
        if psw_free_v:
            wl.append((V, psw_free_v))
        t8 = pr.op("t", f, waits=wl, inc=T)

        s4 = ln_chain(xt, None, 1e-5, waits=[(DY, xl)], center_into=sc1_, par=par)

        def f(eng, xt=xt, h2=h2, sc1_=sc1_, par=par):
            eng.tensor_scalar(sc1_, sc1_, colC[:, par:par + 1], None, OP.mult)
            eng.drain()
            eng.tensor_tensor(h2, PSW[:], bob, OP.add)
            eng.drain()
            eng.tensor_tensor(h2, h2, sc1_, OP.add)
            return eng.drain()
        v12 = pr.op("v", f, waits=[(SS, s4), (T, t8), (DY, bc_ld)], inc=V)
        psw_free_v = v12

        s6 = ln_chain(h2, None, 1e-12, waits=[(V, v12)], par=par)

        def f(eng, h2=h2, sc2_=sc2_, par=par):
            eng.tensor_scalar(h2, h2, colC[:, par:par + 1], None, OP.mult)
            eng.drain()
            eng.tensor_tensor(sc2_, h2, gamb, OP.mult)
            eng.drain()
            eng.tensor_tensor(sc2_, sc2_, betb, OP.add)
            return eng.drain()
        v14 = pr.op("v", f, waits=[(SS, s6)], inc=V)

        s8 = ln_chain(sc2_, None, 1e-5, waits=[(V, v14)], par=par)

        def f(eng, xt=xt, sc2_=sc2_, h2=h2, par=par):
            eng.tensor_scalar(sc2_, sc2_, colC[:, par:par + 1], None, OP.mult)
            eng.drain()
            eng.tensor_tensor(h2, xt, sc2_, OP.add)
            return eng.drain()
        v16 = pr.op("v", f, waits=[(SS, s8)], inc=V)
        p3_last[par] = v16
        pr.dma("s", out1_sp[tt * 128:(tt + 1) * 128, :], h2, waits=[(V, v16)])
    pr.barrier()

    ln2T = C[:, 0:16384]
    lnrow2 = [A[:, 0:1024], A[:, 1024:2048]]
    psw_free_v = 0
    p4_last = {}
    for bb in range(16):
        par = bb % 2
        xt = Bf[:, par * 4096:par * 4096 + 1024]
        lw = [(V, p4_last[par])] if par in p4_last else [(V, V.n)] if bb == 0 else []
        xl = pr.dma("y", xt, out1_sp[bb:bb + 16 * 127 + 1:16, :], waits=lw)
        s10 = ln_chain(xt, None, 1e-5, waits=[(DY, xl)], par=par)

        def f(eng, xt=xt, par=par, bb=bb):
            eng.tensor_scalar(lnrow2[par], xt, colC[:, par:par + 1], None, OP.mult)
            return eng.drain()
        v19 = pr.op("v", f, waits=[(SS, s10)], inc=V)
        p4_last[par] = v19

        def f(eng, par=par):
            r = None
            for j in range(8):
                r = eng.matmul(PSW[:, j * 128:(j + 1) * 128], lnrow2[par][:, j * 128:(j + 1) * 128],
                               ident[:], start=True, stop=True)
            return r
        wl = [(V, v19)]
        if psw_free_v:
            wl.append((V, psw_free_v))
        t9 = pr.op("t", f, waits=wl, inc=T)

        def f(eng, bb=bb):
            src = PSW[:].rearrange("p (j t) -> p j t", t=128)
            dst = ln2T.rearrange("p (j t) -> p j t", t=2048)[:, :, bb::16]
            eng.tensor_copy(dst, src)
            return eng.drain()
        v20 = pr.op("v", f, waits=[(T, t9)], inc=V)
        psw_free_v = v20

        def f(eng, bb=bb):
            r = None
            l2v = ln2T.rearrange("p (j t) -> p j t", t=2048)
            for fi in range(8):
                r = eng.matmul(PS0[:, 0:16], l2v[:, fi, bb::16],
                               sgw[:, fi * 16:(fi + 1) * 16], start=(fi == 0), stop=(fi == 7))
            return r
        t10 = pr.op("t", f, waits=[(V, v20)], inc=T)

        def f(eng, bb=bb):
            eng.tensor_copy(logits[:, bb * 16:(bb + 1) * 16], PS0[:, 0:16])
            return eng.drain()
        pr.op("v", f, waits=[(T, t10)], inc=V)
    pr.barrier()

    if phases < 5:
        return _finish(nc, pr, es)
    def f(eng):
        eng.activation(rt_sc[:], logits[:], AF.Sigmoid)
        return eng.drain()
    s11 = pr.op("s", f, inc=SS)

    def f(eng):
        eng.memset(rt_neg[:], -1e30)
        eng.drain()
        eng.tensor_tensor(rt_sfc[:], rt_sc[:], sgbr[:], OP.add)
        eng.drain()
        g4 = rt_sfc.rearrange("p (x k) -> p x k", k=4)
        eng.tensor_tensor(rt_t[0][:], g4[:, :, 0], g4[:, :, 1], OP.max)
        eng.tensor_tensor(rt_t[1][:], g4[:, :, 0], g4[:, :, 1], OP.min)
        eng.tensor_tensor(rt_t[2][:], g4[:, :, 2], g4[:, :, 3], OP.max)
        eng.tensor_tensor(rt_t[3][:], g4[:, :, 2], g4[:, :, 3], OP.min)
        eng.drain()
        eng.tensor_tensor(rt_t[4][:], rt_t[0][:], rt_t[2][:], OP.max)
        eng.tensor_tensor(rt_t[5][:], rt_t[0][:], rt_t[2][:], OP.min)
        eng.drain()
        eng.tensor_tensor(rt_t[0][:], rt_t[1][:], rt_t[3][:], OP.max)
        eng.drain()
        eng.tensor_tensor(rt_t[5][:], rt_t[5][:], rt_t[0][:], OP.max)
        eng.drain()
        eng.tensor_tensor(rt_gs[:], rt_t[4][:], rt_t[5][:], OP.add)
        return eng.drain()
    v22 = pr.op("v", f, waits=[(SS, s11)], inc=V)

    def f(eng):
        gs4 = rt_gs.rearrange("p (x g) -> p x g", g=4)
        for g in range(4):
            dsts = []
            k = 0
            for g2 in range(4):
                if g2 == g:
                    continue
                eng.tensor_tensor(rt_t[k][:, 0:16], gs4[:, :, g2], gs4[:, :, g], OP.is_gt)
                dsts.append(k)
                k += 1
            for g2 in range(g):
                eng.tensor_tensor(rt_t[k][:, 0:16], gs4[:, :, g2], gs4[:, :, g], OP.is_equal)
                dsts.append(k)
                k += 1
            eng.drain()
            acc = dsts[0]
            for d2 in dsts[1:]:
                eng.tensor_tensor(rt_t[acc][:, 0:16], rt_t[acc][:, 0:16], rt_t[d2][:, 0:16], OP.add)
                eng.drain()
            eng.tensor_scalar(rt_gm[:, g * 16:(g + 1) * 16], rt_t[acc][:, 0:16], 1.5, None, OP.is_lt)
            eng.drain()
        for g in range(4):
            for kk in range(4):
                eng.tensor_copy(rt_em[:, 4 * g + kk::16], rt_gm[:, g * 16:(g + 1) * 16])
        eng.drain()
        eng.select(rt_msk[:], rt_em[:], rt_sfc[:], rt_neg[:], add_drain=True)
        return eng.drain()
    v23 = pr.op("v", f, waits=[(V, v22)], inc=V)

    def f(eng):
        for b in range(16):
            for e in range(NE):
                eng.tensor_scalar(Bf[:, (b * 16 + e) * 16:(b * 16 + e) * 16 + 16],
                                  rt_msk[:, b * 16:(b + 1) * 16],
                                  rt_msk[:, b * 16 + e:b * 16 + e + 1], None, OP.is_gt)
        eng.drain()
        for b in range(16):
            for e in range(NE):
                eng.tensor_reduce(rt_rank[:, b * 16 + e:b * 16 + e + 1],
                                  Bf[:, (b * 16 + e) * 16:(b * 16 + e) * 16 + 16], axis=AX, op=OP.add)
        eng.drain()
        eng.memset(rt_eqr[:], 0.0)
        eng.drain()
        for b in range(16):
            for e in range(1, NE):
                eng.tensor_scalar(Bf[:, (b * 16 + e) * 16:(b * 16 + e) * 16 + e],
                                  rt_msk[:, b * 16:b * 16 + e],
                                  rt_msk[:, b * 16 + e:b * 16 + e + 1], None, OP.is_equal)
        eng.drain()
        for b in range(16):
            for e in range(1, NE):
                eng.tensor_reduce(rt_eqr[:, b * 16 + e:b * 16 + e + 1],
                                  Bf[:, (b * 16 + e) * 16:(b * 16 + e) * 16 + e], axis=AX, op=OP.add)
        eng.drain()
        eng.tensor_tensor(rt_rank[:], rt_rank[:], rt_eqr[:], OP.add)
        eng.drain()
        eng.tensor_scalar(rt_sel[:], rt_rank[:], 3.5, None, OP.is_lt)
        eng.drain()
        eng.tensor_tensor(rt_w[:], rt_sc[:], rt_sel[:], OP.mult)
        eng.drain()
        eng.tensor_reduce(rt_ws[:], rt_w.rearrange("p (x e) -> p x e", e=16), axis=AX, op=OP.add)
        eng.drain()
        eng.tensor_scalar(rt_wsi[:], rt_ws[:], 1e-20, None, OP.add)
        eng.drain()
        eng.reciprocal(rt_ws[:], rt_wsi[:])
        eng.drain()
        for b in range(16):
            eng.tensor_scalar(rt_cw[:, b * 16:(b + 1) * 16], rt_w[:, b * 16:(b + 1) * 16],
                              rt_ws[:, b:b + 1], 2.5, OP.mult, OP.mult)
        return eng.drain()
    v24 = pr.op("v", f, waits=[(V, v23)], inc=V)
    if dbg:
        def f(eng):
            eng.tensor_copy(C[:, 0:512].bitcast(F32), logits[:])
            eng.tensor_copy(C[:, 512:1024].bitcast(F32), rt_sfc[:])
            eng.tensor_copy(C[:, 1024:1152].bitcast(F32), rt_gs[:])
            eng.tensor_copy(C[:, 1152:1280].bitcast(F32), rt_gm[:])
            eng.tensor_copy(C[:, 1536:2048].bitcast(F32), rt_rank[:])
            eng.tensor_copy(C[:, 2048:2560].bitcast(F32), rt_sel[:])
            return eng.drain()
        vdmp = pr.op("v", f, waits=[(V, v24)], inc=V)
        pr.dma("s", rt_dump[:, 0:256], C[:, 0:512].bitcast(F32), waits=[(V, vdmp)])
        pr.dma("s", rt_dump[:, 256:512], C[:, 512:1024].bitcast(F32), waits=[(V, vdmp)])
        pr.dma("s", rt_dump[:, 512:576], C[:, 1024:1152].bitcast(F32), waits=[(V, vdmp)])
        pr.dma("s", rt_dump[:, 640:704], C[:, 1152:1280].bitcast(F32), waits=[(V, vdmp)])
        pr.dma("s", rt_dump[:, 768:1024], C[:, 1536:2048].bitcast(F32), waits=[(V, vdmp)])
        pr.dma("s", rt_dump[:, 1024:1280], C[:, 2048:2560].bitcast(F32), waits=[(V, vdmp)])
    for e in range(NE):
        def f(eng, e=e):
            eng.tensor_copy(rt_c1[:], rt_cw[:, e::16])
            return eng.drain()
        vcp = pr.op("v", f, waits=[(V, v24)] + ([(DS, DS.n)] if e else []), inc=V)
        pr.dma("s", cw_sp[e * NT:(e + 1) * NT].rearrange("(p b) -> p b", b=16),
               rt_c1[:], waits=[(V, vcp)])
    pr.barrier()

    if phases < 6:
        return _finish(nc, pr, es)
    wsg_v = A[:, 16384:24576]
    wsu_v = A[:, 24576:32768]
    wg_v = [A[:, 0:4096], A[:, 4096:8192]]
    wu_v = [A[:, 8192:12288], A[:, 12288:16384]]
    pr.dma("g", wsg_v.rearrange("p (f c) -> p f c", f=8),
           wsgT_d.rearrange("(f p) c -> p f c", p=128))
    pr.dma("g", wsu_v.rearrange("p (f c) -> p f c", f=8),
           wsuT_d.rearrange("(f p) c -> p f c", p=128))
    wslot_free = {}
    ps0_free = 0
    ps1_free = 0
    actb_free = 0
    awb_free = {}
    last_t_of = {}
    cw_free = {}
    aidx = 0
    for e in range(NE + 1):
        shared = (e == NE)
        sl = e % 2
        nit = 8 if shared else 4
        if not shared:
            ww = [(T, last_t_of[e - 2])] if (e - 2) in last_t_of else []
            pr.dma("g", wg_v[sl].rearrange("p (f i) -> p f i", f=8),
                   wgT_d[e].rearrange("(f p) i -> p f i", p=128), waits=ww)
            wload = pr.dma("g", wu_v[sl].rearrange("p (f i) -> p f i", f=8),
                           wuT_d[e].rearrange("(f p) i -> p f i", p=128))
        last_t = 0
        for cc in range(4):
            if not shared:
                cwl = [(GP, cw_free[cc % 2])] if (cc % 2) in cw_free else []
                cwt = pr.dma("y", cwB[cc % 2],
                             cw_sp[e * NT + cc * 512:e * NT + (cc + 1) * 512]
                             .rearrange("(o t) -> o t", o=1).partition_broadcast(128),
                             waits=cwl)
            for ii in range(nit):
                wv1 = wsg_v if shared else wg_v[sl]
                wv2 = wsu_v if shared else wu_v[sl]
                wid = 1024 if shared else 512

                def f(eng, wv1=wv1, cc=cc, ii=ii, wid=wid):
                    r = None
                    for fi in range(8):
                        r = eng.matmul(PS0[:], wv1[:, fi * wid + ii * 128:fi * wid + ii * 128 + 128],
                                       ln2T[:, fi * 2048 + cc * 512:fi * 2048 + (cc + 1) * 512],
                                       start=(fi == 0), stop=(fi == 7))
                    return r
                wl = [] if shared else [(DG, wload)]
                if ps0_free:
                    wl.append((SS, ps0_free))
                tg = pr.op("t", f, waits=wl, inc=T)

                def f(eng, wv2=wv2, cc=cc, ii=ii, wid=wid):
                    r = None
                    for fi in range(8):
                        r = eng.matmul(PS1[:], wv2[:, fi * wid + ii * 128:fi * wid + ii * 128 + 128],
                                       ln2T[:, fi * 2048 + cc * 512:fi * 2048 + (cc + 1) * 512],
                                       start=(fi == 0), stop=(fi == 7))
                    return r
                wl = []
                if ps1_free:
                    wl.append((V, ps1_free))
                tu = pr.op("t", f, waits=wl, inc=T)
                last_t = tu

                def f(eng):
                    eng.activation(silb, PS0[:], AF.Silu)
                    return eng.drain()
                sg2 = pr.op("s", f, waits=[(T, tg)], inc=SS)
                ps0_free = sg2

                asl = aidx % 2
                aw = [(T, tu), (SS, sg2)]
                if shared and asl in awb_free:
                    aw.append((DS, awb_free[asl]))
                if not shared and actb_free:
                    aw.append((GP, actb_free))

                def f(eng, shared=shared, asl=asl):
                    dst = actwb[asl] if shared else actb
                    eng.scalar_tensor_tensor(dst, PS1[:], 1.0, silb, OP.mult, OP.mult)
                    return eng.drain()
                vv = pr.op("v", f, waits=aw, inc=V)
                ps1_free = vv

                if not shared:
                    gw2 = [(V, vv), (DY, cwt)]
                    if asl in awb_free:
                        gw2.append((DS, awb_free[asl]))

                    def f(eng, asl=asl, cc=cc):
                        eng.tensor_tensor(actwb[asl], actb, cwB[cc % 2], OP.mult)
                        return eng.drain()
                    gg = pr.op("g", f, waits=gw2, inc=GP)
                    cw_free[cc % 2] = gg
                    actb_free = gg
                    base = e * EI * NT + ii * 128 * NT
                    dd = pr.dma("s", actw_sp[base:base + 128 * NT]
                                .rearrange("(p t) -> p t", t=NT)[:, cc * 512:(cc + 1) * 512],
                                actwb[asl], waits=[(GP, gg)])
                else:
                    base = ii * 128 * NT
                    dd = pr.dma("s", gs_sp[base:base + 128 * NT]
                                .rearrange("(p t) -> p t", t=NT)[:, cc * 512:(cc + 1) * 512],
                                actwb[asl], waits=[(V, vv)])
                awb_free[asl] = dd
                aidx += 1
        if not shared:
            last_t_of[e] = last_t
    pr.barrier()

    if phases < 7:
        return _finish(nc, pr, es)
    y_ps_free = {}
    last_ty = 0
    prev_out = 0
    for half in range(2):
        wl0 = [(T, T.n)]
        for e in range(NE):
            pr.dma("g", A[:, e * 2048:(e + 1) * 2048].rearrange("p (i c) -> p i c", i=4),
                   wdT_d[e].rearrange("(i p) c -> p i c", p=128)[:, :, half * 512:half * 512 + 512],
                   waits=wl0 if e == 0 else [])
        dwl = pr.dma("g", A[:, 32768:36864].rearrange("p (i c) -> p i c", i=8),
                     wsdT_d.rearrange("(i p) c -> p i c", p=128)[:, :, half * 512:half * 512 + 512])
        for ts_ in range(16):
            for e in range(NE):
                pr.dma("y", C[:, e * 512:(e + 1) * 512].rearrange("p (i t) -> p i t", t=128),
                       actw_sp[e * EI * NT:(e + 1) * EI * NT]
                       .rearrange("(i p t) -> p i t", p=128, t=NT)[:, :, ts_ * 128:(ts_ + 1) * 128],
                       waits=[(T, last_ty)] if e == 0 else [])
            aload = pr.dma("y", C[:, 8192:9216].rearrange("p (i t) -> p i t", t=128),
                           gs_sp.rearrange("(i p t) -> p i t", p=128, t=NT)[:, :, ts_ * 128:(ts_ + 1) * 128])
            slot = ts_ % 2

            def f(eng, slot=slot):
                r = None
                first = True
                for e in range(NE):
                    for ii in range(4):
                        r = eng.matmul(PSS[slot][:], C[:, e * 512 + ii * 128:e * 512 + (ii + 1) * 128],
                                       A[:, e * 2048 + ii * 512:e * 2048 + (ii + 1) * 512],
                                       start=first, stop=False, skip_group_check=True)
                        first = False
                for jj in range(8):
                    r = eng.matmul(PSS[slot][:], C[:, 8192 + jj * 128:8192 + (jj + 1) * 128],
                                   A[:, 32768 + jj * 512:32768 + (jj + 1) * 512],
                                   start=False, stop=(jj == 7), skip_group_check=True)
                return r
            wl = [(DY, aload), (DG, dwl)]
            if (slot, "ps") in y_ps_free:
                wl.append((V, y_ps_free[(slot, "ps")]))
            ty = pr.op("t", f, waits=wl, inc=T)
            last_ty = ty

            if half == 0:
                def f(eng, ts_=ts_, slot=slot):
                    eng.tensor_copy(Bf[:, ts_ * 512:(ts_ + 1) * 512], PSS[slot][:])
                    return eng.drain()
                vv = pr.op("v", f, waits=[(T, ty)], inc=V)
                y_ps_free[(slot, "ps")] = vv
            else:
                xl = pr.dma("y", o1t[:], out1_sp[ts_ * 128:(ts_ + 1) * 128, :], waits=[(V, V.n)])
                ep_w = [(DY, xl)]
                if prev_out:
                    ep_w.append((DS, prev_out))
                s12 = ln_chain(o1t, None, 1e-5, waits=ep_w, center_into=ep1)

                def f(eng, ts_=ts_, slot=slot):
                    eng.tensor_scalar(ep1[:], ep1[:], colC[:, 0:1], None, OP.mult)
                    eng.drain()
                    eng.tensor_tensor(ep2[:, 0:512], ep1[:, 0:512], Bf[:, ts_ * 512:(ts_ + 1) * 512], OP.add)
                    eng.drain()
                    eng.tensor_tensor(ep2[:, 512:1024], ep1[:, 512:1024], PSS[slot][:], OP.add)
                    return eng.drain()
                v25 = pr.op("v", f, waits=[(SS, s12), (T, ty)], inc=V)
                y_ps_free[(slot, "ps")] = v25

                s13 = ln_chain(ep2, None, 1e-5, waits=[(V, v25)])

                def f(eng):
                    eng.tensor_scalar(ep2[:], ep2[:], colC[:, 0:1], None, OP.mult)
                    eng.drain()
                    eng.tensor_tensor(ep1[:], o1t[:], ep2[:], OP.add)
                    return eng.drain()
                v26 = pr.op("v", f, waits=[(SS, s13)], inc=V)
                prev_out = pr.dma("s", outp[ts_ * 128:(ts_ + 1) * 128, :], ep1[:], waits=[(V, v26)])

    return _finish(nc, pr, es)


def host_maps(inp):
    bf = ml_dtypes.bfloat16
    f32 = np.float32
    ac = np.ascontiguousarray
    x = np.asarray(inp["L0_attn_hidden_states"], dtype=f32)
    dist = np.asarray(inp["L0_attn_dist_emb"], dtype=f32)
    Wq = ac(np.asarray(inp["L0_attn_Wq"], f32)); Wk = ac(np.asarray(inp["L0_attn_Wk"], f32))
    Wv = ac(np.asarray(inp["L0_attn_Wv"], f32)); Wo = ac(np.asarray(inp["L0_attn_Wo"], f32))
    bq = np.asarray(inp["L0_attn_bq"], f32); bk = np.asarray(inp["L0_attn_bk"], f32)
    bv = np.asarray(inp["L0_attn_bv"], f32); bo = np.asarray(inp["L0_attn_bo"], f32)
    lng = np.asarray(inp["L0_attn_ln_gamma"], f32); lnb = np.asarray(inp["L0_attn_ln_beta"], f32)
    gw = np.asarray(inp["L0_ffn_gate_w"], f32)
    gbias = np.asarray(inp["L0_ffn_gate_bias"], f32)
    Wg = np.asarray(inp["L0_ffn_Wg"], f32)
    Wu = np.asarray(inp["L0_ffn_Wu"], f32)
    Wd = np.asarray(inp["L0_ffn_Wd"], f32)
    Wsg = np.asarray(inp["L0_ffn_Ws_g"], f32)
    Wsu = np.asarray(inp["L0_ffn_Ws_u"], f32)
    Wsd = np.asarray(inp["L0_ffn_Ws_d"], f32)

    w = np.arange(WQH)
    com = {
        "wq_d": Wq, "wk_d": Wk, "wv_d": Wv, "wo_d": Wo,
        "bqr_d": bq[None, :], "bkr_d": bk[None, :],
        "ones512_d": np.ones((1, 512), bf),
        "bv_row": bv[None, :], "bo_row": bo[None, :],
        "gam_row": lng[None, :], "bet_row": lnb[None, :],
        "tq0_d": ac(np.tile(dist[np.clip(2 * (K0 - w) + 1023, 0, 2046)].T, (2, 1))),
        "tk0_d": ac(np.tile(dist[np.clip(2 * (w - K0) + 1023, 0, 2046)].T, (2, 1))),
        "e0_d": ac(np.tile(np.tile(dist[0][:, None], (1, 512)), (2, 1))),
        "em_d": ac(np.tile(np.tile(dist[2046][:, None], (1, 512)), (2, 1))),
        "id_d": np.eye(128, dtype=bf),
        "onesf_d": np.ones((1, 128), f32),
        "onesb_d": np.ones((1, 128), bf),
        "ones64_d": np.ones((64, 1), bf),
        "gw_d": ac(gw.T),
        "gbr_d": ac(np.tile(gbias, 16)[None, :]),
        "wgT_d": ac(Wg.transpose(0, 2, 1)),
        "wuT_d": ac(Wu.transpose(0, 2, 1)),
        "wdT_d": ac(Wd.transpose(0, 2, 1)),
        "wsgT_d": ac(Wsg.T), "wsuT_d": ac(Wsu.T), "wsdT_d": ac(Wsd.T),
    }
    in_maps = []
    for core in range(8):
        b, h = core // 2, core % 2
        xp = x[b]
        if h:
            xp = xp.reshape(NT, 2, E)[:, ::-1, :].reshape(S, E)
        m = dict(com)
        m["x_b"] = ac(xp)
        m["tq1_d"] = ac(np.tile(dist[np.clip(2 * (K0 - w + h) + 1022, 0, 2046)].T, (2, 1)))
        m["tk1_d"] = ac(np.tile(dist[np.clip(2 * (w - K0 + h) + 1022, 0, 2046)].T, (2, 1)))
        in_maps.append(m)

    return in_maps


def kernel(**inp):
    in_maps = host_maps(inp)
    nc = build_program()
    res = run_bass_kernel_spmd(nc, in_maps, core_ids=list(range(8)))
    out = np.empty((4, S, E), np.float32)
    for core in range(8):
        b, h = core // 2, core % 2
        out[b, h::2] = np.asarray(res.results[core]["outp"], np.float32)
    return out


# version 18
# speedup vs baseline: 1.1639x; 1.0498x over previous
import contextlib
import numpy as np
import ml_dtypes
import concourse.bass as bass
import concourse.mybir as mybir
from concourse.bass_utils import run_bass_kernel_spmd
from concourse.alu_op_type import AluOpType as OP

F32 = mybir.dt.float32
BF16 = mybir.dt.bfloat16
I32 = mybir.dt.int32
AF = mybir.ActivationFunctionType
AX = mybir.AxisListType.X

S, E, NH = 4096, 1024, 16
NT = 2048
K0 = 644
WQH = WKH = 1288
NE, EI, SHI = 16, 512, 1024
SLACK = 16 * WQH


class Sem:
    def __init__(self, nc, es, name):
        self.h = es.enter_context(nc.semaphore(name))
        self.n = 0


class Prog:
    def __init__(self, nc, es):
        self.nc = nc
        self.ops = {k: [] for k in "ygtvs"}
        self.sems = {}
        for nm in ("DY", "DG", "DS", "T", "V", "SS", "GP", "BR"):
            self.sems[nm] = Sem(nc, es, nm)
        self.nbar = 0

    def dma(self, q, out_ap, in_ap, waits=()):
        sem = self.sems[{"y": "DY", "g": "DG", "s": "DS"}[q]]
        sem.n += 16
        tgt = sem.n
        w = tuple(waits)

        def f(eng, o=out_ap, i=in_ap, w=w, sh=sem.h):
            for (s, v) in w:
                eng.wait_ge(s.h, v)
            eng.dma_start(out=o, in_=i).then_inc(sh, 16)
        self.ops[q].append(f)
        return tgt

    def op(self, q, fn, waits=(), inc=None):
        w = tuple(waits)
        tgt = None
        if inc is not None:
            inc.n += 1
            tgt = inc.n

        def f(eng, fn=fn, w=w, inc=inc):
            for (s, v) in w:
                eng.wait_ge(s.h, v)
            r = fn(eng)
            if inc is not None:
                if r is None:
                    r = eng.nop()
                r.then_inc(inc.h, 1)
        self.ops[q].append(f)
        return tgt

    def barrier(self):
        br = self.sems["BR"]
        self.nbar += 1
        tgt = self.nbar * 5
        br.n += 5
        for q in "ygtvs":
            def f(eng, t=tgt, h=br.h):
                eng.nop().then_inc(h, 1)
                eng.wait_ge(h, t)
            self.ops[q].append(f)


def classify(T0, m):
    d2 = 2 * T0 - 256 * m
    if abs(d2) <= 1024:
        return "g"
    return "p" if d2 >= 1280 else "f"




def _finish(nc, pr, es):
    DS = pr.sems["DS"]
    fin = DS.n

    def f(eng, fin=fin):
        eng.wait_ge(DS.h, fin)
        return None
    pr.op("s", f)
    pr.barrier()
    with nc.Block() as block:
        @block.sync
        def _(eng):
            for f2 in pr.ops["y"]:
                f2(eng)

        @block.gpsimd
        def _(eng):
            for f2 in pr.ops["g"]:
                f2(eng)

        @block.tensor
        def _(eng):
            for f2 in pr.ops["t"]:
                f2(eng)

        @block.vector
        def _(eng):
            for f2 in pr.ops["v"]:
                f2(eng)

        @block.scalar
        def _(eng):
            for f2 in pr.ops["s"]:
                f2(eng)
    es.close()
    return nc


def build_program(phases=9, dbg=False, nheads=NH, p2mode=5):
    nc = bass.Bass("TRN2", target_bir_lowering=False)
    es = contextlib.ExitStack()
    pr = Prog(nc, es)
    DY, DG, DS, T, V, SS, GP = (pr.sems[k] for k in ("DY", "DG", "DS", "T", "V", "SS", "GP"))

    def dt(name, shape, dtp, kind="ExternalInput"):
        return nc.dram_tensor(name, shape, dtp, kind=kind).ap()

    x_b = dt("x_b", [S, E], F32)
    wq_d = dt("wq_d", [E, E], F32); wk_d = dt("wk_d", [E, E], F32)
    wv_d = dt("wv_d", [E, E], F32); wo_d = dt("wo_d", [E, E], F32)
    bqr_d = dt("bqr_d", [1, E], F32); bkr_d = dt("bkr_d", [1, E], F32)
    ones512_d = dt("ones512_d", [1, 512], BF16)
    bv_row = dt("bv_row", [1, E], F32); bo_row = dt("bo_row", [1, E], F32)
    gam_row = dt("gam_row", [1, E], F32); bet_row = dt("bet_row", [1, E], F32)
    tq0_d = dt("tq0_d", [128, WQH], F32); tq1_d = dt("tq1_d", [128, WQH], F32)
    tk0_d = dt("tk0_d", [128, WKH], F32); tk1_d = dt("tk1_d", [128, WKH], F32)
    e0_d = dt("e0_d", [128, 512], F32); em_d = dt("em_d", [128, 512], F32)
    id_d = dt("id_d", [128, 128], BF16)
    onesf_d = dt("onesf_d", [1, 128], F32)
    onesb_d = dt("onesb_d", [1, 128], BF16)
    ones64_d = dt("ones64_d", [64, 1], BF16)
    gw_d = dt("gw_d", [E, NE], F32)
    gbr_d = dt("gbr_d", [1, 256], F32)
    wgT_d = dt("wgT_d", [NE, E, EI], F32)
    wuT_d = dt("wuT_d", [NE, E, EI], F32)
    wdT_d = dt("wdT_d", [NE, EI, E], F32)
    wsgT_d = dt("wsgT_d", [E, SHI], F32)
    wsuT_d = dt("wsuT_d", [E, SHI], F32)
    wsdT_d = dt("wsdT_d", [SHI, E], F32)
    outp = dt("outp", [NT, E], F32, kind="ExternalOutput")
    IN = "Internal"
    DBG = "ExternalOutput" if dbg else IN
    kt_sp = dt("kt_sp", [E, S], BF16, kind=DBG)
    qt_sp = dt("qt_sp", [E, NT], BF16, kind=DBG)
    v_sp = dt("v_sp", [S, E], BF16, kind=DBG)
    qdh_sp = dt("qdh_sp", [32 * NT * WQH + 4 * SLACK], BF16, kind=DBG)
    kdh_sp = dt("kdh_sp", [32 * NT * WKH + 4 * SLACK], BF16, kind=DBG)
    out1_sp = dt("out1_sp", [NT, E], F32, kind=DBG)
    actw_sp = dt("actw_sp", [NE * EI * NT], BF16, kind=DBG)
    gs_sp = dt("gs_sp", [SHI * NT], BF16, kind=DBG)
    cw_sp = dt("cw_sp", [NE * NT], BF16, kind=DBG)
    rt_dump = dt("rt_dump", [128, 1536], F32, kind=DBG) if dbg else None

    def sb(name, shape, dtp):
        return es.enter_context(nc.sbuf_tensor(name, shape, dtp))

    A = sb("A", [128, 36864], BF16)
    Bf = sb("Bf", [128, 8192], F32)
    C = sb("C", [128, 16384], BF16)
    fscr = sb("fscr", [128, 1024], F32)
    o1t = C[:, 9216:11264].bitcast(F32)
    ep1 = C[:, 11264:13312].bitcast(F32)
    ep2 = C[:, 13312:15360].bitcast(F32)
    bvb = sb("bvb", [128, E], BF16)
    bob = C[:, 0:2048].bitcast(F32)
    gamb = C[:, 2048:4096].bitcast(F32)
    betb = C[:, 4096:6144].bitcast(F32)
    sbqr = sb("sbqr", [1, E], BF16)
    sbkr = sb("sbkr", [1, E], BF16)
    sbvr = sb("sbvr", [1, E], BF16)
    ones512 = sb("ones512", [1, 512], BF16)
    dq = [sb(f"dq{r}", [128, WQH], BF16) for r in (0, 1)]
    dk = [sb(f"dk{r}", [128, WKH], BF16) for r in (0, 1)]
    se0 = sb("se0", [128, 512], BF16)
    sem_ = sb("sem_", [128, 512], BF16)
    ident = sb("ident", [128, 128], BF16)
    sonesf = sb("sonesf", [1, 128], F32)
    sonesb = sb("sonesb", [1, 128], BF16)
    sones64 = sb("sones64", [64, 1], BF16)
    sgw = sb("sgw", [128, 8 * NE], BF16)
    sgbr = sb("sgbr", [128, 256], F32)
    mxt = sb("mxt", [1, 16], F32)
    chs = sb("chs", [1, 8], F32)
    negC = sb("negC", [1, 1], F32)
    negCb = sb("negCb", [128, 1], F32)
    mxe = sb("mxe", [1, 1], F32)
    rcp = A[:, 34560:35584].bitcast(F32)[0:1, :]
    rcpb = A[:, 35584:36096][0:1, :]
    rcpB = A[64:128, 34560:35072]
    ctxT = sb("ctxT", [128, 8 * NT], BF16)
    colA = sb("colA", [128, 2], F32)
    colB = sb("colB", [128, 2], F32)
    colC = sb("colC", [128, 2], F32)
    rt_sc = A[:, 2048:2560].bitcast(F32)
    rt_sfc = A[:, 2560:3072].bitcast(F32)
    rt_msk = A[:, 3072:3584].bitcast(F32)
    rt_rank = A[:, 3584:4096].bitcast(F32)
    rt_eqr = A[:, 4096:4608].bitcast(F32)
    rt_sel = A[:, 4608:5120].bitcast(F32)
    rt_gs = sb("rt_gs", [128, 64], F32)
    rt_t = [sb(f"rt_t{i}", [128, 64], F32) for i in range(6)]
    rt_gm = sb("rt_gm", [128, 64], F32)
    rt_em = A[:, 5120:5632].bitcast(I32)
    rt_neg = A[:, 5632:6144].bitcast(F32)
    rt_w = A[:, 6144:6656].bitcast(F32)
    rt_ws = sb("rt_ws", [128, 16], F32)
    rt_wsi = sb("rt_wsi", [128, 16], F32)
    rt_cw = sb("rt_cw", [128, 256], BF16)
    rt_c1 = sb("rt_c1", [128, 16], BF16)
    logits = A[:, 6656:7168].bitcast(F32)
    cwB = [A[:, 32768 + i * 512:32768 + (i + 1) * 512] for i in (0, 1)]
    silb = A[:, 33792:34304]
    actb = A[:, 34304:34816]
    actwb = [A[:, 34816 + i * 512:34816 + (i + 1) * 512] for i in (0, 1)]

    def psal(name, shape):
        return es.enter_context(nc.psum_tensor(name, shape, F32))

    PS0 = psal("PS0", [128, 512])
    PS1 = psal("PS1", [128, 512])
    PSW = psal("PSW", [128, 1024])
    PSC = psal("PSC", [128, 2048])
    PSS = [PS0, PS1]
    PSS4 = [PS0, PS1, PSW[:, 0:512], PSW[:, 512:1024]]

    def ln_chain(xt, outbuf, eps, out_bf=None, waits=(), center_into=None, par=0):
        tgt = center_into if center_into is not None else xt
        cA = colA[:, par:par + 1]
        cB = colB[:, par:par + 1]
        cC = colC[:, par:par + 1]

        def f(eng, xt=xt, tgt=tgt, cA=cA):
            eng.tensor_reduce(cA, xt, axis=AX, op=OP.add)
            eng.drain()
            eng.tensor_scalar(cA, cA, 1.0 / E, None, OP.mult)
            eng.drain()
            eng.tensor_scalar(tgt, xt, cA, None, OP.subtract)
            return eng.drain()
        v1 = pr.op("v", f, waits=list(waits), inc=V)

        def f(eng, tgt=tgt, cB=cB):
            eng.activation(fscr[:], tgt, AF.Square, accum_out=cB)
            return eng.drain()
        s1 = pr.op("s", f, waits=[(V, v1)], inc=SS)

        def f(eng, eps=eps, cB=cB):
            eng.tensor_scalar(cB, cB, 1.0 / E, eps, OP.mult, OP.add)
            return eng.drain()
        v2 = pr.op("v", f, waits=[(SS, s1)], inc=V)

        def f(eng, cB=cB, cC=cC):
            eng.activation(cC, cB, AF.Ln)
            eng.drain()
            eng.activation(cC, cC, AF.Exp, scale=-0.5)
            return eng.drain()
        s2 = pr.op("s", f, waits=[(V, v2)], inc=SS)
        return s2

    pr.dma("g", bvb[:], bv_row.partition_broadcast(128))
    pr.dma("g", sbqr[:], bqr_d)
    pr.dma("g", sbkr[:], bkr_d)
    pr.dma("g", sbvr[:], bv_row)
    pr.dma("y", ones512[:], ones512_d)
    for r in (0, 1):
        pr.dma("g", dq[r][:], (tq0_d, tq1_d)[r])
        pr.dma("g", dk[r][:], (tk0_d, tk1_d)[r])
    pr.dma("g", se0[:], e0_d)
    pr.dma("g", sem_[:], em_d)
    pr.dma("y", ident[:], id_d)
    pr.dma("y", sonesf[:], onesf_d)
    pr.dma("y", sonesb[:], onesb_d)
    pr.dma("y", sones64[:], ones64_d)
    pr.dma("g", sgw[:].rearrange("p (f e) -> p f e", e=NE),
           gw_d.rearrange("(f p) e -> p f e", p=128))
    pr.dma("y", sgbr[:], gbr_d.partition_broadcast(128))
    wq_v = A[:, 0:8192]; wk_v = A[:, 8192:16384]; wv_v = A[:, 16384:24576]
    for (w_in, w_sb) in ((wq_d, wq_v), (wk_d, wk_v), (wv_d, wv_v)):
        pr.dma("g", w_sb.rearrange("p (f c) -> p f c", f=8),
               w_in.rearrange("(f p) c -> p f c", p=128))
    dg_p0 = DG.n
    dy_p0 = DY.n
    pr.barrier()

    sqv = [A[0:64, 24576 + i * WKH:24576 + (i + 1) * WKH] for i in (0, 1)]

    def f(eng):
        eng.activation(sqv[0], dk[0][0:64, :], AF.Square)
        eng.drain()
        eng.activation(sqv[1], dk[1][0:64, :], AF.Square)
        return eng.drain()
    sqd = pr.op("s", f, waits=[(DG, dg_p0)], inc=SS)
    nmm = 0
    for r in (0, 1):
        for cc in range(3):
            w0 = cc * 512
            wid = min(512, WKH - w0)

            def f(eng, r=r, w0=w0, wid=wid):
                return eng.matmul(PS0[0:1, 0:wid], sones64[:], sqv[r][:, w0:w0 + wid],
                                  start=True, stop=True)
            tv = pr.op("t", f, waits=[(SS, sqd)] + ([(V, V.n)] if nmm else []), inc=T)

            def f(eng, i=nmm, wid=wid):
                eng.reduce_max(mxt[:, i:i + 1], PS0[0:1, 0:wid], axis=AX)
                return eng.drain()
            pr.op("v", f, waits=[(T, tv)], inc=V)
            nmm += 1

    def f(eng):
        eng.reduce_max(mxe[:], mxt[:, 0:6], axis=AX)
        return eng.drain()
    pr.op("v", f, inc=V)
    pr.barrier()

    if phases < 1:
        return _finish(nc, pr, es)
    ln1T = [A[:, 24576 + k * 4096:24576 + (k + 1) * 4096] for k in (0, 1)]
    lnrow = [A[:, 32768 + k * 1024:32768 + (k + 1) * 1024] for k in range(2)]
    stg = [A[:, 34816 + k * 512:34816 + (k + 1) * 512] for k in range(4)]
    xfree = {}
    psw_free = 0
    lnrow_free = {}
    pss_free = [0, 0]
    stg_free = {}
    kq_iter = 0
    for c in range(8):
        cb = c % 2
        ln_done = []
        for i in range(4):
            xt = Bf[:, (cb * 4 + i) * 1024:(cb * 4 + i + 1) * 1024]
            lw = [(V, xfree[(c - 2, i)])] if (c - 2, i) in xfree else []
            xl = pr.dma("y", xt, x_b[c * 512 + i * 128:c * 512 + (i + 1) * 128, :], waits=lw)
            lrs = lnrow[i % 2]
            lr_w = [(T, lnrow_free[i % 2])] if i % 2 in lnrow_free else []
            s2 = ln_chain(xt, None, 1e-5, waits=[(DY, xl)] + lr_w, par=i % 2)

            def f(eng, xt=xt, lrs=lrs, i=i):
                eng.tensor_scalar(lrs, xt, colC[:, i % 2:i % 2 + 1], None, OP.mult)
                return eng.drain()
            v3 = pr.op("v", f, waits=[(SS, s2)], inc=V)
            xfree[(c, i)] = v3

            def f(eng, lrs=lrs):
                r = None
                for j in range(8):
                    r = eng.matmul(PSW[:, j * 128:(j + 1) * 128], lrs[:, j * 128:(j + 1) * 128],
                                   ident[:], start=True, stop=True)
                return r
            t1 = pr.op("t", f, waits=[(V, v3), (DY, dy_p0)] + ([(V, psw_free)] if psw_free else []), inc=T)
            lnrow_free[i % 2] = t1

            def f(eng, cb=cb, i=i):
                src = PSW[:].rearrange("p (j t) -> p j t", t=128)
                dst = ln1T[cb].rearrange("p (j t) -> p j t", t=512)[:, :, i * 128:(i + 1) * 128]
                eng.tensor_copy(dst, src)
                return eng.drain()
            v4 = pr.op("v", f, waits=[(T, t1)], inc=V)
            psw_free = v4
            ln_done.append(v4)

        for proj in ("k", "q"):
            wmat = wk_v if proj == "k" else wq_v
            for fo in range(8):
                slot = kq_iter % 2
                kq_iter += 1
                pw = [(SS, pss_free[slot][1])] if pss_free[slot] else []

                bvec = sbkr if proj == "k" else sbqr

                def f(eng, wmat=wmat, fo=fo, cb=cb, slot=slot, bvec=bvec):
                    eng.matmul(PSS[slot][:], bvec[0:1, fo * 128:(fo + 1) * 128], ones512[:], start=True, stop=False,
                               skip_group_check=True)
                    r = None
                    for fi in range(8):
                        r = eng.matmul(PSS[slot][:],
                                       wmat[:, fi * 1024 + fo * 128:fi * 1024 + (fo + 1) * 128],
                                       ln1T[cb][:, fi * 512:(fi + 1) * 512],
                                       start=False, stop=(fi == 7), skip_group_check=True)
                    return r
                t2 = pr.op("t", f, waits=[(V, ln_done[3]), (DG, dg_p0)] + pw, inc=T)
                if proj == "k":
                    kw = [(DS, stg_free["k"])] if "k" in stg_free else []

                    def f(eng, fo=fo, slot=slot):
                        eng.copy(stg[0][:, 0:256], PSS[slot][:, 0::2])
                        return eng.copy(stg[1][:, 0:256], PSS[slot][:, 1::2])
                    v5 = pr.op("s", f, waits=[(T, t2)] + kw, inc=SS)
                    pr.dma("s", kt_sp[fo * 128:(fo + 1) * 128, 256 * c:256 * c + 256], stg[0][:, 0:256], waits=[(SS, v5)])
                    d2 = pr.dma("s", kt_sp[fo * 128:(fo + 1) * 128, 2048 + 256 * c:2048 + 256 * c + 256],
                                stg[1][:, 0:256], waits=[(SS, v5)])
                    stg_free["k"] = d2
                    pss_free[slot] = ("s", v5)
                else:
                    qw = [(DS, stg_free["q"])] if "q" in stg_free else []

                    def f(eng, fo=fo, slot=slot):
                        return eng.copy(stg[2][:, 0:256], PSS[slot][:, 0::2])
                    v5 = pr.op("s", f, waits=[(T, t2)] + qw, inc=SS)
                    d1 = pr.dma("s", qt_sp[fo * 128:(fo + 1) * 128, 256 * c:256 * c + 256], stg[2][:, 0:256], waits=[(SS, v5)])
                    stg_free["q"] = d1
                    pss_free[slot] = ("s", v5)
        for i in range(4):
            for half in range(2):
                slot = kq_iter % 2
                kq_iter += 1
                pw = [(SS, pss_free[slot][1])] if pss_free[slot] else []

                def f(eng, cb=cb, i=i, half=half, slot=slot):
                    eng.matmul(PSS[slot][:], sonesb[:], sbvr[0:1, half * 512:half * 512 + 512],
                               start=True, stop=False, skip_group_check=True)
                    r = None
                    for fi in range(8):
                        r = eng.matmul(PSS[slot][:],
                                       ln1T[cb][:, fi * 512 + i * 128:fi * 512 + (i + 1) * 128],
                                       wv_v[:, fi * 1024 + half * 512:fi * 1024 + half * 512 + 512],
                                       start=False, stop=(fi == 7), skip_group_check=True)
                    return r
                t3 = pr.op("t", f, waits=[(V, ln_done[i])] + pw, inc=T)
                vw = [(DS, stg_free["v"])] if "v" in stg_free else []

                def f(eng, half=half, slot=slot):
                    return eng.copy(stg[3], PSS[slot][:])
                v6 = pr.op("s", f, waits=[(T, t3)] + vw, inc=SS)
                d1 = pr.dma("s", v_sp[c * 512 + i * 128:c * 512 + (i + 1) * 128, half * 512:half * 512 + 512],
                            stg[3], waits=[(SS, v6)])
                stg_free["v"] = d1
                pss_free[slot] = ("s", v6)
    pr.barrier()

    if phases < 2:
        return _finish(nc, pr, es)
    kt_v = [A[:, 0:4096], A[:, 4096:8192]]
    qt_v = [A[:, 8192:10240], A[:, 10240:12288]]
    vh_v = [A[:, 12288:14368], A[:, 14368:16448]]
    gq = [A[:, 16448 + k * 512:16448 + (k + 1) * 512] for k in range(4)]
    gk = [A[:, 18496 + k * 512:18496 + (k + 1) * 512] for k in range(4)]
    probs = [A[:, 20544 + k * 2048:20544 + (k + 1) * 2048] for k in range(2)]
    sqq = A[:, 24640:26688]
    stgq = [A[:, 26688 + k * 2624:26688 + (k + 1) * 2624] for k in range(3)]
    stage = [Bf[:, k * 2048:(k + 1) * 2048] for k in range(2)]
    tiles = [(m, rho) for m in range(16) for rho in (0, 1)]

    st2 = {"it": 0, "cons": {}, "gq_free": {}, "gk_free": {}, "probs_free": {},
           "stage_free": {}, "stg_free": {}, "psw_ev": {}, "pswc": 0,
           "slot_T_done": {}, "slot_V_done": {}, "qdkd_done": {}, "psfree0": 0,
           "ctx_V_done": 0}

    def emit_loads(hd, vh=True):
        slot = (hd // 2) % 2
        if hd % 2 == 0:
            w = []
            if hd >= 4:
                w = [(pr.sems["T"], st2["slot_T_done"].get((hd // 2 - 2) % 2, 0)),
                     (pr.sems["V"], st2["slot_V_done"].get((hd // 2 - 2) % 2, 0))]
                w = [x for x in w if x[1]]
            l_kt = pr.dma("y", kt_v[slot][:, :], kt_sp[64 * hd:64 * hd + 128, :], waits=w)
            l_qt = pr.dma("y", qt_v[slot][:, :], qt_sp[64 * hd:64 * hd + 128, :])
            st2.setdefault("plk", {})[slot] = (l_kt, l_qt)
        else:
            l_kt, l_qt = st2["plk"][slot]
        l_vh = DY.n
        vslot = hd % 2
        if not vh:
            return l_kt, l_qt, DY.n
        vw = []
        if hd >= 2:
            vw = [(pr.sems["T"], st2.get("vh_T_done", {}).get(vslot, 0))]
            vw = [x for x in vw if x[1]]
        for (m, rho) in tiles:
            blk = 2 * m + rho
            l_vh = pr.dma("y", vh_v[vslot][:, blk * 65:blk * 65 + 64],
                          v_sp[256 * m + rho:256 * m + rho + 255:2, 64 * hd:64 * hd + 64],
                          waits=vw if (m, rho) == (0, 0) else [])
        if hd < 2:
            def f(eng, vslot=vslot):
                eng.memset(vh_v[vslot].rearrange("p (b c) -> p b c", c=65)[:, :, 64:65], 1.0)
                return eng.drain()
            pr.op("v", f, waits=[(DY, l_vh)], inc=V)
        return l_kt, l_qt, l_vh

    def gen_qdkd(hd, l_kt, l_qt):
        pslt = hd // 2
        kslot = pslt % 2
        sidx = 0
        for kind in ("q", "k"):
            WW = WQH if kind == "q" else WKH
            dtab = dq if kind == "q" else dk
            spd = qdh_sp if kind == "q" else kdh_sp
            statv = qt_v[kslot] if kind == "q" else kt_v[kslot]
            lwait = l_qt if kind == "q" else l_kt
            for rho in (0, 1):
                for tt in range(16):
                    coff = (rho * 2048 if kind == "k" else 0) + tt * 128
                    stgX = stgq[sidx % 3]
                    ev_last = [0, 0]
                    for ci, (c0, c1) in enumerate(((0, 512), (512, 1024), (1024, WW))):
                        bp = st2["pswc"] % 4
                        st2["pswc"] += 1
                        ev_sem = V if bp % 2 == 0 else SS
                        ev_q = "v" if bp % 2 == 0 else "s"
                        wl = [(DY, lwait), (DG, dg_p0)]
                        if bp in st2["psw_ev"]:
                            wl.append((ev_sem, st2["psw_ev"][bp]))

                        def f(eng, statv=statv, coff=coff, dtab=dtab, rho=rho, c0=c0, c1=c1, bp=bp):
                            pa = (PSW[:, 0:c1 - c0], PS0[:, 0:c1 - c0],
                                  PSC[:, 0:c1 - c0], PSC[:, 1024:1024 + c1 - c0])[bp]
                            pb = (PSW[:, 512:512 + c1 - c0], PS1[:, 0:c1 - c0],
                                  PSC[:, 512:512 + c1 - c0], PSC[:, 1536:1536 + c1 - c0])[bp]
                            eng.matmul(pa, statv[0:64, coff:coff + 128],
                                       dtab[rho][0:64, c0:c1], start=True, stop=True,
                                       tile_position=(0, 0))
                            return eng.matmul(pb, statv[64:128, coff:coff + 128],
                                              dtab[rho][64:128, c0:c1], start=True, stop=True,
                                              tile_position=(64, 0))
                        t4 = pr.op("t", f, waits=wl, inc=T)

                        def f(eng, stgX=stgX, c0=c0, c1=c1, WW=WW, bp=bp):
                            cp = eng.tensor_copy if hasattr(eng, "tensor_copy") else eng.copy
                            dst = stgX[:, 0:2 * WW].rearrange("p (h w) -> p h w", h=2)[:, :, c0:c1]
                            if bp == 0:
                                srcv = PSW[:].rearrange("p (h w) -> p h w", h=2)[:, :, 0:c1 - c0]
                                return cp(dst, srcv)
                            if bp == 2:
                                srcv = PSC[:, 0:1024].rearrange("p (h w) -> p h w", h=2)[:, :, 0:c1 - c0]
                                return cp(dst, srcv)
                            if bp == 3:
                                srcv = PSC[:, 1024:2048].rearrange("p (h w) -> p h w", h=2)[:, :, 0:c1 - c0]
                                return cp(dst, srcv)
                            cp(stgX[:, c0:c1], PS0[:, 0:c1 - c0])
                            return cp(stgX[:, WW + c0:WW + c1], PS1[:, 0:c1 - c0])
                        prev = st2["stg_free"].get(sidx % 3)
                        pw_ = []
                        if prev and ci == 0:
                            pq, pv = prev
                            pw_ = [({"s": DS, "y": DY, "g": DG}[pq], pv)]
                        ev = pr.op(ev_q, f, waits=[(T, t4)] + pw_, inc=ev_sem)
                        st2["psw_ev"][bp] = ev
                        ev_last[0 if bp % 2 == 0 else 1] = ev
                    base = pslt * 4 * NT * WW + (rho * NT + tt * 128) * (2 * WW)
                    qq = ("s", "y", "g")[sidx % 3]
                    dw = []
                    if ev_last[0]:
                        dw.append((V, ev_last[0]))
                    if ev_last[1]:
                        dw.append((SS, ev_last[1]))
                    d = pr.dma(qq, spd[base:base + 128 * 2 * WW].rearrange("(p w) -> p w", w=2 * WW),
                               stgX[:, 0:2 * WW], waits=dw)
                    st2["stg_free"][sidx % 3] = (qq, d)
                    sidx += 1
                    yield
        st2["qdkd_done"][pslt] = (DS.n, DY.n, DG.n)
        st2["slot_T_done"][kslot] = T.n
        st2["slot_V_done"][kslot] = V.n

    def emit_chead(hd, l_kt, l_qt, l_vh, feed=None):
        slot = (hd // 2) % 2
        hh = 64 * (hd % 2)

        def fq(eng, slot=slot, hh=hh):
            eng.activation(sqq[0:64, 0:2048], qt_v[slot][hh:hh + 64, :], AF.Square)
            return eng.drain()
        sqd1 = pr.op("s", fq, waits=[(DY, l_qt), (SS, SS.n)], inc=SS)
        nmx = 0
        w0 = []
        if st2["psfree0"]:
            w0 = [(V, st2["psfree0"]), (SS, st2["cons_ss"])] if st2.get("cons_ss") else [(V, st2["psfree0"])]
        vhist = {}
        for cc in range(4):
            pb = PSS[cc % 2]

            def f(eng, cc=cc, pb=pb):
                return eng.matmul(pb[0:1, 0:512], sones64[:], sqq[0:64, cc * 512:(cc + 1) * 512],
                                  start=True, stop=True)
            tv = pr.op("t", f, waits=[(SS, sqd1)] + ([(V, vhist[cc % 2])] if cc % 2 in vhist else [(V, V.n)]) + (w0 if cc < 2 else []), inc=T)

            def f(eng, i=nmx, pb=pb):
                eng.reduce_max(mxt[:, i:i + 1], pb[0:1, 0:512], axis=AX)
                return eng.drain()
            vhist[cc % 2] = pr.op("v", f, waits=[(T, tv)], inc=V)
            nmx += 1

        for half2 in range(2):
            def fk(eng, slot=slot, half2=half2, hh=hh):
                eng.activation(sqq[0:64, 0:2048], kt_v[slot][hh:hh + 64, half2 * 2048:half2 * 2048 + 2048], AF.Square)
                return eng.drain()
            sqd2 = pr.op("s", fk, waits=[(V, V.n), (DY, l_kt)], inc=SS)
            for cc in range(4):
                pb = PSS[cc % 2]

                def f(eng, cc=cc, pb=pb):
                    return eng.matmul(pb[0:1, 0:512], sones64[:], sqq[0:64, cc * 512:(cc + 1) * 512],
                                      start=True, stop=True)
                tv = pr.op("t", f, waits=[(SS, sqd2)] + ([(V, vhist[cc % 2])] if cc % 2 in vhist else [(V, V.n)]), inc=T)

                def f(eng, i=nmx, pb=pb):
                    eng.reduce_max(mxt[:, i:i + 1], pb[0:1, 0:512], axis=AX)
                    return eng.drain()
                vhist[cc % 2] = pr.op("v", f, waits=[(T, tv)], inc=V)
                nmx += 1

        def f(eng):
            eng.reduce_max(chs[:, 0:1], mxt[:, 0:4], axis=AX)
            eng.drain()
            eng.reduce_max(chs[:, 1:2], mxt[:, 4:12], axis=AX)
            eng.drain()
            eng.tensor_tensor(chs[:, 2:3], chs[:, 0:1], chs[:, 1:2], OP.mult)
            eng.drain()
            eng.tensor_tensor(chs[:, 3:4], chs[:, 0:1], mxe[:], OP.mult)
            eng.drain()
            eng.tensor_tensor(chs[:, 4:5], chs[:, 1:2], mxe[:], OP.mult)
            return eng.drain()
        vch = pr.op("v", f, inc=V)

        def f(eng):
            eng.activation(chs[:, 2:5], chs[:, 2:5], AF.Ln)
            eng.drain()
            eng.activation(chs[:, 2:5], chs[:, 2:5], AF.Exp, scale=0.5)
            return eng.drain()
        sch = pr.op("s", f, waits=[(V, vch)], inc=SS)

        def f(eng):
            eng.tensor_reduce(chs[:, 5:6], chs[:, 2:5], axis=AX, op=OP.add)
            eng.drain()
            eng.tensor_scalar(negC[:], chs[:, 5:6], -0.125, None, OP.mult)
            return eng.drain()
        vch2 = pr.op("v", f, waits=[(SS, sch)], inc=V)

        def f(eng):
            return eng.matmul(PS0[:, 0:1], sonesf[:], negC[:], start=True, stop=True)
        tch = pr.op("t", f, waits=[(V, vch2)], inc=T)

        def f(eng):
            eng.tensor_copy(negCb[:], PS0[:, 0:1])
            return eng.drain()
        return pr.op("v", f, waits=[(T, tch)], inc=V)

    def emit_scores(hd, ps0t_free_v, l_vh, feed=None):
        slot = (hd // 2) % 2
        vslot = hd % 2
        hh = 64 * (hd % 2)
        qd_s, qd_y, qd_g = st2["qdkd_done"][hd // 2]
        cons = st2["cons"]
        gq_free = st2["gq_free"]
        gk_free = st2["gk_free"]
        probs_free = st2["probs_free"]
        stage_free2 = st2["stage_free"]
        first_tile = True
        v_ones = V.n
        pend_ctx = [None]

        def flush_ctx():
            if pend_ctx[0] is None:
                return
            fc, cw, tic = pend_ctx[0]
            pend_ctx[0] = None
            probs_free[tic] = pr.op("t", fc, waits=cw, inc=T)

        for (m, rho) in tiles:
            ti = 2 * m + rho
            if feed:
                next(feed, None)
                next(feed, None)
            pslot = ti % 2
            band_cols = []
            for cc in range(4):
                it = st2["it"]
                sl2 = it % 4
                gsl = it % 4
                cls = [classify(cc * 512 + s * 128, m) for s in range(4)]
                runs = []
                s = 0
                while s < 4:
                    e = s
                    while e + 1 < 4 and cls[e + 1] == cls[s]:
                        e += 1
                    runs.append((cls[s], s, e + 1))
                    s = e + 1
                wl = []
                if (it - 4) in cons:
                    cv, csv = cons[it - 4]
                    if cv:
                        wl.append((V, cv))
                    if csv:
                        wl.append((SS, csv))
                if first_tile:
                    wl.append((V, ps0t_free_v))
                g_run = [r for r in runs if r[0] == "g"]
                gq_t = gk_t = None
                if g_run:
                    _, gs_, ge_ = g_run[0]
                    ns = ge_ - gs_
                    T0a = cc * 512 + gs_ * 128
                    wg = [(DS, qd_s), (DG, qd_g)]
                    if gsl in gq_free:
                        wg.append((pr.sems["T"], gq_free[gsl]))
                    baseq = (hd // 2) * 4 * NT * WQH + rho * NT * 2 * WQH + (hd % 2) * WQH \
                        + T0a * (2 * WQH - 1) + K0 + 128 * m
                    srcq = qdh_sp[baseq:baseq + ns * 128 * (2 * WQH - 1)] \
                        .rearrange("(p w) -> p w", w=2 * WQH - 1) \
                        .rearrange("(s p) w -> p s w", p=128)[:, :, 0:128]
                    gq_t = pr.dma("y", gq[gsl][:, 0:ns * 128].rearrange("p (s j) -> p s j", j=128),
                                  srcq, waits=wg)
                    wk2 = [(DS, qd_s), (DG, qd_g)]
                    if gsl in gk_free:
                        wk2.append((V, gk_free[gsl]))
                    basek = (hd // 2) * 4 * NT * WKH + rho * NT * 2 * WKH + (hd % 2) * WKH \
                        + 128 * m * (2 * WKH - 1) + K0 + T0a
                    srck = kdh_sp[basek:basek + 128 * (2 * WKH - 1)] \
                        .rearrange("(p w) -> p w", w=2 * WKH - 1)[0:128, 0:ns * 128]
                    gk_t = pr.dma("y", gk[gsl][:, 0:ns * 128], srck, waits=wk2)

                def f(eng, m=m, rho=rho, cc=cc, sl2=sl2, gsl=gsl, runs=runs, g_run=g_run, slot=slot, hh=hh):
                    ps = PSS4[sl2]
                    ktt = kt_v[slot][hh:hh + 64, rho * 2048 + m * 128:rho * 2048 + (m + 1) * 128]
                    r = eng.matmul(ps[:], ktt, qt_v[slot][hh:hh + 64, cc * 512:(cc + 1) * 512],
                                   start=True, stop=False)
                    for (k_, s_, e_) in runs:
                        if k_ == "f":
                            r = eng.matmul(ps[:, s_ * 128:e_ * 128], ktt,
                                           se0[hh:hh + 64, 0:(e_ - s_) * 128], start=False, stop=False,
                                           skip_group_check=True)
                        elif k_ == "p":
                            r = eng.matmul(ps[:, s_ * 128:e_ * 128], ktt,
                                           sem_[hh:hh + 64, 0:(e_ - s_) * 128], start=False, stop=False,
                                           skip_group_check=True)
                    for (k_, s_, e_) in runs:
                        if k_ == "f":
                            r = eng.matmul(ps[:, s_ * 128:e_ * 128], se0[hh:hh + 64, 0:128],
                                           qt_v[slot][hh:hh + 64, cc * 512 + s_ * 128:cc * 512 + e_ * 128],
                                           start=False, stop=False, skip_group_check=True)
                        elif k_ == "p":
                            r = eng.matmul(ps[:, s_ * 128:e_ * 128], sem_[hh:hh + 64, 0:128],
                                           qt_v[slot][hh:hh + 64, cc * 512 + s_ * 128:cc * 512 + e_ * 128],
                                           start=False, stop=False, skip_group_check=True)
                    if g_run:
                        _, gs_, ge_ = g_run[0]
                        for s_ in range(gs_, ge_):
                            r = eng.matmul(ps[:, s_ * 128:(s_ + 1) * 128],
                                           gq[gsl][:, (s_ - gs_) * 128:(s_ - gs_ + 1) * 128],
                                           ident[:], start=False, stop=(s_ == ge_ - 1),
                                           skip_group_check=True)
                    return r
                wl2 = list(wl)
                if g_run and gq_t:
                    wl2.append((DY, gq_t))
                t5 = pr.op("t", f, waits=wl2, inc=T)
                if cc == 1:
                    flush_ctx()
                if g_run:
                    gq_free[gsl] = t5

                v_mark = 0
                s_mark = 0
                if g_run:
                    _, gs_, ge_ = g_run[0]
                    band_cols.append((cc * 512 + gs_ * 128, cc * 512 + ge_ * 128))
                    ws = [(T, t5)] + ([(DY, gk_t)] if gk_t else [])
                    if stage_free2.get(pslot):
                        ws.append((SS, stage_free2[pslot]))

                    def f(eng, cc=cc, gs_=gs_, ge_=ge_, sl2=sl2, gsl=gsl, pslot=pslot):
                        return eng.tensor_tensor(stage[pslot][:, cc * 512 + gs_ * 128:cc * 512 + ge_ * 128],
                                          PSS4[sl2][:, gs_ * 128:ge_ * 128],
                                          gk[gsl][:, 0:(ge_ - gs_) * 128], OP.add)
                    v8 = pr.op("v", f, waits=ws, inc=V)
                    gk_free[gsl] = v8
                    v_mark = v8
                for (k_, s_, e_) in runs:
                    if k_ == "g":
                        continue
                    pw2 = [(T, t5)]
                    if v_mark:
                        pw2.append((V, v_mark))
                    if (ti - 2) in probs_free or hd > 0:
                        pw2.append((pr.sems["T"], probs_free.get(ti - 2, probs_free.get(30 + (ti % 2), 0)) or T.n - 1000 if False else probs_free.get(ti - 2, 0)))
                    pw2 = [x for x in pw2 if not (x[0] is pr.sems["T"] and x[1] == 0)]

                    def f(eng, cc=cc, s_=s_, e_=e_, sl2=sl2, pslot=pslot):
                        return eng.activation(probs[pslot][:, cc * 512 + s_ * 128:cc * 512 + e_ * 128],
                                       PSS4[sl2][:, s_ * 128:e_ * 128], AF.Exp,
                                       bias=negCb[:], scale=0.125)
                    s_mark = pr.op("s", f, waits=pw2, inc=SS)
                cons[it] = (v_mark, s_mark)
                st2["it"] += 1

            it = st2["it"]
            if band_cols:
                lo = min(b[0] for b in band_cols)
                hi = max(b[1] for b in band_cols)
                vmax = max((cons[it - 1 - k][0] or 0) for k in range(4))
                pw3 = ([(V, vmax)] if vmax else [])
                if (ti - 2) in probs_free:
                    pw3.append((pr.sems["T"], probs_free[ti - 2]))

                def f(eng, lo=lo, hi=hi, pslot=pslot):
                    return eng.activation(probs[pslot][:, lo:hi], stage[pslot][:, lo:hi],
                                   AF.Exp, bias=negCb[:], scale=0.125)
                exp_val = pr.op("s", f, waits=pw3, inc=SS)
                stage_free2[pslot] = exp_val

            ctx_w = [(SS, SS.n), (DY, l_vh)]
            if first_tile:
                ctx_w.append((V, st2["ctx_V_done"]))

            def f(eng, blk=ti, first=first_tile, last=(ti == 31), vslot=vslot, pslot=pslot):
                r = None
                for ccq in range(4):
                    r = eng.matmul(PSC[0:65, ccq * 512:(ccq + 1) * 512],
                                   vh_v[vslot][:, blk * 65:blk * 65 + 65],
                                   probs[pslot][:, ccq * 512:(ccq + 1) * 512],
                                   start=first, stop=last, skip_group_check=True)
                return r
            pend_ctx[0] = (f, ctx_w, ti)
            first_tile = False
        flush_ctx()
        pf30, pf31 = probs_free.get(30, 0), probs_free.get(31, 0)
        probs_free.clear()
        probs_free[-2] = pf30
        probs_free[-1] = pf31
        st2["cons_ss"] = max((cons[st2["it"] - 1 - k][1] or 0) for k in range(2))

        tlast = T.n
        for ccq in range(4):
            if feed:
                next(feed, None)

            def f(eng, ccq=ccq):
                eng.reciprocal(rcp[:, 0:512], PSC[64:65, ccq * 512:(ccq + 1) * 512])
                eng.drain()
                eng.tensor_copy(rcpb[:, 0:512], rcp[:, 0:512])
                return eng.drain()
            v9 = pr.op("v", f, waits=[(T, tlast if ccq == 0 else T.n)], inc=V)

            def f(eng, ccq=ccq):
                return eng.matmul(PS0[0:64, :], sonesb[:, 0:64], rcpb[:, 0:512],
                                  start=True, stop=True)
            t7 = pr.op("t", f, waits=[(V, v9)], inc=T)

            def f(eng, ccq=ccq, hd=hd):
                eng.tensor_copy(rcpB, PS0[0:64, :])
                eng.drain()
                half = hd % 2
                pair = hd // 2
                eng.tensor_tensor(ctxT[half * 64:half * 64 + 64,
                                       pair * 2048 + ccq * 512:pair * 2048 + (ccq + 1) * 512],
                                  PSC[0:64, ccq * 512:(ccq + 1) * 512], rcpB, OP.mult)
                return eng.drain()
            fv = pr.op("v", f, waits=[(T, t7)], inc=V)
        st2["psfree0"] = fv
        st2["ctx_V_done"] = fv
        if hd % 2 == 1:
            st2["slot_T_done"][slot] = T.n
            st2["slot_V_done"][slot] = V.n
        st2.setdefault("vh_T_done", {})[vslot] = T.n

    for hd in range(0, nheads, 2):
        lks = emit_loads(hd, vh=False)
        for _ in gen_qdkd(hd, lks[0], lks[1]):
            pass
    pr.barrier()
    st2["slot_T_done"].clear()
    st2["slot_V_done"].clear()
    st2["plk"] = {}
    pend = None
    for hd in range(nheads):
        lks = emit_loads(hd)
        if pend is not None:
            ncb = emit_chead(pend[0], pend[1], pend[2], pend[3])
            emit_scores(pend[0], ncb, pend[3])
        pend = (hd, lks[0], lks[1], lks[2])
    ncb = emit_chead(pend[0], pend[1], pend[2], pend[3])
    emit_scores(pend[0], ncb, pend[3])
    pr.barrier()


    wo_v = A[:, 0:8192]
    pr.dma("g", wo_v.rearrange("p (f c) -> p f c", f=8),
           wo_d.rearrange("(f p) c -> p f c", p=128))
    bc_ld = pr.dma("y", bob, bo_row.partition_broadcast(128))
    pr.dma("y", gamb, gam_row.partition_broadcast(128))
    bc_ld = pr.dma("y", betb, bet_row.partition_broadcast(128))
    psw_free_v = 0
    p3_last = {}
    for tt in range(16):
        par = tt % 2
        xt = Bf[:, par * 4096:par * 4096 + 1024]
        h2 = Bf[:, par * 4096 + 1024:par * 4096 + 2048]
        sc1_ = Bf[:, par * 4096 + 2048:par * 4096 + 3072]
        sc2_ = Bf[:, par * 4096 + 3072:par * 4096 + 4096]
        lw = [(V, p3_last[par])] if par in p3_last else []
        xl = pr.dma("y", xt, x_b[2 * tt * 128:2 * tt * 128 + 256:2, :], waits=lw)

        def f(eng, tt=tt):
            r = None
            for pp in range(8):
                for half in range(2):
                    r = eng.matmul(PSW[:, half * 512:half * 512 + 512],
                                   ctxT[:, pp * 2048 + tt * 128:pp * 2048 + (tt + 1) * 128],
                                   wo_v[:, pp * 1024 + half * 512:pp * 1024 + half * 512 + 512],
                                   start=(pp == 0), stop=(pp == 7), skip_group_check=True)
            return r
        wl = [(DG, DG.n)]
        if psw_free_v:
            wl.append((V, psw_free_v))
        t8 = pr.op("t", f, waits=wl, inc=T)

        s4 = ln_chain(xt, None, 1e-5, waits=[(DY, xl)], center_into=sc1_, par=par)

        def f(eng, xt=xt, h2=h2, sc1_=sc1_, par=par):
            eng.tensor_scalar(sc1_, sc1_, colC[:, par:par + 1], None, OP.mult)
            eng.drain()
            eng.tensor_tensor(h2, PSW[:], bob, OP.add)
            eng.drain()
            eng.tensor_tensor(h2, h2, sc1_, OP.add)
            return eng.drain()
        v12 = pr.op("v", f, waits=[(SS, s4), (T, t8), (DY, bc_ld)], inc=V)
        psw_free_v = v12

        s6 = ln_chain(h2, None, 1e-12, waits=[(V, v12)], par=par)

        def f(eng, h2=h2, sc2_=sc2_, par=par):
            eng.tensor_scalar(h2, h2, colC[:, par:par + 1], None, OP.mult)
            eng.drain()
            eng.tensor_tensor(sc2_, h2, gamb, OP.mult)
            eng.drain()
            eng.tensor_tensor(sc2_, sc2_, betb, OP.add)
            return eng.drain()
        v14 = pr.op("v", f, waits=[(SS, s6)], inc=V)

        s8 = ln_chain(sc2_, None, 1e-5, waits=[(V, v14)], par=par)

        def f(eng, xt=xt, sc2_=sc2_, h2=h2, par=par):
            eng.tensor_scalar(sc2_, sc2_, colC[:, par:par + 1], None, OP.mult)
            eng.drain()
            eng.tensor_tensor(h2, xt, sc2_, OP.add)
            return eng.drain()
        v16 = pr.op("v", f, waits=[(SS, s8)], inc=V)
        p3_last[par] = v16
        pr.dma("s", out1_sp[tt * 128:(tt + 1) * 128, :], h2, waits=[(V, v16)])
    pr.barrier()

    ln2T = C[:, 0:16384]
    lnrow2 = [A[:, 0:1024], A[:, 1024:2048]]
    psw_free_v = 0
    p4_last = {}
    for bb in range(16):
        par = bb % 2
        xt = Bf[:, par * 4096:par * 4096 + 1024]
        lw = [(V, p4_last[par])] if par in p4_last else [(V, V.n)] if bb == 0 else []
        xl = pr.dma("y", xt, out1_sp[bb:bb + 16 * 127 + 1:16, :], waits=lw)
        s10 = ln_chain(xt, None, 1e-5, waits=[(DY, xl)], par=par)

        def f(eng, xt=xt, par=par, bb=bb):
            eng.tensor_scalar(lnrow2[par], xt, colC[:, par:par + 1], None, OP.mult)
            return eng.drain()
        v19 = pr.op("v", f, waits=[(SS, s10)], inc=V)
        p4_last[par] = v19

        def f(eng, par=par):
            r = None
            for j in range(8):
                r = eng.matmul(PSW[:, j * 128:(j + 1) * 128], lnrow2[par][:, j * 128:(j + 1) * 128],
                               ident[:], start=True, stop=True)
            return r
        wl = [(V, v19)]
        if psw_free_v:
            wl.append((V, psw_free_v))
        t9 = pr.op("t", f, waits=wl, inc=T)

        def f(eng, bb=bb):
            src = PSW[:].rearrange("p (j t) -> p j t", t=128)
            dst = ln2T.rearrange("p (j t) -> p j t", t=2048)[:, :, bb::16]
            eng.tensor_copy(dst, src)
            return eng.drain()
        v20 = pr.op("v", f, waits=[(T, t9)], inc=V)
        psw_free_v = v20

        def f(eng, bb=bb):
            r = None
            l2v = ln2T.rearrange("p (j t) -> p j t", t=2048)
            for fi in range(8):
                r = eng.matmul(PS0[:, 0:16], l2v[:, fi, bb::16],
                               sgw[:, fi * 16:(fi + 1) * 16], start=(fi == 0), stop=(fi == 7))
            return r
        t10 = pr.op("t", f, waits=[(V, v20)], inc=T)

        def f(eng, bb=bb):
            eng.tensor_copy(logits[:, bb * 16:(bb + 1) * 16], PS0[:, 0:16])
            return eng.drain()
        pr.op("v", f, waits=[(T, t10)], inc=V)
    pr.barrier()

    if phases < 5:
        return _finish(nc, pr, es)
    def f(eng):
        eng.activation(rt_sc[:], logits[:], AF.Sigmoid)
        return eng.drain()
    s11 = pr.op("s", f, inc=SS)

    def f(eng):
        eng.memset(rt_neg[:], -1e30)
        eng.drain()
        eng.tensor_tensor(rt_sfc[:], rt_sc[:], sgbr[:], OP.add)
        eng.drain()
        g4 = rt_sfc.rearrange("p (x k) -> p x k", k=4)
        eng.tensor_tensor(rt_t[0][:], g4[:, :, 0], g4[:, :, 1], OP.max)
        eng.tensor_tensor(rt_t[1][:], g4[:, :, 0], g4[:, :, 1], OP.min)
        eng.tensor_tensor(rt_t[2][:], g4[:, :, 2], g4[:, :, 3], OP.max)
        eng.tensor_tensor(rt_t[3][:], g4[:, :, 2], g4[:, :, 3], OP.min)
        eng.drain()
        eng.tensor_tensor(rt_t[4][:], rt_t[0][:], rt_t[2][:], OP.max)
        eng.tensor_tensor(rt_t[5][:], rt_t[0][:], rt_t[2][:], OP.min)
        eng.drain()
        eng.tensor_tensor(rt_t[0][:], rt_t[1][:], rt_t[3][:], OP.max)
        eng.drain()
        eng.tensor_tensor(rt_t[5][:], rt_t[5][:], rt_t[0][:], OP.max)
        eng.drain()
        eng.tensor_tensor(rt_gs[:], rt_t[4][:], rt_t[5][:], OP.add)
        return eng.drain()
    v22 = pr.op("v", f, waits=[(SS, s11)], inc=V)

    def f(eng):
        gs4 = rt_gs.rearrange("p (x g) -> p x g", g=4)
        for g in range(4):
            dsts = []
            k = 0
            for g2 in range(4):
                if g2 == g:
                    continue
                eng.tensor_tensor(rt_t[k][:, 0:16], gs4[:, :, g2], gs4[:, :, g], OP.is_gt)
                dsts.append(k)
                k += 1
            for g2 in range(g):
                eng.tensor_tensor(rt_t[k][:, 0:16], gs4[:, :, g2], gs4[:, :, g], OP.is_equal)
                dsts.append(k)
                k += 1
            eng.drain()
            acc = dsts[0]
            for d2 in dsts[1:]:
                eng.tensor_tensor(rt_t[acc][:, 0:16], rt_t[acc][:, 0:16], rt_t[d2][:, 0:16], OP.add)
                eng.drain()
            eng.tensor_scalar(rt_gm[:, g * 16:(g + 1) * 16], rt_t[acc][:, 0:16], 1.5, None, OP.is_lt)
            eng.drain()
        for g in range(4):
            for kk in range(4):
                eng.tensor_copy(rt_em[:, 4 * g + kk::16], rt_gm[:, g * 16:(g + 1) * 16])
        eng.drain()
        eng.select(rt_msk[:], rt_em[:], rt_sfc[:], rt_neg[:], add_drain=True)
        return eng.drain()
    v23 = pr.op("v", f, waits=[(V, v22)], inc=V)

    def f(eng):
        for b in range(16):
            for e in range(NE):
                eng.tensor_scalar(Bf[:, (b * 16 + e) * 16:(b * 16 + e) * 16 + 16],
                                  rt_msk[:, b * 16:(b + 1) * 16],
                                  rt_msk[:, b * 16 + e:b * 16 + e + 1], None, OP.is_gt)
        eng.drain()
        for b in range(16):
            for e in range(NE):
                eng.tensor_reduce(rt_rank[:, b * 16 + e:b * 16 + e + 1],
                                  Bf[:, (b * 16 + e) * 16:(b * 16 + e) * 16 + 16], axis=AX, op=OP.add)
        eng.drain()
        eng.memset(rt_eqr[:], 0.0)
        eng.drain()
        for b in range(16):
            for e in range(1, NE):
                eng.tensor_scalar(Bf[:, (b * 16 + e) * 16:(b * 16 + e) * 16 + e],
                                  rt_msk[:, b * 16:b * 16 + e],
                                  rt_msk[:, b * 16 + e:b * 16 + e + 1], None, OP.is_equal)
        eng.drain()
        for b in range(16):
            for e in range(1, NE):
                eng.tensor_reduce(rt_eqr[:, b * 16 + e:b * 16 + e + 1],
                                  Bf[:, (b * 16 + e) * 16:(b * 16 + e) * 16 + e], axis=AX, op=OP.add)
        eng.drain()
        eng.tensor_tensor(rt_rank[:], rt_rank[:], rt_eqr[:], OP.add)
        eng.drain()
        eng.tensor_scalar(rt_sel[:], rt_rank[:], 3.5, None, OP.is_lt)
        eng.drain()
        eng.tensor_tensor(rt_w[:], rt_sc[:], rt_sel[:], OP.mult)
        eng.drain()
        eng.tensor_reduce(rt_ws[:], rt_w.rearrange("p (x e) -> p x e", e=16), axis=AX, op=OP.add)
        eng.drain()
        eng.tensor_scalar(rt_wsi[:], rt_ws[:], 1e-20, None, OP.add)
        eng.drain()
        eng.reciprocal(rt_ws[:], rt_wsi[:])
        eng.drain()
        for b in range(16):
            eng.tensor_scalar(rt_cw[:, b * 16:(b + 1) * 16], rt_w[:, b * 16:(b + 1) * 16],
                              rt_ws[:, b:b + 1], 2.5, OP.mult, OP.mult)
        return eng.drain()
    v24 = pr.op("v", f, waits=[(V, v23)], inc=V)
    if dbg:
        def f(eng):
            eng.tensor_copy(C[:, 0:512].bitcast(F32), logits[:])
            eng.tensor_copy(C[:, 512:1024].bitcast(F32), rt_sfc[:])
            eng.tensor_copy(C[:, 1024:1152].bitcast(F32), rt_gs[:])
            eng.tensor_copy(C[:, 1152:1280].bitcast(F32), rt_gm[:])
            eng.tensor_copy(C[:, 1536:2048].bitcast(F32), rt_rank[:])
            eng.tensor_copy(C[:, 2048:2560].bitcast(F32), rt_sel[:])
            return eng.drain()
        vdmp = pr.op("v", f, waits=[(V, v24)], inc=V)
        pr.dma("s", rt_dump[:, 0:256], C[:, 0:512].bitcast(F32), waits=[(V, vdmp)])
        pr.dma("s", rt_dump[:, 256:512], C[:, 512:1024].bitcast(F32), waits=[(V, vdmp)])
        pr.dma("s", rt_dump[:, 512:576], C[:, 1024:1152].bitcast(F32), waits=[(V, vdmp)])
        pr.dma("s", rt_dump[:, 640:704], C[:, 1152:1280].bitcast(F32), waits=[(V, vdmp)])
        pr.dma("s", rt_dump[:, 768:1024], C[:, 1536:2048].bitcast(F32), waits=[(V, vdmp)])
        pr.dma("s", rt_dump[:, 1024:1280], C[:, 2048:2560].bitcast(F32), waits=[(V, vdmp)])
    for e in range(NE):
        def f(eng, e=e):
            eng.tensor_copy(rt_c1[:], rt_cw[:, e::16])
            return eng.drain()
        vcp = pr.op("v", f, waits=[(V, v24)] + ([(DS, DS.n)] if e else []), inc=V)
        pr.dma("s", cw_sp[e * NT:(e + 1) * NT].rearrange("(p b) -> p b", b=16),
               rt_c1[:], waits=[(V, vcp)])
    pr.barrier()

    if phases < 6:
        return _finish(nc, pr, es)
    wsg_v = A[:, 16384:24576]
    wsu_v = A[:, 24576:32768]
    wg_v = [A[:, 0:4096], A[:, 4096:8192]]
    wu_v = [A[:, 8192:12288], A[:, 12288:16384]]
    pr.dma("g", wsg_v.rearrange("p (f c) -> p f c", f=8),
           wsgT_d.rearrange("(f p) c -> p f c", p=128))
    pr.dma("g", wsu_v.rearrange("p (f c) -> p f c", f=8),
           wsuT_d.rearrange("(f p) c -> p f c", p=128))
    wslot_free = {}
    ps0_free = 0
    ps1_free = 0
    actb_free = 0
    awb_free = {}
    last_t_of = {}
    cw_free = {}
    aidx = 0
    for e in range(NE + 1):
        shared = (e == NE)
        sl = e % 2
        nit = 8 if shared else 4
        if not shared:
            ww = [(T, last_t_of[e - 2])] if (e - 2) in last_t_of else []
            pr.dma("g", wg_v[sl].rearrange("p (f i) -> p f i", f=8),
                   wgT_d[e].rearrange("(f p) i -> p f i", p=128), waits=ww)
            wload = pr.dma("g", wu_v[sl].rearrange("p (f i) -> p f i", f=8),
                           wuT_d[e].rearrange("(f p) i -> p f i", p=128))
        last_t = 0
        for cc in range(4):
            if not shared:
                cwl = [(GP, cw_free[cc % 2])] if (cc % 2) in cw_free else []
                cwt = pr.dma("y", cwB[cc % 2],
                             cw_sp[e * NT + cc * 512:e * NT + (cc + 1) * 512]
                             .rearrange("(o t) -> o t", o=1).partition_broadcast(128),
                             waits=cwl)
            for ii in range(nit):
                wv1 = wsg_v if shared else wg_v[sl]
                wv2 = wsu_v if shared else wu_v[sl]
                wid = 1024 if shared else 512

                rot = aidx % 2
                pg = PS0 if rot == 0 else PSW[:, 0:512]
                pu = PS1 if rot == 0 else PSW[:, 512:1024]

                def f(eng, wv1=wv1, cc=cc, ii=ii, wid=wid, pg=pg):
                    r = None
                    for fi in range(8):
                        r = eng.matmul(pg[:] if pg is PS0 else pg,
                                       wv1[:, fi * wid + ii * 128:fi * wid + ii * 128 + 128],
                                       ln2T[:, fi * 2048 + cc * 512:fi * 2048 + (cc + 1) * 512],
                                       start=(fi == 0), stop=(fi == 7))
                    return r
                wl = [] if shared else [(DG, wload)]
                if st2.get(("p5g", rot)):
                    wl.append((SS, st2[("p5g", rot)]))
                tg = pr.op("t", f, waits=wl, inc=T)

                def f(eng, wv2=wv2, cc=cc, ii=ii, wid=wid, pu=pu):
                    r = None
                    for fi in range(8):
                        r = eng.matmul(pu[:] if pu is PS1 else pu,
                                       wv2[:, fi * wid + ii * 128:fi * wid + ii * 128 + 128],
                                       ln2T[:, fi * 2048 + cc * 512:fi * 2048 + (cc + 1) * 512],
                                       start=(fi == 0), stop=(fi == 7))
                    return r
                wl = []
                if st2.get(("p5u", rot)):
                    wl.append((V, st2[("p5u", rot)]))
                tu = pr.op("t", f, waits=wl, inc=T)
                last_t = tu

                def f(eng, pg=pg):
                    eng.activation(silb, pg[:] if pg is PS0 else pg, AF.Silu)
                    return eng.drain()
                silw = [(T, tg)]
                if st2.get("p5_lastvv"):
                    silw.append((V, st2["p5_lastvv"]))
                sg2 = pr.op("s", f, waits=silw, inc=SS)
                st2[("p5g", rot)] = sg2

                asl = aidx % 2
                aw = [(T, tu), (SS, sg2)]
                if shared and asl in awb_free:
                    aw.append((DS, awb_free[asl]))
                if not shared and actb_free:
                    aw.append((GP, actb_free))

                def f(eng, shared=shared, asl=asl, pu=pu):
                    dst = actwb[asl] if shared else actb
                    eng.scalar_tensor_tensor(dst, pu[:] if pu is PS1 else pu, 1.0, silb, OP.mult, OP.mult)
                    return eng.drain()
                vv = pr.op("v", f, waits=aw, inc=V)
                st2[("p5u", rot)] = vv
                st2["p5_lastvv"] = vv

                if not shared:
                    gw2 = [(V, vv), (DY, cwt)]
                    if asl in awb_free:
                        gw2.append((DS, awb_free[asl]))

                    def f(eng, asl=asl, cc=cc):
                        eng.tensor_tensor(actwb[asl], actb, cwB[cc % 2], OP.mult)
                        return eng.drain()
                    gg = pr.op("g", f, waits=gw2, inc=GP)
                    cw_free[cc % 2] = gg
                    actb_free = gg
                    base = e * EI * NT + ii * 128 * NT
                    dd = pr.dma("s", actw_sp[base:base + 128 * NT]
                                .rearrange("(p t) -> p t", t=NT)[:, cc * 512:(cc + 1) * 512],
                                actwb[asl], waits=[(GP, gg)])
                else:
                    base = ii * 128 * NT
                    dd = pr.dma("s", gs_sp[base:base + 128 * NT]
                                .rearrange("(p t) -> p t", t=NT)[:, cc * 512:(cc + 1) * 512],
                                actwb[asl], waits=[(V, vv)])
                awb_free[asl] = dd
                aidx += 1
        if not shared:
            last_t_of[e] = last_t
    pr.barrier()

    if phases < 7:
        return _finish(nc, pr, es)
    y_ps_free = {}
    last_ty = 0
    prev_out = 0
    for half in range(2):
        wl0 = [(T, T.n)]
        for e in range(NE):
            pr.dma("g", A[:, e * 2048:(e + 1) * 2048].rearrange("p (i c) -> p i c", i=4),
                   wdT_d[e].rearrange("(i p) c -> p i c", p=128)[:, :, half * 512:half * 512 + 512],
                   waits=wl0 if e == 0 else [])
        dwl = pr.dma("g", A[:, 32768:36864].rearrange("p (i c) -> p i c", i=8),
                     wsdT_d.rearrange("(i p) c -> p i c", p=128)[:, :, half * 512:half * 512 + 512])
        for ts_ in range(16):
            for e in range(NE):
                pr.dma("y", C[:, e * 512:(e + 1) * 512].rearrange("p (i t) -> p i t", t=128),
                       actw_sp[e * EI * NT:(e + 1) * EI * NT]
                       .rearrange("(i p t) -> p i t", p=128, t=NT)[:, :, ts_ * 128:(ts_ + 1) * 128],
                       waits=[(T, last_ty)] if e == 0 else [])
            aload = pr.dma("y", C[:, 8192:9216].rearrange("p (i t) -> p i t", t=128),
                           gs_sp.rearrange("(i p t) -> p i t", p=128, t=NT)[:, :, ts_ * 128:(ts_ + 1) * 128])
            slot = ts_ % 2

            def f(eng, slot=slot):
                r = None
                first = True
                for e in range(NE):
                    for ii in range(4):
                        r = eng.matmul(PSS[slot][:], C[:, e * 512 + ii * 128:e * 512 + (ii + 1) * 128],
                                       A[:, e * 2048 + ii * 512:e * 2048 + (ii + 1) * 512],
                                       start=first, stop=False, skip_group_check=True)
                        first = False
                for jj in range(8):
                    r = eng.matmul(PSS[slot][:], C[:, 8192 + jj * 128:8192 + (jj + 1) * 128],
                                   A[:, 32768 + jj * 512:32768 + (jj + 1) * 512],
                                   start=False, stop=(jj == 7), skip_group_check=True)
                return r
            wl = [(DY, aload), (DG, dwl)]
            if (slot, "ps") in y_ps_free:
                wl.append((V, y_ps_free[(slot, "ps")]))
            ty = pr.op("t", f, waits=wl, inc=T)
            last_ty = ty

            if half == 0:
                def f(eng, ts_=ts_, slot=slot):
                    eng.tensor_copy(Bf[:, ts_ * 512:(ts_ + 1) * 512], PSS[slot][:])
                    return eng.drain()
                vv = pr.op("v", f, waits=[(T, ty)], inc=V)
                y_ps_free[(slot, "ps")] = vv
            else:
                xl = pr.dma("y", o1t[:], out1_sp[ts_ * 128:(ts_ + 1) * 128, :], waits=[(V, V.n)])
                ep_w = [(DY, xl)]
                if prev_out:
                    ep_w.append((DS, prev_out))
                s12 = ln_chain(o1t, None, 1e-5, waits=ep_w, center_into=ep1)

                def f(eng, ts_=ts_, slot=slot):
                    eng.tensor_scalar(ep1[:], ep1[:], colC[:, 0:1], None, OP.mult)
                    eng.drain()
                    eng.tensor_tensor(ep2[:, 0:512], ep1[:, 0:512], Bf[:, ts_ * 512:(ts_ + 1) * 512], OP.add)
                    eng.drain()
                    eng.tensor_tensor(ep2[:, 512:1024], ep1[:, 512:1024], PSS[slot][:], OP.add)
                    return eng.drain()
                v25 = pr.op("v", f, waits=[(SS, s12), (T, ty)], inc=V)
                y_ps_free[(slot, "ps")] = v25

                s13 = ln_chain(ep2, None, 1e-5, waits=[(V, v25)])

                def f(eng):
                    eng.tensor_scalar(ep2[:], ep2[:], colC[:, 0:1], None, OP.mult)
                    eng.drain()
                    eng.tensor_tensor(ep1[:], o1t[:], ep2[:], OP.add)
                    return eng.drain()
                v26 = pr.op("v", f, waits=[(SS, s13)], inc=V)
                prev_out = pr.dma("s", outp[ts_ * 128:(ts_ + 1) * 128, :], ep1[:], waits=[(V, v26)])

    return _finish(nc, pr, es)


def host_maps(inp):
    bf = ml_dtypes.bfloat16
    f32 = np.float32
    ac = np.ascontiguousarray
    x = np.asarray(inp["L0_attn_hidden_states"], dtype=f32)
    dist = np.asarray(inp["L0_attn_dist_emb"], dtype=f32)
    Wq = ac(np.asarray(inp["L0_attn_Wq"], f32)); Wk = ac(np.asarray(inp["L0_attn_Wk"], f32))
    Wv = ac(np.asarray(inp["L0_attn_Wv"], f32)); Wo = ac(np.asarray(inp["L0_attn_Wo"], f32))
    bq = np.asarray(inp["L0_attn_bq"], f32); bk = np.asarray(inp["L0_attn_bk"], f32)
    bv = np.asarray(inp["L0_attn_bv"], f32); bo = np.asarray(inp["L0_attn_bo"], f32)
    lng = np.asarray(inp["L0_attn_ln_gamma"], f32); lnb = np.asarray(inp["L0_attn_ln_beta"], f32)
    gw = np.asarray(inp["L0_ffn_gate_w"], f32)
    gbias = np.asarray(inp["L0_ffn_gate_bias"], f32)
    Wg = np.asarray(inp["L0_ffn_Wg"], f32)
    Wu = np.asarray(inp["L0_ffn_Wu"], f32)
    Wd = np.asarray(inp["L0_ffn_Wd"], f32)
    Wsg = np.asarray(inp["L0_ffn_Ws_g"], f32)
    Wsu = np.asarray(inp["L0_ffn_Ws_u"], f32)
    Wsd = np.asarray(inp["L0_ffn_Ws_d"], f32)

    w = np.arange(WQH)
    com = {
        "wq_d": Wq, "wk_d": Wk, "wv_d": Wv, "wo_d": Wo,
        "bqr_d": bq[None, :], "bkr_d": bk[None, :],
        "ones512_d": np.ones((1, 512), bf),
        "bv_row": bv[None, :], "bo_row": bo[None, :],
        "gam_row": lng[None, :], "bet_row": lnb[None, :],
        "tq0_d": ac(np.tile(dist[np.clip(2 * (K0 - w) + 1023, 0, 2046)].T, (2, 1))),
        "tk0_d": ac(np.tile(dist[np.clip(2 * (w - K0) + 1023, 0, 2046)].T, (2, 1))),
        "e0_d": ac(np.tile(np.tile(dist[0][:, None], (1, 512)), (2, 1))),
        "em_d": ac(np.tile(np.tile(dist[2046][:, None], (1, 512)), (2, 1))),
        "id_d": np.eye(128, dtype=bf),
        "onesf_d": np.ones((1, 128), f32),
        "onesb_d": np.ones((1, 128), bf),
        "ones64_d": np.ones((64, 1), bf),
        "gw_d": ac(gw.T),
        "gbr_d": ac(np.tile(gbias, 16)[None, :]),
        "wgT_d": ac(Wg.transpose(0, 2, 1)),
        "wuT_d": ac(Wu.transpose(0, 2, 1)),
        "wdT_d": ac(Wd.transpose(0, 2, 1)),
        "wsgT_d": ac(Wsg.T), "wsuT_d": ac(Wsu.T), "wsdT_d": ac(Wsd.T),
    }
    in_maps = []
    for core in range(8):
        b, h = core // 2, core % 2
        xp = x[b]
        if h:
            xp = xp.reshape(NT, 2, E)[:, ::-1, :].reshape(S, E)
        m = dict(com)
        m["x_b"] = ac(xp)
        m["tq1_d"] = ac(np.tile(dist[np.clip(2 * (K0 - w + h) + 1022, 0, 2046)].T, (2, 1)))
        m["tk1_d"] = ac(np.tile(dist[np.clip(2 * (w - K0 + h) + 1022, 0, 2046)].T, (2, 1)))
        in_maps.append(m)

    return in_maps


def kernel(**inp):
    in_maps = host_maps(inp)
    nc = build_program()
    res = run_bass_kernel_spmd(nc, in_maps, core_ids=list(range(8)))
    out = np.empty((4, S, E), np.float32)
    for core in range(8):
        b, h = core // 2, core % 2
        out[b, h::2] = np.asarray(res.results[core]["outp"], np.float32)
    return out


# version 20
# speedup vs baseline: 1.1708x; 1.0060x over previous
import contextlib
import numpy as np
import ml_dtypes
import concourse.bass as bass
import concourse.mybir as mybir
from concourse.bass_utils import run_bass_kernel_spmd
from concourse.alu_op_type import AluOpType as OP

F32 = mybir.dt.float32
BF16 = mybir.dt.bfloat16
I32 = mybir.dt.int32
AF = mybir.ActivationFunctionType
AX = mybir.AxisListType.X

S, E, NH = 4096, 1024, 16
NT = 2048
K0 = 644
WQH = WKH = 1288
NE, EI, SHI = 16, 512, 1024
SLACK = 16 * WQH


class Sem:
    def __init__(self, nc, es, name):
        self.h = es.enter_context(nc.semaphore(name))
        self.n = 0


class Prog:
    def __init__(self, nc, es):
        self.nc = nc
        self.ops = {k: [] for k in "ygtvs"}
        self.sems = {}
        for nm in ("DY", "DG", "DS", "T", "V", "SS", "GP", "BR"):
            self.sems[nm] = Sem(nc, es, nm)
        self.nbar = 0

    def dma(self, q, out_ap, in_ap, waits=()):
        sem = self.sems[{"y": "DY", "g": "DG", "s": "DS"}[q]]
        sem.n += 16
        tgt = sem.n
        w = tuple(waits)

        def f(eng, o=out_ap, i=in_ap, w=w, sh=sem.h):
            for (s, v) in w:
                eng.wait_ge(s.h, v)
            eng.dma_start(out=o, in_=i).then_inc(sh, 16)
        self.ops[q].append(f)
        return tgt

    def op(self, q, fn, waits=(), inc=None):
        w = tuple(waits)
        tgt = None
        if inc is not None:
            inc.n += 1
            tgt = inc.n

        def f(eng, fn=fn, w=w, inc=inc):
            for (s, v) in w:
                eng.wait_ge(s.h, v)
            r = fn(eng)
            if inc is not None:
                if r is None:
                    r = eng.nop()
                r.then_inc(inc.h, 1)
        self.ops[q].append(f)
        return tgt

    def barrier(self):
        br = self.sems["BR"]
        self.nbar += 1
        tgt = self.nbar * 5
        br.n += 5
        dj = {"y": (self.sems["DY"].h, self.sems["DY"].n),
              "g": (self.sems["DG"].h, self.sems["DG"].n),
              "s": (self.sems["DS"].h, self.sems["DS"].n)}
        for q in "ygtvs":
            dw = dj.get(q)

            def f(eng, t=tgt, h=br.h, dw=dw):
                if dw is not None:
                    eng.wait_ge(dw[0], dw[1])
                eng.nop().then_inc(h, 1)
                eng.wait_ge(h, t)
            self.ops[q].append(f)


def classify(T0, m):
    d2 = 2 * T0 - 256 * m
    if abs(d2) <= 1024:
        return "g"
    return "p" if d2 >= 1280 else "f"




def _finish(nc, pr, es):
    DS = pr.sems["DS"]
    fin = DS.n

    def f(eng, fin=fin):
        eng.wait_ge(DS.h, fin)
        return None
    pr.op("s", f)
    pr.barrier()
    with nc.Block() as block:
        @block.sync
        def _(eng):
            for f2 in pr.ops["y"]:
                f2(eng)

        @block.gpsimd
        def _(eng):
            for f2 in pr.ops["g"]:
                f2(eng)

        @block.tensor
        def _(eng):
            for f2 in pr.ops["t"]:
                f2(eng)

        @block.vector
        def _(eng):
            for f2 in pr.ops["v"]:
                f2(eng)

        @block.scalar
        def _(eng):
            for f2 in pr.ops["s"]:
                f2(eng)
    es.close()
    return nc


def build_program(phases=9, dbg=False, nheads=NH, p2mode=5):
    nc = bass.Bass("TRN2", target_bir_lowering=False)
    es = contextlib.ExitStack()
    pr = Prog(nc, es)
    DY, DG, DS, T, V, SS, GP = (pr.sems[k] for k in ("DY", "DG", "DS", "T", "V", "SS", "GP"))

    def dt(name, shape, dtp, kind="ExternalInput"):
        return nc.dram_tensor(name, shape, dtp, kind=kind).ap()

    x_b = dt("x_b", [S, E], F32)
    wq_d = dt("wq_d", [E, E], F32); wk_d = dt("wk_d", [E, E], F32)
    wv_d = dt("wv_d", [E, E], F32); wo_d = dt("wo_d", [E, E], F32)
    bqr_d = dt("bqr_d", [1, E], F32); bkr_d = dt("bkr_d", [1, E], F32)
    ones512_d = dt("ones512_d", [1, 512], BF16)
    bv_row = dt("bv_row", [1, E], F32); bo_row = dt("bo_row", [1, E], F32)
    gam_row = dt("gam_row", [1, E], F32); bet_row = dt("bet_row", [1, E], F32)
    tq0_d = dt("tq0_d", [128, WQH], F32); tq1_d = dt("tq1_d", [128, WQH], F32)
    tk0_d = dt("tk0_d", [128, WKH], F32); tk1_d = dt("tk1_d", [128, WKH], F32)
    e0_d = dt("e0_d", [128, 512], F32); em_d = dt("em_d", [128, 512], F32)
    id_d = dt("id_d", [128, 128], BF16)
    onesf_d = dt("onesf_d", [1, 128], F32)
    onesb_d = dt("onesb_d", [1, 128], BF16)
    ones64_d = dt("ones64_d", [64, 1], BF16)
    gw_d = dt("gw_d", [E, NE], F32)
    gbr_d = dt("gbr_d", [1, 256], F32)
    wgT_d = dt("wgT_d", [NE, E, EI], F32)
    wuT_d = dt("wuT_d", [NE, E, EI], F32)
    wdT_d = dt("wdT_d", [NE, EI, E], F32)
    wsgT_d = dt("wsgT_d", [E, SHI], F32)
    wsuT_d = dt("wsuT_d", [E, SHI], F32)
    wsdT_d = dt("wsdT_d", [SHI, E], F32)
    outp = dt("outp", [NT, E], F32, kind="ExternalOutput")
    IN = "Internal"
    DBG = "ExternalOutput" if dbg else IN
    kt_sp = dt("kt_sp", [E, S], BF16, kind=DBG)
    qt_sp = dt("qt_sp", [E, NT], BF16, kind=DBG)
    v_sp = dt("v_sp", [S, E], BF16, kind=DBG)
    qdh_sp = dt("qdh_sp", [32 * NT * WQH + 4 * SLACK], BF16, kind=DBG)
    kdh_sp = dt("kdh_sp", [32 * NT * WKH + 4 * SLACK], BF16, kind=DBG)
    out1_sp = dt("out1_sp", [NT, E], F32, kind=DBG)
    actw_sp = dt("actw_sp", [NE * EI * NT], BF16, kind=DBG)
    gs_sp = dt("gs_sp", [SHI * NT], BF16, kind=DBG)
    cw_sp = dt("cw_sp", [NE * NT], BF16, kind=DBG)
    rt_dump = dt("rt_dump", [128, 1536], F32, kind=DBG) if dbg else None

    def sb(name, shape, dtp):
        return es.enter_context(nc.sbuf_tensor(name, shape, dtp))

    A = sb("A", [128, 36864], BF16)
    Bf = sb("Bf", [128, 8192], F32)
    C = sb("C", [128, 16384], BF16)
    fscr = sb("fscr", [128, 1024], F32)
    o1t = C[:, 9216:11264].bitcast(F32)
    ep1 = C[:, 11264:13312].bitcast(F32)
    ep2 = C[:, 13312:15360].bitcast(F32)
    bvb = sb("bvb", [128, E], BF16)
    bob = C[:, 0:2048].bitcast(F32)
    gamb = C[:, 2048:4096].bitcast(F32)
    betb = C[:, 4096:6144].bitcast(F32)
    sbqr = sb("sbqr", [1, E], BF16)
    sbkr = sb("sbkr", [1, E], BF16)
    sbvr = sb("sbvr", [1, E], BF16)
    ones512 = sb("ones512", [1, 512], BF16)
    dq = [sb(f"dq{r}", [128, WQH], BF16) for r in (0, 1)]
    dk = [sb(f"dk{r}", [128, WKH], BF16) for r in (0, 1)]
    se0 = sb("se0", [128, 512], BF16)
    sem_ = sb("sem_", [128, 512], BF16)
    ident = sb("ident", [128, 128], BF16)
    sonesf = sb("sonesf", [1, 128], F32)
    sonesb = sb("sonesb", [1, 128], BF16)
    sones64 = sb("sones64", [64, 1], BF16)
    sgw = sb("sgw", [128, 8 * NE], BF16)
    sgbr = sb("sgbr", [128, 256], F32)
    mxt = sb("mxt", [1, 16], F32)
    chs = sb("chs", [1, 8], F32)
    negC = sb("negC", [1, 1], F32)
    negCb = sb("negCb", [128, 1], F32)
    negctab = sb("negctab", [128, 16], F32)
    mxe = sb("mxe", [1, 1], F32)
    rcp = A[:, 34560:35584].bitcast(F32)[0:1, :]
    rcpb = A[:, 35584:36096][0:1, :]
    rcpB = A[64:128, 34560:35072]
    ctxT = sb("ctxT", [128, 8 * NT], BF16)
    colA = sb("colA", [128, 2], F32)
    colB = sb("colB", [128, 2], F32)
    colC = sb("colC", [128, 2], F32)
    rt_sc = A[:, 2048:2560].bitcast(F32)
    rt_sfc = A[:, 2560:3072].bitcast(F32)
    rt_msk = A[:, 3072:3584].bitcast(F32)
    rt_rank = A[:, 3584:4096].bitcast(F32)
    rt_eqr = A[:, 4096:4608].bitcast(F32)
    rt_sel = A[:, 4608:5120].bitcast(F32)
    rt_gs = sb("rt_gs", [128, 64], F32)
    rt_t = [sb(f"rt_t{i}", [128, 64], F32) for i in range(6)]
    rt_gm = sb("rt_gm", [128, 64], F32)
    rt_em = A[:, 5120:5632].bitcast(I32)
    rt_neg = A[:, 5632:6144].bitcast(F32)
    rt_w = A[:, 6144:6656].bitcast(F32)
    rt_ws = sb("rt_ws", [128, 16], F32)
    rt_wsi = sb("rt_wsi", [128, 16], F32)
    rt_cw = sb("rt_cw", [128, 256], BF16)
    rt_c1 = sb("rt_c1", [128, 16], BF16)
    logits = A[:, 6656:7168].bitcast(F32)
    cwB = [A[:, 32768 + i * 512:32768 + (i + 1) * 512] for i in (0, 1)]
    silb = A[:, 33792:34304]
    actb = A[:, 34304:34816]
    actwb = [A[:, 34816 + i * 512:34816 + (i + 1) * 512] for i in (0, 1)]

    def psal(name, shape):
        return es.enter_context(nc.psum_tensor(name, shape, F32))

    PS0 = psal("PS0", [128, 512])
    PS1 = psal("PS1", [128, 512])
    PSW = psal("PSW", [128, 1024])
    PSC = psal("PSC", [128, 2048])
    PSS = [PS0, PS1]
    PSS4 = [PS0, PS1, PSW[:, 0:512], PSW[:, 512:1024]]

    def ln_chain(xt, outbuf, eps, out_bf=None, waits=(), center_into=None, par=0):
        tgt = center_into if center_into is not None else xt
        cA = colA[:, par:par + 1]
        cB = colB[:, par:par + 1]
        cC = colC[:, par:par + 1]

        def f(eng, xt=xt, tgt=tgt, cA=cA):
            eng.tensor_reduce(cA, xt, axis=AX, op=OP.add)
            eng.drain()
            eng.tensor_scalar(cA, cA, 1.0 / E, None, OP.mult)
            eng.drain()
            eng.tensor_scalar(tgt, xt, cA, None, OP.subtract)
            return eng.drain()
        v1 = pr.op("v", f, waits=list(waits), inc=V)

        def f(eng, tgt=tgt, cB=cB):
            eng.activation(fscr[:], tgt, AF.Square, accum_out=cB)
            return eng.drain()
        s1 = pr.op("s", f, waits=[(V, v1)], inc=SS)

        def f(eng, eps=eps, cB=cB):
            eng.tensor_scalar(cB, cB, 1.0 / E, eps, OP.mult, OP.add)
            return eng.drain()
        v2 = pr.op("v", f, waits=[(SS, s1)], inc=V)

        def f(eng, cB=cB, cC=cC):
            eng.activation(cC, cB, AF.Ln)
            eng.drain()
            eng.activation(cC, cC, AF.Exp, scale=-0.5)
            return eng.drain()
        s2 = pr.op("s", f, waits=[(V, v2)], inc=SS)
        return s2

    pr.dma("g", bvb[:], bv_row.partition_broadcast(128))
    pr.dma("g", sbqr[:], bqr_d)
    pr.dma("g", sbkr[:], bkr_d)
    pr.dma("g", sbvr[:], bv_row)
    pr.dma("y", ones512[:], ones512_d)
    for r in (0, 1):
        pr.dma("g", dq[r][:], (tq0_d, tq1_d)[r])
        pr.dma("g", dk[r][:], (tk0_d, tk1_d)[r])
    pr.dma("g", se0[:], e0_d)
    pr.dma("g", sem_[:], em_d)
    pr.dma("y", ident[:], id_d)
    pr.dma("y", sonesf[:], onesf_d)
    pr.dma("y", sonesb[:], onesb_d)
    pr.dma("y", sones64[:], ones64_d)
    pr.dma("g", sgw[:].rearrange("p (f e) -> p f e", e=NE),
           gw_d.rearrange("(f p) e -> p f e", p=128))
    pr.dma("y", sgbr[:], gbr_d.partition_broadcast(128))
    wq_v = A[:, 0:8192]; wk_v = A[:, 8192:16384]; wv_v = A[:, 16384:24576]
    for (w_in, w_sb) in ((wq_d, wq_v), (wk_d, wk_v), (wv_d, wv_v)):
        pr.dma("g", w_sb.rearrange("p (f c) -> p f c", f=8),
               w_in.rearrange("(f p) c -> p f c", p=128))
    dg_p0 = DG.n
    dy_p0 = DY.n
    pr.barrier()

    sqv = [A[0:64, 24576 + i * WKH:24576 + (i + 1) * WKH] for i in (0, 1)]

    def f(eng):
        eng.activation(sqv[0], dk[0][0:64, :], AF.Square)
        eng.drain()
        eng.activation(sqv[1], dk[1][0:64, :], AF.Square)
        return eng.drain()
    sqd = pr.op("s", f, waits=[(DG, dg_p0)], inc=SS)
    nmm = 0
    for r in (0, 1):
        for cc in range(3):
            w0 = cc * 512
            wid = min(512, WKH - w0)

            def f(eng, r=r, w0=w0, wid=wid):
                return eng.matmul(PS0[0:1, 0:wid], sones64[:], sqv[r][:, w0:w0 + wid],
                                  start=True, stop=True)
            tv = pr.op("t", f, waits=[(SS, sqd)] + ([(V, V.n)] if nmm else []), inc=T)

            def f(eng, i=nmm, wid=wid):
                eng.reduce_max(mxt[:, i:i + 1], PS0[0:1, 0:wid], axis=AX)
                return eng.drain()
            pr.op("v", f, waits=[(T, tv)], inc=V)
            nmm += 1

    def f(eng):
        eng.reduce_max(mxe[:], mxt[:, 0:6], axis=AX)
        return eng.drain()
    pr.op("v", f, inc=V)
    pr.barrier()

    if phases < 1:
        return _finish(nc, pr, es)
    ln1T = [A[:, 24576 + k * 4096:24576 + (k + 1) * 4096] for k in (0, 1)]
    lnrow = [A[:, 32768 + k * 1024:32768 + (k + 1) * 1024] for k in range(2)]
    stg = [A[:, 34816 + k * 512:34816 + (k + 1) * 512] for k in range(4)]
    xfree = {}
    psw_free = 0
    lnrow_free = {}
    pss_free = [0, 0]
    stg_free = {}
    kq_iter = 0
    for c in range(8):
        cb = c % 2
        ln_done = []
        for i in range(4):
            xt = Bf[:, (cb * 4 + i) * 1024:(cb * 4 + i + 1) * 1024]
            lw = [(V, xfree[(c - 2, i)])] if (c - 2, i) in xfree else []
            xl = pr.dma("y", xt, x_b[c * 512 + i * 128:c * 512 + (i + 1) * 128, :], waits=lw)
            lrs = lnrow[i % 2]
            lr_w = [(T, lnrow_free[i % 2])] if i % 2 in lnrow_free else []
            s2 = ln_chain(xt, None, 1e-5, waits=[(DY, xl)] + lr_w, par=i % 2)

            def f(eng, xt=xt, lrs=lrs, i=i):
                eng.tensor_scalar(lrs, xt, colC[:, i % 2:i % 2 + 1], None, OP.mult)
                return eng.drain()
            v3 = pr.op("v", f, waits=[(SS, s2)], inc=V)
            xfree[(c, i)] = v3

            def f(eng, lrs=lrs):
                r = None
                for j in range(8):
                    r = eng.matmul(PSW[:, j * 128:(j + 1) * 128], lrs[:, j * 128:(j + 1) * 128],
                                   ident[:], start=True, stop=True)
                return r
            t1 = pr.op("t", f, waits=[(V, v3), (DY, dy_p0)] + ([(V, psw_free)] if psw_free else []), inc=T)
            lnrow_free[i % 2] = t1

            def f(eng, cb=cb, i=i):
                src = PSW[:].rearrange("p (j t) -> p j t", t=128)
                dst = ln1T[cb].rearrange("p (j t) -> p j t", t=512)[:, :, i * 128:(i + 1) * 128]
                eng.tensor_copy(dst, src)
                return eng.drain()
            v4 = pr.op("v", f, waits=[(T, t1)], inc=V)
            psw_free = v4
            ln_done.append(v4)

        for proj in ("k", "q"):
            wmat = wk_v if proj == "k" else wq_v
            for fo in range(8):
                slot = kq_iter % 2
                kq_iter += 1
                pw = [(SS, pss_free[slot][1])] if pss_free[slot] else []

                bvec = sbkr if proj == "k" else sbqr

                def f(eng, wmat=wmat, fo=fo, cb=cb, slot=slot, bvec=bvec):
                    eng.matmul(PSS[slot][:], bvec[0:1, fo * 128:(fo + 1) * 128], ones512[:], start=True, stop=False,
                               skip_group_check=True)
                    r = None
                    for fi in range(8):
                        r = eng.matmul(PSS[slot][:],
                                       wmat[:, fi * 1024 + fo * 128:fi * 1024 + (fo + 1) * 128],
                                       ln1T[cb][:, fi * 512:(fi + 1) * 512],
                                       start=False, stop=(fi == 7), skip_group_check=True)
                    return r
                t2 = pr.op("t", f, waits=[(V, ln_done[3]), (DG, dg_p0)] + pw, inc=T)
                if proj == "k":
                    kw = [(DS, stg_free["k"])] if "k" in stg_free else []

                    def f(eng, fo=fo, slot=slot):
                        eng.copy(stg[0][:, 0:256], PSS[slot][:, 0::2])
                        return eng.copy(stg[1][:, 0:256], PSS[slot][:, 1::2])
                    v5 = pr.op("s", f, waits=[(T, t2)] + kw, inc=SS)
                    pr.dma("s", kt_sp[fo * 128:(fo + 1) * 128, 256 * c:256 * c + 256], stg[0][:, 0:256], waits=[(SS, v5)])
                    d2 = pr.dma("s", kt_sp[fo * 128:(fo + 1) * 128, 2048 + 256 * c:2048 + 256 * c + 256],
                                stg[1][:, 0:256], waits=[(SS, v5)])
                    stg_free["k"] = d2
                    pss_free[slot] = ("s", v5)
                else:
                    qw = [(DS, stg_free["q"])] if "q" in stg_free else []

                    def f(eng, fo=fo, slot=slot):
                        return eng.copy(stg[2][:, 0:256], PSS[slot][:, 0::2])
                    v5 = pr.op("s", f, waits=[(T, t2)] + qw, inc=SS)
                    d1 = pr.dma("s", qt_sp[fo * 128:(fo + 1) * 128, 256 * c:256 * c + 256], stg[2][:, 0:256], waits=[(SS, v5)])
                    stg_free["q"] = d1
                    pss_free[slot] = ("s", v5)
        for i in range(4):
            for half in range(2):
                slot = kq_iter % 2
                kq_iter += 1
                pw = [(SS, pss_free[slot][1])] if pss_free[slot] else []

                def f(eng, cb=cb, i=i, half=half, slot=slot):
                    eng.matmul(PSS[slot][:], sonesb[:], sbvr[0:1, half * 512:half * 512 + 512],
                               start=True, stop=False, skip_group_check=True)
                    r = None
                    for fi in range(8):
                        r = eng.matmul(PSS[slot][:],
                                       ln1T[cb][:, fi * 512 + i * 128:fi * 512 + (i + 1) * 128],
                                       wv_v[:, fi * 1024 + half * 512:fi * 1024 + half * 512 + 512],
                                       start=False, stop=(fi == 7), skip_group_check=True)
                    return r
                t3 = pr.op("t", f, waits=[(V, ln_done[i])] + pw, inc=T)
                vw = [(DS, stg_free["v"])] if "v" in stg_free else []

                def f(eng, half=half, slot=slot):
                    return eng.copy(stg[3], PSS[slot][:])
                v6 = pr.op("s", f, waits=[(T, t3)] + vw, inc=SS)
                d1 = pr.dma("s", v_sp[c * 512 + i * 128:c * 512 + (i + 1) * 128, half * 512:half * 512 + 512],
                            stg[3], waits=[(SS, v6)])
                stg_free["v"] = d1
                pss_free[slot] = ("s", v6)
    pr.barrier()

    if phases < 2:
        return _finish(nc, pr, es)
    kt_v = [A[:, 0:4096], A[:, 4096:8192]]
    qt_v = [A[:, 8192:10240], A[:, 10240:12288]]
    vh_v = [A[:, 12288:14368], A[:, 14368:16448]]
    gq = [A[:, 16448 + k * 512:16448 + (k + 1) * 512] for k in range(4)]
    gk = [A[:, 18496 + k * 512:18496 + (k + 1) * 512] for k in range(4)]
    probs = [A[:, 20544 + k * 2048:20544 + (k + 1) * 2048] for k in range(2)]
    sqq = A[:, 24640:26688]
    stgq = [A[:, 26688 + k * 2624:26688 + (k + 1) * 2624] for k in range(3)]
    stage = [Bf[:, k * 2048:(k + 1) * 2048] for k in range(2)]
    tiles = [(m, rho) for m in range(16) for rho in (0, 1)]

    st2 = {"it": 0, "cons": {}, "gq_free": {}, "gk_free": {}, "probs_free": {},
           "stage_free": {}, "stg_free": {}, "psw_ev": {}, "pswc": 0,
           "slot_T_done": {}, "slot_V_done": {}, "qdkd_done": {}, "psfree0": 0,
           "ctx_V_done": 0}

    def emit_loads(hd, vh=True):
        slot = (hd // 2) % 2
        if hd % 2 == 0:
            w = []
            if hd >= 4:
                w = [(pr.sems["T"], st2["slot_T_done"].get((hd // 2 - 2) % 2, 0)),
                     (pr.sems["V"], st2["slot_V_done"].get((hd // 2 - 2) % 2, 0)),
                     (SS, st2.get("slot_S_done", {}).get((hd // 2 - 2) % 2, 0))]
                w = [x for x in w if x[1]]
            l_kt = pr.dma("y", kt_v[slot][:, :], kt_sp[64 * hd:64 * hd + 128, :], waits=w)
            l_qt = pr.dma("y", qt_v[slot][:, :], qt_sp[64 * hd:64 * hd + 128, :])
            st2.setdefault("plk", {})[slot] = (l_kt, l_qt)
        else:
            l_kt, l_qt = st2["plk"][slot]
        l_vh = DY.n
        vslot = hd % 2
        if not vh:
            return l_kt, l_qt, DY.n
        vw = []
        if hd >= 2:
            vw = [(pr.sems["T"], st2.get("vh_T_done", {}).get(vslot, 0))]
            vw = [x for x in vw if x[1]]
        for (m, rho) in tiles:
            blk = 2 * m + rho
            l_vh = pr.dma("y", vh_v[vslot][:, blk * 65:blk * 65 + 64],
                          v_sp[256 * m + rho:256 * m + rho + 255:2, 64 * hd:64 * hd + 64],
                          waits=vw if (m, rho) == (0, 0) else [])
        if hd < 2:
            def f(eng, vslot=vslot):
                eng.memset(vh_v[vslot].rearrange("p (b c) -> p b c", c=65)[:, :, 64:65], 1.0)
                return eng.drain()
            pr.op("v", f, waits=[(DY, l_vh)], inc=V)
        return l_kt, l_qt, l_vh

    def gen_qdkd(hd, l_kt, l_qt):
        pslt = hd // 2
        kslot = pslt % 2
        sidx = 0
        for kind in ("q", "k"):
            WW = WQH if kind == "q" else WKH
            dtab = dq if kind == "q" else dk
            spd = qdh_sp if kind == "q" else kdh_sp
            statv = qt_v[kslot] if kind == "q" else kt_v[kslot]
            lwait = l_qt if kind == "q" else l_kt
            for rho in (0, 1):
                for tt in range(16):
                    coff = (rho * 2048 if kind == "k" else 0) + tt * 128
                    stgX = stgq[sidx % 3]
                    ev_last = [0, 0]
                    for ci, (c0, c1) in enumerate(((0, 512), (512, 1024), (1024, WW))):
                        bp = st2["pswc"] % 4
                        st2["pswc"] += 1
                        ev_sem = V if bp % 2 == 0 else SS
                        ev_q = "v" if bp % 2 == 0 else "s"
                        wl = [(DY, lwait), (DG, dg_p0)]
                        if bp in st2["psw_ev"]:
                            wl.append(st2["psw_ev"][bp])

                        def f(eng, statv=statv, coff=coff, dtab=dtab, rho=rho, c0=c0, c1=c1, bp=bp):
                            pa = (PSW[:, 0:c1 - c0], PS0[:, 0:c1 - c0],
                                  PSC[:, 0:c1 - c0], PSC[:, 1024:1024 + c1 - c0])[bp]
                            pb = (PSW[:, 512:512 + c1 - c0], PS1[:, 0:c1 - c0],
                                  PSC[:, 512:512 + c1 - c0], PSC[:, 1536:1536 + c1 - c0])[bp]
                            eng.matmul(pa, statv[0:64, coff:coff + 128],
                                       dtab[rho][0:64, c0:c1], start=True, stop=True,
                                       tile_position=(0, 0))
                            return eng.matmul(pb, statv[64:128, coff:coff + 128],
                                              dtab[rho][64:128, c0:c1], start=True, stop=True,
                                              tile_position=(64, 0))
                        t4 = pr.op("t", f, waits=wl, inc=T)

                        def f(eng, stgX=stgX, c0=c0, c1=c1, WW=WW, bp=bp):
                            cp = eng.tensor_copy if hasattr(eng, "tensor_copy") else eng.copy
                            dst = stgX[:, 0:2 * WW].rearrange("p (h w) -> p h w", h=2)[:, :, c0:c1]
                            if bp == 0:
                                srcv = PSW[:].rearrange("p (h w) -> p h w", h=2)[:, :, 0:c1 - c0]
                                return cp(dst, srcv)
                            if bp == 2:
                                srcv = PSC[:, 0:1024].rearrange("p (h w) -> p h w", h=2)[:, :, 0:c1 - c0]
                                return cp(dst, srcv)
                            if bp == 3:
                                srcv = PSC[:, 1024:2048].rearrange("p (h w) -> p h w", h=2)[:, :, 0:c1 - c0]
                                return cp(dst, srcv)
                            cp(stgX[:, c0:c1], PS0[:, 0:c1 - c0])
                            return cp(stgX[:, WW + c0:WW + c1], PS1[:, 0:c1 - c0])
                        prev = st2["stg_free"].get(sidx % 3)
                        pw_ = []
                        if prev and ci == 0:
                            pq, pv = prev
                            pw_ = [({"s": DS, "y": DY, "g": DG}[pq], pv)]
                        ev = pr.op(ev_q, f, waits=[(T, t4)] + pw_, inc=ev_sem)
                        st2["psw_ev"][bp] = (ev_sem, ev)
                        ev_last[0 if bp % 2 == 0 else 1] = ev
                    base = pslt * 4 * NT * WW + (rho * NT + tt * 128) * (2 * WW)
                    qq = ("s", "y", "g")[sidx % 3]
                    dw = []
                    if ev_last[0]:
                        dw.append((V, ev_last[0]))
                    if ev_last[1]:
                        dw.append((SS, ev_last[1]))
                    d = pr.dma(qq, spd[base:base + 128 * 2 * WW].rearrange("(p w) -> p w", w=2 * WW),
                               stgX[:, 0:2 * WW], waits=dw)
                    st2["stg_free"][sidx % 3] = (qq, d)
                    sidx += 1
                    yield
        st2["qdkd_done"][pslt] = (DS.n, DY.n, DG.n)
        st2["slot_T_done"][kslot] = T.n
        st2["slot_V_done"][kslot] = V.n

    def emit_chead2(hd, l_kt, l_qt):
        slot = (hd // 2) % 2
        hh = 64 * (hd % 2)
        nmx = 0
        sqd = 0
        for src in range(3):
            if src == 0:
                def fq(eng, slot=slot, hh=hh):
                    eng.activation(sqq[0:64, 0:2048], qt_v[slot][hh:hh + 64, :], AF.Square)
                    return eng.drain()
                sqd = pr.op("s", fq, waits=[(DY, l_qt), (V, V.n)], inc=SS)
            else:
                def fk(eng, slot=slot, hh=hh, half2=src - 1):
                    eng.activation(sqq[0:64, 0:2048],
                                   kt_v[slot][hh:hh + 64, half2 * 2048:half2 * 2048 + 2048], AF.Square)
                    return eng.drain()
                sqd = pr.op("s", fk, waits=[(DY, l_kt), (V, V.n)], inc=SS)
            for cc in range(4):
                bp = st2["pswc"] % 4
                st2["pswc"] += 1
                wl = [(SS, sqd)]
                if bp in st2["psw_ev"]:
                    wl.append(st2["psw_ev"][bp])

                def f(eng, cc=cc, bp=bp):
                    pa = (PSW, PS0, PSC, PSC[:, 1024:2048])[bp]
                    return eng.matmul(pa[0:1, 0:512], sones64[:], sqq[0:64, cc * 512:(cc + 1) * 512],
                                      start=True, stop=True)
                tv = pr.op("t", f, waits=wl, inc=T)

                def f(eng, i=nmx, bp=bp):
                    pa = (PSW, PS0, PSC, PSC[:, 1024:2048])[bp]
                    eng.reduce_max(mxt[:, i:i + 1], pa[0:1, 0:512], axis=AX)
                    return eng.drain()
                rm = pr.op("v", f, waits=[(T, tv)], inc=V)
                st2["psw_ev"][bp] = (V, rm)
                nmx += 1

        def f(eng):
            eng.reduce_max(chs[:, 0:1], mxt[:, 0:4], axis=AX)
            eng.drain()
            eng.reduce_max(chs[:, 1:2], mxt[:, 4:12], axis=AX)
            eng.drain()
            eng.tensor_tensor(chs[:, 2:3], chs[:, 0:1], chs[:, 1:2], OP.mult)
            eng.drain()
            eng.tensor_tensor(chs[:, 3:4], chs[:, 0:1], mxe[:], OP.mult)
            eng.drain()
            eng.tensor_tensor(chs[:, 4:5], chs[:, 1:2], mxe[:], OP.mult)
            return eng.drain()
        vch = pr.op("v", f, inc=V)

        def f(eng):
            eng.activation(chs[:, 2:5], chs[:, 2:5], AF.Ln)
            eng.drain()
            eng.activation(chs[:, 2:5], chs[:, 2:5], AF.Exp, scale=0.5)
            return eng.drain()
        sch = pr.op("s", f, waits=[(V, vch)], inc=SS)

        def f(eng):
            eng.tensor_reduce(chs[:, 5:6], chs[:, 2:5], axis=AX, op=OP.add)
            eng.drain()
            eng.tensor_scalar(negC[:], chs[:, 5:6], -0.125, None, OP.mult)
            return eng.drain()
        vch2 = pr.op("v", f, waits=[(SS, sch), (T, T.n)], inc=V)

        bp = st2["pswc"] % 4
        st2["pswc"] += 1
        wl = [(V, vch2)]
        if bp in st2["psw_ev"]:
            wl.append(st2["psw_ev"][bp])

        def f(eng, bp=bp):
            pa = (PSW, PS0, PSC, PSC[:, 1024:2048])[bp]
            return eng.matmul(pa[:, 0:1], sonesf[:], negC[:], start=True, stop=True)
        tch = pr.op("t", f, waits=wl, inc=T)

        def f(eng, hd=hd, bp=bp):
            pa = (PSW, PS0, PSC, PSC[:, 1024:2048])[bp]
            eng.tensor_copy(negctab[:, hd:hd + 1], pa[:, 0:1])
            return eng.drain()
        cpv = pr.op("v", f, waits=[(T, tch)], inc=V)
        st2["psw_ev"][bp] = (V, cpv)

    def emit_chead(hd, l_kt, l_qt, l_vh, feed=None):
        slot = (hd // 2) % 2
        hh = 64 * (hd % 2)

        def fq(eng, slot=slot, hh=hh):
            eng.activation(sqq[0:64, 0:2048], qt_v[slot][hh:hh + 64, :], AF.Square)
            return eng.drain()
        sqd1 = pr.op("s", fq, waits=[(DY, l_qt), (SS, SS.n)], inc=SS)
        nmx = 0
        w0 = []
        if st2["psfree0"]:
            w0 = [(V, st2["psfree0"]), (SS, st2["cons_ss"])] if st2.get("cons_ss") else [(V, st2["psfree0"])]
        vhist = {}
        for cc in range(4):
            pb = PSS[cc % 2]

            def f(eng, cc=cc, pb=pb):
                return eng.matmul(pb[0:1, 0:512], sones64[:], sqq[0:64, cc * 512:(cc + 1) * 512],
                                  start=True, stop=True)
            tv = pr.op("t", f, waits=[(SS, sqd1)] + ([(V, vhist[cc % 2])] if cc % 2 in vhist else [(V, V.n)]) + (w0 if cc < 2 else []), inc=T)

            def f(eng, i=nmx, pb=pb):
                eng.reduce_max(mxt[:, i:i + 1], pb[0:1, 0:512], axis=AX)
                return eng.drain()
            vhist[cc % 2] = pr.op("v", f, waits=[(T, tv)], inc=V)
            nmx += 1

        for half2 in range(2):
            def fk(eng, slot=slot, half2=half2, hh=hh):
                eng.activation(sqq[0:64, 0:2048], kt_v[slot][hh:hh + 64, half2 * 2048:half2 * 2048 + 2048], AF.Square)
                return eng.drain()
            sqd2 = pr.op("s", fk, waits=[(V, V.n), (DY, l_kt)], inc=SS)
            for cc in range(4):
                pb = PSS[cc % 2]

                def f(eng, cc=cc, pb=pb):
                    return eng.matmul(pb[0:1, 0:512], sones64[:], sqq[0:64, cc * 512:(cc + 1) * 512],
                                      start=True, stop=True)
                tv = pr.op("t", f, waits=[(SS, sqd2)] + ([(V, vhist[cc % 2])] if cc % 2 in vhist else [(V, V.n)]), inc=T)

                def f(eng, i=nmx, pb=pb):
                    eng.reduce_max(mxt[:, i:i + 1], pb[0:1, 0:512], axis=AX)
                    return eng.drain()
                vhist[cc % 2] = pr.op("v", f, waits=[(T, tv)], inc=V)
                nmx += 1

        def f(eng):
            eng.reduce_max(chs[:, 0:1], mxt[:, 0:4], axis=AX)
            eng.drain()
            eng.reduce_max(chs[:, 1:2], mxt[:, 4:12], axis=AX)
            eng.drain()
            eng.tensor_tensor(chs[:, 2:3], chs[:, 0:1], chs[:, 1:2], OP.mult)
            eng.drain()
            eng.tensor_tensor(chs[:, 3:4], chs[:, 0:1], mxe[:], OP.mult)
            eng.drain()
            eng.tensor_tensor(chs[:, 4:5], chs[:, 1:2], mxe[:], OP.mult)
            return eng.drain()
        vch = pr.op("v", f, inc=V)

        def f(eng):
            eng.activation(chs[:, 2:5], chs[:, 2:5], AF.Ln)
            eng.drain()
            eng.activation(chs[:, 2:5], chs[:, 2:5], AF.Exp, scale=0.5)
            return eng.drain()
        sch = pr.op("s", f, waits=[(V, vch)], inc=SS)

        def f(eng):
            eng.tensor_reduce(chs[:, 5:6], chs[:, 2:5], axis=AX, op=OP.add)
            eng.drain()
            eng.tensor_scalar(negC[:], chs[:, 5:6], -0.125, None, OP.mult)
            return eng.drain()
        vch2 = pr.op("v", f, waits=[(SS, sch)], inc=V)

        def f(eng):
            return eng.matmul(PS0[:, 0:1], sonesf[:], negC[:], start=True, stop=True)
        tch = pr.op("t", f, waits=[(V, vch2)], inc=T)

        def f(eng):
            eng.tensor_copy(negCb[:], PS0[:, 0:1])
            return eng.drain()
        return pr.op("v", f, waits=[(T, tch)], inc=V)

    def emit_scores(hd, ps0t_free_v, l_vh, feed=None):
        slot = (hd // 2) % 2
        vslot = hd % 2
        hh = 64 * (hd % 2)
        qd_s, qd_y, qd_g = st2["qdkd_done"][hd // 2]
        cons = st2["cons"]
        gq_free = st2["gq_free"]
        gk_free = st2["gk_free"]
        probs_free = st2["probs_free"]
        stage_free2 = st2["stage_free"]
        first_tile = True
        v_ones = V.n
        pend_ctx = [None]

        def flush_ctx():
            if pend_ctx[0] is None:
                return
            fc, cw, tic = pend_ctx[0]
            pend_ctx[0] = None
            probs_free[tic] = pr.op("t", fc, waits=cw, inc=T)

        for (m, rho) in tiles:
            ti = 2 * m + rho
            if feed:
                next(feed, None)
                next(feed, None)
            pslot = ti % 2
            band_cols = []
            for cc in range(4):
                it = st2["it"]
                sl2 = it % 4
                gsl = it % 4
                cls = [classify(cc * 512 + s * 128, m) for s in range(4)]
                runs = []
                s = 0
                while s < 4:
                    e = s
                    while e + 1 < 4 and cls[e + 1] == cls[s]:
                        e += 1
                    runs.append((cls[s], s, e + 1))
                    s = e + 1
                wl = []
                if (it - 4) in cons:
                    cv, csv = cons[it - 4]
                    if cv:
                        wl.append((V, cv))
                    if csv:
                        wl.append((SS, csv))
                if first_tile and ps0t_free_v:
                    wl.append((V, ps0t_free_v))
                g_run = [r for r in runs if r[0] == "g"]
                gq_t = gk_t = None
                if g_run:
                    _, gs_, ge_ = g_run[0]
                    ns = ge_ - gs_
                    T0a = cc * 512 + gs_ * 128
                    wg = [(DS, qd_s), (DG, qd_g)]
                    if gsl in gq_free:
                        wg.append((pr.sems["T"], gq_free[gsl]))
                    baseq = (hd // 2) * 4 * NT * WQH + rho * NT * 2 * WQH + (hd % 2) * WQH \
                        + T0a * (2 * WQH - 1) + K0 + 128 * m
                    srcq = qdh_sp[baseq:baseq + ns * 128 * (2 * WQH - 1)] \
                        .rearrange("(p w) -> p w", w=2 * WQH - 1) \
                        .rearrange("(s p) w -> p s w", p=128)[:, :, 0:128]
                    gq_t = pr.dma("y", gq[gsl][:, 0:ns * 128].rearrange("p (s j) -> p s j", j=128),
                                  srcq, waits=wg)
                    wk2 = [(DS, qd_s), (DG, qd_g)]
                    if gsl in gk_free:
                        wk2.append((V, gk_free[gsl]))
                    basek = (hd // 2) * 4 * NT * WKH + rho * NT * 2 * WKH + (hd % 2) * WKH \
                        + 128 * m * (2 * WKH - 1) + K0 + T0a
                    srck = kdh_sp[basek:basek + 128 * (2 * WKH - 1)] \
                        .rearrange("(p w) -> p w", w=2 * WKH - 1)[0:128, 0:ns * 128]
                    gk_t = pr.dma("y", gk[gsl][:, 0:ns * 128], srck, waits=wk2)

                def f(eng, m=m, rho=rho, cc=cc, sl2=sl2, gsl=gsl, runs=runs, g_run=g_run, slot=slot, hh=hh):
                    ps = PSS4[sl2]
                    ktt = kt_v[slot][hh:hh + 64, rho * 2048 + m * 128:rho * 2048 + (m + 1) * 128]
                    r = eng.matmul(ps[:], ktt, qt_v[slot][hh:hh + 64, cc * 512:(cc + 1) * 512],
                                   start=True, stop=False)
                    for (k_, s_, e_) in runs:
                        if k_ == "f":
                            r = eng.matmul(ps[:, s_ * 128:e_ * 128], ktt,
                                           se0[hh:hh + 64, 0:(e_ - s_) * 128], start=False, stop=False,
                                           skip_group_check=True)
                        elif k_ == "p":
                            r = eng.matmul(ps[:, s_ * 128:e_ * 128], ktt,
                                           sem_[hh:hh + 64, 0:(e_ - s_) * 128], start=False, stop=False,
                                           skip_group_check=True)
                    for (k_, s_, e_) in runs:
                        if k_ == "f":
                            r = eng.matmul(ps[:, s_ * 128:e_ * 128], se0[hh:hh + 64, 0:128],
                                           qt_v[slot][hh:hh + 64, cc * 512 + s_ * 128:cc * 512 + e_ * 128],
                                           start=False, stop=False, skip_group_check=True)
                        elif k_ == "p":
                            r = eng.matmul(ps[:, s_ * 128:e_ * 128], sem_[hh:hh + 64, 0:128],
                                           qt_v[slot][hh:hh + 64, cc * 512 + s_ * 128:cc * 512 + e_ * 128],
                                           start=False, stop=False, skip_group_check=True)
                    if g_run:
                        _, gs_, ge_ = g_run[0]
                        for s_ in range(gs_, ge_):
                            r = eng.matmul(ps[:, s_ * 128:(s_ + 1) * 128],
                                           gq[gsl][:, (s_ - gs_) * 128:(s_ - gs_ + 1) * 128],
                                           ident[:], start=False, stop=(s_ == ge_ - 1),
                                           skip_group_check=True)
                    return r
                wl2 = list(wl)
                if g_run and gq_t:
                    wl2.append((DY, gq_t))
                t5 = pr.op("t", f, waits=wl2, inc=T)
                if cc == 1:
                    flush_ctx()
                if g_run:
                    gq_free[gsl] = t5

                v_mark = 0
                s_mark = 0
                if g_run:
                    _, gs_, ge_ = g_run[0]
                    band_cols.append((cc * 512 + gs_ * 128, cc * 512 + ge_ * 128))
                    ws = [(T, t5)] + ([(DY, gk_t)] if gk_t else [])
                    if stage_free2.get(pslot):
                        ws.append((SS, stage_free2[pslot]))

                    def f(eng, cc=cc, gs_=gs_, ge_=ge_, sl2=sl2, gsl=gsl, pslot=pslot):
                        return eng.tensor_tensor(stage[pslot][:, cc * 512 + gs_ * 128:cc * 512 + ge_ * 128],
                                          PSS4[sl2][:, gs_ * 128:ge_ * 128],
                                          gk[gsl][:, 0:(ge_ - gs_) * 128], OP.add)
                    v8 = pr.op("v", f, waits=ws, inc=V)
                    gk_free[gsl] = v8
                    v_mark = v8
                for (k_, s_, e_) in runs:
                    if k_ == "g":
                        continue
                    pw2 = [(T, t5)]
                    if v_mark:
                        pw2.append((V, v_mark))
                    if (ti - 2) in probs_free or hd > 0:
                        pw2.append((pr.sems["T"], probs_free.get(ti - 2, probs_free.get(30 + (ti % 2), 0)) or T.n - 1000 if False else probs_free.get(ti - 2, 0)))
                    pw2 = [x for x in pw2 if not (x[0] is pr.sems["T"] and x[1] == 0)]

                    def f(eng, cc=cc, s_=s_, e_=e_, sl2=sl2, pslot=pslot, hd=hd):
                        return eng.activation(probs[pslot][:, cc * 512 + s_ * 128:cc * 512 + e_ * 128],
                                       PSS4[sl2][:, s_ * 128:e_ * 128], AF.Exp,
                                       bias=negctab[:, hd:hd + 1], scale=0.125)
                    s_mark = pr.op("s", f, waits=pw2, inc=SS)
                cons[it] = (v_mark, s_mark)
                st2["it"] += 1

            it = st2["it"]
            if band_cols:
                lo = min(b[0] for b in band_cols)
                hi = max(b[1] for b in band_cols)
                vmax = max((cons[it - 1 - k][0] or 0) for k in range(4))
                pw3 = ([(V, vmax)] if vmax else [])
                if (ti - 2) in probs_free:
                    pw3.append((pr.sems["T"], probs_free[ti - 2]))

                def f(eng, lo=lo, hi=hi, pslot=pslot, hd=hd):
                    return eng.activation(probs[pslot][:, lo:hi], stage[pslot][:, lo:hi],
                                   AF.Exp, bias=negctab[:, hd:hd + 1], scale=0.125)
                exp_val = pr.op("s", f, waits=pw3, inc=SS)
                stage_free2[pslot] = exp_val

            ctx_w = [(SS, SS.n), (DY, l_vh)]
            if first_tile:
                ctx_w.append((V, st2["ctx_V_done"]))

            def f(eng, blk=ti, first=first_tile, last=(ti == 31), vslot=vslot, pslot=pslot):
                r = None
                for ccq in range(4):
                    r = eng.matmul(PSC[0:65, ccq * 512:(ccq + 1) * 512],
                                   vh_v[vslot][:, blk * 65:blk * 65 + 65],
                                   probs[pslot][:, ccq * 512:(ccq + 1) * 512],
                                   start=first, stop=last, skip_group_check=True)
                return r
            pend_ctx[0] = (f, ctx_w, ti)
            first_tile = False
        flush_ctx()
        pf30, pf31 = probs_free.get(30, 0), probs_free.get(31, 0)
        probs_free.clear()
        probs_free[-2] = pf30
        probs_free[-1] = pf31
        st2["cons_ss"] = max((cons[st2["it"] - 1 - k][1] or 0) for k in range(2))

        tlast = T.n
        for ccq in range(4):
            if feed:
                next(feed, None)

            def f(eng, ccq=ccq):
                eng.reciprocal(rcp[:, 0:512], PSC[64:65, ccq * 512:(ccq + 1) * 512])
                eng.drain()
                eng.tensor_copy(rcpb[:, 0:512], rcp[:, 0:512])
                return eng.drain()
            v9 = pr.op("v", f, waits=[(T, tlast if ccq == 0 else T.n)], inc=V)

            def f(eng, ccq=ccq):
                return eng.matmul(PS0[0:64, :], sonesb[:, 0:64], rcpb[:, 0:512],
                                  start=True, stop=True)
            t7 = pr.op("t", f, waits=[(V, v9)], inc=T)

            def f(eng, ccq=ccq, hd=hd):
                eng.tensor_copy(rcpB, PS0[0:64, :])
                eng.drain()
                half = hd % 2
                pair = hd // 2
                eng.tensor_tensor(ctxT[half * 64:half * 64 + 64,
                                       pair * 2048 + ccq * 512:pair * 2048 + (ccq + 1) * 512],
                                  PSC[0:64, ccq * 512:(ccq + 1) * 512], rcpB, OP.mult)
                return eng.drain()
            fv = pr.op("v", f, waits=[(T, t7)], inc=V)
        st2["psfree0"] = fv
        st2["ctx_V_done"] = fv
        if hd % 2 == 1:
            st2["slot_T_done"][slot] = T.n
            st2["slot_V_done"][slot] = V.n
        st2.setdefault("vh_T_done", {})[vslot] = T.n

    for hd in range(0, nheads, 2):
        lks = emit_loads(hd, vh=False)
        for _ in gen_qdkd(hd, lks[0], lks[1]):
            pass
        emit_chead2(hd, lks[0], lks[1])
        if hd + 1 < nheads:
            emit_chead2(hd + 1, lks[0], lks[1])
        kslot = (hd // 2) % 2
        st2["slot_T_done"][kslot] = T.n
        st2["slot_V_done"][kslot] = V.n
        st2.setdefault("slot_S_done", {})[kslot] = SS.n
    pr.barrier()
    st2["slot_T_done"].clear()
    st2["slot_V_done"].clear()
    st2["slot_S_done"] = {}
    st2["plk"] = {}
    for hd in range(nheads):
        lks = emit_loads(hd)
        emit_scores(hd, st2["psfree0"], lks[2])
    pr.barrier()


    wo_v = A[:, 0:8192]
    pr.dma("g", wo_v.rearrange("p (f c) -> p f c", f=8),
           wo_d.rearrange("(f p) c -> p f c", p=128))
    bc_ld = pr.dma("y", bob, bo_row.partition_broadcast(128))
    pr.dma("y", gamb, gam_row.partition_broadcast(128))
    bc_ld = pr.dma("y", betb, bet_row.partition_broadcast(128))
    psw_free_v = 0
    p3_last = {}
    for tt in range(16):
        par = tt % 2
        xt = Bf[:, par * 4096:par * 4096 + 1024]
        h2 = Bf[:, par * 4096 + 1024:par * 4096 + 2048]
        sc1_ = Bf[:, par * 4096 + 2048:par * 4096 + 3072]
        sc2_ = Bf[:, par * 4096 + 3072:par * 4096 + 4096]
        lw = [(V, p3_last[par])] if par in p3_last else []
        xl = pr.dma("y", xt, x_b[2 * tt * 128:2 * tt * 128 + 256:2, :], waits=lw)

        def f(eng, tt=tt):
            r = None
            for pp in range(8):
                for half in range(2):
                    r = eng.matmul(PSW[:, half * 512:half * 512 + 512],
                                   ctxT[:, pp * 2048 + tt * 128:pp * 2048 + (tt + 1) * 128],
                                   wo_v[:, pp * 1024 + half * 512:pp * 1024 + half * 512 + 512],
                                   start=(pp == 0), stop=(pp == 7), skip_group_check=True)
            return r
        wl = [(DG, DG.n)]
        if psw_free_v:
            wl.append((V, psw_free_v))
        t8 = pr.op("t", f, waits=wl, inc=T)

        s4 = ln_chain(xt, None, 1e-5, waits=[(DY, xl)], center_into=sc1_, par=par)

        def f(eng, xt=xt, h2=h2, sc1_=sc1_, par=par):
            eng.tensor_scalar(sc1_, sc1_, colC[:, par:par + 1], None, OP.mult)
            eng.drain()
            eng.tensor_tensor(h2, PSW[:], bob, OP.add)
            eng.drain()
            eng.tensor_tensor(h2, h2, sc1_, OP.add)
            return eng.drain()
        v12 = pr.op("v", f, waits=[(SS, s4), (T, t8), (DY, bc_ld)], inc=V)
        psw_free_v = v12

        s6 = ln_chain(h2, None, 1e-12, waits=[(V, v12)], par=par)

        def f(eng, h2=h2, sc2_=sc2_, par=par):
            eng.tensor_scalar(h2, h2, colC[:, par:par + 1], None, OP.mult)
            eng.drain()
            eng.tensor_tensor(sc2_, h2, gamb, OP.mult)
            eng.drain()
            eng.tensor_tensor(sc2_, sc2_, betb, OP.add)
            return eng.drain()
        v14 = pr.op("v", f, waits=[(SS, s6)], inc=V)

        s8 = ln_chain(sc2_, None, 1e-5, waits=[(V, v14)], par=par)

        def f(eng, xt=xt, sc2_=sc2_, h2=h2, par=par):
            eng.tensor_scalar(sc2_, sc2_, colC[:, par:par + 1], None, OP.mult)
            eng.drain()
            eng.tensor_tensor(h2, xt, sc2_, OP.add)
            return eng.drain()
        v16 = pr.op("v", f, waits=[(SS, s8)], inc=V)
        p3_last[par] = v16
        pr.dma("s", out1_sp[tt * 128:(tt + 1) * 128, :], h2, waits=[(V, v16)])
    pr.barrier()

    ln2T = C[:, 0:16384]
    lnrow2 = [A[:, 0:1024], A[:, 1024:2048]]
    psw_free_v = 0
    p4_last = {}
    for bb in range(16):
        par = bb % 2
        xt = Bf[:, par * 4096:par * 4096 + 1024]
        lw = [(V, p4_last[par])] if par in p4_last else [(V, V.n)] if bb == 0 else []
        xl = pr.dma("y", xt, out1_sp[bb:bb + 16 * 127 + 1:16, :], waits=lw)
        s10 = ln_chain(xt, None, 1e-5, waits=[(DY, xl)], par=par)

        def f(eng, xt=xt, par=par, bb=bb):
            eng.tensor_scalar(lnrow2[par], xt, colC[:, par:par + 1], None, OP.mult)
            return eng.drain()
        v19 = pr.op("v", f, waits=[(SS, s10)], inc=V)
        p4_last[par] = v19

        def f(eng, par=par):
            r = None
            for j in range(8):
                r = eng.matmul(PSW[:, j * 128:(j + 1) * 128], lnrow2[par][:, j * 128:(j + 1) * 128],
                               ident[:], start=True, stop=True)
            return r
        wl = [(V, v19)]
        if psw_free_v:
            wl.append((V, psw_free_v))
        t9 = pr.op("t", f, waits=wl, inc=T)

        def f(eng, bb=bb):
            src = PSW[:].rearrange("p (j t) -> p j t", t=128)
            dst = ln2T.rearrange("p (j t) -> p j t", t=2048)[:, :, bb::16]
            eng.tensor_copy(dst, src)
            return eng.drain()
        v20 = pr.op("v", f, waits=[(T, t9)], inc=V)
        psw_free_v = v20

        def f(eng, bb=bb):
            r = None
            l2v = ln2T.rearrange("p (j t) -> p j t", t=2048)
            for fi in range(8):
                r = eng.matmul(PS0[:, 0:16], l2v[:, fi, bb::16],
                               sgw[:, fi * 16:(fi + 1) * 16], start=(fi == 0), stop=(fi == 7))
            return r
        t10 = pr.op("t", f, waits=[(V, v20)], inc=T)

        def f(eng, bb=bb):
            eng.tensor_copy(logits[:, bb * 16:(bb + 1) * 16], PS0[:, 0:16])
            return eng.drain()
        pr.op("v", f, waits=[(T, t10)], inc=V)
    pr.barrier()

    if phases < 5:
        return _finish(nc, pr, es)
    def f(eng):
        eng.activation(rt_sc[:], logits[:], AF.Sigmoid)
        return eng.drain()
    s11 = pr.op("s", f, inc=SS)

    def f(eng):
        eng.memset(rt_neg[:], -1e30)
        eng.drain()
        eng.tensor_tensor(rt_sfc[:], rt_sc[:], sgbr[:], OP.add)
        eng.drain()
        g4 = rt_sfc.rearrange("p (x k) -> p x k", k=4)
        eng.tensor_tensor(rt_t[0][:], g4[:, :, 0], g4[:, :, 1], OP.max)
        eng.tensor_tensor(rt_t[1][:], g4[:, :, 0], g4[:, :, 1], OP.min)
        eng.tensor_tensor(rt_t[2][:], g4[:, :, 2], g4[:, :, 3], OP.max)
        eng.tensor_tensor(rt_t[3][:], g4[:, :, 2], g4[:, :, 3], OP.min)
        eng.drain()
        eng.tensor_tensor(rt_t[4][:], rt_t[0][:], rt_t[2][:], OP.max)
        eng.tensor_tensor(rt_t[5][:], rt_t[0][:], rt_t[2][:], OP.min)
        eng.drain()
        eng.tensor_tensor(rt_t[0][:], rt_t[1][:], rt_t[3][:], OP.max)
        eng.drain()
        eng.tensor_tensor(rt_t[5][:], rt_t[5][:], rt_t[0][:], OP.max)
        eng.drain()
        eng.tensor_tensor(rt_gs[:], rt_t[4][:], rt_t[5][:], OP.add)
        return eng.drain()
    v22 = pr.op("v", f, waits=[(SS, s11)], inc=V)

    def f(eng):
        gs4 = rt_gs.rearrange("p (x g) -> p x g", g=4)
        for g in range(4):
            dsts = []
            k = 0
            for g2 in range(4):
                if g2 == g:
                    continue
                eng.tensor_tensor(rt_t[k][:, 0:16], gs4[:, :, g2], gs4[:, :, g], OP.is_gt)
                dsts.append(k)
                k += 1
            for g2 in range(g):
                eng.tensor_tensor(rt_t[k][:, 0:16], gs4[:, :, g2], gs4[:, :, g], OP.is_equal)
                dsts.append(k)
                k += 1
            eng.drain()
            acc = dsts[0]
            for d2 in dsts[1:]:
                eng.tensor_tensor(rt_t[acc][:, 0:16], rt_t[acc][:, 0:16], rt_t[d2][:, 0:16], OP.add)
                eng.drain()
            eng.tensor_scalar(rt_gm[:, g * 16:(g + 1) * 16], rt_t[acc][:, 0:16], 1.5, None, OP.is_lt)
            eng.drain()
        for g in range(4):
            for kk in range(4):
                eng.tensor_copy(rt_em[:, 4 * g + kk::16], rt_gm[:, g * 16:(g + 1) * 16])
        eng.drain()
        eng.select(rt_msk[:], rt_em[:], rt_sfc[:], rt_neg[:], add_drain=True)
        return eng.drain()
    v23 = pr.op("v", f, waits=[(V, v22)], inc=V)

    def f(eng):
        for b in range(16):
            for e in range(NE):
                eng.tensor_scalar(Bf[:, (b * 16 + e) * 16:(b * 16 + e) * 16 + 16],
                                  rt_msk[:, b * 16:(b + 1) * 16],
                                  rt_msk[:, b * 16 + e:b * 16 + e + 1], None, OP.is_gt)
        eng.drain()
        for b in range(16):
            for e in range(NE):
                eng.tensor_reduce(rt_rank[:, b * 16 + e:b * 16 + e + 1],
                                  Bf[:, (b * 16 + e) * 16:(b * 16 + e) * 16 + 16], axis=AX, op=OP.add)
        eng.drain()
        eng.memset(rt_eqr[:], 0.0)
        eng.drain()
        for b in range(16):
            for e in range(1, NE):
                eng.tensor_scalar(Bf[:, (b * 16 + e) * 16:(b * 16 + e) * 16 + e],
                                  rt_msk[:, b * 16:b * 16 + e],
                                  rt_msk[:, b * 16 + e:b * 16 + e + 1], None, OP.is_equal)
        eng.drain()
        for b in range(16):
            for e in range(1, NE):
                eng.tensor_reduce(rt_eqr[:, b * 16 + e:b * 16 + e + 1],
                                  Bf[:, (b * 16 + e) * 16:(b * 16 + e) * 16 + e], axis=AX, op=OP.add)
        eng.drain()
        eng.tensor_tensor(rt_rank[:], rt_rank[:], rt_eqr[:], OP.add)
        eng.drain()
        eng.tensor_scalar(rt_sel[:], rt_rank[:], 3.5, None, OP.is_lt)
        eng.drain()
        eng.tensor_tensor(rt_w[:], rt_sc[:], rt_sel[:], OP.mult)
        eng.drain()
        eng.tensor_reduce(rt_ws[:], rt_w.rearrange("p (x e) -> p x e", e=16), axis=AX, op=OP.add)
        eng.drain()
        eng.tensor_scalar(rt_wsi[:], rt_ws[:], 1e-20, None, OP.add)
        eng.drain()
        eng.reciprocal(rt_ws[:], rt_wsi[:])
        eng.drain()
        for b in range(16):
            eng.tensor_scalar(rt_cw[:, b * 16:(b + 1) * 16], rt_w[:, b * 16:(b + 1) * 16],
                              rt_ws[:, b:b + 1], 2.5, OP.mult, OP.mult)
        return eng.drain()
    v24 = pr.op("v", f, waits=[(V, v23)], inc=V)
    if dbg:
        def f(eng):
            eng.tensor_copy(C[:, 0:512].bitcast(F32), logits[:])
            eng.tensor_copy(C[:, 512:1024].bitcast(F32), rt_sfc[:])
            eng.tensor_copy(C[:, 1024:1152].bitcast(F32), rt_gs[:])
            eng.tensor_copy(C[:, 1152:1280].bitcast(F32), rt_gm[:])
            eng.tensor_copy(C[:, 1536:2048].bitcast(F32), rt_rank[:])
            eng.tensor_copy(C[:, 2048:2560].bitcast(F32), rt_sel[:])
            return eng.drain()
        vdmp = pr.op("v", f, waits=[(V, v24)], inc=V)
        pr.dma("s", rt_dump[:, 0:256], C[:, 0:512].bitcast(F32), waits=[(V, vdmp)])
        pr.dma("s", rt_dump[:, 256:512], C[:, 512:1024].bitcast(F32), waits=[(V, vdmp)])
        pr.dma("s", rt_dump[:, 512:576], C[:, 1024:1152].bitcast(F32), waits=[(V, vdmp)])
        pr.dma("s", rt_dump[:, 640:704], C[:, 1152:1280].bitcast(F32), waits=[(V, vdmp)])
        pr.dma("s", rt_dump[:, 768:1024], C[:, 1536:2048].bitcast(F32), waits=[(V, vdmp)])
        pr.dma("s", rt_dump[:, 1024:1280], C[:, 2048:2560].bitcast(F32), waits=[(V, vdmp)])
    for e in range(NE):
        def f(eng, e=e):
            eng.tensor_copy(rt_c1[:], rt_cw[:, e::16])
            return eng.drain()
        vcp = pr.op("v", f, waits=[(V, v24)] + ([(DS, DS.n)] if e else []), inc=V)
        pr.dma("s", cw_sp[e * NT:(e + 1) * NT].rearrange("(p b) -> p b", b=16),
               rt_c1[:], waits=[(V, vcp)])
    pr.barrier()

    if phases < 6:
        return _finish(nc, pr, es)
    wsg_v = A[:, 16384:24576]
    wsu_v = A[:, 24576:32768]
    wg_v = [A[:, 0:4096], A[:, 4096:8192]]
    wu_v = [A[:, 8192:12288], A[:, 12288:16384]]
    pr.dma("g", wsg_v.rearrange("p (f c) -> p f c", f=8),
           wsgT_d.rearrange("(f p) c -> p f c", p=128))
    pr.dma("g", wsu_v.rearrange("p (f c) -> p f c", f=8),
           wsuT_d.rearrange("(f p) c -> p f c", p=128))
    wslot_free = {}
    ps0_free = 0
    ps1_free = 0
    actb_free = 0
    awb_free = {}
    last_t_of = {}
    cw_free = {}
    aidx = 0
    for e in range(NE + 1):
        shared = (e == NE)
        sl = e % 2
        nit = 8 if shared else 4
        if not shared:
            ww = [(T, last_t_of[e - 2])] if (e - 2) in last_t_of else []
            pr.dma("g", wg_v[sl].rearrange("p (f i) -> p f i", f=8),
                   wgT_d[e].rearrange("(f p) i -> p f i", p=128), waits=ww)
            wload = pr.dma("g", wu_v[sl].rearrange("p (f i) -> p f i", f=8),
                           wuT_d[e].rearrange("(f p) i -> p f i", p=128))
        last_t = 0
        for cc in range(4):
            if not shared:
                cwl = [(GP, cw_free[cc % 2])] if (cc % 2) in cw_free else []
                cwt = pr.dma("y", cwB[cc % 2],
                             cw_sp[e * NT + cc * 512:e * NT + (cc + 1) * 512]
                             .rearrange("(o t) -> o t", o=1).partition_broadcast(128),
                             waits=cwl)
            for ii in range(nit):
                wv1 = wsg_v if shared else wg_v[sl]
                wv2 = wsu_v if shared else wu_v[sl]
                wid = 1024 if shared else 512

                rot = aidx % 2
                pg = PS0 if rot == 0 else PSW[:, 0:512]
                pu = PS1 if rot == 0 else PSW[:, 512:1024]

                def f(eng, wv1=wv1, cc=cc, ii=ii, wid=wid, pg=pg):
                    r = None
                    for fi in range(8):
                        r = eng.matmul(pg[:] if pg is PS0 else pg,
                                       wv1[:, fi * wid + ii * 128:fi * wid + ii * 128 + 128],
                                       ln2T[:, fi * 2048 + cc * 512:fi * 2048 + (cc + 1) * 512],
                                       start=(fi == 0), stop=(fi == 7))
                    return r
                wl = [] if shared else [(DG, wload)]
                if st2.get(("p5g", rot)):
                    wl.append((SS, st2[("p5g", rot)]))
                tg = pr.op("t", f, waits=wl, inc=T)

                def f(eng, wv2=wv2, cc=cc, ii=ii, wid=wid, pu=pu):
                    r = None
                    for fi in range(8):
                        r = eng.matmul(pu[:] if pu is PS1 else pu,
                                       wv2[:, fi * wid + ii * 128:fi * wid + ii * 128 + 128],
                                       ln2T[:, fi * 2048 + cc * 512:fi * 2048 + (cc + 1) * 512],
                                       start=(fi == 0), stop=(fi == 7))
                    return r
                wl = []
                if st2.get(("p5u", rot)):
                    wl.append((V, st2[("p5u", rot)]))
                tu = pr.op("t", f, waits=wl, inc=T)
                last_t = tu

                def f(eng, pg=pg):
                    eng.activation(silb, pg[:] if pg is PS0 else pg, AF.Silu)
                    return eng.drain()
                silw = [(T, tg)]
                if st2.get("p5_lastvv"):
                    silw.append((V, st2["p5_lastvv"]))
                sg2 = pr.op("s", f, waits=silw, inc=SS)
                st2[("p5g", rot)] = sg2

                asl = aidx % 2
                aw = [(T, tu), (SS, sg2)]
                if shared and asl in awb_free:
                    aw.append((DS, awb_free[asl]))
                if not shared and actb_free:
                    aw.append((GP, actb_free))

                def f(eng, shared=shared, asl=asl, pu=pu):
                    dst = actwb[asl] if shared else actb
                    eng.scalar_tensor_tensor(dst, pu[:] if pu is PS1 else pu, 1.0, silb, OP.mult, OP.mult)
                    return eng.drain()
                vv = pr.op("v", f, waits=aw, inc=V)
                st2[("p5u", rot)] = vv
                st2["p5_lastvv"] = vv

                if not shared:
                    gw2 = [(V, vv), (DY, cwt)]
                    if asl in awb_free:
                        gw2.append((DS, awb_free[asl]))

                    def f(eng, asl=asl, cc=cc):
                        eng.tensor_tensor(actwb[asl], actb, cwB[cc % 2], OP.mult)
                        return eng.drain()
                    gg = pr.op("g", f, waits=gw2, inc=GP)
                    cw_free[cc % 2] = gg
                    actb_free = gg
                    base = e * EI * NT + ii * 128 * NT
                    dd = pr.dma("s", actw_sp[base:base + 128 * NT]
                                .rearrange("(p t) -> p t", t=NT)[:, cc * 512:(cc + 1) * 512],
                                actwb[asl], waits=[(GP, gg)])
                else:
                    base = ii * 128 * NT
                    dd = pr.dma("s", gs_sp[base:base + 128 * NT]
                                .rearrange("(p t) -> p t", t=NT)[:, cc * 512:(cc + 1) * 512],
                                actwb[asl], waits=[(V, vv)])
                awb_free[asl] = dd
                aidx += 1
        if not shared:
            last_t_of[e] = last_t
    pr.barrier()

    if phases < 7:
        return _finish(nc, pr, es)
    y_ps_free = {}
    last_ty = 0
    prev_out = 0
    for half in range(2):
        wl0 = [(T, T.n)]
        for e in range(NE):
            pr.dma("g", A[:, e * 2048:(e + 1) * 2048].rearrange("p (i c) -> p i c", i=4),
                   wdT_d[e].rearrange("(i p) c -> p i c", p=128)[:, :, half * 512:half * 512 + 512],
                   waits=wl0 if e == 0 else [])
        dwl = pr.dma("g", A[:, 32768:36864].rearrange("p (i c) -> p i c", i=8),
                     wsdT_d.rearrange("(i p) c -> p i c", p=128)[:, :, half * 512:half * 512 + 512])
        for ts_ in range(16):
            for e in range(NE):
                pr.dma("y", C[:, e * 512:(e + 1) * 512].rearrange("p (i t) -> p i t", t=128),
                       actw_sp[e * EI * NT:(e + 1) * EI * NT]
                       .rearrange("(i p t) -> p i t", p=128, t=NT)[:, :, ts_ * 128:(ts_ + 1) * 128],
                       waits=[(T, last_ty)] if e == 0 else [])
            aload = pr.dma("y", C[:, 8192:9216].rearrange("p (i t) -> p i t", t=128),
                           gs_sp.rearrange("(i p t) -> p i t", p=128, t=NT)[:, :, ts_ * 128:(ts_ + 1) * 128])
            slot = ts_ % 2

            def f(eng, slot=slot):
                r = None
                first = True
                for e in range(NE):
                    for ii in range(4):
                        r = eng.matmul(PSS[slot][:], C[:, e * 512 + ii * 128:e * 512 + (ii + 1) * 128],
                                       A[:, e * 2048 + ii * 512:e * 2048 + (ii + 1) * 512],
                                       start=first, stop=False, skip_group_check=True)
                        first = False
                for jj in range(8):
                    r = eng.matmul(PSS[slot][:], C[:, 8192 + jj * 128:8192 + (jj + 1) * 128],
                                   A[:, 32768 + jj * 512:32768 + (jj + 1) * 512],
                                   start=False, stop=(jj == 7), skip_group_check=True)
                return r
            wl = [(DY, aload), (DG, dwl)]
            if (slot, "ps") in y_ps_free:
                wl.append((V, y_ps_free[(slot, "ps")]))
            ty = pr.op("t", f, waits=wl, inc=T)
            last_ty = ty

            if half == 0:
                def f(eng, ts_=ts_, slot=slot):
                    eng.tensor_copy(Bf[:, ts_ * 512:(ts_ + 1) * 512], PSS[slot][:])
                    return eng.drain()
                vv = pr.op("v", f, waits=[(T, ty)], inc=V)
                y_ps_free[(slot, "ps")] = vv
            else:
                xl = pr.dma("y", o1t[:], out1_sp[ts_ * 128:(ts_ + 1) * 128, :], waits=[(V, V.n)])
                ep_w = [(DY, xl)]
                if prev_out:
                    ep_w.append((DS, prev_out))
                s12 = ln_chain(o1t, None, 1e-5, waits=ep_w, center_into=ep1)

                def f(eng, ts_=ts_, slot=slot):
                    eng.tensor_scalar(ep1[:], ep1[:], colC[:, 0:1], None, OP.mult)
                    eng.drain()
                    eng.tensor_tensor(ep2[:, 0:512], ep1[:, 0:512], Bf[:, ts_ * 512:(ts_ + 1) * 512], OP.add)
                    eng.drain()
                    eng.tensor_tensor(ep2[:, 512:1024], ep1[:, 512:1024], PSS[slot][:], OP.add)
                    return eng.drain()
                v25 = pr.op("v", f, waits=[(SS, s12), (T, ty)], inc=V)
                y_ps_free[(slot, "ps")] = v25

                s13 = ln_chain(ep2, None, 1e-5, waits=[(V, v25)])

                def f(eng):
                    eng.tensor_scalar(ep2[:], ep2[:], colC[:, 0:1], None, OP.mult)
                    eng.drain()
                    eng.tensor_tensor(ep1[:], o1t[:], ep2[:], OP.add)
                    return eng.drain()
                v26 = pr.op("v", f, waits=[(SS, s13)], inc=V)
                prev_out = pr.dma("s", outp[ts_ * 128:(ts_ + 1) * 128, :], ep1[:], waits=[(V, v26)])

    return _finish(nc, pr, es)


def host_maps(inp):
    bf = ml_dtypes.bfloat16
    f32 = np.float32
    ac = np.ascontiguousarray
    x = np.asarray(inp["L0_attn_hidden_states"], dtype=f32)
    dist = np.asarray(inp["L0_attn_dist_emb"], dtype=f32)
    Wq = ac(np.asarray(inp["L0_attn_Wq"], f32)); Wk = ac(np.asarray(inp["L0_attn_Wk"], f32))
    Wv = ac(np.asarray(inp["L0_attn_Wv"], f32)); Wo = ac(np.asarray(inp["L0_attn_Wo"], f32))
    bq = np.asarray(inp["L0_attn_bq"], f32); bk = np.asarray(inp["L0_attn_bk"], f32)
    bv = np.asarray(inp["L0_attn_bv"], f32); bo = np.asarray(inp["L0_attn_bo"], f32)
    lng = np.asarray(inp["L0_attn_ln_gamma"], f32); lnb = np.asarray(inp["L0_attn_ln_beta"], f32)
    gw = np.asarray(inp["L0_ffn_gate_w"], f32)
    gbias = np.asarray(inp["L0_ffn_gate_bias"], f32)
    Wg = np.asarray(inp["L0_ffn_Wg"], f32)
    Wu = np.asarray(inp["L0_ffn_Wu"], f32)
    Wd = np.asarray(inp["L0_ffn_Wd"], f32)
    Wsg = np.asarray(inp["L0_ffn_Ws_g"], f32)
    Wsu = np.asarray(inp["L0_ffn_Ws_u"], f32)
    Wsd = np.asarray(inp["L0_ffn_Ws_d"], f32)

    w = np.arange(WQH)
    com = {
        "wq_d": Wq, "wk_d": Wk, "wv_d": Wv, "wo_d": Wo,
        "bqr_d": bq[None, :], "bkr_d": bk[None, :],
        "ones512_d": np.ones((1, 512), bf),
        "bv_row": bv[None, :], "bo_row": bo[None, :],
        "gam_row": lng[None, :], "bet_row": lnb[None, :],
        "tq0_d": ac(np.tile(dist[np.clip(2 * (K0 - w) + 1023, 0, 2046)].T, (2, 1))),
        "tk0_d": ac(np.tile(dist[np.clip(2 * (w - K0) + 1023, 0, 2046)].T, (2, 1))),
        "e0_d": ac(np.tile(np.tile(dist[0][:, None], (1, 512)), (2, 1))),
        "em_d": ac(np.tile(np.tile(dist[2046][:, None], (1, 512)), (2, 1))),
        "id_d": np.eye(128, dtype=bf),
        "onesf_d": np.ones((1, 128), f32),
        "onesb_d": np.ones((1, 128), bf),
        "ones64_d": np.ones((64, 1), bf),
        "gw_d": ac(gw.T),
        "gbr_d": ac(np.tile(gbias, 16)[None, :]),
        "wgT_d": ac(Wg.transpose(0, 2, 1)),
        "wuT_d": ac(Wu.transpose(0, 2, 1)),
        "wdT_d": ac(Wd.transpose(0, 2, 1)),
        "wsgT_d": ac(Wsg.T), "wsuT_d": ac(Wsu.T), "wsdT_d": ac(Wsd.T),
    }
    in_maps = []
    for core in range(8):
        b, h = core // 2, core % 2
        xp = x[b]
        if h:
            xp = xp.reshape(NT, 2, E)[:, ::-1, :].reshape(S, E)
        m = dict(com)
        m["x_b"] = ac(xp)
        m["tq1_d"] = ac(np.tile(dist[np.clip(2 * (K0 - w + h) + 1022, 0, 2046)].T, (2, 1)))
        m["tk1_d"] = ac(np.tile(dist[np.clip(2 * (w - K0 + h) + 1022, 0, 2046)].T, (2, 1)))
        in_maps.append(m)

    return in_maps


def kernel(**inp):
    in_maps = host_maps(inp)
    nc = build_program()
    res = run_bass_kernel_spmd(nc, in_maps, core_ids=list(range(8)))
    out = np.empty((4, S, E), np.float32)
    for core in range(8):
        b, h = core // 2, core % 2
        out[b, h::2] = np.asarray(res.results[core]["outp"], np.float32)
    return out
